# version 08
# baseline (speedup 1.0000x reference)
import contextlib
import numpy as np
import ml_dtypes
import concourse.bass as bass
import concourse.mybir as mybir
from concourse.bass_utils import run_bass_kernel_spmd

F32 = mybir.dt.float32
BF16 = mybir.dt.bfloat16
U32 = mybir.dt.uint32
AF = mybir.ActivationFunctionType
ALU = mybir.AluOpType
AX = mybir.AxisListType

NC = 8
B = 2
S = 8192
E = 1024
TOKC = 2048
NT = TOKC // 128
NJ = E // 128
KD = 64
F0 = 4096
FSL = F0 // NC
HSL = E // NC
H1 = 16
C1 = 64
NE = 8
CAP = 4096
DFF = 3072
NCH = 2 * TOKC
KVN = KD * TOKC + TOKC * 65
MW1N = E * FSL
MW2N = F0 * HSL
MWN = 2 * MW1N + 2 * MW2N
EPS = 1e-5
SENT = 1.0e9
BFD = ml_dtypes.bfloat16


class Ctr:
    def __init__(self, nc, name, step=1):
        self.sem = nc.alloc_semaphore(name)
        self.n = 0
        self.step = step

    def inc(self, inst, times=1):
        inst.then_inc(self.sem, self.step * times)
        self.n += self.step * times
        return self.n

    def mark(self):
        return (self.sem, self.n)


def wt(eng, mark):
    sem, val = mark
    if val > 0:
        eng.wait_ge(sem, val)


class K:
    pass


def V(k, v, fn):
    if k.ve.n:
        v.wait_ge(k.ve.sem, k.ve.n)
    return k.ve.inc(fn())


def Vf(k, fn):
    return k.ve.inc(fn())


def Vbar(k, v):
    if k.ve.n:
        v.wait_ge(k.ve.sem, k.ve.n)


def A(k, s, fn):
    if k.ae.n:
        s.wait_ge(k.ae.sem, k.ae.n)
    return k.ae.inc(fn())


def G(k, g, fn):
    if k.ge.n:
        g.wait_ge(k.ge.sem, k.ge.n)
    return k.ge.inc(fn())


def build_nc(dump=None, upto=99):
    nc = bass.Bass("TRN2", target_bir_lowering=False)
    k = K()
    k.nc = nc
    k.dump = dump
    k.upto = upto
    d = {}
    k.d = d

    def din(name, shape, dt=F32):
        d[name] = nc.dram_tensor(name, list(shape), dt, kind="ExternalInput")

    def dint(name, shape, dt=F32):
        d[name] = nc.dram_tensor(name, list(shape), dt, kind="Internal")

    din("x", (TOKC, E))
    din("wq0", (E, KD)); din("wk0", (E, KD)); din("wv1", (E, KD)); din("wv2", (KD, E))
    din("gw1T", (E, E)); din("gb1", (1, E)); din("gw2T", (E, NE)); din("gb2", (1, NE))
    din("W1T_s", (E, FSL)); din("TW1T_s", (NE, E, FSL))
    din("W2T_s", (F0, HSL)); din("TW2T_s", (NE, F0, HSL))
    din("b1", (1, F0)); din("Tb1", (NE, F0)); din("b2", (1, E)); din("Tb2", (NE, E))
    din("Wq1", (E, E)); din("Wk1", (E, E)); din("Wv1_", (E, E)); din("Wo1", (E, E))
    din("bq", (1, E)); din("bk", (1, E)); din("bv", (1, E)); din("bo", (1, E))
    din("wgate", (E, NE))
    din("cfc", (E, DFF)); din("cpj", (DFF // 2, E))
    din("identf", (128, 128)); din("identb", (128, 128), BF16)
    din("maskL0", (128, 4 * 512), BF16)
    din("maskL1", (128, C1 * C1), BF16)
    din("iota8", (128, NE))
    din("tokid_pm", (128, 16))
    din("onesf", (128, 128))
    d["y"] = nc.dram_tensor("y", [TOKC, E], F32, kind="ExternalOutput")
    if dump is not None:
        d["dbg"] = nc.dram_tensor("dbg", [128, 2048], F32, kind="ExternalOutput")

    dint("ar_ph_in", (B, 128, NJ)); dint("ar_ph_out", (B, 128, NJ))
    dint("kv_in", (1, KVN), BF16); dint("kv_out", (NC, KVN), BF16)
    dint("mw_in", (1, MWN), BF16); dint("mw_out", (NC, MWN), BF16)
    dint("mw1t_mine", (E, F0), BF16)
    dint("mw2t_mine", (F0, E), BF16)
    dint("u_dram", (F0 // 128, 128, TOKC), BF16)
    dint("xn1_in", (TOKC, E), BF16); dint("xn1_out", (NC * TOKC, E), BF16)
    dint("cnt_in", (1, NE)); dint("cnt_out", (NC, NE))
    dint("cm_in", (1, NC * CAP)); dint("cm_out1", (1, NC * CAP))
    dint("eo_in", (CAP, E), BF16); dint("eo_out", (NC * CAP, E), BF16)

    k.gd = Ctr(nc, "gdma", 16)
    k.hd = Ctr(nc, "hdma", 16)
    k.ve = Ctr(nc, "vsem")
    k.ae = Ctr(nc, "asem")
    k.ge = Ctr(nc, "gsem")
    k.pe = Ctr(nc, "pesem")
    k.cc = [Ctr(nc, "cc%d" % i) for i in range(10)]
    k.ncc = 0
    k.sp = [Ctr(nc, "sp%d" % i, 16) for i in range(10)]
    k.gsp = [Ctr(nc, "gsp%d" % i, 16) for i in range(4)]
    k.all_ctrs = [k.gd, k.hd, k.ve, k.ae, k.ge, k.pe] + k.cc + k.sp + k.gsp

    k.sbn = {}

    def salloc(stack, name, shape, dt=F32):
        t = stack.enter_context(nc.sbuf_tensor(name, list(shape), dt))
        k.sbn[name] = t
        return t

    k.salloc = salloc
    glob = contextlib.ExitStack()
    k.glob = glob
    k.xT = salloc(glob, "xT", (128, NJ * TOKC))
    k.idf = salloc(glob, "idf", (128, 128))
    k.idb = salloc(glob, "idb", (128, 128), BF16)
    k.onesf = salloc(glob, "ones_sb", (128, 128))
    k.pmv = salloc(glob, "pmv", (128, 128))
    k.PM = {"gb1": 0, "b1": 8, "b2": 40, "bq": 48, "bk": 56, "bv": 64,
            "bo": 72, "mb1": 80, "mb2": 112}
    k.rtrw = salloc(glob, "rtrw", (128, NT * 2))
    k.iota8_row = salloc(glob, "iota8r", (1, NE))
    k.rtri = salloc(glob, "rtri", (128, NT * 2))
    k.oh1 = salloc(glob, "oh1", (128, NT * NE))
    k.oh2 = salloc(glob, "oh2", (128, NT * NE))
    if dump is not None:
        k.dbg_sb = salloc(glob, "dbg_sb", (128, 128))
        k.dbg_col = 0

    k.engines = [nc.gpsimd, nc.scalar, nc.tensor, nc.vector, nc.sync]
    k.pid = [e.partition_id() for e in k.engines]

    span17 = contextlib.ExitStack()
    k.xnT = salloc(span17, "xnT", (128, NJ * TOKC), BF16)
    span35 = contextlib.ExitStack()
    k.qp = salloc(span35, "qp", (KD, TOKC), BF16)
    k.wv2f = salloc(span35, "wv2f", (KD, E))
    k.mask0 = salloc(span35, "mask0", (128, 4 * 512), BF16)

    p1_ln(k)
    if upto >= 3:
        p3_proj(k)
    if upto >= 4:
        p4_gmw(k)
    if upto >= 5:
        p5_attn0(k)
    span35.close()
    if upto >= 6:
        p6_ffn(k)
    if upto >= 7:
        p7_l1(k)
    if hasattr(k, 'p7loc'):
        k.p7loc.close()
    span17.close()
    if upto >= 8:
        p8_moe(k)
    finish(k)
    return nc, k


def core_switch(k, body_fn):
    snap = [c.n for c in k.all_ctrs]
    final = None
    for cid in k.nc.Switch(engines=k.engines, index=k.pid, n=NC):
        for c, v in zip(k.all_ctrs, snap):
            c.n = v
        body_fn(cid)
        now = [c.n for c in k.all_ctrs]
        if final is None:
            final = now
        else:
            assert now == final, ("switch bodies diverge in sem counts",
                                  now, final)
    for c, v in zip(k.all_ctrs, final):
        c.n = v


def coll(k, gp, kind, op, src, dst):
    cc = k.cc[k.ncc]
    k.ncc += 1
    inst = gp.collective_compute(
        kind, op, replica_groups=[list(range(NC))],
        ins=[src.ap().opt()], outs=[dst.ap().opt()])
    inst.then_inc(cc.sem, 1)
    cc.n += 1
    return cc.mark()


def dmp(k, get_ap, rows=128):
    if k.dump is None:
        return
    nc = k.nc
    src_ap = get_ap()
    ncol = 1
    for dsz in src_ap.shape[1:]:
        ncol *= dsz
    for c0 in range(0, ncol, 128):
        w = min(128, ncol - c0)
        col = k.dbg_col + c0
        sub = src_ap[:, c0:c0 + w] if len(src_ap.shape) == 2 else \
            src_ap.rearrange("p a b -> p (a b)")[:, c0:c0 + w]
        with nc.Block() as blk:
            @blk.vector
            def _(v, sub=sub, w=w):
                V(k, v, lambda: v.tensor_copy(k.dbg_sb[:rows, 0:w], sub))
            mk = k.ve.n

            @blk.sync
            def _(s, w=w, col=col, mk=mk):
                s.wait_ge(k.ve.sem, mk)
                k.hd.inc(s.dma_start(
                    bass.AP(k.d["dbg"], col, [[2048, rows], [1, w]]),
                    k.dbg_sb[:rows, 0:w]))
                s.wait_ge(k.hd.sem, k.hd.n)
    k.dbg_col += ncol


def finish(k):
    nc = k.nc
    d = k.d
    if k.upto < 8:
        ytile = nc.alloc_sbuf_tensor("fin_yt", [128, E], F32)
        with nc.psum_tensor("finp", [128, 128], F32) as ps:
            for i in range(NT):
                with nc.Block() as blk:
                    pe_af = {}

                    @blk.tensor
                    def _(t, i=i):
                        for j in range(NJ):
                            if j >= 1:
                                t.wait_ge(k.ve.sem, k.ve.n + j)
                            k.pe.inc(t.transpose(
                                ps[:, :],
                                k.xT[:, j * TOKC + i * 128:j * TOKC + (i + 1) * 128],
                                k.idf[:, :]))
                            pe_af[j] = k.pe.n

                    ve0 = k.ve.n

                    @blk.vector
                    def _(v, i=i):
                        for j in range(NJ):
                            v.wait_ge(k.pe.sem, pe_af[j])
                            V(k, v, lambda j=j: v.tensor_copy(
                                ytile[:, j * 128:(j + 1) * 128], ps[:, :]))
                    vm = k.ve.mark()

                    @blk.sync
                    def _(s, i=i):
                        wt(s, vm)
                        k.hd.inc(s.dma_start(d["y"].ap()[i * 128:(i + 1) * 128, :],
                                             ytile[:, :]))
                        s.wait_ge(k.hd.sem, k.hd.n)
    with nc.Block() as blk:
        @blk.sync
        def _(s):
            s.wait_ge(k.hd.sem, k.hd.n)
            for c_ in k.sp + k.gsp:
                if c_.n:
                    s.wait_ge(c_.sem, c_.n)

        @blk.gpsimd
        def _(g):
            g.wait_ge(k.gd.sem, k.gd.n)


def p1_ln(k):
    nc = k.nc
    d = k.d
    loc = contextlib.ExitStack()
    xtt = k.salloc(loc, "xtt", (128, 2 * E))
    xnt = k.salloc(loc, "xnt", (128, 2 * E), BF16)
    junk = k.salloc(loc, "p1junk", (128, E), BF16)
    st = k.salloc(loc, "p1st", (128, 8 * NT))

    with nc.Block() as blk:
        @blk.gpsimd
        def _(g):
            k.gd.inc(g.dma_start(k.idf[:, :], d["identf"].ap()))
            k.gd.inc(g.dma_start(k.idb[:, :], d["identb"].ap()))
            k.gd.inc(g.dma_start(k.onesf[:, :], d["onesf"].ap()))
            k.gd.inc(g.dma_start(k.mask0[:, :], d["maskL0"].ap()))
            k.gd.inc(g.dma_start(k.wv2f[:, :], d["wv2"].ap()))

        if k.dump is not None:
            @blk.vector
            def _(v):
                V(k, v, lambda: v.memset(k.dbg_sb[:, :], 0.0))

    with (
        nc.psum_tensor("p1f0", [128, 128], F32) as pf0,
        nc.psum_tensor("p1f1", [128, 128], F32) as pf1,
        nc.psum_tensor("p1b0", [128, 128], BF16) as pb0,
        nc.psum_tensor("p1b1", [128, 128], BF16) as pb1,
    ):
        pfs = [pf0, pf1]
        pbs = [pb0, pb1]
        with nc.Block() as blk:
            xs = [k.sp[0], k.sp[1]]
            xs0 = [xs[0].n, xs[1].n]
            gdm = k.gd.mark()
            ve0 = k.ve.n
            ae0 = k.ae.n
            pe0 = k.pe.n
            DVE_PER = 15
            ACT_PER = 10
            PE_PER = 16

            def ve_at(i, op):
                return ve0 + DVE_PER * i + op

            def ae_at(i, op):
                return ae0 + ACT_PER * i + op

            def pe_at(i, op):
                return pe0 + PE_PER * i + op

            @blk.sync
            def _(s):
                for i in range(NT):
                    if i >= 2:
                        s.wait_ge(k.pe.sem, pe_at(i - 2, 8))
                        s.wait_ge(k.ve.sem, ve_at(i - 2, 7))
                    xs[i % 2].inc(s.dma_start(xtt[:, (i % 2) * E:(i % 2 + 1) * E],
                                              d["x"].ap()[i * 128:(i + 1) * 128, :]))

            @blk.tensor
            def _(t):
                wt(t, gdm)
                for i in range(NT):
                    xt = xtt[:, (i % 2) * E:(i % 2 + 1) * E]
                    xn = xnt[:, (i % 2) * E:(i % 2 + 1) * E]
                    t.wait_ge(xs[i % 2].sem, xs0[i % 2] + 16 * (i // 2 + 1))
                    for j in range(NJ):
                        if i * NJ + j >= 2:
                            pj = (i * NJ + j) - 2
                            pi, pjj = pj // NJ, pj % NJ
                            if pjj % 2 == 0:
                                t.wait_ge(k.ae.sem, ae_at(pi, 2 + pjj // 2 + 1))
                            else:
                                t.wait_ge(k.ve.sem, ve_at(pi, 6 + 1 + (pjj - 1) // 2 + 1))
                        k.pe.inc(t.transpose(pfs[j % 2][:, :],
                                             xt[:, j * 128:(j + 1) * 128],
                                             k.idf[:, :]))
                    t.wait_ge(k.ve.sem, ve_at(i, 7))
                    for j in range(NJ):
                        if i * NJ + j >= 2:
                            pj = (i * NJ + j) - 2
                            pi, pjj = pj // NJ, pj % NJ
                            if pjj % 2 == 0:
                                t.wait_ge(k.ae.sem, ae_at(pi, 6 + pjj // 2 + 1))
                            else:
                                t.wait_ge(k.ve.sem, ve_at(pi, 10 + 1 + (pjj - 1) // 2 + 1))
                        k.pe.inc(t.transpose(pbs[j % 2][:, :],
                                             xn[:, j * 128:(j + 1) * 128],
                                             k.idb[:, :]))

            @blk.vector
            def _(v):
                for i in range(NT):
                    c = 8 * i
                    xt = xtt[:, (i % 2) * E:(i % 2 + 1) * E]
                    xn = xnt[:, (i % 2) * E:(i % 2 + 1) * E]
                    v.wait_ge(xs[i % 2].sem, xs0[i % 2] + 16 * (i // 2 + 1))
                    V(k, v, lambda xt=xt, c=c: v.reduce_sum(st[:, c:c + 1], xt, axis=AX.X))
                    V(k, v, lambda xt=xt, c=c: v.scalar_tensor_tensor(
                        junk[:, :], xt, 1.0, xt, op0=ALU.bypass, op1=ALU.mult,
                        accum_out=st[:, c + 1:c + 2]))
                    V(k, v, lambda c=c: v.tensor_scalar_mul(
                        st[:, c + 2:c + 3], st[:, c:c + 1], 1.0 / E))
                    V(k, v, lambda c=c: v.tensor_scalar_mul(
                        st[:, c + 3:c + 4], st[:, c + 1:c + 2], 1.0 / E))
                    V(k, v, lambda c=c: v.scalar_tensor_tensor(
                        st[:, c + 3:c + 4], st[:, c + 2:c + 3], st[:, c + 2:c + 3],
                        st[:, c + 3:c + 4], op0=ALU.mult, op1=ALU.subtract))
                    V(k, v, lambda c=c: v.tensor_scalar(
                        st[:, c + 4:c + 5], st[:, c + 3:c + 4], -1.0, EPS,
                        op0=ALU.mult, op1=ALU.add))
                    v.wait_ge(k.ae.sem, ae_at(i, 2))
                    V(k, v, lambda xt=xt, xn=xn, c=c: v.tensor_scalar(
                        xn, xt, st[:, c + 2:c + 3], st[:, c + 5:c + 6],
                        op0=ALU.subtract, op1=ALU.mult))
                    for j in range(1, NJ, 2):
                        v.wait_ge(k.pe.sem, pe_at(i, j + 1))
                        V(k, v, lambda i=i, j=j: v.tensor_copy(
                            k.xT[:, j * TOKC + i * 128:j * TOKC + (i + 1) * 128],
                            pfs[j % 2][:, :]))
                    for j in range(1, NJ, 2):
                        v.wait_ge(k.pe.sem, pe_at(i, 8 + j + 1))
                        V(k, v, lambda i=i, j=j: v.tensor_copy(
                            k.xnT[:, j * TOKC + i * 128:j * TOKC + (i + 1) * 128],
                            pbs[j % 2][:, :]))

            @blk.scalar
            def _(s):
                for i in range(NT):
                    c = 8 * i
                    s.wait_ge(k.ve.sem, ve_at(i, 6))
                    A(k, s, lambda c=c: s.activation(st[:, c + 5:c + 6],
                                                     st[:, c + 4:c + 5], AF.Ln))
                    A(k, s, lambda c=c: s.activation(st[:, c + 5:c + 6],
                                                     st[:, c + 5:c + 6], AF.Exp,
                                                     scale=-0.5))
                    for j in range(0, NJ, 2):
                        s.wait_ge(k.pe.sem, pe_at(i, j + 1))
                        A(k, s, lambda i=i, j=j: s.copy(
                            k.xT[:, j * TOKC + i * 128:j * TOKC + (i + 1) * 128],
                            pfs[j % 2][:, :]))
                    for j in range(0, NJ, 2):
                        s.wait_ge(k.pe.sem, pe_at(i, 8 + j + 1))
                        A(k, s, lambda i=i, j=j: s.copy(
                            k.xnT[:, j * TOKC + i * 128:j * TOKC + (i + 1) * 128],
                            pbs[j % 2][:, :]))

    loc.close()
    if k.dump is not None:
        dmp(k, lambda: k.xT[:, 0:256])
        dmp(k, lambda: k.xnT[:, 0:256])


def p3_proj(k):
    nc = k.nc
    d = k.d
    loc = contextlib.ExitStack()
    kown = k.salloc(loc, "kown", (KD, TOKC), BF16)
    vstage = k.salloc(loc, "vstage", (KD, TOKC), BF16)
    v65own = k.salloc(loc, "v65own", (128, NT * 65), BF16)
    wq_sb = k.salloc(loc, "wq_sb", (128, NJ * KD), BF16)
    wk_sb = k.salloc(loc, "wk_sb", (128, NJ * KD), BF16)
    wv_sb = k.salloc(loc, "wv_sb", (128, NJ * KD), BF16)
    gw1T_sb = k.salloc(loc, "gw1T_sb", (128, NJ * E), BF16)
    relu_junk = k.salloc(loc, "relu_junk", (128, 512), BF16)
    phacc = k.salloc(loc, "phacc", (128, NJ * 4))
    zrow = k.salloc(loc, "zrow8", (128, NJ))
    vrow = k.salloc(loc, "vrow", (1, F0))
    k.ph = k.salloc(loc, "ph", (128, NJ))

    with nc.psum_tensor("pmps", [128, 32], F32) as pmps:
        for name, n, col in [("gb1", E, k.PM["gb1"]), ("b1", F0, k.PM["b1"]),
                             ("b2", E, k.PM["b2"]), ("bq", E, k.PM["bq"]),
                             ("bk", E, k.PM["bk"]), ("bv", E, k.PM["bv"]),
                             ("bo", E, k.PM["bo"])]:
            with nc.Block() as blk:
                @blk.sync
                def _(s, name=name, n=n):
                    k.hd.inc(s.dma_start(vrow[:1, 0:n], d[name].ap()))
                hm = k.hd.mark()

                @blk.tensor
                def _(t, n=n):
                    wt(t, hm)
                    for ch in range(n // 128):
                        k.pe.inc(t.transpose(pmps[:, ch:ch + 1],
                                             vrow[:1, ch * 128:(ch + 1) * 128],
                                             k.idf[:1, :1]))
                pm = k.pe.mark()

                @blk.vector
                def _(v, n=n, col=col):
                    wt(v, pm)
                    V(k, v, lambda: v.tensor_copy(k.pmv[:, col:col + n // 128],
                                                  pmps[:, 0:n // 128]))

    with nc.Block() as blk:
        @blk.gpsimd
        def _(g):
            for nm, dst in [("wq0", wq_sb), ("wk0", wk_sb), ("wv1", wv_sb)]:
                src = bass.AP(d[nm], 0, [[KD, 128], [128 * KD, NJ], [1, KD]])
                k.gd.inc(g.dma_start(dst[:, :], src))
            src = bass.AP(d["gw1T"], 0, [[E, 128], [128 * E, NJ], [1, E]])
            k.gd.inc(g.dma_start(gw1T_sb[:, :], src))

    with (
        nc.psum_tensor("p3q", [KD, 512], F32) as psq,
        nc.psum_tensor("p3k", [KD, 512], F32) as psk,
        nc.psum_tensor("p3v", [KD, 512], F32) as psv,
    ):
        with nc.Block() as blk:
            ae0 = k.ae.n
            pe_mark = {}
            gdm3 = k.gd.mark()

            @blk.tensor
            def _(t):
                wt(t, gdm3)
                for ts in range(4):
                    if ts > 0:
                        t.wait_ge(k.ae.sem, ae0 + 3 * ts)
                    for pi, (ps, wsb) in enumerate([(psq, wq_sb), (psk, wk_sb),
                                                    (psv, wv_sb)]):
                        for j in range(NJ):
                            k.pe.inc(t.matmul(
                                ps[:, :], wsb[:, j * KD:(j + 1) * KD],
                                k.xnT[:, j * TOKC + ts * 512:j * TOKC + (ts + 1) * 512],
                                start=(j == 0), stop=(j == NJ - 1)))
                        pe_mark[(pi, ts)] = k.pe.n

            @blk.scalar
            def _(s):
                for ts in range(4):
                    for pi, (ps, dst, sc) in enumerate([
                            (psq, k.qp, 0.125), (psk, kown, 1.0),
                            (psv, vstage, 1.0)]):
                        s.wait_ge(k.pe.sem, pe_mark[(pi, ts)])
                        A(k, s, lambda ps=ps, dst=dst, sc=sc, ts=ts: s.activation(
                            dst[:, ts * 512:(ts + 1) * 512], ps[:, :], AF.Copy,
                            scale=sc))

    with nc.psum_tensor("p3vt", [128, KD], BF16) as psvt:
        with nc.Block() as blk:
            @blk.vector
            def _(v):
                V(k, v, lambda: v.memset(v65own[:, :], 1.0))
            vm0 = k.ve.mark()
            pe2 = {}

            @blk.tensor
            def _(t):
                wt(t, vm0)
                for i in range(NT):
                    if i >= 1:
                        t.wait_ge(k.ve.sem, vm0[1] + i)
                    k.pe.inc(t.transpose(psvt[:, :],
                                         vstage[:, i * 128:(i + 1) * 128],
                                         k.idb[:KD, :KD]))
                    pe2[i] = k.pe.n

            @blk.vector
            def _(v):
                for i in range(NT):
                    v.wait_ge(k.pe.sem, pe2[i])
                    V(k, v, lambda i=i: v.tensor_copy(
                        v65own[:, i * 65:i * 65 + KD], psvt[:, :]))

    with nc.psum_tensor("p3h", [128, 512], F32) as psh:
        with nc.Block() as blk:
            ae0 = k.ae.n
            pe3 = {}

            @blk.tensor
            def _(t):
                wt(t, k.gd.mark())
                n = 0
                for oc in range(NJ):
                    for ts in range(4):
                        if n >= 1:
                            t.wait_ge(k.ae.sem, ae0 + n)
                        for j in range(NJ):
                            k.pe.inc(t.matmul(
                                psh[:, :],
                                gw1T_sb[:, j * E + oc * 128:j * E + (oc + 1) * 128],
                                k.xnT[:, j * TOKC + ts * 512:j * TOKC + (ts + 1) * 512],
                                start=(j == 0), stop=(j == NJ - 1)))
                        pe3[n] = k.pe.n
                        n += 1

            @blk.scalar
            def _(s):
                n = 0
                for oc in range(NJ):
                    for ts in range(4):
                        s.wait_ge(k.pe.sem, pe3[n])
                        A(k, s, lambda oc=oc, ts=ts: s.activation(
                            relu_junk[:, :], psh[:, :], AF.Relu,
                            bias=k.pmv[:, k.PM["gb1"] + oc:k.PM["gb1"] + oc + 1],
                            accum_out=phacc[:, oc * 4 + ts:oc * 4 + ts + 1]))
                        n += 1

    with nc.Block() as blk:
        @blk.vector
        def _(v):
            pa = phacc[:, :].rearrange("p (o t) -> p o t", t=4)
            V(k, v, lambda: v.reduce_sum(k.ph[:, :], pa, axis=AX.X))
            V(k, v, lambda: v.memset(zrow[:, :], 0.0))

        @blk.sync
        def _(s):
            k.hd.inc(s.dma_start(
                bass.AP(d["kv_in"], 0, [[TOKC, KD], [1, TOKC]]), kown[:, :]))
            k.hd.inc(s.dma_start(
                bass.AP(d["kv_in"], KD * TOKC, [[65, 128], [65 * 128, NT], [1, 65]]),
                v65own[:, :].rearrange("p (i c) -> p i c", c=65)))

    def phstore(cid):
        b = cid // 4
        with nc.Block() as blk:
            @blk.sync
            def _(s, b=b):
                k.hd.inc(s.dma_start(d["ar_ph_in"].ap()[b, :, :], k.ph[:, :]))
                k.hd.inc(s.dma_start(d["ar_ph_in"].ap()[1 - b, :, :], zrow[:, :]))
    core_switch(k, phstore)

    with nc.Block() as blk:
        @blk.gpsimd
        def _(g):
            g.wait_ge(k.hd.sem, k.hd.n)
            k.agkv = coll(k, g, "AllGather", ALU.bypass, d["kv_in"], d["kv_out"])
            k.arph = coll(k, g, "AllReduce", ALU.add, d["ar_ph_in"], d["ar_ph_out"])

    if k.dump is not None:
        dmp(k, lambda: k.qp[:KD, 0:128], rows=KD)
        dmp(k, lambda: k.ph[:, :])
    loc.close()


def p4_gmw(k):
    nc = k.nc
    d = k.d
    loc = contextlib.ExitStack()
    ph2 = k.salloc(loc, "ph2", (128, 2 * NJ))
    ph2b = k.salloc(loc, "ph2b", (128, 2 * NJ), BF16)
    gw2T_sb = k.salloc(loc, "gw2T_sb", (128, NJ * NE), BF16)
    gb2_sb = k.salloc(loc, "gb2_sb", (1, NE))
    tb1_sb = k.salloc(loc, "tb1_sb", (NE, F0), BF16)
    tb2_sb = k.salloc(loc, "tb2_sb", (NE, E), BF16)
    stg = k.salloc(loc, "mwstg", (128, 2 * 9 * 512), BF16)
    mwo = k.salloc(loc, "mwo", (128, 2 * 512), BF16)
    k.g2 = k.salloc(loc, "g2", (1, 2 * NE))
    k.gcol = k.salloc(loc, "gcol", (NE, 2), BF16)
    k.gbc = k.salloc(loc, "gbc", (128, 2 * NE))
    k.ige = k.salloc(loc, "ige", (128, 17 * 128), BF16)

    with nc.Block() as blk:
        @blk.gpsimd
        def _(g):
            wt(g, k.arph)
            for b in range(B):
                k.gd.inc(g.dma_start(ph2[:, b * NJ:(b + 1) * NJ],
                                     d["ar_ph_out"].ap()[b, :, :]))
            src = bass.AP(d["gw2T"], 0, [[NE, 128], [128 * NE, NJ], [1, NE]])
            k.gd.inc(g.dma_start(gw2T_sb[:, :], src))
            k.gd.inc(g.dma_start(gb2_sb[:1, :], d["gb2"].ap()))
            k.gd.inc(g.dma_start(tb1_sb[:, :], d["Tb1"].ap()))
            k.gd.inc(g.dma_start(tb2_sb[:, :], d["Tb2"].ap()))
        gm = k.gd.mark()

        @blk.vector
        def _(v):
            wt(v, gm)
            V(k, v, lambda: v.tensor_scalar_mul(ph2b[:, :], ph2[:, :], 1.0 / S))

    with (
        nc.psum_tensor("p4g", [1, 2 * NE], F32) as psg,
        nc.psum_tensor("p4b", [128, 2 * NE], F32) as psgb,
        nc.psum_tensor("p4gc", [NE, 2], F32) as psgc,
        nc.psum_tensor("p4m1", [128, 64], F32) as psmb1,
        nc.psum_tensor("p4m2", [128, 16], F32) as psmb2,
    ):
        with nc.Block() as blk:
            @blk.tensor
            def _(t):
                for b in range(B):
                    for j in range(NJ):
                        k.pe.inc(t.matmul(
                            psg[:, b * NE:(b + 1) * NE],
                            ph2b[:, b * NJ + j:b * NJ + j + 1],
                            gw2T_sb[:, j * NE:(j + 1) * NE],
                            start=(j == 0), stop=(j == NJ - 1)))
            pm = k.pe.mark()

            @blk.vector
            def _(v):
                wt(v, pm)
                for b in range(B):
                    V(k, v, lambda b=b: v.tensor_tensor(
                        k.g2[:1, b * NE:(b + 1) * NE], psg[:1, b * NE:(b + 1) * NE],
                        gb2_sb[:1, :], op=ALU.add))
            vm = k.ve.mark()

            @blk.tensor
            def _(t):
                wt(t, vm)
                k.pe.inc(t.matmul(psgb[:, :], k.onesf[:1, :], k.g2[:1, :],
                                  start=True, stop=True))
                for b in range(B):
                    k.pe.inc(t.transpose(psgc[:, b:b + 1],
                                         k.g2[:1, b * NE:(b + 1) * NE],
                                         k.idf[:1, :1]))
            pm2 = k.pe.mark()

            @blk.vector
            def _(v):
                wt(v, pm2)
                V(k, v, lambda: v.tensor_copy(k.gbc[:, :], psgb[:, :]))
                V(k, v, lambda: v.tensor_copy(k.gcol[:, :], psgc[:, :]))

        with nc.Block() as blk:
            @blk.vector
            def _(v):
                V(k, v, lambda: v.tensor_copy(k.ige[:, 0:128], k.idb[:, :]))
                for b in range(B):
                    for e in range(NE):
                        i = 1 + b * NE + e
                        V(k, v, lambda i=i, b=b, e=e: v.tensor_scalar_mul(
                            k.ige[:, i * 128:(i + 1) * 128], k.idb[:, :],
                            k.gbc[:, b * NE + e:b * NE + e + 1]))

        with nc.Block() as blk:
            @blk.tensor
            def _(t):
                for b in range(B):
                    for ch in range(F0 // 128):
                        k.pe.inc(t.matmul(psmb1[:, b * 32 + ch:b * 32 + ch + 1],
                                          tb1_sb[:, ch * 128:(ch + 1) * 128],
                                          k.gcol[:, b:b + 1],
                                          start=True, stop=True))
                    for ch in range(NJ):
                        k.pe.inc(t.matmul(psmb2[:, b * 8 + ch:b * 8 + ch + 1],
                                          tb2_sb[:, ch * 128:(ch + 1) * 128],
                                          k.gcol[:, b:b + 1],
                                          start=True, stop=True))
            k.pm3 = k.pe.mark()

        def mbsel(cid):
            b = cid // 4
            with nc.Block() as blk:
                @blk.vector
                def _(v, b=b):
                    wt(v, k.pm3)
                    V(k, v, lambda b=b: v.tensor_tensor(
                        k.pmv[:, k.PM["mb1"]:k.PM["mb1"] + 32],
                        psmb1[:, b * 32:(b + 1) * 32],
                        k.pmv[:, k.PM["b1"]:k.PM["b1"] + 32], op=ALU.add))
                    V(k, v, lambda b=b: v.tensor_tensor(
                        k.pmv[:, k.PM["mb2"]:k.PM["mb2"] + 8],
                        psmb2[:, b * 8:(b + 1) * 8],
                        k.pmv[:, k.PM["b2"]:k.PM["b2"] + 8], op=ALU.add))
        core_switch(k, mbsel)

    jobs = [("w1", jh) for jh in range(NJ)] + [("w2", jf) for jf in range(F0 // 128)]
    npair = 2 * len(jobs)

    def pair_region(m):
        n, b = m // 2, m % 2
        kind, idx = jobs[n]
        if kind == "w1":
            off = b * MW1N + idx * 128 * 512
            return bass.AP(d["mw_in"], off, [[512, 128], [1, 512]]), 512
        off = 2 * MW1N + b * MW2N + idx * 128 * 128
        return bass.AP(d["mw_in"], off, [[128, 128], [1, 128]]), 128

    with (
        nc.psum_tensor("p4c0", [128, 512], F32) as pc0,
        nc.psum_tensor("p4c1", [128, 512], F32) as pc1,
    ):
        pcs = [pc0, pc1]
        with nc.Block() as blk:
            ldm = [k.gsp[0], k.gsp[1]]
            ldm0 = [ldm[0].n, ldm[1].n]
            stm = [k.sp[2], k.sp[3]]
            stm0 = [stm[0].n, stm[1].n]
            pe0 = k.pe.n
            ae0 = k.ae.n

            @blk.gpsimd
            def _(g):
                for n, (kind, idx) in enumerate(jobs):
                    if n >= 2:
                        g.wait_ge(k.pe.sem, pe0 + 18 * (n - 1))
                    buf = stg[:, (n % 2) * 9 * 512:(n % 2 + 1) * 9 * 512]
                    sl = ldm[n % 2]
                    if kind == "w1":
                        sl.inc(g.dma_start(
                            buf[:, 0:512],
                            d["W1T_s"].ap()[idx * 128:(idx + 1) * 128, :]))
                        for e in range(NE):
                            sl.inc(g.dma_start(
                                buf[:, (1 + e) * 512:(2 + e) * 512],
                                d["TW1T_s"].ap()[e, idx * 128:(idx + 1) * 128, :]))
                    else:
                        sl.inc(g.dma_start(
                            buf[:, 0:128],
                            d["W2T_s"].ap()[idx * 128:(idx + 1) * 128, :]))
                        for e in range(NE):
                            sl.inc(g.dma_start(
                                buf[:, 512 + e * 128:512 + (e + 1) * 128],
                                d["TW2T_s"].ap()[e, idx * 128:(idx + 1) * 128, :]))

            @blk.tensor
            def _(t):
                for m in range(npair):
                    n, b = m // 2, m % 2
                    kind, idx = jobs[n]
                    buf = stg[:, (n % 2) * 9 * 512:(n % 2 + 1) * 9 * 512]
                    t.wait_ge(ldm[n % 2].sem, ldm0[n % 2] + 16 * 9 * (n // 2 + 1))
                    if m >= 2:
                        t.wait_ge(k.ae.sem, ae0 + m - 1)
                    ps = pcs[m % 2]
                    if kind == "w1":
                        w0, wd = buf[:, 0:512], 512
                        srcs = [buf[:, (1 + e) * 512:(2 + e) * 512] for e in range(NE)]
                    else:
                        w0, wd = buf[:, 0:128], 128
                        srcs = [buf[:, 512 + e * 128:512 + (e + 1) * 128]
                                for e in range(NE)]
                    k.pe.inc(t.matmul(ps[:, 0:wd], k.ige[:, 0:128], w0,
                                      start=True, stop=False))
                    for e in range(NE):
                        ig = k.ige[:, (1 + b * NE + e) * 128:(2 + b * NE + e) * 128]
                        k.pe.inc(t.matmul(ps[:, 0:wd], ig, srcs[e],
                                          start=False, stop=(e == NE - 1)))

            @blk.scalar
            def _(s):
                for m in range(npair):
                    s.wait_ge(k.pe.sem, pe0 + 9 * (m + 1))
                    if m >= 2:
                        pm_ = m - 2
                        s.wait_ge(stm[pm_ % 2].sem,
                                  stm0[pm_ % 2] + 16 * (pm_ // 2 + 1))
                    _, w = pair_region(m)
                    A(k, s, lambda m=m, w=w: s.copy(
                        mwo[:, (m % 2) * 512:(m % 2) * 512 + w],
                        pcs[m % 2][:, 0:w]))

            @blk.sync
            def _(s):
                for m in range(npair):
                    s.wait_ge(k.ae.sem, ae0 + m + 1)
                    ap, w = pair_region(m)
                    stm[m % 2].inc(s.dma_start(
                        ap, mwo[:, (m % 2) * 512:(m % 2) * 512 + w]))

    with nc.Block() as blk:
        @blk.gpsimd
        def _(g):
            g.wait_ge(k.sp[2].sem, k.sp[2].n)
            g.wait_ge(k.sp[3].sem, k.sp[3].n)
            k.agmw = coll(k, g, "AllGather", ALU.bypass, d["mw_in"], d["mw_out"])

    if k.dump is not None:
        dmp(k, lambda: k.gbc[:, 0:16])
        dmp(k, lambda: k.pmv[:, 80:120])
        mwchk = k.salloc(loc, "mwchk", (128, 512), BF16)
        with nc.Block() as blk:
            @blk.gpsimd
            def _(g):
                wt(g, k.agmw)
                k.gd.inc(g.dma_start(
                    mwchk[:, :], bass.AP(d["mw_out"], 0, [[512, 128], [1, 512]])))
        gm2 = k.gd.mark()
        with nc.Block() as blk:
            @blk.vector
            def _(v):
                wt(v, gm2)
        dmp(k, lambda: mwchk[:, :])
    loc.close()


def p5_attn0(k):
    nc = k.nc
    d = k.d
    loc = contextlib.ExitStack()
    ksb = k.salloc(loc, "ksb", (KD, S), BF16)
    v65 = k.salloc(loc, "v65", (128, (S // 128) * 65), BF16)
    peb = k.salloc(loc, "peb", (128, 2 * 512), BF16)
    pvs = k.salloc(loc, "pvs", (KD, 512))
    zrow = k.salloc(loc, "zrow", (1, 512))
    rz = k.salloc(loc, "rz", (1, 512))
    rzs = k.salloc(loc, "rzs", (128, 512))
    tmpb = k.salloc(loc, "p5tmp", (128, 512))

    with (
        nc.psum_tensor("p5s0", [128, 512], F32) as ps0,
        nc.psum_tensor("p5s1", [128, 512], F32) as ps1,
        nc.psum_tensor("p5pv", [65, 512], F32) as ps65,
        nc.psum_tensor("p5bc", [128, 512], F32) as psB,
        nc.psum_tensor("p5o0", [128, 512], F32) as po0,
        nc.psum_tensor("p5o1", [128, 512], F32) as po1,
    ):
        pss = [ps0, ps1]
        pos = [po0, po1]

        def body(cid):
            b = cid // 4
            q0 = (cid % 4) * TOKC
            nreg = (cid % 4) + 1

            with nc.Block() as blk:
                @blk.sync
                def _(s):
                    wt(s, k.agkv)
                    for r in range(nreg):
                        src = bass.AP(d["kv_out"], (4 * b + r) * KVN,
                                      [[TOKC, KD], [1, TOKC]])
                        k.hd.inc(s.dma_start(ksb[:, r * TOKC:(r + 1) * TOKC], src))
                        srcv = bass.AP(d["kv_out"],
                                       (4 * b + r) * KVN + KD * TOKC,
                                       [[65, 128], [65 * 128, NT], [1, 65]])
                        k.hd.inc(s.dma_start(
                            v65[:, r * NT * 65:(r + 1) * NT * 65]
                            .rearrange("p (i c) -> p i c", c=65), srcv))
                    for _ in range(2 * (4 - nreg)):
                        s.wait_ge(k.hd.sem, k.hd.n)
                        k.hd.inc(s.nop(nofuse=True))

            for i in range(4):
                nkt = (q0 + 512 * (i + 1)) // 128
                nkt_max = (3 * TOKC + 512 * (i + 1)) // 128
                qs = slice(i * 512, (i + 1) * 512)
                with nc.Block() as blk:
                    pe0 = k.pe.n
                    ae0 = k.ae.n
                    ve0 = k.ve.n
                    hdm = k.hd.mark()
                    s_af = {}
                    pv_af = {}

                    def diag_vi(kt):
                        return kt - (nkt - 4) if kt >= nkt - 4 else None

                    @blk.tensor
                    def _(t):
                        wt(t, hdm)

                        def S(kt):
                            if kt >= 2:
                                t.wait_ge(k.ae.sem, ae0 + kt - 1)
                            k.pe.inc(t.matmul(
                                pss[kt % 2][:, :],
                                ksb[:, kt * 128:(kt + 1) * 128],
                                k.qp[:, qs], start=True, stop=True))
                            s_af[kt] = k.pe.n

                        S(0)
                        for kt in range(nkt):
                            if kt + 1 < nkt:
                                S(kt + 1)
                            t.wait_ge(k.ae.sem, ae0 + kt + 1)
                            vi = diag_vi(kt)
                            if vi is not None:
                                t.wait_ge(k.ve.sem, ve0 + vi + 1)
                            k.pe.inc(t.matmul(
                                ps65[:, :], v65[:, kt * 65:(kt + 1) * 65],
                                peb[:, (kt % 2) * 512:(kt % 2 + 1) * 512],
                                start=(kt == 0), stop=(kt == nkt - 1)))
                            pv_af[kt] = k.pe.n
                        t.wait_ge(k.ve.sem, ve0 + 6)
                        k.pe.inc(t.matmul(psB[:, :], k.onesf[:1, :], rz[:1, :],
                                          start=True, stop=True))
                        bc_af = k.pe.n
                        t.wait_ge(k.ae.sem, ae0 + nkt + 1)
                        for j in range(NJ):
                            if j >= 2:
                                t.wait_ge(k.ve.sem, ve0 + 8 + 2 * (j - 2))
                            k.pe.inc(t.matmul(pos[j % 2][:, :],
                                              k.wv2f[:, j * 128:(j + 1) * 128],
                                              pvs[:, :], start=True, stop=True))
                            pv_af[("o", j)] = k.pe.n
                        for _ in range(2 * (nkt_max - nkt)):
                            t.wait_ge(k.pe.sem, k.pe.n)
                            k.pe.inc(t.nop(nofuse=True))

                    @blk.scalar
                    def _(s):
                        for kt in range(nkt):
                            s.wait_ge(k.pe.sem, s_af[kt])
                            if kt >= 2:
                                s.wait_ge(k.pe.sem, pv_af[kt - 2])
                            A(k, s, lambda kt=kt: s.activation(
                                peb[:, (kt % 2) * 512:(kt % 2 + 1) * 512],
                                pss[kt % 2][:, :], AF.Exp))
                        s.wait_ge(k.pe.sem, pv_af[nkt - 1])
                        A(k, s, lambda: s.copy(pvs[:, :], ps65[0:KD, :]))
                        for _ in range(nkt_max - nkt):
                            s.wait_ge(k.ae.sem, k.ae.n)
                            k.ae.inc(s.nop(nofuse=True))

                    @blk.vector
                    def _(v):
                        for kt in range(nkt - 4, nkt):
                            vi = kt - (nkt - 4)
                            dv = (kt * 128 - (q0 + 512 * i)) // 128
                            v.wait_ge(k.ae.sem, ae0 + kt + 1)
                            V(k, v, lambda kt=kt, dv=dv: v.tensor_tensor(
                                peb[:, (kt % 2) * 512:(kt % 2 + 1) * 512],
                                peb[:, (kt % 2) * 512:(kt % 2 + 1) * 512],
                                k.mask0[:, dv * 512:(dv + 1) * 512], op=ALU.mult))
                        v.wait_ge(k.pe.sem, pv_af[nkt - 1])
                        V(k, v, lambda: v.tensor_copy(zrow[:1, :], ps65[KD:KD + 1, :]))
                        V(k, v, lambda: v.reciprocal(rz[:1, :], zrow[:1, :]))
                        v.wait_ge(k.pe.sem, pv_af[nkt - 1] + 1)
                        V(k, v, lambda: v.tensor_copy(rzs[:, :], psB[:, :]))
                        for j in range(NJ):
                            v.wait_ge(k.pe.sem, pv_af[("o", j)])
                            V(k, v, lambda j=j: v.tensor_mul(
                                tmpb[:, :], pos[j % 2][:, :], rzs[:, :]))
                            xsl = k.xT[:, j * TOKC + i * 512:j * TOKC + (i + 1) * 512]
                            V(k, v, lambda xsl=xsl: v.tensor_add(xsl, xsl, tmpb[:, :]))

        core_switch(k, body)
    if k.dump is not None:
        dmp(k, lambda: k.xT[:, 0:256])
    loc.close()


def p6_ffn(k):
    nc = k.nc
    d = k.d
    loc = contextlib.ExitStack()
    stat1 = k.salloc(loc, "stat1", (128, 2 * 8 * 128), BF16)
    ugel = k.salloc(loc, "ugel", (128, 2 * 512), BF16)
    u_all = k.salloc(loc, "u_all", (128, 32 * 512), BF16)
    stat2 = k.salloc(loc, "stat2", (128, 2 * 8 * 128), BF16)
    tmpo = k.salloc(loc, "p6tmp", (128, 512))

    def mwcopy(cid):
        b = cid // 4
        with nc.Block() as blk:
            @blk.sync
            def _(s, b=b):
                wt(s, k.agmw)
                for r in range(NC):
                    src = bass.AP(d["mw_out"], r * MWN + b * MW1N,
                                  [[512, 1024], [1, 512]])
                    dst = bass.AP(d["mw1t_mine"], r * 512, [[4096, 1024], [1, 512]])
                    k.hd.inc(s.dma_start(dst, src))
                    src2 = bass.AP(d["mw_out"], r * MWN + 2 * MW1N + b * MW2N,
                                   [[128, 4096], [1, 128]])
                    dst2 = bass.AP(d["mw2t_mine"], r * 128, [[1024, 4096], [1, 128]])
                    k.hd.inc(s.dma_start(dst2, src2))
    core_switch(k, mwcopy)
    mwcm = k.hd.mark()

    with (
        nc.psum_tensor("p6u0", [128, 512], F32) as pu0,
        nc.psum_tensor("p6u1", [128, 512], F32) as pu1,
    ):
        pus = [pu0, pu1]
        with nc.Block() as blk:
            lds = [k.sp[4], k.sp[5]]
            lds0 = [lds[0].n, lds[1].n]
            sts = [k.sp[6], k.sp[7]]
            sts0 = [sts[0].n, sts[1].n]
            pe0 = k.pe.n
            ae0 = k.ae.n
            pe_rec = {}

            @blk.sync
            def _(s):
                wt(s, mwcm)
                for ft in range(32):
                    if ft >= 2:
                        s.wait_ge(k.pe.sem, pe0 + 8 * (4 * (ft - 2) + 4))
                    src = bass.AP(d["mw1t_mine"], ft * 128,
                                  [[4096, 128], [128 * 4096, 8], [1, 128]])
                    lds[ft % 2].inc(s.dma_start(
                        stat1[:, (ft % 2) * 1024:(ft % 2 + 1) * 1024], src))

            @blk.tensor
            def _(t):
                for m in range(128):
                    ft, ts = m // 4, m % 4
                    t.wait_ge(lds[ft % 2].sem, lds0[ft % 2] + 16 * (ft // 2 + 1))
                    if m >= 2:
                        t.wait_ge(k.ae.sem, ae0 + m - 1)
                    for jh in range(NJ):
                        k.pe.inc(t.matmul(
                            pus[m % 2][:, :],
                            stat1[:, (ft % 2) * 1024 + jh * 128:
                                  (ft % 2) * 1024 + (jh + 1) * 128],
                            k.xnT[:, jh * TOKC + ts * 512:jh * TOKC + (ts + 1) * 512],
                            start=(jh == 0), stop=(jh == NJ - 1)))
                    pe_rec[m] = k.pe.n

            @blk.scalar
            def _(s):
                for m in range(128):
                    ft = m // 4
                    s.wait_ge(k.pe.sem, pe_rec[m])
                    if m >= 2:
                        pm_ = m - 2
                        s.wait_ge(sts[pm_ % 2].sem, sts0[pm_ % 2] + 16 * (pm_ // 2 + 1))
                    A(k, s, lambda m=m, ft=ft: s.activation(
                        ugel[:, (m % 2) * 512:(m % 2 + 1) * 512], pus[m % 2][:, :],
                        AF.Gelu_apprx_tanh,
                        bias=k.pmv[:, k.PM["mb1"] + ft:k.PM["mb1"] + ft + 1]))

            @blk.gpsimd
            def _(g):
                for m in range(128):
                    ft, ts = m // 4, m % 4
                    g.wait_ge(k.ae.sem, ae0 + m + 1)
                    sts[m % 2].inc(g.dma_start(
                        d["u_dram"].ap()[ft, :, ts * 512:(ts + 1) * 512],
                        ugel[:, (m % 2) * 512:(m % 2 + 1) * 512]))

    ustm = [k.sp[6].mark(), k.sp[7].mark()]
    for ts in range(4):
        with contextlib.ExitStack() as pstk:
            psE = [pstk.enter_context(
                nc.psum_tensor("p6o%d_%d" % (ts, j), [128, 512], F32))
                for j in range(NJ)]
            with nc.Block() as blk:
                ldu = k.sp[8]
                ldu0 = ldu.n
                ld2 = [k.sp[4], k.sp[5]]
                ld20 = [ld2[0].n, ld2[1].n]
                pe0 = k.pe.n
                pe_rec = {}

                @blk.sync
                def _(s):
                    wt(s, ustm[0])
                    wt(s, ustm[1])
                    src = bass.AP(d["u_dram"], ts * 512,
                                  [[TOKC, 128], [128 * TOKC, 32], [1, 512]])
                    ldu.inc(s.dma_start(u_all[:, :], src))
                    for ft in range(32):
                        if ft >= 2:
                            s.wait_ge(k.pe.sem, pe0 + 8 * (ft - 1))
                        src2 = bass.AP(d["mw2t_mine"], ft * 128 * 1024,
                                       [[1024, 128], [128, 8], [1, 128]])
                        ld2[ft % 2].inc(s.dma_start(
                            stat2[:, (ft % 2) * 1024:(ft % 2 + 1) * 1024], src2))

                @blk.tensor
                def _(t):
                    t.wait_ge(ldu.sem, ldu0 + 16)
                    for ft in range(32):
                        t.wait_ge(ld2[ft % 2].sem, ld20[ft % 2] + 16 * (ft // 2 + 1))
                        for jE in range(NJ):
                            k.pe.inc(t.matmul(
                                psE[jE][:, :],
                                stat2[:, (ft % 2) * 1024 + jE * 128:
                                      (ft % 2) * 1024 + (jE + 1) * 128],
                                u_all[:, ft * 512:(ft + 1) * 512],
                                start=(ft == 0), stop=(ft == 31)))
                        pe_rec[ft] = k.pe.n
                pem = k.pe.mark()

                @blk.vector
                def _(v):
                    wt(v, pem)
                    for jE in range(NJ):
                        xsl = k.xT[:, jE * TOKC + ts * 512:jE * TOKC + (ts + 1) * 512]
                        V(k, v, lambda jE=jE, xsl=xsl: v.scalar_tensor_tensor(
                            tmpo[:, :], psE[jE][:, :],
                            k.pmv[:, k.PM["mb2"] + jE:k.PM["mb2"] + jE + 1],
                            xsl, op0=ALU.add, op1=ALU.add))
                        V(k, v, lambda jE=jE, xsl=xsl: v.tensor_copy(xsl, tmpo[:, :]))

    if k.dump is not None:
        dmp(k, lambda: k.xT[:, 0:256])
    loc.close()


def p7_l1(k):
    nc = k.nc
    d = k.d
    loc = contextlib.ExitStack()
    x1tt = k.salloc(loc, "x1tt", (128, 2 * E))
    xn1t = k.salloc(loc, "xn1t", (128, 2 * E), BF16)
    st7 = k.salloc(loc, "p7st", (128, 8 * NT))
    mask1 = k.salloc(loc, "mask1", (128, 256), BF16)
    iota8 = k.salloc(loc, "iota8_sb", (128, NE))
    wg_sb = k.salloc(loc, "wg_sb", (128, NJ * NE), BF16)
    lg_tm = k.salloc(loc, "lg_tm", (128, NT * NE))
    r8 = k.salloc(loc, "r8", (128, 16))
    rsc = k.salloc(loc, "rsc", (128, 8))
    idxu = k.salloc(loc, "idxu", (128, 8), U32)
    ohs = k.salloc(loc, "ohs", (128, 8))
    cntr = k.salloc(loc, "cntr", (1, NE))
    wst = k.salloc(loc, "wst", (128, 2 * 8 * 128), BF16)
    qstg = k.salloc(loc, "qstg", (128, 2 * 512), BF16)
    q_tm = k.salloc(loc, "q_tm", (128, 4 * E), BF16)
    k_tm = k.salloc(loc, "k_tm", (128, 4 * E), BF16)
    v_tm = k.salloc(loc, "v_tm", (128, 4 * E), BF16)
    S2 = k.salloc(loc, "S2", (128, 4 * C1 * C1), BF16)
    Ebuf = k.salloc(loc, "Ebuf", (128, 2 * C1 * C1), BF16)
    junk7 = Ebuf[:, 0:E]
    attn_tm = k.salloc(loc, "attn_tm", (128, 4 * E), BF16)
    attnT = k.salloc(loc, "attnT", (128, NJ * 512), BF16)
    tmpo7 = x1tt[:, 0:512]
    Nf = x1tt[:, 512:512 + 2 * C1]
    Zf = x1tt[:, 640:640 + 2 * C1]
    rZf = x1tt[:, 768:768 + 2 * C1]
    lgf = None

    with (
        nc.psum_tensor("p7f0", [128, 128], F32) as pf0,
        nc.psum_tensor("p7f1", [128, 128], F32) as pf1,
        nc.psum_tensor("p7b0", [128, 128], BF16) as pb0,
        nc.psum_tensor("p7b1", [128, 128], BF16) as pb1,
    ):
        pfs = [pf0, pf1]
        pbs = [pb0, pb1]
        with nc.Block() as blk:
            xst = [k.sp[0], k.sp[1]]
            xst0 = [xst[0].n, xst[1].n]
            ve0 = k.ve.n
            ae0 = k.ae.n
            pe0 = k.pe.n

            def pe_at(i, op):
                return pe0 + 16 * i + op

            def ae_at(i, op):
                return ae0 + 10 * i + op

            def ve_at(i, op):
                return ve0 + 15 * i + op

            @blk.tensor
            def _(t):
                for i in range(NT):
                    for j in range(NJ):
                        if j >= 2:
                            if j % 2 == 0:
                                t.wait_ge(k.ae.sem, ae_at(i, (j - 2) // 2 + 1))
                            else:
                                t.wait_ge(k.ve.sem, ve_at(i, (j - 3) // 2 + 1))
                        elif i >= 1:
                            if j % 2 == 0:
                                t.wait_ge(k.ae.sem, ae_at(i - 1, 4))
                            else:
                                t.wait_ge(k.ve.sem, ve_at(i - 1, 4))
                        k.pe.inc(t.transpose(
                            pfs[j % 2][:, :],
                            k.xT[:, j * TOKC + i * 128:j * TOKC + (i + 1) * 128],
                            k.idf[:, :]))
                    t.wait_ge(k.ve.sem, ve_at(i, 11))
                    for j in range(NJ):
                        if j >= 2:
                            if j % 2 == 0:
                                t.wait_ge(k.ae.sem, ae_at(i, 6 + (j - 2) // 2 + 1))
                            else:
                                t.wait_ge(k.ve.sem, ve_at(i, 11 + (j - 3) // 2 + 1))
                        elif i >= 1:
                            if j % 2 == 0:
                                t.wait_ge(k.ae.sem, ae_at(i - 1, 10))
                            else:
                                t.wait_ge(k.ve.sem, ve_at(i - 1, 15))
                        k.pe.inc(t.transpose(
                            pbs[j % 2][:, :],
                            xn1t[:, (i % 2) * E + j * 128:(i % 2) * E + (j + 1) * 128],
                            k.idb[:, :]))

            @blk.scalar
            def _(s):
                for i in range(NT):
                    c = 8 * i
                    for j in range(0, NJ, 2):
                        s.wait_ge(k.pe.sem, pe_at(i, j + 1))
                        if i >= 2 and j == 0:
                            s.wait_ge(k.ve.sem, ve_at(i - 2, 11))
                        A(k, s, lambda i=i, j=j: s.copy(
                            x1tt[:, (i % 2) * E + j * 128:(i % 2) * E + (j + 1) * 128],
                            pfs[j % 2][:, :]))
                    s.wait_ge(k.ve.sem, ve_at(i, 10))
                    A(k, s, lambda c=c: s.activation(st7[:, c + 5:c + 6],
                                                     st7[:, c + 4:c + 5], AF.Ln))
                    A(k, s, lambda c=c: s.activation(st7[:, c + 5:c + 6],
                                                     st7[:, c + 5:c + 6], AF.Exp,
                                                     scale=-0.5))
                    for j in range(0, NJ, 2):
                        s.wait_ge(k.pe.sem, pe_at(i, 8 + j + 1))
                        A(k, s, lambda i=i, j=j: s.copy(
                            k.xnT[:, j * TOKC + i * 128:j * TOKC + (i + 1) * 128],
                            pbs[j % 2][:, :]))

            @blk.vector
            def _(v):
                for i in range(NT):
                    c = 8 * i
                    xt = x1tt[:, (i % 2) * E:(i % 2 + 1) * E]
                    xn = xn1t[:, (i % 2) * E:(i % 2 + 1) * E]
                    for j in range(1, NJ, 2):
                        v.wait_ge(k.pe.sem, pe_at(i, j + 1))
                        V(k, v, lambda i=i, j=j: v.tensor_copy(
                            x1tt[:, (i % 2) * E + j * 128:(i % 2) * E + (j + 1) * 128],
                            pfs[j % 2][:, :]))
                    v.wait_ge(k.ae.sem, ae_at(i, 4))
                    V(k, v, lambda xt=xt, c=c: v.reduce_sum(st7[:, c:c + 1], xt, axis=AX.X))
                    V(k, v, lambda xt=xt, c=c: v.scalar_tensor_tensor(
                        junk7, xt, 1.0, xt, op0=ALU.bypass, op1=ALU.mult,
                        accum_out=st7[:, c + 1:c + 2]))
                    V(k, v, lambda c=c: v.tensor_scalar_mul(
                        st7[:, c + 2:c + 3], st7[:, c:c + 1], 1.0 / E))
                    V(k, v, lambda c=c: v.tensor_scalar_mul(
                        st7[:, c + 3:c + 4], st7[:, c + 1:c + 2], 1.0 / E))
                    V(k, v, lambda c=c: v.scalar_tensor_tensor(
                        st7[:, c + 3:c + 4], st7[:, c + 2:c + 3], st7[:, c + 2:c + 3],
                        st7[:, c + 3:c + 4], op0=ALU.mult, op1=ALU.subtract))
                    V(k, v, lambda c=c: v.tensor_scalar(
                        st7[:, c + 4:c + 5], st7[:, c + 3:c + 4], -1.0, EPS,
                        op0=ALU.mult, op1=ALU.add))
                    v.wait_ge(k.ae.sem, ae_at(i, 6))
                    if i >= 2:
                        v.wait_ge(k.pe.sem, pe_at(i - 2, 16))
                        v.wait_ge(xst[i % 2].sem, xst0[i % 2] + 16 * (i // 2))
                    V(k, v, lambda xt=xt, xn=xn, c=c: v.tensor_scalar(
                        xn, xt, st7[:, c + 2:c + 3], st7[:, c + 5:c + 6],
                        op0=ALU.subtract, op1=ALU.mult))
                    for j in range(1, NJ, 2):
                        v.wait_ge(k.pe.sem, pe_at(i, 8 + j + 1))
                        V(k, v, lambda i=i, j=j: v.tensor_copy(
                            k.xnT[:, j * TOKC + i * 128:j * TOKC + (i + 1) * 128],
                            pbs[j % 2][:, :]))

            @blk.sync
            def _(s):
                for i in range(NT):
                    s.wait_ge(k.ve.sem, ve_at(i, 11))
                    xst[i % 2].inc(s.dma_start(
                        d["xn1_in"].ap()[i * 128:(i + 1) * 128, :],
                        xn1t[:, (i % 2) * E:(i % 2 + 1) * E]))

            @blk.gpsimd
            def _(g):
                k.gd.inc(g.dma_start(
                    mask1[:, :],
                    bass.AP(d["maskL1"], 0, [[4096, 128], [64, 16], [1, 16]])))
                k.gd.inc(g.dma_start(iota8[:, :], d["iota8"].ap()))
                src = bass.AP(d["wgate"], 0, [[NE, 128], [128 * NE, NJ], [1, NE]])
                k.gd.inc(g.dma_start(wg_sb[:, :], src))
        ln1m = (k.gd.mark(), xst[0].mark(), xst[1].mark())

    lgf = x1tt

    with nc.Block() as blk:
        @blk.gpsimd
        def _(g):
            wt(g, ln1m[1])
            wt(g, ln1m[2])
            k.agxn1 = coll(k, g, "AllGather", ALU.bypass, d["xn1_in"], d["xn1_out"])

    with (
        nc.psum_tensor("p7r0", [NE, 512], F32) as pr0,
        nc.psum_tensor("p7r1", [NE, 512], F32) as pr1,
        nc.psum_tensor("p7t0", [128, NE], F32) as pt0,
        nc.psum_tensor("p7t1", [128, NE], F32) as pt1,
        nc.psum_tensor("p7c", [1, NE], F32) as pcnt,
    ):
        prs = [pr0, pr1]
        pts = [pt0, pt1]
        with nc.Block() as blk:
            ve0 = k.ve.n
            pe_rec = {}

            @blk.tensor
            def _(t):
                wt(t, ln1m[0])
                for ts in range(4):
                    if ts >= 2:
                        t.wait_ge(k.ve.sem, ve0 + ts - 1)
                    for j in range(NJ):
                        k.pe.inc(t.matmul(
                            prs[ts % 2][:, :], wg_sb[:, j * NE:(j + 1) * NE],
                            k.xnT[:, j * TOKC + ts * 512:j * TOKC + (ts + 1) * 512],
                            start=(j == 0), stop=(j == NJ - 1)))
                    pe_rec[ts] = k.pe.n

            @blk.vector
            def _(v):
                for ts in range(4):
                    v.wait_ge(k.pe.sem, pe_rec[ts])
                    V(k, v, lambda ts=ts: v.tensor_copy(
                        lgf[0:NE, ts * 512:(ts + 1) * 512], prs[ts % 2][:, :]))

        with nc.Block() as blk:
            ve0 = k.ve.n
            pe_rec = {}

            @blk.tensor
            def _(t):
                for i in range(NT):
                    if i >= 2:
                        t.wait_ge(k.ve.sem, ve0 + i - 1)
                    k.pe.inc(t.transpose(pts[i % 2][:, :],
                                         lgf[0:NE, i * 128:(i + 1) * 128],
                                         k.idf[:NE, :NE]))
                    pe_rec[i] = k.pe.n

            @blk.vector
            def _(v):
                for i in range(NT):
                    v.wait_ge(k.pe.sem, pe_rec[i])
                    V(k, v, lambda i=i: v.tensor_copy(
                        lg_tm[:, i * NE:(i + 1) * NE], pts[i % 2][:, :]))

        with nc.Block() as blk:
            ve_rec = {}
            ae_rec = {}
            ve0 = k.ve.n
            ae0 = k.ae.n
            pe0c = k.pe.n

            @blk.vector
            def _(v):
                for i in range(NT):
                    lt = lg_tm[:, i * NE:(i + 1) * NE]
                    V(k, v, lambda lt=lt: v.max(r8[:, 0:8], lt))
                    V(k, v, lambda lt=lt: v.max_index(idxu[:, :], r8[:, 0:8], lt))
                    V(k, v, lambda i=i: v.tensor_copy(k.rtri[:, 2 * i:2 * i + 1],
                                                      idxu[:, 0:1]))
                    V(k, v, lambda i=i: v.tensor_copy(k.rtri[:, 2 * i + 1:2 * i + 2],
                                                      idxu[:, 1:2]))
                    V(k, v, lambda: v.tensor_scalar_mul(rsc[:, 0:1], r8[:, 0:1], -1.0))
                    ve_rec[("pre", i)] = k.ve.n
                    v.wait_ge(k.ae.sem, ae0 + 2 * i + 1)
                    V(k, v, lambda: v.reciprocal(rsc[:, 2:3], rsc[:, 1:2]))
                    V(k, v, lambda i=i: v.tensor_copy(k.rtrw[:, 2 * i:2 * i + 1],
                                                      rsc[:, 2:3]))
                    v.wait_ge(k.ae.sem, ae0 + 2 * i + 2)
                    V(k, v, lambda i=i: v.tensor_mul(k.rtrw[:, 2 * i + 1:2 * i + 2],
                                                     rsc[:, 3:4], rsc[:, 2:3]))
                    V(k, v, lambda i=i: v.tensor_scalar(
                        k.oh1[:, i * NE:(i + 1) * NE], iota8[:, :],
                        k.rtri[:, 2 * i:2 * i + 1], None, op0=ALU.is_equal))
                    V(k, v, lambda i=i: v.tensor_scalar(
                        k.oh2[:, i * NE:(i + 1) * NE], iota8[:, :],
                        k.rtri[:, 2 * i + 1:2 * i + 2], None, op0=ALU.is_equal))
                    if i >= 1:
                        v.wait_ge(k.pe.sem, pe0c + i)
                    V(k, v, lambda i=i: v.tensor_add(
                        ohs[:, :], k.oh1[:, i * NE:(i + 1) * NE],
                        k.oh2[:, i * NE:(i + 1) * NE]))
                    ve_rec[("ohs", i)] = k.ve.n

            @blk.scalar
            def _(s):
                for i in range(NT):
                    lt = lg_tm[:, i * NE:(i + 1) * NE]
                    s.wait_ge(k.ve.sem, ve_rec[("pre", i)])
                    A(k, s, lambda lt=lt: s.activation(
                        r8[:, 8:16], lt, AF.Exp, bias=rsc[:, 0:1],
                        accum_out=rsc[:, 1:2]))
                    A(k, s, lambda: s.activation(rsc[:, 3:4], r8[:, 1:2], AF.Exp,
                                                 bias=rsc[:, 0:1]))

            @blk.tensor
            def _(t):
                for i in range(NT):
                    t.wait_ge(k.ve.sem, ve_rec[("ohs", i)])
                    k.pe.inc(t.matmul(pcnt[:, :], k.onesf[:, 0:1], ohs[:, :],
                                      start=(i == 0), stop=(i == NT - 1)))
            pcm = k.pe.mark()

            @blk.vector
            def _(v):
                wt(v, pcm)
                V(k, v, lambda: v.tensor_copy(cntr[:1, :], pcnt[:1, :]))
            cvm = k.ve.mark()

            @blk.sync
            def _(s):
                wt(s, cvm)
                k.hd.inc(s.dma_start(d["cnt_in"].ap(), cntr[:1, :]))

        with nc.Block() as blk:
            @blk.gpsimd
            def _(g):
                g.wait_ge(k.hd.sem, k.hd.n)
                k.agcnt = coll(k, g, "AllGather", ALU.bypass, d["cnt_in"], d["cnt_out"])

    with (
        nc.psum_tensor("p7q0", [128, 512], F32) as pq0,
        nc.psum_tensor("p7q1", [128, 512], F32) as pq1,
        nc.psum_tensor("p7x0", [128, 128], BF16) as px0,
        nc.psum_tensor("p7x1", [128, 128], BF16) as px1,
        nc.psum_tensor("p7w0", [128, 512], F32) as pw0,
        nc.psum_tensor("p7w1", [128, 512], F32) as pw1,
    ):
        pqs = [pq0, pq1]
        pxs = [px0, px1]
        pws = [pw0, pw1]
        mats = [("Wq1", q_tm, "bq", 0.125), ("Wk1", k_tm, "bk", 1.0),
                ("Wv1_", v_tm, "bv", 1.0)]

        with nc.Block() as blk:
            lds = [k.gsp[2], k.gsp[3]]
            lds0 = [lds[0].n, lds[1].n]
            ae0 = k.ae.n
            ve0 = k.ve.n
            pe_rec = {}
            petp_rec = {}
            pe_rec_pre = k.pe.n

            @blk.gpsimd
            def _(g):
                for m in range(24):
                    mi, jo = m // 8, m % 8
                    if m >= 2:
                        g.wait_ge(k.pe.sem, pe_rec_pre + 12 * (m - 1))
                    src = bass.AP(d[mats[mi][0]], jo * 128,
                                  [[E, 128], [128 * E, NJ], [1, 128]])
                    lds[m % 2].inc(g.dma_start(
                        wst[:, (m % 2) * 1024:(m % 2 + 1) * 1024], src))

            @blk.tensor
            def _(t):
                for m in range(24):
                    mi, jo = m // 8, m % 8
                    t.wait_ge(lds[m % 2].sem, lds0[m % 2] + 16 * (m // 2 + 1))
                    if m >= 2:
                        t.wait_ge(k.ae.sem, ae0 + 3 * (m - 2) + 1)
                    for j in range(NJ):
                        k.pe.inc(t.matmul(
                            pqs[m % 2][:, :],
                            wst[:, (m % 2) * 1024 + j * 128:
                                (m % 2) * 1024 + (j + 1) * 128],
                            k.xnT[:, j * TOKC:j * TOKC + 512],
                            start=(j == 0), stop=(j == NJ - 1)))
                    pe_rec[m] = k.pe.n
                    t.wait_ge(k.ae.sem, ae0 + 3 * m + 1)
                    for tt in range(4):
                        if m * 4 + tt >= 2:
                            pn = m * 4 + tt - 2
                            pm_, ptt = pn // 4, pn % 4
                            if ptt % 2 == 0:
                                t.wait_ge(k.ae.sem, ae0 + 3 * pm_ + 1 + ptt // 2 + 1)
                            else:
                                t.wait_ge(k.ve.sem, ve0 + 2 * pm_ + (ptt - 1) // 2 + 1)
                        k.pe.inc(t.transpose(
                            pxs[tt % 2][:, :],
                            qstg[:, (m % 2) * 512 + tt * 128:
                                 (m % 2) * 512 + (tt + 1) * 128],
                            k.idb[:, :]))
                        petp_rec[(m, tt)] = k.pe.n

            @blk.scalar
            def _(s):
                for m in range(24):
                    mi, jo = m // 8, m % 8
                    nm, dstt, bn, sc = mats[mi]
                    s.wait_ge(k.pe.sem, pe_rec[m])
                    A(k, s, lambda jo=jo, bn=bn, sc=sc, m=m:
                      s.activation(qstg[:, (m % 2) * 512:(m % 2 + 1) * 512],
                                   pqs[m % 2][:, :], AF.Identity, scale=sc,
                                   bias=k.pmv[:, k.PM[bn] + jo:k.PM[bn] + jo + 1]))
                    for tt in (0, 2):
                        s.wait_ge(k.pe.sem, petp_rec[(m, tt)])
                        A(k, s, lambda dstt=dstt, jo=jo, tt=tt: s.copy(
                            dstt[:, tt * E + jo * 128:tt * E + (jo + 1) * 128],
                            pxs[tt % 2][:, :]))

            @blk.vector
            def _(v):
                for m in range(24):
                    mi, jo = m // 8, m % 8
                    dstt = mats[mi][1]
                    for tt in (1, 3):
                        v.wait_ge(k.pe.sem, petp_rec[(m, tt)])
                        V(k, v, lambda dstt=dstt, jo=jo, tt=tt: v.tensor_copy(
                            dstt[:, tt * E + jo * 128:tt * E + (jo + 1) * 128],
                            pxs[tt % 2][:, :]))

        for ts in range(4):
            with_qkv = ts < 3
            with_wo = ts >= 1
            tsn = ts + 1
            with nc.Block() as blk:
                lds = [k.gsp[2], k.gsp[3]]
                lds0 = [lds[0].n, lds[1].n]
                ae0 = k.ae.n
                pe0 = k.pe.n
                ve_mask = {}
                ve_fin = {}
                ve_woadd = {}
                E4 = Ebuf[:, :].rearrange("p (i a b) -> p i a b", i=2, b=C1)
                LV = [(32, 32), (16, 16), (8, 0)]
                M3 = mask1[:, :].rearrange("p (a b) -> p a b", b=16)

                def sl4_of(p):
                    return S2[:, (p % 2) * 8192:(p % 2 + 1) * 8192]\
                        .rearrange("p (i a b) -> p i a b", i=2, b=C1)

                nWo = 8 if with_wo else 0
                wo_mm_done = {}
                mm_done = {}
                tp_done = {}
                atp_done = {}
                cons_of_load = []
                pxs_seq = []
                cnt = pe0
                for jE in range(nWo):
                    cnt += 8
                    wo_mm_done[jE] = cnt
                    cons_of_load.append(cnt)
                for tt in range(4):
                    if with_qkv:
                        for m in range(24):
                            cnt += 8
                            mm_done[(tt, m)] = cnt
                            cons_of_load.append(cnt)
                            cnt += 1
                            tp_done[(tt, m)] = cnt
                            pxs_seq.append(("q", tt, m))
                    for n in range(8):
                        cnt += 1
                        atp_done[(tt, n)] = cnt
                        pxs_seq.append(("a", tt, n))
                pxs_slot = {u: i % 2 for i, u in enumerate(pxs_seq)}
                pxs_prev = {u: (pxs_seq[i - 2] if i >= 2 else None)
                            for i, u in enumerate(pxs_seq)}

                exp_done = {}
                evac1_done = {}
                tpev_done = {}
                aev_done = {}
                acnt = ae0
                pending = []
                sched = []
                ins_at = {8: 0, 16: 1, 24: 2, 31: 3}
                for p in range(32):
                    sched.append(("exp", p, 0))
                    acnt += 4
                    exp_done[p] = acnt
                    if p in ins_at:
                        tt = ins_at[p]
                        pending.append(("gate", tt, 0))
                        if with_qkv:
                            for m in range(24):
                                pending.append(("e1", tt, m))
                                pending.append(("qtpv", tt, m))
                        for n in range(8):
                            pending.append(("aev", tt, n))
                    take = 12 if p != 31 else 10 ** 9
                    taken = 0
                    while pending and taken < take:
                        op = pending.pop(0)
                        sched.append(op)
                        if op[0] == "gate":
                            continue
                        acnt += 1
                        taken += 1
                        if op[0] == "e1":
                            evac1_done[(op[1], op[2])] = acnt
                        elif op[0] == "qtpv":
                            tpev_done[(op[1], op[2])] = acnt
                        else:
                            aev_done[(op[1], op[2])] = acnt

                def ev_mark_of(user):
                    return tpev_done[(user[1], user[2])] if user[0] == "q" \
                        else aev_done[(user[1], user[2])]

                @blk.vector
                def _(v):
                    Vf(k, lambda: v.memset(Ebuf[:, :], 0.0))

                    def build(p):
                        tt, hp = p // 8, p % 8
                        base = tt * E + 2 * hp * C1
                        sl4 = sl4_of(p)
                        q2 = q_tm[:, base:base + 128]\
                            .rearrange("p (i c) -> p i c", i=2)
                        k2 = k_tm[:, base:base + 128]\
                            .rearrange("p (i c) -> p i c", i=2)
                        if p >= 2:
                            v.wait_ge(k.ae.sem, exp_done[p - 2])
                        for qq in range(4):
                            Kq = 16 * (qq + 1)
                            qap = q2[:, :, 16 * qq:16 * (qq + 1)]\
                                .unsqueeze(3).broadcast_to([128, 2, 16, Kq])
                            kap = k2[:, :, 0:Kq]\
                                .unsqueeze(2).broadcast_to([128, 2, 16, Kq])
                            Vf(k, lambda sl4=sl4, qq=qq, Kq=Kq, qap=qap,
                               kap=kap: v.tensor_tensor(
                                sl4[:, :, 16 * qq:16 * (qq + 1), 0:Kq],
                                qap, kap, op=ALU.mult))
                        Vbar(k, v)
                        mbc = M3.unsqueeze(1).broadcast_to([128, 2, 16, 16])
                        for qq in range(4):
                            Vf(k, lambda sl4=sl4, qq=qq, mbc=mbc: v.tensor_add(
                                sl4[:, :, 16 * qq:16 * (qq + 1),
                                    16 * qq:16 * (qq + 1)],
                                sl4[:, :, 16 * qq:16 * (qq + 1),
                                    16 * qq:16 * (qq + 1)],
                                mbc))
                        ve_mask[p] = k.ve.n

                    build(0)
                    for p in range(32):
                        tt, hp = p // 8, p % 8
                        base = tt * E + 2 * hp * C1
                        if p + 1 < 32:
                            build(p + 1)
                        v.wait_ge(k.ae.sem, exp_done[p])
                        EV4 = sl4_of(p)
                        vap = v_tm[:, base:base + 128]\
                            .rearrange("p (i c) -> p i c", i=2)\
                            .unsqueeze(2).broadcast_to([128, 2, C1, C1])
                        Vbar(k, v)
                        Vf(k, lambda EV4=EV4, vap=vap: v.tensor_tensor(
                            EV4, E4, vap, op=ALU.mult))
                        for hw, rlo in LV:
                            Vbar(k, v)
                            for b4 in (E4, EV4):
                                Vf(k, lambda b4=b4, hw=hw, rlo=rlo:
                                   v.tensor_add(
                                    b4[:, :, rlo:C1, 0:hw],
                                    b4[:, :, rlo:C1, 0:hw],
                                    b4[:, :, rlo:C1, hw:2 * hw]))
                        Vbar(k, v)
                        Vf(k, lambda EV4=EV4: v.reduce_sum(
                            Zf, E4[:, :, :, 0:8], axis=AX.X))
                        Vf(k, lambda EV4=EV4: v.reduce_sum(
                            Nf, EV4[:, :, :, 0:8], axis=AX.X))
                        Vbar(k, v)
                        Vf(k, lambda: v.reciprocal(rZf, Zf))
                        Vbar(k, v)
                        Vf(k, lambda base=base: v.tensor_mul(
                            attn_tm[:, base:base + 128],
                            Nf, rZf))
                        ve_fin[p] = k.ve.n
                        if with_wo and p < 8:
                            jE = p
                            v.wait_ge(k.pe.sem, wo_mm_done[jE])
                            xsl = k.xT[:, jE * TOKC + (ts - 1) * 512:
                                       jE * TOKC + ts * 512]
                            Vbar(k, v)
                            Vf(k, lambda jE=jE, xsl=xsl:
                               v.scalar_tensor_tensor(
                                xsl, pws[jE % 2][:, :],
                                k.pmv[:, k.PM["bo"] + jE:k.PM["bo"] + jE + 1],
                                xsl, op0=ALU.add, op1=ALU.add))
                            ve_woadd[jE] = k.ve.n

                @blk.tensor
                def _(t):
                    li = 0
                    for jE in range(nWo):
                        t.wait_ge(lds[li % 2].sem,
                                  lds0[li % 2] + 16 * (li // 2 + 1))
                        if jE >= 2:
                            t.wait_ge(k.ve.sem, ve_woadd[jE - 2])
                        for j in range(NJ):
                            k.pe.inc(t.matmul(
                                pws[jE % 2][:, :],
                                wst[:, (li % 2) * 1024 + j * 128:
                                    (li % 2) * 1024 + (j + 1) * 128],
                                attnT[:, j * 512:(j + 1) * 512],
                                start=(j == 0), stop=(j == NJ - 1)))
                        assert k.pe.n == wo_mm_done[jE]
                        li += 1
                    for tt in range(4):
                        if with_qkv:
                            for m in range(24):
                                t.wait_ge(lds[li % 2].sem,
                                          lds0[li % 2] + 16 * (li // 2 + 1))
                                pu = (tt, m - 2) if m >= 2 else \
                                    ((tt - 1, 22 + m) if tt >= 1 else None)
                                if pu is not None:
                                    t.wait_ge(k.ae.sem, evac1_done[pu])
                                for j in range(NJ):
                                    k.pe.inc(t.matmul(
                                        pqs[m % 2][:, 0:128],
                                        wst[:, (li % 2) * 1024 + j * 128:
                                            (li % 2) * 1024 + (j + 1) * 128],
                                        k.xnT[:, j * TOKC + tsn * 512 + tt * 128:
                                              j * TOKC + tsn * 512 +
                                              (tt + 1) * 128],
                                        start=(j == 0), stop=(j == NJ - 1)))
                                assert k.pe.n == mm_done[(tt, m)]
                                li += 1
                                t.wait_ge(k.ae.sem, evac1_done[(tt, m)])
                                pv = pxs_prev[("q", tt, m)]
                                if pv is not None:
                                    t.wait_ge(k.ae.sem, ev_mark_of(pv))
                                k.pe.inc(t.transpose(
                                    pxs[pxs_slot[("q", tt, m)]][:, :],
                                    qstg[:, (m % 2) * 512:(m % 2) * 512 + 128],
                                    k.idb[:, :]))
                                assert k.pe.n == tp_done[(tt, m)]
                        t.wait_ge(k.ve.sem, ve_fin[8 * tt + 7])
                        for n in range(8):
                            pv = pxs_prev[("a", tt, n)]
                            if pv is not None:
                                t.wait_ge(k.ae.sem, ev_mark_of(pv))
                            k.pe.inc(t.transpose(
                                pxs[pxs_slot[("a", tt, n)]][:, :],
                                attn_tm[:, tt * E + n * 128:
                                        tt * E + (n + 1) * 128],
                                k.idb[:, :]))
                            assert k.pe.n == atp_done[(tt, n)]

                @blk.gpsimd
                def _(g):
                    li = 0
                    for jE in range(nWo):
                        if li >= 2:
                            g.wait_ge(k.pe.sem, cons_of_load[li - 2])
                        src = bass.AP(d["Wo1"], jE * 128,
                                      [[E, 128], [128 * E, NJ], [1, 128]])
                        lds[li % 2].inc(g.dma_start(
                            wst[:, (li % 2) * 1024:(li % 2 + 1) * 1024], src))
                        li += 1
                    if with_qkv:
                        for tt in range(4):
                            for m in range(24):
                                mi, jo = m // 8, m % 8
                                if li >= 2:
                                    g.wait_ge(k.pe.sem, cons_of_load[li - 2])
                                src = bass.AP(d[mats[mi][0]], jo * 128,
                                              [[E, 128], [128 * E, NJ],
                                               [1, 128]])
                                lds[li % 2].inc(g.dma_start(
                                    wst[:, (li % 2) * 1024:(li % 2 + 1) * 1024],
                                    src))
                                li += 1

                @blk.scalar
                def _(s):
                    for op, a1, a2 in sched:
                        if op == "exp":
                            p = a1
                            s.wait_ge(k.ve.sem, ve_mask[p])
                            if p >= 1:
                                s.wait_ge(k.ve.sem, ve_fin[p - 1])
                            sl4 = sl4_of(p)
                            for qq in range(4):
                                Kq = 16 * (qq + 1)
                                A(k, s, lambda sl4=sl4, qq=qq, Kq=Kq:
                                  s.activation(
                                    E4[:, :, 16 * qq:16 * (qq + 1), 0:Kq],
                                    sl4[:, :, 16 * qq:16 * (qq + 1), 0:Kq],
                                    AF.Exp))
                        elif op == "gate":
                            s.wait_ge(k.ve.sem, ve_fin[8 * a1 + 7])
                        elif op == "e1":
                            tt, m = a1, a2
                            mi, jo = m // 8, m % 8
                            nm, dstt, bn, sc = mats[mi]
                            s.wait_ge(k.pe.sem, mm_done[(tt, m)])
                            pu = (tt, m - 2) if m >= 2 else \
                                ((tt - 1, 22 + m) if tt >= 1 else None)
                            if pu is not None:
                                s.wait_ge(k.pe.sem, tp_done[pu])
                            A(k, s, lambda jo=jo, bn=bn, sc=sc, m=m:
                              s.activation(
                                qstg[:, (m % 2) * 512:(m % 2) * 512 + 128],
                                pqs[m % 2][:, 0:128], AF.Identity, scale=sc,
                                bias=k.pmv[:, k.PM[bn] + jo:k.PM[bn] + jo + 1]))
                        elif op == "qtpv":
                            tt, m = a1, a2
                            mi, jo = m // 8, m % 8
                            dstt = mats[mi][1]
                            s.wait_ge(k.pe.sem, tp_done[(tt, m)])
                            A(k, s, lambda dstt=dstt, jo=jo, tt=tt, m=m:
                              s.copy(
                                dstt[:, tt * E + jo * 128:
                                     tt * E + (jo + 1) * 128],
                                pxs[pxs_slot[("q", tt, m)]][:, :]))
                        else:
                            tt, n = a1, a2
                            s.wait_ge(k.pe.sem, atp_done[(tt, n)])
                            A(k, s, lambda tt=tt, n=n: s.copy(
                                attnT[:, n * 512 + tt * 128:
                                      n * 512 + (tt + 1) * 128],
                                pxs[pxs_slot[("a", tt, n)]][:, :]))
                    assert k.ae.n == acnt

        with nc.Block() as blk:
            lds = [k.gsp[2], k.gsp[3]]
            lds0 = [lds[0].n, lds[1].n]
            ve0 = k.ve.n
            pe_rec = {}
            pe_rec_pre2 = k.pe.n

            @blk.gpsimd
            def _(g):
                for jE in range(NJ):
                    if jE >= 2:
                        g.wait_ge(k.pe.sem, pe_rec_pre2 + 8 * (jE - 1))
                    src = bass.AP(d["Wo1"], jE * 128,
                                  [[E, 128], [128 * E, NJ], [1, 128]])
                    lds[jE % 2].inc(g.dma_start(
                        wst[:, (jE % 2) * 1024:(jE % 2 + 1) * 1024], src))

            @blk.tensor
            def _(t):
                for jE in range(NJ):
                    t.wait_ge(lds[jE % 2].sem, lds0[jE % 2] + 16 * (jE // 2 + 1))
                    if jE >= 2:
                        t.wait_ge(k.ve.sem, ve0 + 2 * (jE - 1))
                    for j in range(NJ):
                        k.pe.inc(t.matmul(
                            pws[jE % 2][:, :],
                            wst[:, (jE % 2) * 1024 + j * 128:
                                (jE % 2) * 1024 + (j + 1) * 128],
                            attnT[:, j * 512:(j + 1) * 512],
                            start=(j == 0), stop=(j == NJ - 1)))
                    pe_rec[jE] = k.pe.n

            @blk.vector
            def _(v):
                for jE in range(NJ):
                    v.wait_ge(k.pe.sem, pe_rec[jE])
                    xsl = k.xT[:, jE * TOKC + 3 * 512:jE * TOKC + 4 * 512]
                    V(k, v, lambda jE=jE, xsl=xsl: v.scalar_tensor_tensor(
                        tmpo7, pws[jE % 2][:, :],
                        k.pmv[:, k.PM["bo"] + jE:k.PM["bo"] + jE + 1],
                        xsl, op0=ALU.add, op1=ALU.add))
                    V(k, v, lambda jE=jE, xsl=xsl: v.tensor_copy(xsl, tmpo7))

    if k.dump is not None:
        dmp(k, lambda: k.xT[:, 0:256])
        dmp(k, lambda: k.xnT[:, 0:256])


def p3_proj(k):
    nc = k.nc
    d = k.d
    loc = contextlib.ExitStack()
    kown = k.salloc(loc, "kown", (KD, TOKC), BF16)
    vstage = k.salloc(loc, "vstage", (KD, TOKC), BF16)
    v65own = k.salloc(loc, "v65own", (128, NT * 65), BF16)
    wq_sb = k.salloc(loc, "wq_sb", (128, NJ * KD), BF16)
    wk_sb = k.salloc(loc, "wk_sb", (128, NJ * KD), BF16)
    wv_sb = k.salloc(loc, "wv_sb", (128, NJ * KD), BF16)
    gw1T_sb = k.salloc(loc, "gw1T_sb", (128, NJ * E), BF16)
    relu_junk = k.salloc(loc, "relu_junk", (128, 512), BF16)
    phacc = k.salloc(loc, "phacc", (128, NJ * 4))
    zrow = k.salloc(loc, "zrow8", (128, NJ))
    vrow = k.salloc(loc, "vrow", (1, F0))
    k.ph = k.salloc(loc, "ph", (128, NJ))

    with nc.psum_tensor("pmps", [128, 32], F32) as pmps:
        for name, n, col in [("gb1", E, k.PM["gb1"]), ("b1", F0, k.PM["b1"]),
                             ("b2", E, k.PM["b2"]), ("bq", E, k.PM["bq"]),
                             ("bk", E, k.PM["bk"]), ("bv", E, k.PM["bv"]),
                             ("bo", E, k.PM["bo"])]:
            with nc.Block() as blk:
                @blk.sync
                def _(s, name=name, n=n):
                    k.hd.inc(s.dma_start(vrow[:1, 0:n], d[name].ap()))
                hm = k.hd.mark()

                @blk.tensor
                def _(t, n=n):
                    wt(t, hm)
                    for ch in range(n // 128):
                        k.pe.inc(t.transpose(pmps[:, ch:ch + 1],
                                             vrow[:1, ch * 128:(ch + 1) * 128],
                                             k.idf[:1, :1]))
                pm = k.pe.mark()

                @blk.vector
                def _(v, n=n, col=col):
                    wt(v, pm)
                    V(k, v, lambda: v.tensor_copy(k.pmv[:, col:col + n // 128],
                                                  pmps[:, 0:n // 128]))

    with nc.Block() as blk:
        @blk.gpsimd
        def _(g):
            for nm, dst in [("wq0", wq_sb), ("wk0", wk_sb), ("wv1", wv_sb)]:
                src = bass.AP(d[nm], 0, [[KD, 128], [128 * KD, NJ], [1, KD]])
                k.gd.inc(g.dma_start(dst[:, :], src))
            src = bass.AP(d["gw1T"], 0, [[E, 128], [128 * E, NJ], [1, E]])
            k.gd.inc(g.dma_start(gw1T_sb[:, :], src))

    with (
        nc.psum_tensor("p3q", [KD, 512], F32) as psq,
        nc.psum_tensor("p3k", [KD, 512], F32) as psk,
        nc.psum_tensor("p3v", [KD, 512], F32) as psv,
    ):
        with nc.Block() as blk:
            ae0 = k.ae.n
            pe_mark = {}
            gdm3 = k.gd.mark()

            @blk.tensor
            def _(t):
                wt(t, gdm3)
                for ts in range(4):
                    if ts > 0:
                        t.wait_ge(k.ae.sem, ae0 + 3 * ts)
                    for pi, (ps, wsb) in enumerate([(psq, wq_sb), (psk, wk_sb),
                                                    (psv, wv_sb)]):
                        for j in range(NJ):
                            k.pe.inc(t.matmul(
                                ps[:, :], wsb[:, j * KD:(j + 1) * KD],
                                k.xnT[:, j * TOKC + ts * 512:j * TOKC + (ts + 1) * 512],
                                start=(j == 0), stop=(j == NJ - 1)))
                        pe_mark[(pi, ts)] = k.pe.n

            @blk.scalar
            def _(s):
                for ts in range(4):
                    for pi, (ps, dst, sc) in enumerate([
                            (psq, k.qp, 0.125), (psk, kown, 1.0),
                            (psv, vstage, 1.0)]):
                        s.wait_ge(k.pe.sem, pe_mark[(pi, ts)])
                        A(k, s, lambda ps=ps, dst=dst, sc=sc, ts=ts: s.activation(
                            dst[:, ts * 512:(ts + 1) * 512], ps[:, :], AF.Copy,
                            scale=sc))

    with nc.psum_tensor("p3vt", [128, KD], BF16) as psvt:
        with nc.Block() as blk:
            @blk.vector
            def _(v):
                V(k, v, lambda: v.memset(v65own[:, :], 1.0))
            vm0 = k.ve.mark()
            pe2 = {}

            @blk.tensor
            def _(t):
                wt(t, vm0)
                for i in range(NT):
                    if i >= 1:
                        t.wait_ge(k.ve.sem, vm0[1] + i)
                    k.pe.inc(t.transpose(psvt[:, :],
                                         vstage[:, i * 128:(i + 1) * 128],
                                         k.idb[:KD, :KD]))
                    pe2[i] = k.pe.n

            @blk.vector
            def _(v):
                for i in range(NT):
                    v.wait_ge(k.pe.sem, pe2[i])
                    V(k, v, lambda i=i: v.tensor_copy(
                        v65own[:, i * 65:i * 65 + KD], psvt[:, :]))

    with nc.psum_tensor("p3h", [128, 512], F32) as psh:
        with nc.Block() as blk:
            ae0 = k.ae.n
            pe3 = {}

            @blk.tensor
            def _(t):
                wt(t, k.gd.mark())
                n = 0
                for oc in range(NJ):
                    for ts in range(4):
                        if n >= 1:
                            t.wait_ge(k.ae.sem, ae0 + n)
                        for j in range(NJ):
                            k.pe.inc(t.matmul(
                                psh[:, :],
                                gw1T_sb[:, j * E + oc * 128:j * E + (oc + 1) * 128],
                                k.xnT[:, j * TOKC + ts * 512:j * TOKC + (ts + 1) * 512],
                                start=(j == 0), stop=(j == NJ - 1)))
                        pe3[n] = k.pe.n
                        n += 1

            @blk.scalar
            def _(s):
                n = 0
                for oc in range(NJ):
                    for ts in range(4):
                        s.wait_ge(k.pe.sem, pe3[n])
                        A(k, s, lambda oc=oc, ts=ts: s.activation(
                            relu_junk[:, :], psh[:, :], AF.Relu,
                            bias=k.pmv[:, k.PM["gb1"] + oc:k.PM["gb1"] + oc + 1],
                            accum_out=phacc[:, oc * 4 + ts:oc * 4 + ts + 1]))
                        n += 1

    with nc.Block() as blk:
        @blk.vector
        def _(v):
            pa = phacc[:, :].rearrange("p (o t) -> p o t", t=4)
            V(k, v, lambda: v.reduce_sum(k.ph[:, :], pa, axis=AX.X))
            V(k, v, lambda: v.memset(zrow[:, :], 0.0))

        @blk.sync
        def _(s):
            k.hd.inc(s.dma_start(
                bass.AP(d["kv_in"], 0, [[TOKC, KD], [1, TOKC]]), kown[:, :]))
            k.hd.inc(s.dma_start(
                bass.AP(d["kv_in"], KD * TOKC, [[65, 128], [65 * 128, NT], [1, 65]]),
                v65own[:, :].rearrange("p (i c) -> p i c", c=65)))

    def phstore(cid):
        b = cid // 4
        with nc.Block() as blk:
            @blk.sync
            def _(s, b=b):
                k.hd.inc(s.dma_start(d["ar_ph_in"].ap()[b, :, :], k.ph[:, :]))
                k.hd.inc(s.dma_start(d["ar_ph_in"].ap()[1 - b, :, :], zrow[:, :]))
    core_switch(k, phstore)

    with nc.Block() as blk:
        @blk.gpsimd
        def _(g):
            g.wait_ge(k.hd.sem, k.hd.n)
            k.agkv = coll(k, g, "AllGather", ALU.bypass, d["kv_in"], d["kv_out"])
            k.arph = coll(k, g, "AllReduce", ALU.add, d["ar_ph_in"], d["ar_ph_out"])

    if k.dump is not None:
        dmp(k, lambda: k.qp[:KD, 0:128], rows=KD)
        dmp(k, lambda: k.ph[:, :])
    loc.close()


def p4_gmw(k):
    nc = k.nc
    d = k.d
    loc = contextlib.ExitStack()
    ph2 = k.salloc(loc, "ph2", (128, 2 * NJ))
    ph2b = k.salloc(loc, "ph2b", (128, 2 * NJ), BF16)
    gw2T_sb = k.salloc(loc, "gw2T_sb", (128, NJ * NE), BF16)
    gb2_sb = k.salloc(loc, "gb2_sb", (1, NE))
    tb1_sb = k.salloc(loc, "tb1_sb", (NE, F0), BF16)
    tb2_sb = k.salloc(loc, "tb2_sb", (NE, E), BF16)
    stg = k.salloc(loc, "mwstg", (128, 2 * 9 * 512), BF16)
    mwo = k.salloc(loc, "mwo", (128, 2 * 512), BF16)
    k.g2 = k.salloc(loc, "g2", (1, 2 * NE))
    k.gcol = k.salloc(loc, "gcol", (NE, 2), BF16)
    k.gbc = k.salloc(loc, "gbc", (128, 2 * NE))
    k.ige = k.salloc(loc, "ige", (128, 17 * 128), BF16)

    with nc.Block() as blk:
        @blk.gpsimd
        def _(g):
            wt(g, k.arph)
            for b in range(B):
                k.gd.inc(g.dma_start(ph2[:, b * NJ:(b + 1) * NJ],
                                     d["ar_ph_out"].ap()[b, :, :]))
            src = bass.AP(d["gw2T"], 0, [[NE, 128], [128 * NE, NJ], [1, NE]])
            k.gd.inc(g.dma_start(gw2T_sb[:, :], src))
            k.gd.inc(g.dma_start(gb2_sb[:1, :], d["gb2"].ap()))
            k.gd.inc(g.dma_start(tb1_sb[:, :], d["Tb1"].ap()))
            k.gd.inc(g.dma_start(tb2_sb[:, :], d["Tb2"].ap()))
        gm = k.gd.mark()

        @blk.vector
        def _(v):
            wt(v, gm)
            V(k, v, lambda: v.tensor_scalar_mul(ph2b[:, :], ph2[:, :], 1.0 / S))

    with (
        nc.psum_tensor("p4g", [1, 2 * NE], F32) as psg,
        nc.psum_tensor("p4b", [128, 2 * NE], F32) as psgb,
        nc.psum_tensor("p4gc", [NE, 2], F32) as psgc,
        nc.psum_tensor("p4m1", [128, 64], F32) as psmb1,
        nc.psum_tensor("p4m2", [128, 16], F32) as psmb2,
    ):
        with nc.Block() as blk:
            @blk.tensor
            def _(t):
                for b in range(B):
                    for j in range(NJ):
                        k.pe.inc(t.matmul(
                            psg[:, b * NE:(b + 1) * NE],
                            ph2b[:, b * NJ + j:b * NJ + j + 1],
                            gw2T_sb[:, j * NE:(j + 1) * NE],
                            start=(j == 0), stop=(j == NJ - 1)))
            pm = k.pe.mark()

            @blk.vector
            def _(v):
                wt(v, pm)
                for b in range(B):
                    V(k, v, lambda b=b: v.tensor_tensor(
                        k.g2[:1, b * NE:(b + 1) * NE], psg[:1, b * NE:(b + 1) * NE],
                        gb2_sb[:1, :], op=ALU.add))
            vm = k.ve.mark()

            @blk.tensor
            def _(t):
                wt(t, vm)
                k.pe.inc(t.matmul(psgb[:, :], k.onesf[:1, :], k.g2[:1, :],
                                  start=True, stop=True))
                for b in range(B):
                    k.pe.inc(t.transpose(psgc[:, b:b + 1],
                                         k.g2[:1, b * NE:(b + 1) * NE],
                                         k.idf[:1, :1]))
            pm2 = k.pe.mark()

            @blk.vector
            def _(v):
                wt(v, pm2)
                V(k, v, lambda: v.tensor_copy(k.gbc[:, :], psgb[:, :]))
                V(k, v, lambda: v.tensor_copy(k.gcol[:, :], psgc[:, :]))

        with nc.Block() as blk:
            @blk.vector
            def _(v):
                V(k, v, lambda: v.tensor_copy(k.ige[:, 0:128], k.idb[:, :]))
                for b in range(B):
                    for e in range(NE):
                        i = 1 + b * NE + e
                        V(k, v, lambda i=i, b=b, e=e: v.tensor_scalar_mul(
                            k.ige[:, i * 128:(i + 1) * 128], k.idb[:, :],
                            k.gbc[:, b * NE + e:b * NE + e + 1]))

        with nc.Block() as blk:
            @blk.tensor
            def _(t):
                for b in range(B):
                    for ch in range(F0 // 128):
                        k.pe.inc(t.matmul(psmb1[:, b * 32 + ch:b * 32 + ch + 1],
                                          tb1_sb[:, ch * 128:(ch + 1) * 128],
                                          k.gcol[:, b:b + 1],
                                          start=True, stop=True))
                    for ch in range(NJ):
                        k.pe.inc(t.matmul(psmb2[:, b * 8 + ch:b * 8 + ch + 1],
                                          tb2_sb[:, ch * 128:(ch + 1) * 128],
                                          k.gcol[:, b:b + 1],
                                          start=True, stop=True))
            k.pm3 = k.pe.mark()

        def mbsel(cid):
            b = cid // 4
            with nc.Block() as blk:
                @blk.vector
                def _(v, b=b):
                    wt(v, k.pm3)
                    V(k, v, lambda b=b: v.tensor_tensor(
                        k.pmv[:, k.PM["mb1"]:k.PM["mb1"] + 32],
                        psmb1[:, b * 32:(b + 1) * 32],
                        k.pmv[:, k.PM["b1"]:k.PM["b1"] + 32], op=ALU.add))
                    V(k, v, lambda b=b: v.tensor_tensor(
                        k.pmv[:, k.PM["mb2"]:k.PM["mb2"] + 8],
                        psmb2[:, b * 8:(b + 1) * 8],
                        k.pmv[:, k.PM["b2"]:k.PM["b2"] + 8], op=ALU.add))
        core_switch(k, mbsel)

    jobs = [("w1", jh) for jh in range(NJ)] + [("w2", jf) for jf in range(F0 // 128)]
    npair = 2 * len(jobs)

    def pair_region(m):
        n, b = m // 2, m % 2
        kind, idx = jobs[n]
        if kind == "w1":
            off = b * MW1N + idx * 128 * 512
            return bass.AP(d["mw_in"], off, [[512, 128], [1, 512]]), 512
        off = 2 * MW1N + b * MW2N + idx * 128 * 128
        return bass.AP(d["mw_in"], off, [[128, 128], [1, 128]]), 128

    with (
        nc.psum_tensor("p4c0", [128, 512], F32) as pc0,
        nc.psum_tensor("p4c1", [128, 512], F32) as pc1,
    ):
        pcs = [pc0, pc1]
        with nc.Block() as blk:
            ldm = [k.gsp[0], k.gsp[1]]
            ldm0 = [ldm[0].n, ldm[1].n]
            stm = [k.sp[2], k.sp[3]]
            stm0 = [stm[0].n, stm[1].n]
            pe0 = k.pe.n
            ae0 = k.ae.n

            @blk.gpsimd
            def _(g):
                for n, (kind, idx) in enumerate(jobs):
                    if n >= 2:
                        g.wait_ge(k.pe.sem, pe0 + 18 * (n - 1))
                    buf = stg[:, (n % 2) * 9 * 512:(n % 2 + 1) * 9 * 512]
                    sl = ldm[n % 2]
                    if kind == "w1":
                        sl.inc(g.dma_start(
                            buf[:, 0:512],
                            d["W1T_s"].ap()[idx * 128:(idx + 1) * 128, :]))
                        for e in range(NE):
                            sl.inc(g.dma_start(
                                buf[:, (1 + e) * 512:(2 + e) * 512],
                                d["TW1T_s"].ap()[e, idx * 128:(idx + 1) * 128, :]))
                    else:
                        sl.inc(g.dma_start(
                            buf[:, 0:128],
                            d["W2T_s"].ap()[idx * 128:(idx + 1) * 128, :]))
                        for e in range(NE):
                            sl.inc(g.dma_start(
                                buf[:, 512 + e * 128:512 + (e + 1) * 128],
                                d["TW2T_s"].ap()[e, idx * 128:(idx + 1) * 128, :]))

            @blk.tensor
            def _(t):
                for m in range(npair):
                    n, b = m // 2, m % 2
                    kind, idx = jobs[n]
                    buf = stg[:, (n % 2) * 9 * 512:(n % 2 + 1) * 9 * 512]
                    t.wait_ge(ldm[n % 2].sem, ldm0[n % 2] + 16 * 9 * (n // 2 + 1))
                    if m >= 2:
                        t.wait_ge(k.ae.sem, ae0 + m - 1)
                    ps = pcs[m % 2]
                    if kind == "w1":
                        w0, wd = buf[:, 0:512], 512
                        srcs = [buf[:, (1 + e) * 512:(2 + e) * 512] for e in range(NE)]
                    else:
                        w0, wd = buf[:, 0:128], 128
                        srcs = [buf[:, 512 + e * 128:512 + (e + 1) * 128]
                                for e in range(NE)]
                    k.pe.inc(t.matmul(ps[:, 0:wd], k.ige[:, 0:128], w0,
                                      start=True, stop=False))
                    for e in range(NE):
                        ig = k.ige[:, (1 + b * NE + e) * 128:(2 + b * NE + e) * 128]
                        k.pe.inc(t.matmul(ps[:, 0:wd], ig, srcs[e],
                                          start=False, stop=(e == NE - 1)))

            @blk.scalar
            def _(s):
                for m in range(npair):
                    s.wait_ge(k.pe.sem, pe0 + 9 * (m + 1))
                    if m >= 2:
                        pm_ = m - 2
                        s.wait_ge(stm[pm_ % 2].sem,
                                  stm0[pm_ % 2] + 16 * (pm_ // 2 + 1))
                    _, w = pair_region(m)
                    A(k, s, lambda m=m, w=w: s.copy(
                        mwo[:, (m % 2) * 512:(m % 2) * 512 + w],
                        pcs[m % 2][:, 0:w]))

            @blk.sync
            def _(s):
                for m in range(npair):
                    s.wait_ge(k.ae.sem, ae0 + m + 1)
                    ap, w = pair_region(m)
                    stm[m % 2].inc(s.dma_start(
                        ap, mwo[:, (m % 2) * 512:(m % 2) * 512 + w]))

    with nc.Block() as blk:
        @blk.gpsimd
        def _(g):
            g.wait_ge(k.sp[2].sem, k.sp[2].n)
            g.wait_ge(k.sp[3].sem, k.sp[3].n)
            k.agmw = coll(k, g, "AllGather", ALU.bypass, d["mw_in"], d["mw_out"])

    if k.dump is not None:
        dmp(k, lambda: k.gbc[:, 0:16])
        dmp(k, lambda: k.pmv[:, 80:120])
        mwchk = k.salloc(loc, "mwchk", (128, 512), BF16)
        with nc.Block() as blk:
            @blk.gpsimd
            def _(g):
                wt(g, k.agmw)
                k.gd.inc(g.dma_start(
                    mwchk[:, :], bass.AP(d["mw_out"], 0, [[512, 128], [1, 512]])))
        gm2 = k.gd.mark()
        with nc.Block() as blk:
            @blk.vector
            def _(v):
                wt(v, gm2)
        dmp(k, lambda: mwchk[:, :])
    loc.close()


def p5_attn0(k):
    nc = k.nc
    d = k.d
    loc = contextlib.ExitStack()
    ksb = k.salloc(loc, "ksb", (KD, S), BF16)
    v65 = k.salloc(loc, "v65", (128, (S // 128) * 65), BF16)
    peb = k.salloc(loc, "peb", (128, 2 * 512), BF16)
    pvs = k.salloc(loc, "pvs", (KD, 512))
    zrow = k.salloc(loc, "zrow", (1, 512))
    rz = k.salloc(loc, "rz", (1, 512))
    rzs = k.salloc(loc, "rzs", (128, 512))
    tmpb = k.salloc(loc, "p5tmp", (128, 512))

    with (
        nc.psum_tensor("p5s0", [128, 512], F32) as ps0,
        nc.psum_tensor("p5s1", [128, 512], F32) as ps1,
        nc.psum_tensor("p5pv", [65, 512], F32) as ps65,
        nc.psum_tensor("p5bc", [128, 512], F32) as psB,
        nc.psum_tensor("p5o0", [128, 512], F32) as po0,
        nc.psum_tensor("p5o1", [128, 512], F32) as po1,
    ):
        pss = [ps0, ps1]
        pos = [po0, po1]

        def body(cid):
            b = cid // 4
            q0 = (cid % 4) * TOKC
            nreg = (cid % 4) + 1

            with nc.Block() as blk:
                @blk.sync
                def _(s):
                    wt(s, k.agkv)
                    for r in range(nreg):
                        src = bass.AP(d["kv_out"], (4 * b + r) * KVN,
                                      [[TOKC, KD], [1, TOKC]])
                        k.hd.inc(s.dma_start(ksb[:, r * TOKC:(r + 1) * TOKC], src))
                        srcv = bass.AP(d["kv_out"],
                                       (4 * b + r) * KVN + KD * TOKC,
                                       [[65, 128], [65 * 128, NT], [1, 65]])
                        k.hd.inc(s.dma_start(
                            v65[:, r * NT * 65:(r + 1) * NT * 65]
                            .rearrange("p (i c) -> p i c", c=65), srcv))
                    for _ in range(2 * (4 - nreg)):
                        s.wait_ge(k.hd.sem, k.hd.n)
                        k.hd.inc(s.nop(nofuse=True))

            for i in range(4):
                nkt = (q0 + 512 * (i + 1)) // 128
                nkt_max = (3 * TOKC + 512 * (i + 1)) // 128
                qs = slice(i * 512, (i + 1) * 512)
                with nc.Block() as blk:
                    pe0 = k.pe.n
                    ae0 = k.ae.n
                    ve0 = k.ve.n
                    hdm = k.hd.mark()
                    s_af = {}
                    pv_af = {}

                    def diag_vi(kt):
                        return kt - (nkt - 4) if kt >= nkt - 4 else None

                    @blk.tensor
                    def _(t):
                        wt(t, hdm)

                        def S(kt):
                            if kt >= 2:
                                t.wait_ge(k.ae.sem, ae0 + kt - 1)
                            k.pe.inc(t.matmul(
                                pss[kt % 2][:, :],
                                ksb[:, kt * 128:(kt + 1) * 128],
                                k.qp[:, qs], start=True, stop=True))
                            s_af[kt] = k.pe.n

                        S(0)
                        for kt in range(nkt):
                            if kt + 1 < nkt:
                                S(kt + 1)
                            t.wait_ge(k.ae.sem, ae0 + kt + 1)
                            vi = diag_vi(kt)
                            if vi is not None:
                                t.wait_ge(k.ve.sem, ve0 + vi + 1)
                            k.pe.inc(t.matmul(
                                ps65[:, :], v65[:, kt * 65:(kt + 1) * 65],
                                peb[:, (kt % 2) * 512:(kt % 2 + 1) * 512],
                                start=(kt == 0), stop=(kt == nkt - 1)))
                            pv_af[kt] = k.pe.n
                        t.wait_ge(k.ve.sem, ve0 + 6)
                        k.pe.inc(t.matmul(psB[:, :], k.onesf[:1, :], rz[:1, :],
                                          start=True, stop=True))
                        bc_af = k.pe.n
                        t.wait_ge(k.ae.sem, ae0 + nkt + 1)
                        for j in range(NJ):
                            if j >= 2:
                                t.wait_ge(k.ve.sem, ve0 + 8 + 2 * (j - 2))
                            k.pe.inc(t.matmul(pos[j % 2][:, :],
                                              k.wv2f[:, j * 128:(j + 1) * 128],
                                              pvs[:, :], start=True, stop=True))
                            pv_af[("o", j)] = k.pe.n
                        for _ in range(2 * (nkt_max - nkt)):
                            t.wait_ge(k.pe.sem, k.pe.n)
                            k.pe.inc(t.nop(nofuse=True))

                    @blk.scalar
                    def _(s):
                        for kt in range(nkt):
                            s.wait_ge(k.pe.sem, s_af[kt])
                            if kt >= 2:
                                s.wait_ge(k.pe.sem, pv_af[kt - 2])
                            A(k, s, lambda kt=kt: s.activation(
                                peb[:, (kt % 2) * 512:(kt % 2 + 1) * 512],
                                pss[kt % 2][:, :], AF.Exp))
                        s.wait_ge(k.pe.sem, pv_af[nkt - 1])
                        A(k, s, lambda: s.copy(pvs[:, :], ps65[0:KD, :]))
                        for _ in range(nkt_max - nkt):
                            s.wait_ge(k.ae.sem, k.ae.n)
                            k.ae.inc(s.nop(nofuse=True))

                    @blk.vector
                    def _(v):
                        for kt in range(nkt - 4, nkt):
                            vi = kt - (nkt - 4)
                            dv = (kt * 128 - (q0 + 512 * i)) // 128
                            v.wait_ge(k.ae.sem, ae0 + kt + 1)
                            V(k, v, lambda kt=kt, dv=dv: v.tensor_tensor(
                                peb[:, (kt % 2) * 512:(kt % 2 + 1) * 512],
                                peb[:, (kt % 2) * 512:(kt % 2 + 1) * 512],
                                k.mask0[:, dv * 512:(dv + 1) * 512], op=ALU.mult))
                        v.wait_ge(k.pe.sem, pv_af[nkt - 1])
                        V(k, v, lambda: v.tensor_copy(zrow[:1, :], ps65[KD:KD + 1, :]))
                        V(k, v, lambda: v.reciprocal(rz[:1, :], zrow[:1, :]))
                        v.wait_ge(k.pe.sem, pv_af[nkt - 1] + 1)
                        V(k, v, lambda: v.tensor_copy(rzs[:, :], psB[:, :]))
                        for j in range(NJ):
                            v.wait_ge(k.pe.sem, pv_af[("o", j)])
                            V(k, v, lambda j=j: v.tensor_mul(
                                tmpb[:, :], pos[j % 2][:, :], rzs[:, :]))
                            xsl = k.xT[:, j * TOKC + i * 512:j * TOKC + (i + 1) * 512]
                            V(k, v, lambda xsl=xsl: v.tensor_add(xsl, xsl, tmpb[:, :]))

        core_switch(k, body)
    if k.dump is not None:
        dmp(k, lambda: k.xT[:, 0:256])
    loc.close()


def p6_ffn(k):
    nc = k.nc
    d = k.d
    loc = contextlib.ExitStack()
    stat1 = k.salloc(loc, "stat1", (128, 2 * 8 * 128), BF16)
    ugel = k.salloc(loc, "ugel", (128, 2 * 512), BF16)
    u_all = k.salloc(loc, "u_all", (128, 32 * 512), BF16)
    stat2 = k.salloc(loc, "stat2", (128, 2 * 8 * 128), BF16)
    tmpo = k.salloc(loc, "p6tmp", (128, 512))

    def mwcopy(cid):
        b = cid // 4
        with nc.Block() as blk:
            @blk.sync
            def _(s, b=b):
                wt(s, k.agmw)
                for r in range(NC):
                    src = bass.AP(d["mw_out"], r * MWN + b * MW1N,
                                  [[512, 1024], [1, 512]])
                    dst = bass.AP(d["mw1t_mine"], r * 512, [[4096, 1024], [1, 512]])
                    k.hd.inc(s.dma_start(dst, src))
                    src2 = bass.AP(d["mw_out"], r * MWN + 2 * MW1N + b * MW2N,
                                   [[128, 4096], [1, 128]])
                    dst2 = bass.AP(d["mw2t_mine"], r * 128, [[1024, 4096], [1, 128]])
                    k.hd.inc(s.dma_start(dst2, src2))
    core_switch(k, mwcopy)
    mwcm = k.hd.mark()

    with (
        nc.psum_tensor("p6u0", [128, 512], F32) as pu0,
        nc.psum_tensor("p6u1", [128, 512], F32) as pu1,
    ):
        pus = [pu0, pu1]
        with nc.Block() as blk:
            lds = [k.sp[4], k.sp[5]]
            lds0 = [lds[0].n, lds[1].n]
            sts = [k.sp[6], k.sp[7]]
            sts0 = [sts[0].n, sts[1].n]
            pe0 = k.pe.n
            ae0 = k.ae.n
            pe_rec = {}

            @blk.sync
            def _(s):
                wt(s, mwcm)
                for ft in range(32):
                    if ft >= 2:
                        s.wait_ge(k.pe.sem, pe0 + 8 * (4 * (ft - 2) + 4))
                    src = bass.AP(d["mw1t_mine"], ft * 128,
                                  [[4096, 128], [128 * 4096, 8], [1, 128]])
                    lds[ft % 2].inc(s.dma_start(
                        stat1[:, (ft % 2) * 1024:(ft % 2 + 1) * 1024], src))

            @blk.tensor
            def _(t):
                for m in range(128):
                    ft, ts = m // 4, m % 4
                    t.wait_ge(lds[ft % 2].sem, lds0[ft % 2] + 16 * (ft // 2 + 1))
                    if m >= 2:
                        t.wait_ge(k.ae.sem, ae0 + m - 1)
                    for jh in range(NJ):
                        k.pe.inc(t.matmul(
                            pus[m % 2][:, :],
                            stat1[:, (ft % 2) * 1024 + jh * 128:
                                  (ft % 2) * 1024 + (jh + 1) * 128],
                            k.xnT[:, jh * TOKC + ts * 512:jh * TOKC + (ts + 1) * 512],
                            start=(jh == 0), stop=(jh == NJ - 1)))
                    pe_rec[m] = k.pe.n

            @blk.scalar
            def _(s):
                for m in range(128):
                    ft = m // 4
                    s.wait_ge(k.pe.sem, pe_rec[m])
                    if m >= 2:
                        pm_ = m - 2
                        s.wait_ge(sts[pm_ % 2].sem, sts0[pm_ % 2] + 16 * (pm_ // 2 + 1))
                    A(k, s, lambda m=m, ft=ft: s.activation(
                        ugel[:, (m % 2) * 512:(m % 2 + 1) * 512], pus[m % 2][:, :],
                        AF.Gelu_apprx_tanh,
                        bias=k.pmv[:, k.PM["mb1"] + ft:k.PM["mb1"] + ft + 1]))

            @blk.gpsimd
            def _(g):
                for m in range(128):
                    ft, ts = m // 4, m % 4
                    g.wait_ge(k.ae.sem, ae0 + m + 1)
                    sts[m % 2].inc(g.dma_start(
                        d["u_dram"].ap()[ft, :, ts * 512:(ts + 1) * 512],
                        ugel[:, (m % 2) * 512:(m % 2 + 1) * 512]))

    ustm = [k.sp[6].mark(), k.sp[7].mark()]
    for ts in range(4):
        with contextlib.ExitStack() as pstk:
            psE = [pstk.enter_context(
                nc.psum_tensor("p6o%d_%d" % (ts, j), [128, 512], F32))
                for j in range(NJ)]
            with nc.Block() as blk:
                ldu = k.sp[8]
                ldu0 = ldu.n
                ld2 = [k.sp[4], k.sp[5]]
                ld20 = [ld2[0].n, ld2[1].n]
                pe0 = k.pe.n
                pe_rec = {}

                @blk.sync
                def _(s):
                    wt(s, ustm[0])
                    wt(s, ustm[1])
                    src = bass.AP(d["u_dram"], ts * 512,
                                  [[TOKC, 128], [128 * TOKC, 32], [1, 512]])
                    ldu.inc(s.dma_start(u_all[:, :], src))
                    for ft in range(32):
                        if ft >= 2:
                            s.wait_ge(k.pe.sem, pe0 + 8 * (ft - 1))
                        src2 = bass.AP(d["mw2t_mine"], ft * 128 * 1024,
                                       [[1024, 128], [128, 8], [1, 128]])
                        ld2[ft % 2].inc(s.dma_start(
                            stat2[:, (ft % 2) * 1024:(ft % 2 + 1) * 1024], src2))

                @blk.tensor
                def _(t):
                    t.wait_ge(ldu.sem, ldu0 + 16)
                    for ft in range(32):
                        t.wait_ge(ld2[ft % 2].sem, ld20[ft % 2] + 16 * (ft // 2 + 1))
                        for jE in range(NJ):
                            k.pe.inc(t.matmul(
                                psE[jE][:, :],
                                stat2[:, (ft % 2) * 1024 + jE * 128:
                                      (ft % 2) * 1024 + (jE + 1) * 128],
                                u_all[:, ft * 512:(ft + 1) * 512],
                                start=(ft == 0), stop=(ft == 31)))
                        pe_rec[ft] = k.pe.n
                pem = k.pe.mark()

                @blk.vector
                def _(v):
                    wt(v, pem)
                    for jE in range(NJ):
                        xsl = k.xT[:, jE * TOKC + ts * 512:jE * TOKC + (ts + 1) * 512]
                        V(k, v, lambda jE=jE, xsl=xsl: v.scalar_tensor_tensor(
                            tmpo[:, :], psE[jE][:, :],
                            k.pmv[:, k.PM["mb2"] + jE:k.PM["mb2"] + jE + 1],
                            xsl, op0=ALU.add, op1=ALU.add))
                        V(k, v, lambda jE=jE, xsl=xsl: v.tensor_copy(xsl, tmpo[:, :]))

    if k.dump is not None:
        dmp(k, lambda: k.xT[:, 0:256])
    loc.close()


def p7_l1(k):
    nc = k.nc
    d = k.d
    loc = contextlib.ExitStack()
    x1tt = k.salloc(loc, "x1tt", (128, 2 * E))
    xn1t = k.salloc(loc, "xn1t", (128, 2 * E), BF16)
    st7 = k.salloc(loc, "p7st", (128, 8 * NT))
    mask1 = k.salloc(loc, "mask1", (128, 256), BF16)
    iota8 = k.salloc(loc, "iota8_sb", (128, NE))
    wg_sb = k.salloc(loc, "wg_sb", (128, NJ * NE), BF16)
    lg_tm = k.salloc(loc, "lg_tm", (128, NT * NE))
    r8 = k.salloc(loc, "r8", (128, 16))
    rsc = k.salloc(loc, "rsc", (128, 8))
    idxu = k.salloc(loc, "idxu", (128, 8), U32)
    ohs = k.salloc(loc, "ohs", (128, 8))
    cntr = k.salloc(loc, "cntr", (1, NE))
    wst = k.salloc(loc, "wst", (128, 2 * 8 * 128), BF16)
    qstg = k.salloc(loc, "qstg", (128, 2 * 512), BF16)
    q_tm = k.salloc(loc, "q_tm", (128, 4 * E), BF16)
    k_tm = k.salloc(loc, "k_tm", (128, 4 * E), BF16)
    v_tm = k.salloc(loc, "v_tm", (128, 4 * E), BF16)
    S2 = k.salloc(loc, "S2", (128, 4 * C1 * C1), BF16)
    Ebuf = k.salloc(loc, "Ebuf", (128, 2 * C1 * C1), BF16)
    junk7 = Ebuf[:, 0:E]
    attn_tm = k.salloc(loc, "attn_tm", (128, 4 * E), BF16)
    attnT = k.salloc(loc, "attnT", (128, NJ * 512), BF16)
    tmpo7 = x1tt[:, 0:512]
    Nf = x1tt[:, 512:512 + 2 * C1]
    Zf = x1tt[:, 640:640 + 2 * C1]
    rZf = x1tt[:, 768:768 + 2 * C1]
    lgf = None

    with (
        nc.psum_tensor("p7f0", [128, 128], F32) as pf0,
        nc.psum_tensor("p7f1", [128, 128], F32) as pf1,
        nc.psum_tensor("p7b0", [128, 128], BF16) as pb0,
        nc.psum_tensor("p7b1", [128, 128], BF16) as pb1,
    ):
        pfs = [pf0, pf1]
        pbs = [pb0, pb1]
        with nc.Block() as blk:
            xst = [k.sp[0], k.sp[1]]
            xst0 = [xst[0].n, xst[1].n]
            ve0 = k.ve.n
            ae0 = k.ae.n
            pe0 = k.pe.n

            def pe_at(i, op):
                return pe0 + 16 * i + op

            def ae_at(i, op):
                return ae0 + 10 * i + op

            def ve_at(i, op):
                return ve0 + 15 * i + op

            @blk.tensor
            def _(t):
                for i in range(NT):
                    for j in range(NJ):
                        if j >= 2:
                            if j % 2 == 0:
                                t.wait_ge(k.ae.sem, ae_at(i, (j - 2) // 2 + 1))
                            else:
                                t.wait_ge(k.ve.sem, ve_at(i, (j - 3) // 2 + 1))
                        elif i >= 1:
                            if j % 2 == 0:
                                t.wait_ge(k.ae.sem, ae_at(i - 1, 4))
                            else:
                                t.wait_ge(k.ve.sem, ve_at(i - 1, 4))
                        k.pe.inc(t.transpose(
                            pfs[j % 2][:, :],
                            k.xT[:, j * TOKC + i * 128:j * TOKC + (i + 1) * 128],
                            k.idf[:, :]))
                    t.wait_ge(k.ve.sem, ve_at(i, 11))
                    for j in range(NJ):
                        if j >= 2:
                            if j % 2 == 0:
                                t.wait_ge(k.ae.sem, ae_at(i, 6 + (j - 2) // 2 + 1))
                            else:
                                t.wait_ge(k.ve.sem, ve_at(i, 11 + (j - 3) // 2 + 1))
                        elif i >= 1:
                            if j % 2 == 0:
                                t.wait_ge(k.ae.sem, ae_at(i - 1, 10))
                            else:
                                t.wait_ge(k.ve.sem, ve_at(i - 1, 15))
                        k.pe.inc(t.transpose(
                            pbs[j % 2][:, :],
                            xn1t[:, (i % 2) * E + j * 128:(i % 2) * E + (j + 1) * 128],
                            k.idb[:, :]))

            @blk.scalar
            def _(s):
                for i in range(NT):
                    c = 8 * i
                    for j in range(0, NJ, 2):
                        s.wait_ge(k.pe.sem, pe_at(i, j + 1))
                        if i >= 2 and j == 0:
                            s.wait_ge(k.ve.sem, ve_at(i - 2, 11))
                        A(k, s, lambda i=i, j=j: s.copy(
                            x1tt[:, (i % 2) * E + j * 128:(i % 2) * E + (j + 1) * 128],
                            pfs[j % 2][:, :]))
                    s.wait_ge(k.ve.sem, ve_at(i, 10))
                    A(k, s, lambda c=c: s.activation(st7[:, c + 5:c + 6],
                                                     st7[:, c + 4:c + 5], AF.Ln))
                    A(k, s, lambda c=c: s.activation(st7[:, c + 5:c + 6],
                                                     st7[:, c + 5:c + 6], AF.Exp,
                                                     scale=-0.5))
                    for j in range(0, NJ, 2):
                        s.wait_ge(k.pe.sem, pe_at(i, 8 + j + 1))
                        A(k, s, lambda i=i, j=j: s.copy(
                            k.xnT[:, j * TOKC + i * 128:j * TOKC + (i + 1) * 128],
                            pbs[j % 2][:, :]))

            @blk.vector
            def _(v):
                for i in range(NT):
                    c = 8 * i
                    xt = x1tt[:, (i % 2) * E:(i % 2 + 1) * E]
                    xn = xn1t[:, (i % 2) * E:(i % 2 + 1) * E]
                    for j in range(1, NJ, 2):
                        v.wait_ge(k.pe.sem, pe_at(i, j + 1))
                        V(k, v, lambda i=i, j=j: v.tensor_copy(
                            x1tt[:, (i % 2) * E + j * 128:(i % 2) * E + (j + 1) * 128],
                            pfs[j % 2][:, :]))
                    v.wait_ge(k.ae.sem, ae_at(i, 4))
                    V(k, v, lambda xt=xt, c=c: v.reduce_sum(st7[:, c:c + 1], xt, axis=AX.X))
                    V(k, v, lambda xt=xt, c=c: v.scalar_tensor_tensor(
                        junk7, xt, 1.0, xt, op0=ALU.bypass, op1=ALU.mult,
                        accum_out=st7[:, c + 1:c + 2]))
                    V(k, v, lambda c=c: v.tensor_scalar_mul(
                        st7[:, c + 2:c + 3], st7[:, c:c + 1], 1.0 / E))
                    V(k, v, lambda c=c: v.tensor_scalar_mul(
                        st7[:, c + 3:c + 4], st7[:, c + 1:c + 2], 1.0 / E))
                    V(k, v, lambda c=c: v.scalar_tensor_tensor(
                        st7[:, c + 3:c + 4], st7[:, c + 2:c + 3], st7[:, c + 2:c + 3],
                        st7[:, c + 3:c + 4], op0=ALU.mult, op1=ALU.subtract))
                    V(k, v, lambda c=c: v.tensor_scalar(
                        st7[:, c + 4:c + 5], st7[:, c + 3:c + 4], -1.0, EPS,
                        op0=ALU.mult, op1=ALU.add))
                    v.wait_ge(k.ae.sem, ae_at(i, 6))
                    if i >= 2:
                        v.wait_ge(k.pe.sem, pe_at(i - 2, 16))
                        v.wait_ge(xst[i % 2].sem, xst0[i % 2] + 16 * (i // 2))
                    V(k, v, lambda xt=xt, xn=xn, c=c: v.tensor_scalar(
                        xn, xt, st7[:, c + 2:c + 3], st7[:, c + 5:c + 6],
                        op0=ALU.subtract, op1=ALU.mult))
                    for j in range(1, NJ, 2):
                        v.wait_ge(k.pe.sem, pe_at(i, 8 + j + 1))
                        V(k, v, lambda i=i, j=j: v.tensor_copy(
                            k.xnT[:, j * TOKC + i * 128:j * TOKC + (i + 1) * 128],
                            pbs[j % 2][:, :]))

            @blk.sync
            def _(s):
                for i in range(NT):
                    s.wait_ge(k.ve.sem, ve_at(i, 11))
                    xst[i % 2].inc(s.dma_start(
                        d["xn1_in"].ap()[i * 128:(i + 1) * 128, :],
                        xn1t[:, (i % 2) * E:(i % 2 + 1) * E]))

            @blk.gpsimd
            def _(g):
                k.gd.inc(g.dma_start(
                    mask1[:, :],
                    bass.AP(d["maskL1"], 0, [[4096, 128], [64, 16], [1, 16]])))
                k.gd.inc(g.dma_start(iota8[:, :], d["iota8"].ap()))
                src = bass.AP(d["wgate"], 0, [[NE, 128], [128 * NE, NJ], [1, NE]])
                k.gd.inc(g.dma_start(wg_sb[:, :], src))
        ln1m = (k.gd.mark(), xst[0].mark(), xst[1].mark())

    lgf = x1tt

    with nc.Block() as blk:
        @blk.gpsimd
        def _(g):
            wt(g, ln1m[1])
            wt(g, ln1m[2])
            k.agxn1 = coll(k, g, "AllGather", ALU.bypass, d["xn1_in"], d["xn1_out"])

    with (
        nc.psum_tensor("p7r0", [NE, 512], F32) as pr0,
        nc.psum_tensor("p7r1", [NE, 512], F32) as pr1,
        nc.psum_tensor("p7t0", [128, NE], F32) as pt0,
        nc.psum_tensor("p7t1", [128, NE], F32) as pt1,
        nc.psum_tensor("p7c", [1, NE], F32) as pcnt,
    ):
        prs = [pr0, pr1]
        pts = [pt0, pt1]
        with nc.Block() as blk:
            ve0 = k.ve.n
            pe_rec = {}

            @blk.tensor
            def _(t):
                wt(t, ln1m[0])
                for ts in range(4):
                    if ts >= 2:
                        t.wait_ge(k.ve.sem, ve0 + ts - 1)
                    for j in range(NJ):
                        k.pe.inc(t.matmul(
                            prs[ts % 2][:, :], wg_sb[:, j * NE:(j + 1) * NE],
                            k.xnT[:, j * TOKC + ts * 512:j * TOKC + (ts + 1) * 512],
                            start=(j == 0), stop=(j == NJ - 1)))
                    pe_rec[ts] = k.pe.n

            @blk.vector
            def _(v):
                for ts in range(4):
                    v.wait_ge(k.pe.sem, pe_rec[ts])
                    V(k, v, lambda ts=ts: v.tensor_copy(
                        lgf[0:NE, ts * 512:(ts + 1) * 512], prs[ts % 2][:, :]))

        with nc.Block() as blk:
            ve0 = k.ve.n
            pe_rec = {}

            @blk.tensor
            def _(t):
                for i in range(NT):
                    if i >= 2:
                        t.wait_ge(k.ve.sem, ve0 + i - 1)
                    k.pe.inc(t.transpose(pts[i % 2][:, :],
                                         lgf[0:NE, i * 128:(i + 1) * 128],
                                         k.idf[:NE, :NE]))
                    pe_rec[i] = k.pe.n

            @blk.vector
            def _(v):
                for i in range(NT):
                    v.wait_ge(k.pe.sem, pe_rec[i])
                    V(k, v, lambda i=i: v.tensor_copy(
                        lg_tm[:, i * NE:(i + 1) * NE], pts[i % 2][:, :]))

        with nc.Block() as blk:
            ve_rec = {}
            ae_rec = {}
            ve0 = k.ve.n
            ae0 = k.ae.n
            pe0c = k.pe.n

            @blk.vector
            def _(v):
                for i in range(NT):
                    lt = lg_tm[:, i * NE:(i + 1) * NE]
                    V(k, v, lambda lt=lt: v.max(r8[:, 0:8], lt))
                    V(k, v, lambda lt=lt: v.max_index(idxu[:, :], r8[:, 0:8], lt))
                    V(k, v, lambda i=i: v.tensor_copy(k.rtri[:, 2 * i:2 * i + 1],
                                                      idxu[:, 0:1]))
                    V(k, v, lambda i=i: v.tensor_copy(k.rtri[:, 2 * i + 1:2 * i + 2],
                                                      idxu[:, 1:2]))
                    V(k, v, lambda: v.tensor_scalar_mul(rsc[:, 0:1], r8[:, 0:1], -1.0))
                    ve_rec[("pre", i)] = k.ve.n
                    v.wait_ge(k.ae.sem, ae0 + 2 * i + 1)
                    V(k, v, lambda: v.reciprocal(rsc[:, 2:3], rsc[:, 1:2]))
                    V(k, v, lambda i=i: v.tensor_copy(k.rtrw[:, 2 * i:2 * i + 1],
                                                      rsc[:, 2:3]))
                    v.wait_ge(k.ae.sem, ae0 + 2 * i + 2)
                    V(k, v, lambda i=i: v.tensor_mul(k.rtrw[:, 2 * i + 1:2 * i + 2],
                                                     rsc[:, 3:4], rsc[:, 2:3]))
                    V(k, v, lambda i=i: v.tensor_scalar(
                        k.oh1[:, i * NE:(i + 1) * NE], iota8[:, :],
                        k.rtri[:, 2 * i:2 * i + 1], None, op0=ALU.is_equal))
                    V(k, v, lambda i=i: v.tensor_scalar(
                        k.oh2[:, i * NE:(i + 1) * NE], iota8[:, :],
                        k.rtri[:, 2 * i + 1:2 * i + 2], None, op0=ALU.is_equal))
                    if i >= 1:
                        v.wait_ge(k.pe.sem, pe0c + i)
                    V(k, v, lambda i=i: v.tensor_add(
                        ohs[:, :], k.oh1[:, i * NE:(i + 1) * NE],
                        k.oh2[:, i * NE:(i + 1) * NE]))
                    ve_rec[("ohs", i)] = k.ve.n

            @blk.scalar
            def _(s):
                for i in range(NT):
                    lt = lg_tm[:, i * NE:(i + 1) * NE]
                    s.wait_ge(k.ve.sem, ve_rec[("pre", i)])
                    A(k, s, lambda lt=lt: s.activation(
                        r8[:, 8:16], lt, AF.Exp, bias=rsc[:, 0:1],
                        accum_out=rsc[:, 1:2]))
                    A(k, s, lambda: s.activation(rsc[:, 3:4], r8[:, 1:2], AF.Exp,
                                                 bias=rsc[:, 0:1]))

            @blk.tensor
            def _(t):
                for i in range(NT):
                    t.wait_ge(k.ve.sem, ve_rec[("ohs", i)])
                    k.pe.inc(t.matmul(pcnt[:, :], k.onesf[:, 0:1], ohs[:, :],
                                      start=(i == 0), stop=(i == NT - 1)))
            pcm = k.pe.mark()

            @blk.vector
            def _(v):
                wt(v, pcm)
                V(k, v, lambda: v.tensor_copy(cntr[:1, :], pcnt[:1, :]))
            cvm = k.ve.mark()

            @blk.sync
            def _(s):
                wt(s, cvm)
                k.hd.inc(s.dma_start(d["cnt_in"].ap(), cntr[:1, :]))

        with nc.Block() as blk:
            @blk.gpsimd
            def _(g):
                g.wait_ge(k.hd.sem, k.hd.n)
                k.agcnt = coll(k, g, "AllGather", ALU.bypass, d["cnt_in"], d["cnt_out"])

    with (
        nc.psum_tensor("p7q0", [128, 512], F32) as pq0,
        nc.psum_tensor("p7q1", [128, 512], F32) as pq1,
        nc.psum_tensor("p7x0", [128, 128], BF16) as px0,
        nc.psum_tensor("p7x1", [128, 128], BF16) as px1,
        nc.psum_tensor("p7w0", [128, 512], F32) as pw0,
        nc.psum_tensor("p7w1", [128, 512], F32) as pw1,
    ):
        pqs = [pq0, pq1]
        pxs = [px0, px1]
        pws = [pw0, pw1]
        mats = [("Wq1", q_tm, "bq", 0.125), ("Wk1", k_tm, "bk", 1.0),
                ("Wv1_", v_tm, "bv", 1.0)]

        with nc.Block() as blk:
            lds = [k.gsp[2], k.gsp[3]]
            lds0 = [lds[0].n, lds[1].n]
            ae0 = k.ae.n
            ve0 = k.ve.n
            pe_rec = {}
            petp_rec = {}
            pe_rec_pre = k.pe.n

            @blk.gpsimd
            def _(g):
                for m in range(24):
                    mi, jo = m // 8, m % 8
                    if m >= 2:
                        g.wait_ge(k.pe.sem, pe_rec_pre + 12 * (m - 1))
                    src = bass.AP(d[mats[mi][0]], jo * 128,
                                  [[E, 128], [128 * E, NJ], [1, 128]])
                    lds[m % 2].inc(g.dma_start(
                        wst[:, (m % 2) * 1024:(m % 2 + 1) * 1024], src))

            @blk.tensor
            def _(t):
                for m in range(24):
                    mi, jo = m // 8, m % 8
                    t.wait_ge(lds[m % 2].sem, lds0[m % 2] + 16 * (m // 2 + 1))
                    if m >= 2:
                        t.wait_ge(k.ae.sem, ae0 + 3 * (m - 2) + 1)
                    for j in range(NJ):
                        k.pe.inc(t.matmul(
                            pqs[m % 2][:, :],
                            wst[:, (m % 2) * 1024 + j * 128:
                                (m % 2) * 1024 + (j + 1) * 128],
                            k.xnT[:, j * TOKC:j * TOKC + 512],
                            start=(j == 0), stop=(j == NJ - 1)))
                    pe_rec[m] = k.pe.n
                    t.wait_ge(k.ae.sem, ae0 + 3 * m + 1)
                    for tt in range(4):
                        if m * 4 + tt >= 2:
                            pn = m * 4 + tt - 2
                            pm_, ptt = pn // 4, pn % 4
                            if ptt % 2 == 0:
                                t.wait_ge(k.ae.sem, ae0 + 3 * pm_ + 1 + ptt // 2 + 1)
                            else:
                                t.wait_ge(k.ve.sem, ve0 + 2 * pm_ + (ptt - 1) // 2 + 1)
                        k.pe.inc(t.transpose(
                            pxs[tt % 2][:, :],
                            qstg[:, (m % 2) * 512 + tt * 128:
                                 (m % 2) * 512 + (tt + 1) * 128],
                            k.idb[:, :]))
                        petp_rec[(m, tt)] = k.pe.n

            @blk.scalar
            def _(s):
                for m in range(24):
                    mi, jo = m // 8, m % 8
                    nm, dstt, bn, sc = mats[mi]
                    s.wait_ge(k.pe.sem, pe_rec[m])
                    A(k, s, lambda jo=jo, bn=bn, sc=sc, m=m:
                      s.activation(qstg[:, (m % 2) * 512:(m % 2 + 1) * 512],
                                   pqs[m % 2][:, :], AF.Identity, scale=sc,
                                   bias=k.pmv[:, k.PM[bn] + jo:k.PM[bn] + jo + 1]))
                    for tt in (0, 2):
                        s.wait_ge(k.pe.sem, petp_rec[(m, tt)])
                        A(k, s, lambda dstt=dstt, jo=jo, tt=tt: s.copy(
                            dstt[:, tt * E + jo * 128:tt * E + (jo + 1) * 128],
                            pxs[tt % 2][:, :]))

            @blk.vector
            def _(v):
                for m in range(24):
                    mi, jo = m // 8, m % 8
                    dstt = mats[mi][1]
                    for tt in (1, 3):
                        v.wait_ge(k.pe.sem, petp_rec[(m, tt)])
                        V(k, v, lambda dstt=dstt, jo=jo, tt=tt: v.tensor_copy(
                            dstt[:, tt * E + jo * 128:tt * E + (jo + 1) * 128],
                            pxs[tt % 2][:, :]))

        for ts in range(4):
            with_qkv = ts < 3
            tsn = ts + 1
            with nc.Block() as blk:
                lds = [k.gsp[2], k.gsp[3]]
                lds0 = [lds[0].n, lds[1].n]
                ae0 = k.ae.n
                pe0 = k.pe.n
                ve_mask = {}
                ve_fin = {}
                E4 = Ebuf[:, :].rearrange("p (i a b) -> p i a b", i=2, b=C1)
                LV = [(32, 32), (16, 16), (8, 0)]
                M3 = mask1[:, :].rearrange("p (a b) -> p a b", b=16)

                def sl4_of(p):
                    return S2[:, (p % 2) * 8192:(p % 2 + 1) * 8192]\
                        .rearrange("p (i a b) -> p i a b", i=2, b=C1)

                def prevu(tt, m):
                    if m >= 2:
                        return (tt, m - 2)
                    if tt >= 1:
                        return (tt - 1, 22 + m)
                    return None

                mm_done = {}
                tp_done = {}
                cnt = pe0
                if with_qkv:
                    for tt in range(4):
                        for m in range(24):
                            cnt += 8
                            mm_done[(tt, m)] = cnt
                            cnt += 1
                            tp_done[(tt, m)] = cnt
                exp_done = {}
                evac1_done = {}
                tpev_done = {}
                acnt = ae0
                pending = []
                ins_at = {8: 0, 16: 1, 24: 2, 31: 3} if with_qkv else {}
                sched = []
                for p in range(32):
                    sched.append(("exp", p, 0))
                    acnt += 4
                    exp_done[p] = acnt
                    if p in ins_at:
                        tt = ins_at[p]
                        pending.append(("gate", tt, 0))
                        for m in range(24):
                            pending.append(("e1", tt, m))
                            pending.append(("tp", tt, m))
                    take = 12 if p != 31 else 10 ** 9
                    taken = 0
                    while pending and taken < take:
                        op = pending.pop(0)
                        sched.append(op)
                        if op[0] in ("e1", "tp"):
                            acnt += 1
                            taken += 1
                            if op[0] == "e1":
                                evac1_done[(op[1], op[2])] = acnt
                            else:
                                tpev_done[(op[1], op[2])] = acnt

                @blk.vector
                def _(v):
                    Vf(k, lambda: v.memset(Ebuf[:, :], 0.0))

                    def build(p):
                        tt, hp = p // 8, p % 8
                        base = tt * E + 2 * hp * C1
                        sl4 = sl4_of(p)
                        q2 = q_tm[:, base:base + 128]\
                            .rearrange("p (i c) -> p i c", i=2)
                        k2 = k_tm[:, base:base + 128]\
                            .rearrange("p (i c) -> p i c", i=2)
                        if p >= 2:
                            v.wait_ge(k.ae.sem, exp_done[p - 2])
                        for qq in range(4):
                            Kq = 16 * (qq + 1)
                            qap = q2[:, :, 16 * qq:16 * (qq + 1)]\
                                .unsqueeze(3).broadcast_to([128, 2, 16, Kq])
                            kap = k2[:, :, 0:Kq]\
                                .unsqueeze(2).broadcast_to([128, 2, 16, Kq])
                            Vf(k, lambda sl4=sl4, qq=qq, Kq=Kq, qap=qap,
                               kap=kap: v.tensor_tensor(
                                sl4[:, :, 16 * qq:16 * (qq + 1), 0:Kq],
                                qap, kap, op=ALU.mult))
                        Vbar(k, v)
                        mbc = M3.unsqueeze(1).broadcast_to([128, 2, 16, 16])
                        for qq in range(4):
                            Vf(k, lambda sl4=sl4, qq=qq, mbc=mbc: v.tensor_add(
                                sl4[:, :, 16 * qq:16 * (qq + 1),
                                    16 * qq:16 * (qq + 1)],
                                sl4[:, :, 16 * qq:16 * (qq + 1),
                                    16 * qq:16 * (qq + 1)],
                                mbc))
                        ve_mask[p] = k.ve.n

                    build(0)
                    for p in range(32):
                        tt, hp = p // 8, p % 8
                        base = tt * E + 2 * hp * C1
                        if p + 1 < 32:
                            build(p + 1)
                        v.wait_ge(k.ae.sem, exp_done[p])
                        EV4 = sl4_of(p)
                        vap = v_tm[:, base:base + 128]\
                            .rearrange("p (i c) -> p i c", i=2)\
                            .unsqueeze(2).broadcast_to([128, 2, C1, C1])
                        Vbar(k, v)
                        Vf(k, lambda EV4=EV4, vap=vap: v.tensor_tensor(
                            EV4, E4, vap, op=ALU.mult))
                        for hw, rlo in LV:
                            Vbar(k, v)
                            for b4 in (E4, EV4):
                                Vf(k, lambda b4=b4, hw=hw, rlo=rlo:
                                   v.tensor_add(
                                    b4[:, :, rlo:C1, 0:hw],
                                    b4[:, :, rlo:C1, 0:hw],
                                    b4[:, :, rlo:C1, hw:2 * hw]))
                        Vbar(k, v)
                        Vf(k, lambda EV4=EV4: v.reduce_sum(
                            Zf, E4[:, :, :, 0:8], axis=AX.X))
                        Vf(k, lambda EV4=EV4: v.reduce_sum(
                            Nf, EV4[:, :, :, 0:8], axis=AX.X))
                        Vbar(k, v)
                        Vf(k, lambda: v.reciprocal(rZf, Zf))
                        Vbar(k, v)
                        Vf(k, lambda base=base: v.tensor_mul(
                            attn_tm[:, base:base + 128],
                            Nf, rZf))
                        ve_fin[p] = k.ve.n

                @blk.tensor
                def _(t):
                    if not with_qkv:
                        return
                    for tt in range(4):
                        for m in range(24):
                            mi, jo = m // 8, m % 8
                            L = tt * 24 + m
                            t.wait_ge(lds[m % 2].sem,
                                      lds0[m % 2] + 16 * (L // 2 + 1))
                            pu = prevu(tt, m)
                            if pu is not None:
                                t.wait_ge(k.ae.sem, evac1_done[pu])
                            for j in range(NJ):
                                k.pe.inc(t.matmul(
                                    pqs[m % 2][:, 0:128],
                                    wst[:, (m % 2) * 1024 + j * 128:
                                        (m % 2) * 1024 + (j + 1) * 128],
                                    k.xnT[:, j * TOKC + tsn * 512 + tt * 128:
                                          j * TOKC + tsn * 512 + (tt + 1) * 128],
                                    start=(j == 0), stop=(j == NJ - 1)))
                            assert k.pe.n == mm_done[(tt, m)]
                            t.wait_ge(k.ae.sem, evac1_done[(tt, m)])
                            pu = prevu(tt, m)
                            if pu is not None:
                                t.wait_ge(k.ae.sem, tpev_done[pu])
                            k.pe.inc(t.transpose(
                                pxs[m % 2][:, :],
                                qstg[:, (m % 2) * 512:(m % 2) * 512 + 128],
                                k.idb[:, :]))
                            assert k.pe.n == tp_done[(tt, m)]

                @blk.gpsimd
                def _(g):
                    if not with_qkv:
                        return
                    for tt in range(4):
                        for m in range(24):
                            mi, jo = m // 8, m % 8
                            pu = prevu(tt, m)
                            if pu is not None:
                                g.wait_ge(k.pe.sem, mm_done[pu])
                            src = bass.AP(d[mats[mi][0]], jo * 128,
                                          [[E, 128], [128 * E, NJ], [1, 128]])
                            lds[m % 2].inc(g.dma_start(
                                wst[:, (m % 2) * 1024:(m % 2 + 1) * 1024], src))

                @blk.scalar
                def _(s):
                    for op, a1, a2 in sched:
                        if op == "exp":
                            p = a1
                            s.wait_ge(k.ve.sem, ve_mask[p])
                            if p >= 1:
                                s.wait_ge(k.ve.sem, ve_fin[p - 1])
                            sl4 = sl4_of(p)
                            for qq in range(4):
                                Kq = 16 * (qq + 1)
                                A(k, s, lambda sl4=sl4, qq=qq, Kq=Kq:
                                  s.activation(
                                    E4[:, :, 16 * qq:16 * (qq + 1), 0:Kq],
                                    sl4[:, :, 16 * qq:16 * (qq + 1), 0:Kq],
                                    AF.Exp))
                        elif op == "gate":
                            s.wait_ge(k.ve.sem, ve_fin[8 * a1 + 7])
                        elif op == "e1":
                            tt, m = a1, a2
                            mi, jo = m // 8, m % 8
                            nm, dstt, bn, sc = mats[mi]
                            s.wait_ge(k.pe.sem, mm_done[(tt, m)])
                            pu = prevu(tt, m)
                            if pu is not None:
                                s.wait_ge(k.pe.sem, tp_done[pu])
                            A(k, s, lambda jo=jo, bn=bn, sc=sc, m=m:
                              s.activation(
                                qstg[:, (m % 2) * 512:(m % 2) * 512 + 128],
                                pqs[m % 2][:, 0:128], AF.Identity, scale=sc,
                                bias=k.pmv[:, k.PM[bn] + jo:k.PM[bn] + jo + 1]))
                        else:
                            tt, m = a1, a2
                            mi, jo = m // 8, m % 8
                            dstt = mats[mi][1]
                            s.wait_ge(k.pe.sem, tp_done[(tt, m)])
                            A(k, s, lambda dstt=dstt, jo=jo, tt=tt, m=m:
                              s.copy(
                                dstt[:, tt * E + jo * 128:
                                     tt * E + (jo + 1) * 128],
                                pxs[m % 2][:, :]))
                    assert k.ae.n == acnt

            with nc.Block() as blk:
                ve0 = k.ve.n
                ae0 = k.ae.n
                pe_rec = {}

                @blk.tensor
                def _(t):
                    for n in range(32):
                        jo, tt = n // 4, n % 4
                        if n >= 2:
                            pn = n - 2
                            if pn % 2 == 0:
                                t.wait_ge(k.ae.sem, ae0 + pn // 2 + 1)
                            else:
                                t.wait_ge(k.ve.sem, ve0 + pn // 2 + 1)
                        k.pe.inc(t.transpose(
                            pxs[n % 2][:, :],
                            attn_tm[:, tt * E + jo * 128:tt * E + (jo + 1) * 128],
                            k.idb[:, :]))
                        pe_rec[n] = k.pe.n

                @blk.scalar
                def _(s):
                    for n in range(32):
                        jo, tt = n // 4, n % 4
                        if n % 2 == 0:
                            s.wait_ge(k.pe.sem, pe_rec[n])
                            A(k, s, lambda jo=jo, tt=tt, n=n: s.copy(
                                attnT[:, jo * 512 + tt * 128:jo * 512 + (tt + 1) * 128],
                                pxs[n % 2][:, :]))

                @blk.vector
                def _(v):
                    for n in range(32):
                        jo, tt = n // 4, n % 4
                        if n % 2 == 1:
                            v.wait_ge(k.pe.sem, pe_rec[n])
                            V(k, v, lambda jo=jo, tt=tt, n=n: v.tensor_copy(
                                attnT[:, jo * 512 + tt * 128:jo * 512 + (tt + 1) * 128],
                                pxs[n % 2][:, :]))

            with nc.Block() as blk:
                lds = [k.gsp[2], k.gsp[3]]
                lds0 = [lds[0].n, lds[1].n]
                ve0 = k.ve.n
                pe_rec = {}
                pe_rec_pre2 = k.pe.n

                @blk.gpsimd
                def _(g):
                    for jE in range(NJ):
                        if jE >= 2:
                            g.wait_ge(k.pe.sem, pe_rec_pre2 + 8 * (jE - 1))
                        src = bass.AP(d["Wo1"], jE * 128,
                                      [[E, 128], [128 * E, NJ], [1, 128]])
                        lds[jE % 2].inc(g.dma_start(
                            wst[:, (jE % 2) * 1024:(jE % 2 + 1) * 1024], src))

                @blk.tensor
                def _(t):
                    for jE in range(NJ):
                        t.wait_ge(lds[jE % 2].sem, lds0[jE % 2] + 16 * (jE // 2 + 1))
                        if jE >= 2:
                            t.wait_ge(k.ve.sem, ve0 + 2 * (jE - 1))
                        for j in range(NJ):
                            k.pe.inc(t.matmul(
                                pws[jE % 2][:, :],
                                wst[:, (jE % 2) * 1024 + j * 128:
                                    (jE % 2) * 1024 + (j + 1) * 128],
                                attnT[:, j * 512:(j + 1) * 512],
                                start=(j == 0), stop=(j == NJ - 1)))
                        pe_rec[jE] = k.pe.n

                @blk.vector
                def _(v):
                    for jE in range(NJ):
                        v.wait_ge(k.pe.sem, pe_rec[jE])
                        xsl = k.xT[:, jE * TOKC + ts * 512:jE * TOKC + (ts + 1) * 512]
                        V(k, v, lambda jE=jE, xsl=xsl: v.scalar_tensor_tensor(
                            tmpo7, pws[jE % 2][:, :],
                            k.pmv[:, k.PM["bo"] + jE:k.PM["bo"] + jE + 1],
                            xsl, op0=ALU.add, op1=ALU.add))
                        V(k, v, lambda jE=jE, xsl=xsl: v.tensor_copy(xsl, tmpo7))

    if k.dump is not None:
        dmp(k, lambda: k.xT[:, 0:256])
        dmp(k, lambda: lg_tm[:, 0:16])
        dmp(k, lambda: k.rtrw[:, 0:8])
        dmp(k, lambda: k.rtri[:, 0:8])
        dmp(k, lambda: cntr[:1, :], rows=1)
    k.p7loc = loc


def p8_moe(k):
    nc = k.nc
    d = k.d
    loc = contextlib.ExitStack()
    smu = k.salloc(loc, "smu", (128, 32), U32)
    k.tokpm = k.salloc(loc, "tokpm", (128, 16))
    sepm = k.salloc(loc, "sepm", (128, 16), U32)
    sopm = k.salloc(loc, "sopm", (128, 16), U32)
    w1c = k.rtrw

    dsc = contextlib.ExitStack()
    cnt8 = k.salloc(dsc, "cnt8", (NE, NE))
    cntT = k.salloc(dsc, "cntT", (NE, NE))
    OHp = k.salloc(dsc, "OHp", (NE, NCH))
    OHc = k.salloc(dsc, "OHc", (NE, NCH))
    base8 = k.salloc(dsc, "base8", (NE, 4))
    posr = k.salloc(dsc, "posr", (1, NCH))
    fer = k.salloc(dsc, "fer", (1, NCH))
    so2 = k.salloc(dsc, "so2", (1, NCH))
    vmr = k.salloc(dsc, "vmr", (1, NCH))
    sentb = k.salloc(dsc, "sentb", (128, 64))
    iotac = k.salloc(dsc, "iotac", (NE, 1))
    smf = k.salloc(dsc, "smf", (128, 32))
    cmhl = k.salloc(dsc, "cmhl", (1, 2048))

    with (
        nc.psum_tensor("p8a", [NE, NE], F32) as p8a,
        nc.psum_tensor("p8b", [NE, 128], F32) as p8t,
        nc.psum_tensor("p8pos", [1, 512], F32) as p8pos,
        nc.psum_tensor("p8fe", [1, 512], F32) as p8fe,
        nc.psum_tensor("p8i", [NE, 1], F32) as p8i,
        nc.psum_tensor("p8c", [128, 1], F32) as p8c,
    ):
        with nc.Block() as blk:
            @blk.gpsimd
            def _(g):
                wt(g, k.agcnt)
                k.gd.inc(g.dma_start(cnt8[:, :], d["cnt_out"].ap()))
                k.gd.inc(g.dma_start(k.tokpm[:, :], d["tokid_pm"].ap()))
                k.gd.inc(g.dma_start(k.iota8_row[:1, :], d["iota8"].ap()[0:1, :]))
            gm = k.gd.mark()

            @blk.tensor
            def _(t):
                wt(t, gm)
                k.pe.inc(t.transpose(p8a[:, :], cnt8[:, :], k.idf[:NE, :NE]))
                k.pe.inc(t.transpose(p8i[:, :], k.iota8_row[:1, :], k.idf[:1, :1]))
            pm = k.pe.mark()

            @blk.vector
            def _(v):
                wt(v, pm)
                V(k, v, lambda: v.tensor_copy(cntT[:, :], p8a[:, :]))
                V(k, v, lambda: v.tensor_tensor_scan(
                    OHc[:, 0:NE], cntT[:, :], cntT[:, :], 0.0,
                    op0=ALU.add, op1=ALU.bypass))
                V(k, v, lambda: v.tensor_sub(cntT[:, :], OHc[:, 0:NE], cntT[:, :]))
                V(k, v, lambda: v.tensor_copy(iotac[:, 0:1], p8i[:, 0:1]))

        if getattr(k, "p8stop", 99) <= 1:
            dsc.close(); loc.close(); return

        def basesel(cid):
            with nc.Block() as blk:
                @blk.vector
                def _(v, cid=cid):
                    V(k, v, lambda: v.tensor_scalar(
                        base8[:, 0:1], cntT[:, cid:cid + 1], -1.0, None,
                        op0=ALU.add))
        core_switch(k, basesel)

        if getattr(k, "p8stop", 99) <= 2:
            dsc.close(); loc.close(); return

        with nc.Block() as blk:
            pe_rec = {}
            ve0 = k.ve.n

            @blk.tensor
            def _(t):
                for n in range(2 * NT):
                    i, which = n // 2, n % 2
                    src = (k.oh1 if which == 0 else k.oh2)[:, i * NE:(i + 1) * NE]
                    if n >= 1:
                        t.wait_ge(k.ve.sem, ve0 + n)
                    k.pe.inc(t.transpose(p8t[:, :], src, k.idf[:, :]))
                    pe_rec[n] = k.pe.n

            @blk.vector
            def _(v):
                for n in range(2 * NT):
                    i, which = n // 2, n % 2
                    v.wait_ge(k.pe.sem, pe_rec[n])
                    V(k, v, lambda i=i, which=which: v.tensor_copy(
                        OHp[0:NE, :].rearrange("p (a b) -> p a b", b=2)
                        [:, i * 128:(i + 1) * 128, which:which + 1],
                        p8t[0:NE, :].unsqueeze(2)))

        if getattr(k, "p8stop", 99) <= 3:
            dsc.close(); loc.close(); return

        with nc.Block() as blk:
            ve_rec = {}
            pe0 = k.pe.n

            @blk.vector
            def _(v):
                V(k, v, lambda: v.tensor_tensor_scan(
                    OHc[:, :], OHp[:, :], OHp[:, :], 0.0,
                    op0=ALU.add, op1=ALU.bypass))
                V(k, v, lambda: v.tensor_scalar(
                    OHc[:, :], OHc[:, :], base8[:, 0:1], None, op0=ALU.add))
                V(k, v, lambda: v.tensor_mul(OHc[:, :], OHc[:, :], OHp[:, :]))
                ve_rec[0] = k.ve.n
                for ch in range(NCH // 512):
                    v.wait_ge(k.pe.sem, pe0 + 2 * ch + 2)
                    V(k, v, lambda ch=ch: v.tensor_copy(
                        posr[:1, ch * 512:(ch + 1) * 512], p8pos[:1, :]))
                    V(k, v, lambda ch=ch: v.tensor_copy(
                        fer[:1, ch * 512:(ch + 1) * 512], p8fe[:1, :]))
                V(k, v, lambda: v.tensor_scalar(
                    vmr[:1, :], posr[:1, :], float(CAP - 1), None, op0=ALU.is_gt))
                V(k, v, lambda: v.tensor_scalar(
                    vmr[:1, :], vmr[:1, :], SENT, None, op0=ALU.mult))
                V(k, v, lambda: v.scalar_tensor_tensor(
                    so2[:1, :], fer[:1, :], float(CAP), posr[:1, :],
                    op0=ALU.mult, op1=ALU.add))
                V(k, v, lambda: v.tensor_add(so2[:1, :], so2[:1, :], vmr[:1, :]))
                V(k, v, lambda: v.memset(sentb[:, :], SENT))

            @blk.tensor
            def _(t):
                wt(t, (k.ve.sem, ve_rec[0]))
                for ch in range(NCH // 512):
                    if ch >= 1:
                        t.wait_ge(k.ve.sem, ve_rec[0] + 2 * ch)
                    k.pe.inc(t.matmul(p8pos[:, :], k.onesf[0:NE, 0:1],
                                      OHc[:, ch * 512:(ch + 1) * 512],
                                      start=True, stop=True))
                    k.pe.inc(t.matmul(p8fe[:, :], iotac[:, 0:1],
                                      OHp[:, ch * 512:(ch + 1) * 512],
                                      start=True, stop=True))

        if getattr(k, "p8stop", 99) <= 4:
            dsc.close(); loc.close(); return

        with nc.Block() as blk:
            pe_rec = {}
            ve0 = k.ve.n

            @blk.tensor
            def _(t):
                n = 0
                for which in range(2):
                    for gch in range(16):
                        if n >= 1:
                            t.wait_ge(k.ve.sem, ve0 + n + (1 if n >= 17 else 0))
                        ap = so2[:1, :].rearrange("p (g c w) -> p g c w", c=128, w=2)\
                            [:, gch, :, which]
                        k.pe.inc(t.transpose(p8c[:, :], ap, k.idf[:1, :1]))
                        pe_rec[n] = k.pe.n
                        n += 1

            @blk.vector
            def _(v):
                n = 0
                for which in range(2):
                    dst = sepm if which == 0 else sopm
                    for gch in range(16):
                        v.wait_ge(k.pe.sem, pe_rec[n])
                        V(k, v, lambda dst=dst, gch=gch: v.tensor_copy(
                            smf[:, gch:gch + 1], p8c[:, :]))
                        n += 1
                    V(k, v, lambda dst=dst: v.tensor_copy(dst[:, :], smf[:, 0:16]))

        if getattr(k, "p8stop", 99) <= 5:
            if k.dump is not None:
                k.dbg_col = 0
                dmp(k, lambda: posr[:1, 0:512], rows=1)
                dmp(k, lambda: fer[:1, 0:512], rows=1)
                dmp(k, lambda: OHp[0:NE, 0:512], rows=NE)
                dmp(k, lambda: OHc[0:NE, 0:256], rows=NE)
                dmp(k, lambda: cntT[0:NE, 0:NE], rows=NE)
                dmp(k, lambda: base8[0:NE, 0:1], rows=NE)

            dsc.close(); loc.close(); return

        with nc.Block() as blk:
            vem = k.ve.mark()

            @blk.gpsimd
            def _(g):
                wt(g, vem)
                breg = g.alloc_register("bnd8")
                g.reg_mov(breg, NC * CAP - 1)
                for rep in range(4):
                    k.gd.inc(g.dma_start(
                        bass.AP(d["cm_in"], rep * 8192, [[64, 128], [1, 64]]),
                        sentb[:, :]))
                g.wait_ge(k.gd.sem, k.gd.n)
                for which, idx in ((0, sepm), (1, sopm)):
                    for gch in range(16):
                        k.gd.inc(g.indirect_dma_start(
                            d["cm_in"].ap(),
                            bass.IndirectOffsetOnAxis(idx[:, gch:gch + 1], 1),
                            k.tokpm[:, gch:gch + 1], None,
                            bounds_check=breg, oob_is_err=False))
                g.wait_ge(k.gd.sem, k.gd.n)
                k.armn = coll(k, g, "AllReduce", ALU.min, d["cm_in"], d["cm_out1"])

        def ownersel(cid):
            with nc.Block() as blk:
                @blk.sync
                def _(s, cid=cid):
                    wt(s, k.armn)
                    k.hd.inc(s.dma_start(
                        cmhl[:1, :],
                        bass.AP(d["cm_out1"], cid * CAP, [[1, 2048]])))
                    s.wait_ge(k.hd.sem, k.hd.n)
                hm = k.hd.mark()
                pe_rec = {}
                ve0 = k.ve.n

                @blk.tensor
                def _(t, cid=cid):
                    wt(t, hm)
                    for gch in range(16):
                        if gch >= 1:
                            t.wait_ge(k.ve.sem, ve0 + gch)
                        k.pe.inc(t.transpose(
                            p8c[:, :], cmhl[:1, gch * 128:(gch + 1) * 128],
                            k.idf[:1, :1]))
                        pe_rec[gch] = k.pe.n

                @blk.vector
                def _(v, cid=cid):
                    for gch in range(16):
                        v.wait_ge(k.pe.sem, pe_rec[gch])
                        V(k, v, lambda gch=gch: v.tensor_copy(
                            smf[:, gch:gch + 1], p8c[:, :]))
        core_switch(k, ownersel)

        def ownersel2(cid):
            with nc.Block() as blk:
                @blk.sync
                def _(s, cid=cid):
                    k.hd.inc(s.dma_start(
                        cmhl[:1, :],
                        bass.AP(d["cm_out1"], cid * CAP + 2048, [[1, 2048]])))
                    s.wait_ge(k.hd.sem, k.hd.n)
                hm = k.hd.mark()
                pe_rec = {}
                ve0 = k.ve.n

                @blk.tensor
                def _(t, cid=cid):
                    wt(t, hm)
                    for gch in range(16):
                        if gch >= 1:
                            t.wait_ge(k.ve.sem, ve0 + gch)
                        k.pe.inc(t.transpose(
                            p8c[:, :], cmhl[:1, gch * 128:(gch + 1) * 128],
                            k.idf[:1, :1]))
                        pe_rec[gch] = k.pe.n

                @blk.vector
                def _(v, cid=cid):
                    for gch in range(16):
                        v.wait_ge(k.pe.sem, pe_rec[gch])
                        V(k, v, lambda gch=gch: v.tensor_copy(
                            smf[:, 16 + gch:16 + gch + 1], p8c[:, :]))
        core_switch(k, ownersel2)

        with nc.Block() as blk:
            @blk.vector
            def _(v):
                V(k, v, lambda: v.tensor_copy(smu[:, :], smf[:, :]))
    dsc.close()

    xsc = contextlib.ExitStack()
    exp_xT = k.salloc(xsc, "exp_xT", (128, NJ * CAP), BF16)
    esc = contextlib.ExitStack()
    exprow = k.salloc(esc, "exprow", (128, E), BF16)
    with nc.psum_tensor("p8x", [128, E], BF16) as p8x:
        with nc.Block() as blk:
            gsw = [k.gsp[0], k.gsp[1]]
            gsw0 = [gsw[0].n, gsw[1].n]
            ve0 = k.ve.n
            pe_rec = {}
            pe_rec_pre = k.pe.n

            @blk.vector
            def _(v):
                for g_ in range(32):
                    if g_ >= 1:
                        v.wait_ge(k.pe.sem, pe_rec_pre + 8 * g_)
                    V(k, v, lambda: v.memset(exprow[:, :], 0.0))
                    v.wait_ge(k.pe.sem, pe_rec_pre + 8 * (g_ + 1))
                    V(k, v, lambda g_=g_: v.tensor_copy(
                        exp_xT[:, :].rearrange("p (j t) -> p j t", t=CAP)
                        [:, :, g_ * 128:(g_ + 1) * 128],
                        p8x[:, :].rearrange("p (j t) -> p j t", t=128)))

            @blk.gpsimd
            def _(g):
                wt(g, k.agxn1)
                breg = g.alloc_register("bnd9")
                g.reg_mov(breg, NC * TOKC - 1)
                for g_ in range(32):
                    g.wait_ge(k.ve.sem, ve0 + 2 * g_ + 1)
                    gsw[g_ % 2].inc(g.indirect_dma_start(
                        exprow[:, :], None,
                        bass.AP(d["xn1_out"], 0, [[E, NC * TOKC], [1, E]]),
                        bass.IndirectOffsetOnAxis(smu[:, g_:g_ + 1], 0),
                        bounds_check=breg, oob_is_err=False))

            @blk.tensor
            def _(t):
                for g_ in range(32):
                    t.wait_ge(gsw[g_ % 2].sem, gsw0[g_ % 2] + 16 * (g_ // 2 + 1))
                    if g_ >= 1:
                        t.wait_ge(k.ve.sem, ve0 + 2 * g_)
                    for j in range(NJ):
                        k.pe.inc(t.transpose(
                            p8x[:, j * 128:(j + 1) * 128],
                            exprow[:, j * 128:(j + 1) * 128], k.idb[:, :]))
                    pe_rec[g_] = k.pe.n
    esc.close()

    fsc = contextlib.ExitStack()
    wste = k.salloc(fsc, "wste", (128, 2 * 12 * 128), BF16)
    h1sb = k.salloc(fsc, "h1sb", (128, 24 * 512), BF16)
    asil = k.salloc(fsc, "asil", (128, 512), BF16)
    hp = k.salloc(fsc, "hp", (128, 12 * 512), BF16)
    outsb = k.salloc(fsc, "outsb", (128, 2 * 512), BF16)
    eostg = k.salloc(fsc, "eostg", (128, 4 * E), BF16)
    with (
        nc.psum_tensor("p8h0", [128, 512], F32) as ph0,
        nc.psum_tensor("p8h1", [128, 512], F32) as ph1,
        nc.psum_tensor("p8e", [128, 128], BF16) as p8e,
    ):
        phs = [ph0, ph1]
        for st_ in range(8):
            ssl = slice(st_ * 512, (st_ + 1) * 512)
            with nc.Block() as blk:
                gsw = [k.gsp[2], k.gsp[3]]
                gsw0 = [gsw[0].n, gsw[1].n]
                ae0 = k.ae.n
                pe_rec = {}
                pe_pre = k.pe.n

                @blk.gpsimd
                def _(g):
                    for fc in range(24):
                        if fc >= 2:
                            g.wait_ge(k.pe.sem, pe_pre + 8 * (fc - 1))
                        src = bass.AP(d["cfc"], fc * 128,
                                      [[DFF, 128], [128 * DFF, NJ], [1, 128]])
                        gsw[fc % 2].inc(g.dma_start(
                            wste[:, (fc % 2) * 1024:(fc % 2 + 1) * 1024], src))

                @blk.tensor
                def _(t):
                    for fc in range(24):
                        t.wait_ge(gsw[fc % 2].sem, gsw0[fc % 2] + 16 * (fc // 2 + 1))
                        if fc >= 2:
                            t.wait_ge(k.ae.sem, ae0 + fc - 1)
                        for j in range(NJ):
                            k.pe.inc(t.matmul(
                                phs[fc % 2][:, :],
                                wste[:, (fc % 2) * 1024 + j * 128:
                                     (fc % 2) * 1024 + (j + 1) * 128],
                                exp_xT[:, j * CAP + st_ * 512:j * CAP + (st_ + 1) * 512],
                                start=(j == 0), stop=(j == NJ - 1)))
                        pe_rec[fc] = k.pe.n

                @blk.scalar
                def _(s):
                    for fc in range(24):
                        s.wait_ge(k.pe.sem, pe_rec[fc])
                        A(k, s, lambda fc=fc: s.copy(
                            h1sb[:, fc * 512:(fc + 1) * 512], phs[fc % 2][:, :]))

            with nc.Block() as blk:
                ae0 = k.ae.n
                ve_pre = k.ve.n

                @blk.scalar
                def _(s):
                    for fc in range(12):
                        if fc >= 1:
                            s.wait_ge(k.ve.sem, ve_pre + fc)
                        A(k, s, lambda fc=fc: s.activation(
                            asil[:, :], h1sb[:, fc * 512:(fc + 1) * 512], AF.Silu))

                @blk.vector
                def _(v):
                    for fc in range(12):
                        v.wait_ge(k.ae.sem, ae0 + fc + 1)
                        V(k, v, lambda fc=fc: v.tensor_mul(
                            hp[:, fc * 512:(fc + 1) * 512], asil[:, :],
                            h1sb[:, (12 + fc) * 512:(13 + fc) * 512]))

            with nc.Block() as blk:
                gsw = [k.gsp[2], k.gsp[3]]
                gsw0 = [gsw[0].n, gsw[1].n]
                ae0 = k.ae.n
                ve0 = k.ve.n
                pe_rec = {}
                petp = {}
                pe_pre = k.pe.n
                hdm0 = k.hd.mark()

                @blk.gpsimd
                def _(g):
                    for jE in range(NJ):
                        if jE >= 2:
                            g.wait_ge(k.pe.sem, pe_pre + 16 * (jE - 1))
                        src = bass.AP(d["cpj"], jE * 128,
                                      [[E, 128], [128 * E, 12], [1, 128]])
                        gsw[jE % 2].inc(g.dma_start(
                            wste[:, (jE % 2) * 1536:(jE % 2) * 1536 + 1536], src))

                @blk.tensor
                def _(t):
                    for jE in range(NJ):
                        t.wait_ge(gsw[jE % 2].sem, gsw0[jE % 2] + 16 * (jE // 2 + 1))
                        if jE >= 2:
                            t.wait_ge(k.ae.sem, ae0 + 3 * (jE - 2) + 1)
                        for fc in range(12):
                            k.pe.inc(t.matmul(
                                phs[jE % 2][:, :],
                                wste[:, (jE % 2) * 1536 + fc * 128:
                                     (jE % 2) * 1536 + (fc + 1) * 128],
                                hp[:, fc * 512:(fc + 1) * 512],
                                start=(fc == 0), stop=(fc == 11)))
                        pe_rec[jE] = k.pe.n
                        t.wait_ge(k.ae.sem, ae0 + 3 * jE + 1)
                        for tt in range(4):
                            gidx = jE * 4 + tt
                            if gidx >= 1:
                                pg = gidx - 1
                                pj, ptt = pg // 4, pg % 4
                                if ptt % 2 == 0:
                                    t.wait_ge(k.ae.sem, ae0 + 3 * pj + 1 + ptt // 2 + 1)
                                else:
                                    t.wait_ge(k.ve.sem, ve0 + 2 * pj + (ptt - 1) // 2 + 1)
                            k.pe.inc(t.transpose(
                                p8e[:, :],
                                outsb[:, (jE % 2) * 512 + tt * 128:
                                      (jE % 2) * 512 + (tt + 1) * 128],
                                k.idb[:, :]))
                            petp[(jE, tt)] = k.pe.n

                @blk.scalar
                def _(s):
                    wt(s, hdm0)
                    for jE in range(NJ):
                        s.wait_ge(k.pe.sem, pe_rec[jE])
                        A(k, s, lambda jE=jE: s.copy(
                            outsb[:, (jE % 2) * 512:(jE % 2 + 1) * 512],
                            phs[jE % 2][:, :]))
                        for tt in (0, 2):
                            s.wait_ge(k.pe.sem, petp[(jE, tt)])
                            A(k, s, lambda jE=jE, tt=tt: s.copy(
                                eostg[:, tt * E + jE * 128:tt * E + (jE + 1) * 128],
                                p8e[:, :]))

                @blk.vector
                def _(v):
                    wt(v, hdm0)
                    for jE in range(NJ):
                        for tt in (1, 3):
                            v.wait_ge(k.pe.sem, petp[(jE, tt)])
                            V(k, v, lambda jE=jE, tt=tt: v.tensor_copy(
                                eostg[:, tt * E + jE * 128:tt * E + (jE + 1) * 128],
                                p8e[:, :]))
                vem = k.ve.mark()
                aem = k.ae.mark()

                @blk.sync
                def _(s):
                    wt(s, vem)
                    wt(s, aem)
                    for tt in range(4):
                        k.hd.inc(s.dma_start(
                            d["eo_in"].ap()[st_ * 512 + tt * 128:
                                            st_ * 512 + (tt + 1) * 128, :],
                            eostg[:, tt * E:(tt + 1) * E]))
    fsc.close()
    xsc.close()

    with nc.Block() as blk:
        @blk.gpsimd
        def _(g):
            g.wait_ge(k.hd.sem, k.hd.n)
            k.ageo = coll(k, g, "AllGather", ALU.bypass, d["eo_in"], d["eo_out"])

    csc = contextlib.ExitStack()
    geb = k.salloc(csc, "geb", (128, E), BF16)
    gob = k.salloc(csc, "gob", (128, E), BF16)
    tmpf = k.salloc(csc, "tmpf", (128, E))
    moeb = k.salloc(csc, "moeb", (128, E))
    ysb = k.salloc(csc, "ysb", (128, 2 * E))
    with nc.psum_tensor("p8y", [128, E], F32) as p8y:
        with nc.Block() as blk:
            gsw = [k.gsp[0], k.gsp[1]]
            gsw0 = [gsw[0].n, gsw[1].n]
            yst = [k.sp[0], k.sp[1]]
            yst0 = [yst[0].n, yst[1].n]
            ve0 = k.ve.n
            pe_rec = {}
            pe_pre = k.pe.n

            @blk.vector
            def _(v):
                for g_ in range(NT):
                    if g_ >= 1:
                        v.wait_ge(gsw[0].sem, gsw0[0] + 16 * g_)
                        v.wait_ge(gsw[1].sem, gsw0[1] + 16 * g_)
                    V(k, v, lambda: v.memset(geb[:, :], 0.0))
                    V(k, v, lambda: v.memset(gob[:, :], 0.0))
                    v.wait_ge(gsw[0].sem, gsw0[0] + 16 * (g_ + 1))
                    v.wait_ge(gsw[1].sem, gsw0[1] + 16 * (g_ + 1))
                    V(k, v, lambda g_=g_: v.tensor_scalar(
                        tmpf[:, :], geb[:, :], k.rtrw[:, 2 * g_:2 * g_ + 1], None,
                        op0=ALU.mult))
                    V(k, v, lambda g_=g_: v.scalar_tensor_tensor(
                        moeb[:, :], gob[:, :], k.rtrw[:, 2 * g_ + 1:2 * g_ + 2],
                        tmpf[:, :], op0=ALU.mult, op1=ALU.add))
                    v.wait_ge(k.pe.sem, pe_pre + 8 * (g_ + 1))
                    if g_ >= 2:
                        v.wait_ge(yst[g_ % 2].sem, yst0[g_ % 2] + 16 * (g_ // 2))
                    V(k, v, lambda g_=g_: v.tensor_add(
                        ysb[:, (g_ % 2) * E:(g_ % 2 + 1) * E], p8y[:, :], moeb[:, :]))

            @blk.gpsimd
            def _(g):
                wt(g, k.ageo)
                breg = g.alloc_register("bnd10")
                g.reg_mov(breg, NC * CAP - 1)
                for g_ in range(NT):
                    g.wait_ge(k.ve.sem, ve0 + 5 * g_ + 2)
                    gsw[0].inc(g.indirect_dma_start(
                        geb[:, :], None,
                        bass.AP(d["eo_out"], 0, [[E, NC * CAP], [1, E]]),
                        bass.IndirectOffsetOnAxis(sepm[:, g_:g_ + 1], 0),
                        bounds_check=breg, oob_is_err=False))
                    gsw[1].inc(g.indirect_dma_start(
                        gob[:, :], None,
                        bass.AP(d["eo_out"], 0, [[E, NC * CAP], [1, E]]),
                        bass.IndirectOffsetOnAxis(sopm[:, g_:g_ + 1], 0),
                        bounds_check=breg, oob_is_err=False))

            @blk.tensor
            def _(t):
                for g_ in range(NT):
                    if g_ >= 1:
                        t.wait_ge(k.ve.sem, ve0 + 5 * g_)
                    for j in range(NJ):
                        k.pe.inc(t.transpose(
                            p8y[:, j * 128:(j + 1) * 128],
                            k.xT[:, j * TOKC + g_ * 128:j * TOKC + (g_ + 1) * 128],
                            k.idf[:, :]))

            @blk.sync
            def _(s):
                for g_ in range(NT):
                    s.wait_ge(k.ve.sem, ve0 + 5 * (g_ + 1))
                    yst[g_ % 2].inc(s.dma_start(
                        d["y"].ap()[g_ * 128:(g_ + 1) * 128, :],
                        ysb[:, (g_ % 2) * E:(g_ % 2 + 1) * E]))
    csc.close()
    loc.close()


def _consts(c):
    ident = np.eye(128, dtype=np.float32)
    m0 = np.zeros((128, 4 * 512), np.float32)
    kp = np.arange(128)[:, None]
    qq = np.arange(512)[None, :]
    for vvar in range(4):
        m0[:, vvar * 512:(vvar + 1) * 512] = (kp <= qq - vvar * 128)
    qq1 = np.arange(C1)[:, None]
    kk1 = np.arange(C1)[None, :]
    m1 = np.broadcast_to(
        np.where(kk1 > qq1, np.float32(-1e30), np.float32(0.0)).reshape(1, -1),
        (128, C1 * C1)).copy()
    iota8 = np.broadcast_to(np.arange(NE, dtype=np.float32), (128, NE)).copy()
    ch = np.arange(128)[:, None] + 128 * np.arange(16)[None, :]
    tokid = (2048 * c + ch).astype(np.float32)
    return {
        "identf": ident,
        "identb": ident.astype(BFD),
        "maskL0": m0.astype(BFD),
        "maskL1": m1.astype(BFD),
        "iota8": iota8,
        "tokid_pm": tokid,
        "onesf": np.ones((128, 128), np.float32),
    }


def make_inmaps(inputs):
    f = {n: np.ascontiguousarray(np.asarray(v, dtype=np.float32))
         for n, v in inputs.items()}
    x = f["L0_attn_x"].reshape(B * S, E)
    gw1T = np.ascontiguousarray(f["L0_ffn_gate_w1"].T)
    gw2T = np.ascontiguousarray(f["L0_ffn_gate_w2"].T)
    in_maps = []
    for c in range(NC):
        fsl = slice(c * FSL, (c + 1) * FSL)
        hsl = slice(c * HSL, (c + 1) * HSL)
        m = {
            "x": np.ascontiguousarray(x[c * TOKC:(c + 1) * TOKC]),
            "wq0": f["L0_attn_w_q"], "wk0": f["L0_attn_w_k"],
            "wv1": f["L0_attn_w_v1"], "wv2": f["L0_attn_w_v2"],
            "gw1T": gw1T, "gb1": f["L0_ffn_gate_b1"].reshape(1, E),
            "gw2T": gw2T, "gb2": f["L0_ffn_gate_b2"].reshape(1, NE),
            "W1T_s": np.ascontiguousarray(f["L0_ffn_W1"][fsl].T),
            "TW1T_s": np.ascontiguousarray(
                f["L0_ffn_TW1"][:, fsl, :].transpose(0, 2, 1)),
            "W2T_s": np.ascontiguousarray(f["L0_ffn_W2"][hsl].T),
            "TW2T_s": np.ascontiguousarray(
                f["L0_ffn_TW2"][:, hsl, :].transpose(0, 2, 1)),
            "b1": f["L0_ffn_b1"].reshape(1, F0), "Tb1": f["L0_ffn_Tb1"],
            "b2": f["L0_ffn_b2"].reshape(1, E), "Tb2": f["L0_ffn_Tb2"],
            "Wq1": f["L1_attn_Wq"], "Wk1": f["L1_attn_Wk"],
            "Wv1_": f["L1_attn_Wv"], "Wo1": f["L1_attn_Wo"],
            "bq": f["L1_attn_bq"].reshape(1, E), "bk": f["L1_attn_bk"].reshape(1, E),
            "bv": f["L1_attn_bv"].reshape(1, E), "bo": f["L1_attn_bo"].reshape(1, E),
            "wgate": f["L1_ffn_w_gate"],
            "cfc": np.ascontiguousarray(f["L1_ffn_c_fc"][c]),
            "cpj": np.ascontiguousarray(f["L1_ffn_c_proj"][c]),
        }
        m.update(_consts(c))
        in_maps.append(m)
    return in_maps


def kernel(**inputs):
    nc, _ = build_nc()
    in_maps = make_inmaps(inputs)
    res = run_bass_kernel_spmd(nc, in_maps, core_ids=list(range(NC)))
    outs = [res.results[c]["y"] for c in range(NC)]
    return np.concatenate(outs, axis=0).reshape(B, S, E).astype(np.float32)


# version 10
# speedup vs baseline: 1.0481x; 1.0481x over previous
import contextlib
import numpy as np
import ml_dtypes
import concourse.bass as bass
import concourse.mybir as mybir
from concourse.bass_utils import run_bass_kernel_spmd

F32 = mybir.dt.float32
BF16 = mybir.dt.bfloat16
U32 = mybir.dt.uint32
AF = mybir.ActivationFunctionType
ALU = mybir.AluOpType
AX = mybir.AxisListType

NC = 8
B = 2
S = 8192
E = 1024
TOKC = 2048
NT = TOKC // 128
NJ = E // 128
KD = 64
F0 = 4096
FSL = F0 // NC
HSL = E // NC
H1 = 16
C1 = 64
NE = 8
CAP = 4096
DFF = 3072
NCH = 2 * TOKC
KVN = KD * TOKC + TOKC * 65
MW1N = E * FSL
MW2N = F0 * HSL
MWN = 2 * MW1N + 2 * MW2N
EPS = 1e-5
SENT = 1.0e9
BFD = ml_dtypes.bfloat16


class Ctr:
    def __init__(self, nc, name, step=1):
        self.sem = nc.alloc_semaphore(name)
        self.n = 0
        self.step = step

    def inc(self, inst, times=1):
        inst.then_inc(self.sem, self.step * times)
        self.n += self.step * times
        return self.n

    def mark(self):
        return (self.sem, self.n)


def wt(eng, mark):
    sem, val = mark
    if val > 0:
        eng.wait_ge(sem, val)


class K:
    pass


def V(k, v, fn):
    if k.ve.n:
        v.wait_ge(k.ve.sem, k.ve.n)
    return k.ve.inc(fn())


def Vf(k, fn):
    return k.ve.inc(fn())


def Vbar(k, v):
    if k.ve.n:
        v.wait_ge(k.ve.sem, k.ve.n)


def A(k, s, fn):
    if k.ae.n:
        s.wait_ge(k.ae.sem, k.ae.n)
    return k.ae.inc(fn())


def G(k, g, fn):
    if k.ge.n:
        g.wait_ge(k.ge.sem, k.ge.n)
    return k.ge.inc(fn())


def build_nc(dump=None, upto=99):
    nc = bass.Bass("TRN2", target_bir_lowering=False)
    k = K()
    k.nc = nc
    k.dump = dump
    k.upto = upto
    d = {}
    k.d = d

    def din(name, shape, dt=F32):
        d[name] = nc.dram_tensor(name, list(shape), dt, kind="ExternalInput")

    def dint(name, shape, dt=F32, shared=False):
        if shared:
            d[name] = nc.dram_tensor(name, list(shape), dt, kind="Internal",
                                     addr_space="Shared")
        else:
            d[name] = nc.dram_tensor(name, list(shape), dt, kind="Internal")

    din("x", (TOKC, E))
    din("wq0", (E, KD)); din("wk0", (E, KD)); din("wv1", (E, KD)); din("wv2", (KD, E))
    din("gw1T", (E, E)); din("gb1", (1, E)); din("gw2T", (E, NE)); din("gb2", (1, NE))
    din("W1T_s", (E, FSL)); din("TW1T_s", (NE, E, FSL))
    din("W2T_s", (F0, HSL)); din("TW2T_s", (NE, F0, HSL))
    din("b1", (1, F0)); din("Tb1", (NE, F0)); din("b2", (1, E)); din("Tb2", (NE, E))
    din("Wq1", (E, E)); din("Wk1", (E, E)); din("Wv1_", (E, E)); din("Wo1", (E, E))
    din("bq", (1, E)); din("bk", (1, E)); din("bv", (1, E)); din("bo", (1, E))
    din("wgate", (E, NE))
    din("cfc", (E, DFF)); din("cpj", (DFF // 2, E))
    din("identf", (128, 128)); din("identb", (128, 128), BF16)
    din("maskL0", (128, 4 * 512), BF16)
    din("maskL1", (128, C1 * C1), BF16)
    din("iota8", (128, NE))
    din("tokid_pm", (128, 16))
    din("onesf", (128, 128))
    d["y"] = nc.dram_tensor("y", [TOKC, E], F32, kind="ExternalOutput")
    if dump is not None:
        d["dbg"] = nc.dram_tensor("dbg", [128, 2048], F32, kind="ExternalOutput")

    dint("ar_ph_in", (B, 128, NJ)); dint("ar_ph_out", (B, 128, NJ))
    dint("kv_in", (1, KVN), BF16); dint("kv_out", (NC, KVN), BF16, shared=True)
    dint("mw_in", (1, MWN), BF16); dint("mw_out", (NC, MWN), BF16, shared=True)
    dint("mw1t_mine", (E, F0), BF16)
    dint("mw2t_mine", (F0, E), BF16)
    dint("u_dram", (F0 // 128, 128, TOKC), BF16)
    dint("xn1_in", (TOKC, E), BF16); dint("xn1_out", (NC * TOKC, E), BF16, shared=True)
    dint("cnt_in", (1, NE)); dint("cnt_out", (NC, NE))
    dint("cm_in", (1, NC * CAP)); dint("cm_out1", (1, NC * CAP))
    dint("eo_in", (CAP, E), BF16); dint("eo_out", (NC * CAP, E), BF16, shared=True)

    k.gd = Ctr(nc, "gdma", 16)
    k.hd = Ctr(nc, "hdma", 16)
    k.ve = Ctr(nc, "vsem")
    k.ae = Ctr(nc, "asem")
    k.ge = Ctr(nc, "gsem")
    k.pe = Ctr(nc, "pesem")
    k.cc = [Ctr(nc, "cc%d" % i) for i in range(10)]
    k.ncc = 0
    k.sp = [Ctr(nc, "sp%d" % i, 16) for i in range(10)]
    k.gsp = [Ctr(nc, "gsp%d" % i, 16) for i in range(4)]
    k.all_ctrs = [k.gd, k.hd, k.ve, k.ae, k.ge, k.pe] + k.cc + k.sp + k.gsp

    k.sbn = {}

    def salloc(stack, name, shape, dt=F32):
        t = stack.enter_context(nc.sbuf_tensor(name, list(shape), dt))
        k.sbn[name] = t
        return t

    k.salloc = salloc
    glob = contextlib.ExitStack()
    k.glob = glob
    k.xT = salloc(glob, "xT", (128, NJ * TOKC))
    k.idf = salloc(glob, "idf", (128, 128))
    k.idb = salloc(glob, "idb", (128, 128), BF16)
    k.onesf = salloc(glob, "ones_sb", (128, 128))
    k.pmv = salloc(glob, "pmv", (128, 128))
    k.PM = {"gb1": 0, "b1": 8, "b2": 40, "bq": 48, "bk": 56, "bv": 64,
            "bo": 72, "mb1": 80, "mb2": 112}
    k.rtrw = salloc(glob, "rtrw", (128, NT * 2))
    k.iota8_row = salloc(glob, "iota8r", (1, NE))
    k.rtri = salloc(glob, "rtri", (128, NT * 2))
    k.oh1 = salloc(glob, "oh1", (128, NT * NE))
    k.oh2 = salloc(glob, "oh2", (128, NT * NE))
    if dump is not None:
        k.dbg_sb = salloc(glob, "dbg_sb", (128, 128))
        k.dbg_col = 0

    k.engines = [nc.gpsimd, nc.scalar, nc.tensor, nc.vector, nc.sync]
    k.pid = [e.partition_id() for e in k.engines]

    span17 = contextlib.ExitStack()
    k.xnT = salloc(span17, "xnT", (128, NJ * TOKC), BF16)
    span35 = contextlib.ExitStack()
    k.qp = salloc(span35, "qp", (KD, TOKC), BF16)
    k.wv2f = salloc(span35, "wv2f", (KD, E))
    k.mask0 = salloc(span35, "mask0", (128, 4 * 512), BF16)

    p1_ln(k)
    if upto >= 3:
        p3_proj(k)
    if upto >= 4:
        p4_gmw(k)
    if upto >= 5:
        p5_attn0(k)
    span35.close()
    if upto >= 6:
        p6_ffn(k)
    if upto >= 7:
        p7_l1(k)
    if hasattr(k, 'p7loc'):
        k.p7loc.close()
    span17.close()
    if upto >= 8:
        p8_moe(k)
    finish(k)
    return nc, k


def core_switch(k, body_fn):
    snap = [c.n for c in k.all_ctrs]
    final = None
    for cid in k.nc.Switch(engines=k.engines, index=k.pid, n=NC):
        for c, v in zip(k.all_ctrs, snap):
            c.n = v
        body_fn(cid)
        now = [c.n for c in k.all_ctrs]
        if final is None:
            final = now
        else:
            assert now == final, ("switch bodies diverge in sem counts",
                                  now, final)
    for c, v in zip(k.all_ctrs, final):
        c.n = v


def coll(k, gp, kind, op, src, dst):
    cc = k.cc[k.ncc]
    k.ncc += 1
    sap = src if isinstance(src, bass.AP) else src.ap()
    dap = dst if isinstance(dst, bass.AP) else dst.ap()
    inst = gp.collective_compute(
        kind, op, replica_groups=[list(range(NC))],
        ins=[sap.opt()], outs=[dap.opt()])
    inst.then_inc(cc.sem, 1)
    cc.n += 1
    return cc.mark()


def dmp(k, get_ap, rows=128):
    if k.dump is None:
        return
    nc = k.nc
    src_ap = get_ap()
    ncol = 1
    for dsz in src_ap.shape[1:]:
        ncol *= dsz
    for c0 in range(0, ncol, 128):
        w = min(128, ncol - c0)
        col = k.dbg_col + c0
        sub = src_ap[:, c0:c0 + w] if len(src_ap.shape) == 2 else \
            src_ap.rearrange("p a b -> p (a b)")[:, c0:c0 + w]
        with nc.Block() as blk:
            @blk.vector
            def _(v, sub=sub, w=w):
                V(k, v, lambda: v.tensor_copy(k.dbg_sb[:rows, 0:w], sub))
            mk = k.ve.n

            @blk.sync
            def _(s, w=w, col=col, mk=mk):
                s.wait_ge(k.ve.sem, mk)
                k.hd.inc(s.dma_start(
                    bass.AP(k.d["dbg"], col, [[2048, rows], [1, w]]),
                    k.dbg_sb[:rows, 0:w]))
                s.wait_ge(k.hd.sem, k.hd.n)
    k.dbg_col += ncol


def finish(k):
    nc = k.nc
    d = k.d
    if k.upto < 8:
        ytile = nc.alloc_sbuf_tensor("fin_yt", [128, E], F32)
        with nc.psum_tensor("finp", [128, 128], F32) as ps:
            for i in range(NT):
                with nc.Block() as blk:
                    pe_af = {}

                    @blk.tensor
                    def _(t, i=i):
                        for j in range(NJ):
                            if j >= 1:
                                t.wait_ge(k.ve.sem, k.ve.n + j)
                            k.pe.inc(t.transpose(
                                ps[:, :],
                                k.xT[:, j * TOKC + i * 128:j * TOKC + (i + 1) * 128],
                                k.idf[:, :]))
                            pe_af[j] = k.pe.n

                    ve0 = k.ve.n

                    @blk.vector
                    def _(v, i=i):
                        for j in range(NJ):
                            v.wait_ge(k.pe.sem, pe_af[j])
                            V(k, v, lambda j=j: v.tensor_copy(
                                ytile[:, j * 128:(j + 1) * 128], ps[:, :]))
                    vm = k.ve.mark()

                    @blk.sync
                    def _(s, i=i):
                        wt(s, vm)
                        k.hd.inc(s.dma_start(d["y"].ap()[i * 128:(i + 1) * 128, :],
                                             ytile[:, :]))
                        s.wait_ge(k.hd.sem, k.hd.n)
    with nc.Block() as blk:
        @blk.sync
        def _(s):
            s.wait_ge(k.hd.sem, k.hd.n)
            for c_ in k.sp + k.gsp:
                if c_.n:
                    s.wait_ge(c_.sem, c_.n)

        @blk.gpsimd
        def _(g):
            g.wait_ge(k.gd.sem, k.gd.n)


def p1_ln(k):
    nc = k.nc
    d = k.d
    loc = contextlib.ExitStack()
    xtt = k.salloc(loc, "xtt", (128, 2 * E))
    xnt = k.salloc(loc, "xnt", (128, 2 * E), BF16)
    junk = k.salloc(loc, "p1junk", (128, E), BF16)
    st = k.salloc(loc, "p1st", (128, 8 * NT))

    with nc.Block() as blk:
        @blk.gpsimd
        def _(g):
            k.gd.inc(g.dma_start(k.idf[:, :], d["identf"].ap()))
            k.gd.inc(g.dma_start(k.idb[:, :], d["identb"].ap()))
            k.gd.inc(g.dma_start(k.onesf[:, :], d["onesf"].ap()))
            k.gd.inc(g.dma_start(k.mask0[:, :], d["maskL0"].ap()))
            k.gd.inc(g.dma_start(k.wv2f[:, :], d["wv2"].ap()))

        if k.dump is not None:
            @blk.vector
            def _(v):
                V(k, v, lambda: v.memset(k.dbg_sb[:, :], 0.0))

    with (
        nc.psum_tensor("p1f0", [128, 128], F32) as pf0,
        nc.psum_tensor("p1f1", [128, 128], F32) as pf1,
        nc.psum_tensor("p1b0", [128, 128], BF16) as pb0,
        nc.psum_tensor("p1b1", [128, 128], BF16) as pb1,
    ):
        pfs = [pf0, pf1]
        pbs = [pb0, pb1]
        with nc.Block() as blk:
            xs = [k.sp[0], k.sp[1]]
            xs0 = [xs[0].n, xs[1].n]
            gdm = k.gd.mark()
            ve0 = k.ve.n
            ae0 = k.ae.n
            pe0 = k.pe.n
            DVE_PER = 15
            ACT_PER = 10
            PE_PER = 16

            def ve_at(i, op):
                return ve0 + DVE_PER * i + op

            def ae_at(i, op):
                return ae0 + ACT_PER * i + op

            def pe_at(i, op):
                return pe0 + PE_PER * i + op

            @blk.sync
            def _(s):
                for i in range(NT):
                    if i >= 2:
                        s.wait_ge(k.pe.sem, pe_at(i - 2, 8))
                        s.wait_ge(k.ve.sem, ve_at(i - 2, 7))
                    xs[i % 2].inc(s.dma_start(xtt[:, (i % 2) * E:(i % 2 + 1) * E],
                                              d["x"].ap()[i * 128:(i + 1) * 128, :]))

            @blk.tensor
            def _(t):
                wt(t, gdm)
                for i in range(NT):
                    xt = xtt[:, (i % 2) * E:(i % 2 + 1) * E]
                    xn = xnt[:, (i % 2) * E:(i % 2 + 1) * E]
                    t.wait_ge(xs[i % 2].sem, xs0[i % 2] + 16 * (i // 2 + 1))
                    for j in range(NJ):
                        if i * NJ + j >= 2:
                            pj = (i * NJ + j) - 2
                            pi, pjj = pj // NJ, pj % NJ
                            if pjj % 2 == 0:
                                t.wait_ge(k.ae.sem, ae_at(pi, 2 + pjj // 2 + 1))
                            else:
                                t.wait_ge(k.ve.sem, ve_at(pi, 6 + 1 + (pjj - 1) // 2 + 1))
                        k.pe.inc(t.transpose(pfs[j % 2][:, :],
                                             xt[:, j * 128:(j + 1) * 128],
                                             k.idf[:, :]))
                    t.wait_ge(k.ve.sem, ve_at(i, 7))
                    for j in range(NJ):
                        if i * NJ + j >= 2:
                            pj = (i * NJ + j) - 2
                            pi, pjj = pj // NJ, pj % NJ
                            if pjj % 2 == 0:
                                t.wait_ge(k.ae.sem, ae_at(pi, 6 + pjj // 2 + 1))
                            else:
                                t.wait_ge(k.ve.sem, ve_at(pi, 10 + 1 + (pjj - 1) // 2 + 1))
                        k.pe.inc(t.transpose(pbs[j % 2][:, :],
                                             xn[:, j * 128:(j + 1) * 128],
                                             k.idb[:, :]))

            @blk.vector
            def _(v):
                for i in range(NT):
                    c = 8 * i
                    xt = xtt[:, (i % 2) * E:(i % 2 + 1) * E]
                    xn = xnt[:, (i % 2) * E:(i % 2 + 1) * E]
                    v.wait_ge(xs[i % 2].sem, xs0[i % 2] + 16 * (i // 2 + 1))
                    V(k, v, lambda xt=xt, c=c: v.reduce_sum(st[:, c:c + 1], xt, axis=AX.X))
                    V(k, v, lambda xt=xt, c=c: v.scalar_tensor_tensor(
                        junk[:, :], xt, 1.0, xt, op0=ALU.bypass, op1=ALU.mult,
                        accum_out=st[:, c + 1:c + 2]))
                    V(k, v, lambda c=c: v.tensor_scalar_mul(
                        st[:, c + 2:c + 3], st[:, c:c + 1], 1.0 / E))
                    V(k, v, lambda c=c: v.tensor_scalar_mul(
                        st[:, c + 3:c + 4], st[:, c + 1:c + 2], 1.0 / E))
                    V(k, v, lambda c=c: v.scalar_tensor_tensor(
                        st[:, c + 3:c + 4], st[:, c + 2:c + 3], st[:, c + 2:c + 3],
                        st[:, c + 3:c + 4], op0=ALU.mult, op1=ALU.subtract))
                    V(k, v, lambda c=c: v.tensor_scalar(
                        st[:, c + 4:c + 5], st[:, c + 3:c + 4], -1.0, EPS,
                        op0=ALU.mult, op1=ALU.add))
                    v.wait_ge(k.ae.sem, ae_at(i, 2))
                    V(k, v, lambda xt=xt, xn=xn, c=c: v.tensor_scalar(
                        xn, xt, st[:, c + 2:c + 3], st[:, c + 5:c + 6],
                        op0=ALU.subtract, op1=ALU.mult))
                    for j in range(1, NJ, 2):
                        v.wait_ge(k.pe.sem, pe_at(i, j + 1))
                        V(k, v, lambda i=i, j=j: v.tensor_copy(
                            k.xT[:, j * TOKC + i * 128:j * TOKC + (i + 1) * 128],
                            pfs[j % 2][:, :]))
                    for j in range(1, NJ, 2):
                        v.wait_ge(k.pe.sem, pe_at(i, 8 + j + 1))
                        V(k, v, lambda i=i, j=j: v.tensor_copy(
                            k.xnT[:, j * TOKC + i * 128:j * TOKC + (i + 1) * 128],
                            pbs[j % 2][:, :]))

            @blk.scalar
            def _(s):
                for i in range(NT):
                    c = 8 * i
                    s.wait_ge(k.ve.sem, ve_at(i, 6))
                    A(k, s, lambda c=c: s.activation(st[:, c + 5:c + 6],
                                                     st[:, c + 4:c + 5], AF.Ln))
                    A(k, s, lambda c=c: s.activation(st[:, c + 5:c + 6],
                                                     st[:, c + 5:c + 6], AF.Exp,
                                                     scale=-0.5))
                    for j in range(0, NJ, 2):
                        s.wait_ge(k.pe.sem, pe_at(i, j + 1))
                        A(k, s, lambda i=i, j=j: s.copy(
                            k.xT[:, j * TOKC + i * 128:j * TOKC + (i + 1) * 128],
                            pfs[j % 2][:, :]))
                    for j in range(0, NJ, 2):
                        s.wait_ge(k.pe.sem, pe_at(i, 8 + j + 1))
                        A(k, s, lambda i=i, j=j: s.copy(
                            k.xnT[:, j * TOKC + i * 128:j * TOKC + (i + 1) * 128],
                            pbs[j % 2][:, :]))

    loc.close()
    if k.dump is not None:
        dmp(k, lambda: k.xT[:, 0:256])
        dmp(k, lambda: k.xnT[:, 0:256])


def p3_proj(k):
    nc = k.nc
    d = k.d
    loc = contextlib.ExitStack()
    kown = k.salloc(loc, "kown", (KD, TOKC), BF16)
    vstage = k.salloc(loc, "vstage", (KD, TOKC), BF16)
    v65own = k.salloc(loc, "v65own", (128, NT * 65), BF16)
    wq_sb = k.salloc(loc, "wq_sb", (128, NJ * KD), BF16)
    wk_sb = k.salloc(loc, "wk_sb", (128, NJ * KD), BF16)
    wv_sb = k.salloc(loc, "wv_sb", (128, NJ * KD), BF16)
    gw1T_sb = k.salloc(loc, "gw1T_sb", (128, NJ * E), BF16)
    relu_junk = k.salloc(loc, "relu_junk", (128, 512), BF16)
    phacc = k.salloc(loc, "phacc", (128, NJ * 4))
    zrow = k.salloc(loc, "zrow8", (128, NJ))
    vrow = k.salloc(loc, "vrow", (1, F0))
    k.ph = k.salloc(loc, "ph", (128, NJ))

    with nc.psum_tensor("pmps", [128, 32], F32) as pmps:
        for name, n, col in [("gb1", E, k.PM["gb1"]), ("b1", F0, k.PM["b1"]),
                             ("b2", E, k.PM["b2"]), ("bq", E, k.PM["bq"]),
                             ("bk", E, k.PM["bk"]), ("bv", E, k.PM["bv"]),
                             ("bo", E, k.PM["bo"])]:
            with nc.Block() as blk:
                @blk.sync
                def _(s, name=name, n=n):
                    k.hd.inc(s.dma_start(vrow[:1, 0:n], d[name].ap()))
                hm = k.hd.mark()

                @blk.tensor
                def _(t, n=n):
                    wt(t, hm)
                    for ch in range(n // 128):
                        k.pe.inc(t.transpose(pmps[:, ch:ch + 1],
                                             vrow[:1, ch * 128:(ch + 1) * 128],
                                             k.idf[:1, :1]))
                pm = k.pe.mark()

                @blk.vector
                def _(v, n=n, col=col):
                    wt(v, pm)
                    V(k, v, lambda: v.tensor_copy(k.pmv[:, col:col + n // 128],
                                                  pmps[:, 0:n // 128]))

    with nc.Block() as blk:
        @blk.gpsimd
        def _(g):
            for nm, dst in [("wq0", wq_sb), ("wk0", wk_sb), ("wv1", wv_sb)]:
                src = bass.AP(d[nm], 0, [[KD, 128], [128 * KD, NJ], [1, KD]])
                k.gd.inc(g.dma_start(dst[:, :], src))
            src = bass.AP(d["gw1T"], 0, [[E, 128], [128 * E, NJ], [1, E]])
            k.gd.inc(g.dma_start(gw1T_sb[:, :], src))

    with (
        nc.psum_tensor("p3q", [KD, 512], F32) as psq,
        nc.psum_tensor("p3k", [KD, 512], F32) as psk,
        nc.psum_tensor("p3v", [KD, 512], F32) as psv,
    ):
        with nc.Block() as blk:
            ae0 = k.ae.n
            pe_mark = {}
            gdm3 = k.gd.mark()

            @blk.tensor
            def _(t):
                wt(t, gdm3)
                for ts in range(4):
                    if ts > 0:
                        t.wait_ge(k.ae.sem, ae0 + 3 * ts)
                    for pi, (ps, wsb) in enumerate([(psq, wq_sb), (psk, wk_sb),
                                                    (psv, wv_sb)]):
                        for j in range(NJ):
                            k.pe.inc(t.matmul(
                                ps[:, :], wsb[:, j * KD:(j + 1) * KD],
                                k.xnT[:, j * TOKC + ts * 512:j * TOKC + (ts + 1) * 512],
                                start=(j == 0), stop=(j == NJ - 1)))
                        pe_mark[(pi, ts)] = k.pe.n

            @blk.scalar
            def _(s):
                for ts in range(4):
                    for pi, (ps, dst, sc) in enumerate([
                            (psq, k.qp, 0.125), (psk, kown, 1.0),
                            (psv, vstage, 1.0)]):
                        s.wait_ge(k.pe.sem, pe_mark[(pi, ts)])
                        A(k, s, lambda ps=ps, dst=dst, sc=sc, ts=ts: s.activation(
                            dst[:, ts * 512:(ts + 1) * 512], ps[:, :], AF.Copy,
                            scale=sc))

    with nc.psum_tensor("p3vt", [128, KD], BF16) as psvt:
        with nc.Block() as blk:
            @blk.vector
            def _(v):
                V(k, v, lambda: v.memset(v65own[:, :], 1.0))
            vm0 = k.ve.mark()
            pe2 = {}

            @blk.tensor
            def _(t):
                wt(t, vm0)
                for i in range(NT):
                    if i >= 1:
                        t.wait_ge(k.ve.sem, vm0[1] + i)
                    k.pe.inc(t.transpose(psvt[:, :],
                                         vstage[:, i * 128:(i + 1) * 128],
                                         k.idb[:KD, :KD]))
                    pe2[i] = k.pe.n

            @blk.vector
            def _(v):
                for i in range(NT):
                    v.wait_ge(k.pe.sem, pe2[i])
                    V(k, v, lambda i=i: v.tensor_copy(
                        v65own[:, i * 65:i * 65 + KD], psvt[:, :]))

    with nc.psum_tensor("p3h", [128, 512], F32) as psh:
        with nc.Block() as blk:
            ae0 = k.ae.n
            pe3 = {}

            @blk.tensor
            def _(t):
                wt(t, k.gd.mark())
                n = 0
                for oc in range(NJ):
                    for ts in range(4):
                        if n >= 1:
                            t.wait_ge(k.ae.sem, ae0 + n)
                        for j in range(NJ):
                            k.pe.inc(t.matmul(
                                psh[:, :],
                                gw1T_sb[:, j * E + oc * 128:j * E + (oc + 1) * 128],
                                k.xnT[:, j * TOKC + ts * 512:j * TOKC + (ts + 1) * 512],
                                start=(j == 0), stop=(j == NJ - 1)))
                        pe3[n] = k.pe.n
                        n += 1

            @blk.scalar
            def _(s):
                n = 0
                for oc in range(NJ):
                    for ts in range(4):
                        s.wait_ge(k.pe.sem, pe3[n])
                        A(k, s, lambda oc=oc, ts=ts: s.activation(
                            relu_junk[:, :], psh[:, :], AF.Relu,
                            bias=k.pmv[:, k.PM["gb1"] + oc:k.PM["gb1"] + oc + 1],
                            accum_out=phacc[:, oc * 4 + ts:oc * 4 + ts + 1]))
                        n += 1

    with nc.Block() as blk:
        @blk.vector
        def _(v):
            pa = phacc[:, :].rearrange("p (o t) -> p o t", t=4)
            V(k, v, lambda: v.reduce_sum(k.ph[:, :], pa, axis=AX.X))
            V(k, v, lambda: v.memset(zrow[:, :], 0.0))

        @blk.sync
        def _(s):
            k.hd.inc(s.dma_start(
                bass.AP(d["kv_in"], 0, [[TOKC, KD], [1, TOKC]]), kown[:, :]))
            k.hd.inc(s.dma_start(
                bass.AP(d["kv_in"], KD * TOKC, [[65, 128], [65 * 128, NT], [1, 65]]),
                v65own[:, :].rearrange("p (i c) -> p i c", c=65)))

    def phstore(cid):
        b = cid // 4
        with nc.Block() as blk:
            @blk.sync
            def _(s, b=b):
                k.hd.inc(s.dma_start(d["ar_ph_in"].ap()[b, :, :], k.ph[:, :]))
                k.hd.inc(s.dma_start(d["ar_ph_in"].ap()[1 - b, :, :], zrow[:, :]))
    core_switch(k, phstore)

    with nc.Block() as blk:
        @blk.gpsimd
        def _(g):
            g.wait_ge(k.hd.sem, k.hd.n)
            k.agkv = coll(k, g, "AllGather", ALU.bypass, d["kv_in"], d["kv_out"])
            k.arph = coll(k, g, "AllReduce", ALU.add, d["ar_ph_in"], d["ar_ph_out"])

    if k.dump is not None:
        dmp(k, lambda: k.qp[:KD, 0:128], rows=KD)
        dmp(k, lambda: k.ph[:, :])
    loc.close()


def p4_gmw(k):
    nc = k.nc
    d = k.d
    loc = contextlib.ExitStack()
    ph2 = k.salloc(loc, "ph2", (128, 2 * NJ))
    ph2b = k.salloc(loc, "ph2b", (128, 2 * NJ), BF16)
    gw2T_sb = k.salloc(loc, "gw2T_sb", (128, NJ * NE), BF16)
    gb2_sb = k.salloc(loc, "gb2_sb", (1, NE))
    tb1_sb = k.salloc(loc, "tb1_sb", (NE, F0), BF16)
    tb2_sb = k.salloc(loc, "tb2_sb", (NE, E), BF16)
    stg = k.salloc(loc, "mwstg", (128, 2 * 9 * 512), BF16)
    mwo = k.salloc(loc, "mwo", (128, 2 * 512), BF16)
    k.g2 = k.salloc(loc, "g2", (1, 2 * NE))
    k.gcol = k.salloc(loc, "gcol", (NE, 2), BF16)
    k.gbc = k.salloc(loc, "gbc", (128, 2 * NE))
    k.ige = k.salloc(loc, "ige", (128, 17 * 128), BF16)

    with nc.Block() as blk:
        @blk.gpsimd
        def _(g):
            wt(g, k.arph)
            for b in range(B):
                k.gd.inc(g.dma_start(ph2[:, b * NJ:(b + 1) * NJ],
                                     d["ar_ph_out"].ap()[b, :, :]))
            src = bass.AP(d["gw2T"], 0, [[NE, 128], [128 * NE, NJ], [1, NE]])
            k.gd.inc(g.dma_start(gw2T_sb[:, :], src))
            k.gd.inc(g.dma_start(gb2_sb[:1, :], d["gb2"].ap()))
            k.gd.inc(g.dma_start(tb1_sb[:, :], d["Tb1"].ap()))
            k.gd.inc(g.dma_start(tb2_sb[:, :], d["Tb2"].ap()))
        gm = k.gd.mark()

        @blk.vector
        def _(v):
            wt(v, gm)
            V(k, v, lambda: v.tensor_scalar_mul(ph2b[:, :], ph2[:, :], 1.0 / S))

    with (
        nc.psum_tensor("p4g", [1, 2 * NE], F32) as psg,
        nc.psum_tensor("p4b", [128, 2 * NE], F32) as psgb,
        nc.psum_tensor("p4gc", [NE, 2], F32) as psgc,
        nc.psum_tensor("p4m1", [128, 64], F32) as psmb1,
        nc.psum_tensor("p4m2", [128, 16], F32) as psmb2,
    ):
        with nc.Block() as blk:
            @blk.tensor
            def _(t):
                for b in range(B):
                    for j in range(NJ):
                        k.pe.inc(t.matmul(
                            psg[:, b * NE:(b + 1) * NE],
                            ph2b[:, b * NJ + j:b * NJ + j + 1],
                            gw2T_sb[:, j * NE:(j + 1) * NE],
                            start=(j == 0), stop=(j == NJ - 1)))
            pm = k.pe.mark()

            @blk.vector
            def _(v):
                wt(v, pm)
                for b in range(B):
                    V(k, v, lambda b=b: v.tensor_tensor(
                        k.g2[:1, b * NE:(b + 1) * NE], psg[:1, b * NE:(b + 1) * NE],
                        gb2_sb[:1, :], op=ALU.add))
            vm = k.ve.mark()

            @blk.tensor
            def _(t):
                wt(t, vm)
                k.pe.inc(t.matmul(psgb[:, :], k.onesf[:1, :], k.g2[:1, :],
                                  start=True, stop=True))
                for b in range(B):
                    k.pe.inc(t.transpose(psgc[:, b:b + 1],
                                         k.g2[:1, b * NE:(b + 1) * NE],
                                         k.idf[:1, :1]))
            pm2 = k.pe.mark()

            @blk.vector
            def _(v):
                wt(v, pm2)
                V(k, v, lambda: v.tensor_copy(k.gbc[:, :], psgb[:, :]))
                V(k, v, lambda: v.tensor_copy(k.gcol[:, :], psgc[:, :]))

        with nc.Block() as blk:
            @blk.vector
            def _(v):
                V(k, v, lambda: v.tensor_copy(k.ige[:, 0:128], k.idb[:, :]))
                for b in range(B):
                    for e in range(NE):
                        i = 1 + b * NE + e
                        V(k, v, lambda i=i, b=b, e=e: v.tensor_scalar_mul(
                            k.ige[:, i * 128:(i + 1) * 128], k.idb[:, :],
                            k.gbc[:, b * NE + e:b * NE + e + 1]))

        with nc.Block() as blk:
            @blk.tensor
            def _(t):
                for b in range(B):
                    for ch in range(F0 // 128):
                        k.pe.inc(t.matmul(psmb1[:, b * 32 + ch:b * 32 + ch + 1],
                                          tb1_sb[:, ch * 128:(ch + 1) * 128],
                                          k.gcol[:, b:b + 1],
                                          start=True, stop=True))
                    for ch in range(NJ):
                        k.pe.inc(t.matmul(psmb2[:, b * 8 + ch:b * 8 + ch + 1],
                                          tb2_sb[:, ch * 128:(ch + 1) * 128],
                                          k.gcol[:, b:b + 1],
                                          start=True, stop=True))
            k.pm3 = k.pe.mark()

        def mbsel(cid):
            b = cid // 4
            with nc.Block() as blk:
                @blk.vector
                def _(v, b=b):
                    wt(v, k.pm3)
                    V(k, v, lambda b=b: v.tensor_tensor(
                        k.pmv[:, k.PM["mb1"]:k.PM["mb1"] + 32],
                        psmb1[:, b * 32:(b + 1) * 32],
                        k.pmv[:, k.PM["b1"]:k.PM["b1"] + 32], op=ALU.add))
                    V(k, v, lambda b=b: v.tensor_tensor(
                        k.pmv[:, k.PM["mb2"]:k.PM["mb2"] + 8],
                        psmb2[:, b * 8:(b + 1) * 8],
                        k.pmv[:, k.PM["b2"]:k.PM["b2"] + 8], op=ALU.add))
        core_switch(k, mbsel)

    jobs = [("w1", jh) for jh in range(NJ)] + [("w2", jf) for jf in range(F0 // 128)]
    npair = 2 * len(jobs)

    def pair_region(m):
        n, b = m // 2, m % 2
        kind, idx = jobs[n]
        if kind == "w1":
            off = b * MW1N + idx * 128 * 512
            return bass.AP(d["mw_in"], off, [[512, 128], [1, 512]]), 512
        off = 2 * MW1N + b * MW2N + idx * 128 * 128
        return bass.AP(d["mw_in"], off, [[128, 128], [1, 128]]), 128

    with (
        nc.psum_tensor("p4c0", [128, 512], F32) as pc0,
        nc.psum_tensor("p4c1", [128, 512], F32) as pc1,
    ):
        pcs = [pc0, pc1]
        with nc.Block() as blk:
            ldm = [k.gsp[0], k.gsp[1]]
            ldm0 = [ldm[0].n, ldm[1].n]
            stm = [k.sp[2], k.sp[3]]
            stm0 = [stm[0].n, stm[1].n]
            pe0 = k.pe.n
            ae0 = k.ae.n

            @blk.gpsimd
            def _(g):
                for n, (kind, idx) in enumerate(jobs):
                    if n >= 2:
                        g.wait_ge(k.pe.sem, pe0 + 18 * (n - 1))
                    buf = stg[:, (n % 2) * 9 * 512:(n % 2 + 1) * 9 * 512]
                    sl = ldm[n % 2]
                    if kind == "w1":
                        sl.inc(g.dma_start(
                            buf[:, 0:512],
                            d["W1T_s"].ap()[idx * 128:(idx + 1) * 128, :]))
                        for e in range(NE):
                            sl.inc(g.dma_start(
                                buf[:, (1 + e) * 512:(2 + e) * 512],
                                d["TW1T_s"].ap()[e, idx * 128:(idx + 1) * 128, :]))
                    else:
                        sl.inc(g.dma_start(
                            buf[:, 0:128],
                            d["W2T_s"].ap()[idx * 128:(idx + 1) * 128, :]))
                        for e in range(NE):
                            sl.inc(g.dma_start(
                                buf[:, 512 + e * 128:512 + (e + 1) * 128],
                                d["TW2T_s"].ap()[e, idx * 128:(idx + 1) * 128, :]))

            @blk.tensor
            def _(t):
                for m in range(npair):
                    n, b = m // 2, m % 2
                    kind, idx = jobs[n]
                    buf = stg[:, (n % 2) * 9 * 512:(n % 2 + 1) * 9 * 512]
                    t.wait_ge(ldm[n % 2].sem, ldm0[n % 2] + 16 * 9 * (n // 2 + 1))
                    if m >= 2:
                        t.wait_ge(k.ae.sem, ae0 + m - 1)
                    ps = pcs[m % 2]
                    if kind == "w1":
                        w0, wd = buf[:, 0:512], 512
                        srcs = [buf[:, (1 + e) * 512:(2 + e) * 512] for e in range(NE)]
                    else:
                        w0, wd = buf[:, 0:128], 128
                        srcs = [buf[:, 512 + e * 128:512 + (e + 1) * 128]
                                for e in range(NE)]
                    k.pe.inc(t.matmul(ps[:, 0:wd], k.ige[:, 0:128], w0,
                                      start=True, stop=False))
                    for e in range(NE):
                        ig = k.ige[:, (1 + b * NE + e) * 128:(2 + b * NE + e) * 128]
                        k.pe.inc(t.matmul(ps[:, 0:wd], ig, srcs[e],
                                          start=False, stop=(e == NE - 1)))

            @blk.scalar
            def _(s):
                for m in range(npair):
                    s.wait_ge(k.pe.sem, pe0 + 9 * (m + 1))
                    if m >= 2:
                        pm_ = m - 2
                        s.wait_ge(stm[pm_ % 2].sem,
                                  stm0[pm_ % 2] + 16 * (pm_ // 2 + 1))
                    _, w = pair_region(m)
                    A(k, s, lambda m=m, w=w: s.copy(
                        mwo[:, (m % 2) * 512:(m % 2) * 512 + w],
                        pcs[m % 2][:, 0:w]))

            @blk.sync
            def _(s):
                for m in range(npair):
                    s.wait_ge(k.ae.sem, ae0 + m + 1)
                    ap, w = pair_region(m)
                    stm[m % 2].inc(s.dma_start(
                        ap, mwo[:, (m % 2) * 512:(m % 2) * 512 + w]))

    with nc.Block() as blk:
        @blk.gpsimd
        def _(g):
            g.wait_ge(k.sp[2].sem, k.sp[2].n)
            g.wait_ge(k.sp[3].sem, k.sp[3].n)
            k.agmw = coll(k, g, "AllGather", ALU.bypass, d["mw_in"], d["mw_out"])

    if k.dump is not None:
        dmp(k, lambda: k.gbc[:, 0:16])
        dmp(k, lambda: k.pmv[:, 80:120])
        mwchk = k.salloc(loc, "mwchk", (128, 512), BF16)
        with nc.Block() as blk:
            @blk.gpsimd
            def _(g):
                wt(g, k.agmw)
                k.gd.inc(g.dma_start(
                    mwchk[:, :], bass.AP(d["mw_out"], 0, [[512, 128], [1, 512]])))
        gm2 = k.gd.mark()
        with nc.Block() as blk:
            @blk.vector
            def _(v):
                wt(v, gm2)
        dmp(k, lambda: mwchk[:, :])
    loc.close()


def p5_attn0(k):
    nc = k.nc
    d = k.d
    loc = contextlib.ExitStack()
    ksb = k.salloc(loc, "ksb", (KD, S), BF16)
    v65 = k.salloc(loc, "v65", (128, (S // 128) * 65), BF16)
    peb = k.salloc(loc, "peb", (128, 2 * 512), BF16)
    pvs = k.salloc(loc, "pvs", (KD, 512))
    zrow = k.salloc(loc, "zrow", (1, 512))
    rz = k.salloc(loc, "rz", (1, 512))
    rzs = k.salloc(loc, "rzs", (128, 512))
    tmpb = k.salloc(loc, "p5tmp", (128, 512))

    with (
        nc.psum_tensor("p5s0", [128, 512], F32) as ps0,
        nc.psum_tensor("p5s1", [128, 512], F32) as ps1,
        nc.psum_tensor("p5pv", [65, 512], F32) as ps65,
        nc.psum_tensor("p5bc", [128, 512], F32) as psB,
        nc.psum_tensor("p5o0", [128, 512], F32) as po0,
        nc.psum_tensor("p5o1", [128, 512], F32) as po1,
    ):
        pss = [ps0, ps1]
        pos = [po0, po1]

        def body(cid):
            b = cid // 4
            q0 = (cid % 4) * TOKC
            nreg = (cid % 4) + 1

            with nc.Block() as blk:
                @blk.sync
                def _(s):
                    wt(s, k.agkv)
                    for r in range(nreg):
                        src = bass.AP(d["kv_out"], (4 * b + r) * KVN,
                                      [[TOKC, KD], [1, TOKC]])
                        k.hd.inc(s.dma_start(ksb[:, r * TOKC:(r + 1) * TOKC], src))
                        srcv = bass.AP(d["kv_out"],
                                       (4 * b + r) * KVN + KD * TOKC,
                                       [[65, 128], [65 * 128, NT], [1, 65]])
                        k.hd.inc(s.dma_start(
                            v65[:, r * NT * 65:(r + 1) * NT * 65]
                            .rearrange("p (i c) -> p i c", c=65), srcv))
                    for _ in range(2 * (4 - nreg)):
                        s.wait_ge(k.hd.sem, k.hd.n)
                        k.hd.inc(s.nop(nofuse=True))

            for i in range(4):
                nkt = (q0 + 512 * (i + 1)) // 128
                nkt_max = (3 * TOKC + 512 * (i + 1)) // 128
                qs = slice(i * 512, (i + 1) * 512)
                with nc.Block() as blk:
                    pe0 = k.pe.n
                    ae0 = k.ae.n
                    ve0 = k.ve.n
                    hdm = k.hd.mark()
                    s_af = {}
                    pv_af = {}

                    def diag_vi(kt):
                        return kt - (nkt - 4) if kt >= nkt - 4 else None

                    @blk.tensor
                    def _(t):
                        wt(t, hdm)

                        def S(kt):
                            if kt >= 2:
                                t.wait_ge(k.ae.sem, ae0 + kt - 1)
                            k.pe.inc(t.matmul(
                                pss[kt % 2][:, :],
                                ksb[:, kt * 128:(kt + 1) * 128],
                                k.qp[:, qs], start=True, stop=True))
                            s_af[kt] = k.pe.n

                        S(0)
                        for kt in range(nkt):
                            if kt + 1 < nkt:
                                S(kt + 1)
                            t.wait_ge(k.ae.sem, ae0 + kt + 1)
                            vi = diag_vi(kt)
                            if vi is not None:
                                t.wait_ge(k.ve.sem, ve0 + vi + 1)
                            k.pe.inc(t.matmul(
                                ps65[:, :], v65[:, kt * 65:(kt + 1) * 65],
                                peb[:, (kt % 2) * 512:(kt % 2 + 1) * 512],
                                start=(kt == 0), stop=(kt == nkt - 1)))
                            pv_af[kt] = k.pe.n
                        t.wait_ge(k.ve.sem, ve0 + 6)
                        k.pe.inc(t.matmul(psB[:, :], k.onesf[:1, :], rz[:1, :],
                                          start=True, stop=True))
                        bc_af = k.pe.n
                        t.wait_ge(k.ae.sem, ae0 + nkt + 1)
                        for j in range(NJ):
                            if j >= 2:
                                t.wait_ge(k.ve.sem, ve0 + 8 + 2 * (j - 2))
                            k.pe.inc(t.matmul(pos[j % 2][:, :],
                                              k.wv2f[:, j * 128:(j + 1) * 128],
                                              pvs[:, :], start=True, stop=True))
                            pv_af[("o", j)] = k.pe.n
                        for _ in range(2 * (nkt_max - nkt)):
                            t.wait_ge(k.pe.sem, k.pe.n)
                            k.pe.inc(t.nop(nofuse=True))

                    @blk.scalar
                    def _(s):
                        for kt in range(nkt):
                            s.wait_ge(k.pe.sem, s_af[kt])
                            if kt >= 2:
                                s.wait_ge(k.pe.sem, pv_af[kt - 2])
                            A(k, s, lambda kt=kt: s.activation(
                                peb[:, (kt % 2) * 512:(kt % 2 + 1) * 512],
                                pss[kt % 2][:, :], AF.Exp))
                        s.wait_ge(k.pe.sem, pv_af[nkt - 1])
                        A(k, s, lambda: s.copy(pvs[:, :], ps65[0:KD, :]))
                        for _ in range(nkt_max - nkt):
                            s.wait_ge(k.ae.sem, k.ae.n)
                            k.ae.inc(s.nop(nofuse=True))

                    @blk.vector
                    def _(v):
                        for kt in range(nkt - 4, nkt):
                            vi = kt - (nkt - 4)
                            dv = (kt * 128 - (q0 + 512 * i)) // 128
                            v.wait_ge(k.ae.sem, ae0 + kt + 1)
                            V(k, v, lambda kt=kt, dv=dv: v.tensor_tensor(
                                peb[:, (kt % 2) * 512:(kt % 2 + 1) * 512],
                                peb[:, (kt % 2) * 512:(kt % 2 + 1) * 512],
                                k.mask0[:, dv * 512:(dv + 1) * 512], op=ALU.mult))
                        v.wait_ge(k.pe.sem, pv_af[nkt - 1])
                        V(k, v, lambda: v.tensor_copy(zrow[:1, :], ps65[KD:KD + 1, :]))
                        V(k, v, lambda: v.reciprocal(rz[:1, :], zrow[:1, :]))
                        v.wait_ge(k.pe.sem, pv_af[nkt - 1] + 1)
                        V(k, v, lambda: v.tensor_copy(rzs[:, :], psB[:, :]))
                        for j in range(NJ):
                            v.wait_ge(k.pe.sem, pv_af[("o", j)])
                            V(k, v, lambda j=j: v.tensor_mul(
                                tmpb[:, :], pos[j % 2][:, :], rzs[:, :]))
                            xsl = k.xT[:, j * TOKC + i * 512:j * TOKC + (i + 1) * 512]
                            V(k, v, lambda xsl=xsl: v.tensor_add(xsl, xsl, tmpb[:, :]))

        core_switch(k, body)
    if k.dump is not None:
        dmp(k, lambda: k.xT[:, 0:256])
    loc.close()


def p6_ffn(k):
    nc = k.nc
    d = k.d
    loc = contextlib.ExitStack()
    stat1 = k.salloc(loc, "stat1", (128, 2 * 8 * 128), BF16)
    ugel = k.salloc(loc, "ugel", (128, 2 * 512), BF16)
    u_all = k.salloc(loc, "u_all", (128, 32 * 512), BF16)
    stat2 = k.salloc(loc, "stat2", (128, 2 * 8 * 128), BF16)
    tmpo = k.salloc(loc, "p6tmp", (128, 512))

    def mwcopy(cid):
        b = cid // 4
        with nc.Block() as blk:
            @blk.sync
            def _(s, b=b):
                wt(s, k.agmw)
                for r in range(NC):
                    src = bass.AP(d["mw_out"], r * MWN + b * MW1N,
                                  [[512, 1024], [1, 512]])
                    dst = bass.AP(d["mw1t_mine"], r * 512, [[4096, 1024], [1, 512]])
                    k.hd.inc(s.dma_start(dst, src))
                    src2 = bass.AP(d["mw_out"], r * MWN + 2 * MW1N + b * MW2N,
                                   [[128, 4096], [1, 128]])
                    dst2 = bass.AP(d["mw2t_mine"], r * 128, [[1024, 4096], [1, 128]])
                    k.hd.inc(s.dma_start(dst2, src2))
    core_switch(k, mwcopy)
    mwcm = k.hd.mark()

    with (
        nc.psum_tensor("p6u0", [128, 512], F32) as pu0,
        nc.psum_tensor("p6u1", [128, 512], F32) as pu1,
    ):
        pus = [pu0, pu1]
        with nc.Block() as blk:
            lds = [k.sp[4], k.sp[5]]
            lds0 = [lds[0].n, lds[1].n]
            sts = [k.sp[6], k.sp[7]]
            sts0 = [sts[0].n, sts[1].n]
            pe0 = k.pe.n
            ae0 = k.ae.n
            pe_rec = {}

            @blk.sync
            def _(s):
                wt(s, mwcm)
                for ft in range(32):
                    if ft >= 2:
                        s.wait_ge(k.pe.sem, pe0 + 8 * (4 * (ft - 2) + 4))
                    src = bass.AP(d["mw1t_mine"], ft * 128,
                                  [[4096, 128], [128 * 4096, 8], [1, 128]])
                    lds[ft % 2].inc(s.dma_start(
                        stat1[:, (ft % 2) * 1024:(ft % 2 + 1) * 1024], src))

            @blk.tensor
            def _(t):
                for m in range(128):
                    ft, ts = m // 4, m % 4
                    t.wait_ge(lds[ft % 2].sem, lds0[ft % 2] + 16 * (ft // 2 + 1))
                    if m >= 2:
                        t.wait_ge(k.ae.sem, ae0 + m - 1)
                    for jh in range(NJ):
                        k.pe.inc(t.matmul(
                            pus[m % 2][:, :],
                            stat1[:, (ft % 2) * 1024 + jh * 128:
                                  (ft % 2) * 1024 + (jh + 1) * 128],
                            k.xnT[:, jh * TOKC + ts * 512:jh * TOKC + (ts + 1) * 512],
                            start=(jh == 0), stop=(jh == NJ - 1)))
                    pe_rec[m] = k.pe.n

            @blk.scalar
            def _(s):
                for m in range(128):
                    ft = m // 4
                    s.wait_ge(k.pe.sem, pe_rec[m])
                    if m >= 2:
                        pm_ = m - 2
                        s.wait_ge(sts[pm_ % 2].sem, sts0[pm_ % 2] + 16 * (pm_ // 2 + 1))
                    A(k, s, lambda m=m, ft=ft: s.activation(
                        ugel[:, (m % 2) * 512:(m % 2 + 1) * 512], pus[m % 2][:, :],
                        AF.Gelu_apprx_tanh,
                        bias=k.pmv[:, k.PM["mb1"] + ft:k.PM["mb1"] + ft + 1]))

            @blk.gpsimd
            def _(g):
                for m in range(128):
                    ft, ts = m // 4, m % 4
                    g.wait_ge(k.ae.sem, ae0 + m + 1)
                    sts[m % 2].inc(g.dma_start(
                        d["u_dram"].ap()[ft, :, ts * 512:(ts + 1) * 512],
                        ugel[:, (m % 2) * 512:(m % 2 + 1) * 512]))

    ustm = [k.sp[6].mark(), k.sp[7].mark()]
    for ts in range(4):
        with contextlib.ExitStack() as pstk:
            psE = [pstk.enter_context(
                nc.psum_tensor("p6o%d_%d" % (ts, j), [128, 512], F32))
                for j in range(NJ)]
            with nc.Block() as blk:
                ldu = k.sp[8]
                ldu0 = ldu.n
                ld2 = [k.sp[4], k.sp[5]]
                ld20 = [ld2[0].n, ld2[1].n]
                pe0 = k.pe.n
                pe_rec = {}

                @blk.sync
                def _(s):
                    wt(s, ustm[0])
                    wt(s, ustm[1])
                    src = bass.AP(d["u_dram"], ts * 512,
                                  [[TOKC, 128], [128 * TOKC, 32], [1, 512]])
                    ldu.inc(s.dma_start(u_all[:, :], src))
                    for ft in range(32):
                        if ft >= 2:
                            s.wait_ge(k.pe.sem, pe0 + 8 * (ft - 1))
                        src2 = bass.AP(d["mw2t_mine"], ft * 128 * 1024,
                                       [[1024, 128], [128, 8], [1, 128]])
                        ld2[ft % 2].inc(s.dma_start(
                            stat2[:, (ft % 2) * 1024:(ft % 2 + 1) * 1024], src2))

                @blk.tensor
                def _(t):
                    t.wait_ge(ldu.sem, ldu0 + 16)
                    for ft in range(32):
                        t.wait_ge(ld2[ft % 2].sem, ld20[ft % 2] + 16 * (ft // 2 + 1))
                        for jE in range(NJ):
                            k.pe.inc(t.matmul(
                                psE[jE][:, :],
                                stat2[:, (ft % 2) * 1024 + jE * 128:
                                      (ft % 2) * 1024 + (jE + 1) * 128],
                                u_all[:, ft * 512:(ft + 1) * 512],
                                start=(ft == 0), stop=(ft == 31)))
                        pe_rec[ft] = k.pe.n
                pem = k.pe.mark()

                @blk.vector
                def _(v):
                    wt(v, pem)
                    for jE in range(NJ):
                        xsl = k.xT[:, jE * TOKC + ts * 512:jE * TOKC + (ts + 1) * 512]
                        V(k, v, lambda jE=jE, xsl=xsl: v.scalar_tensor_tensor(
                            tmpo[:, :], psE[jE][:, :],
                            k.pmv[:, k.PM["mb2"] + jE:k.PM["mb2"] + jE + 1],
                            xsl, op0=ALU.add, op1=ALU.add))
                        V(k, v, lambda jE=jE, xsl=xsl: v.tensor_copy(xsl, tmpo[:, :]))

    if k.dump is not None:
        dmp(k, lambda: k.xT[:, 0:256])
    loc.close()


def p7_l1(k):
    nc = k.nc
    d = k.d
    loc = contextlib.ExitStack()
    x1tt = k.salloc(loc, "x1tt", (128, 2 * E))
    xn1t = k.salloc(loc, "xn1t", (128, 2 * E), BF16)
    st7 = k.salloc(loc, "p7st", (128, 8 * NT))
    mask1 = k.salloc(loc, "mask1", (128, 256), BF16)
    iota8 = k.salloc(loc, "iota8_sb", (128, NE))
    wg_sb = k.salloc(loc, "wg_sb", (128, NJ * NE), BF16)
    lg_tm = k.salloc(loc, "lg_tm", (128, NT * NE))
    r8 = k.salloc(loc, "r8", (128, 16))
    rsc = k.salloc(loc, "rsc", (128, 8))
    idxu = k.salloc(loc, "idxu", (128, 8), U32)
    ohs = k.salloc(loc, "ohs", (128, 8))
    cntr = k.salloc(loc, "cntr", (1, NE))
    wst = k.salloc(loc, "wst", (128, 2 * 8 * 128), BF16)
    qstg = k.salloc(loc, "qstg", (128, 2 * 512), BF16)
    q_tm = k.salloc(loc, "q_tm", (128, 4 * E), BF16)
    k_tm = k.salloc(loc, "k_tm", (128, 4 * E), BF16)
    v_tm = k.salloc(loc, "v_tm", (128, 4 * E), BF16)
    S2 = k.salloc(loc, "S2", (128, 4 * C1 * C1), BF16)
    Ebuf = k.salloc(loc, "Ebuf", (128, 2 * C1 * C1), BF16)
    junk7 = Ebuf[:, 0:E]
    attn_tm = k.salloc(loc, "attn_tm", (128, 4 * E), BF16)
    attnT = k.salloc(loc, "attnT", (128, NJ * 512), BF16)
    tmpo7 = x1tt[:, 0:512]
    Nf = x1tt[:, 512:512 + 2 * C1]
    Zf = x1tt[:, 640:640 + 2 * C1]
    rZf = x1tt[:, 768:768 + 2 * C1]
    lgf = None

    with (
        nc.psum_tensor("p7f0", [128, 128], F32) as pf0,
        nc.psum_tensor("p7f1", [128, 128], F32) as pf1,
        nc.psum_tensor("p7b0", [128, 128], BF16) as pb0,
        nc.psum_tensor("p7b1", [128, 128], BF16) as pb1,
    ):
        pfs = [pf0, pf1]
        pbs = [pb0, pb1]
        with nc.Block() as blk:
            xst = [k.sp[0], k.sp[1]]
            xst0 = [xst[0].n, xst[1].n]
            ve0 = k.ve.n
            ae0 = k.ae.n
            pe0 = k.pe.n

            def pe_at(i, op):
                return pe0 + 16 * i + op

            def ae_at(i, op):
                return ae0 + 10 * i + op

            def ve_at(i, op):
                return ve0 + 15 * i + op

            @blk.tensor
            def _(t):
                for i in range(NT):
                    for j in range(NJ):
                        if j >= 2:
                            if j % 2 == 0:
                                t.wait_ge(k.ae.sem, ae_at(i, (j - 2) // 2 + 1))
                            else:
                                t.wait_ge(k.ve.sem, ve_at(i, (j - 3) // 2 + 1))
                        elif i >= 1:
                            if j % 2 == 0:
                                t.wait_ge(k.ae.sem, ae_at(i - 1, 4))
                            else:
                                t.wait_ge(k.ve.sem, ve_at(i - 1, 4))
                        k.pe.inc(t.transpose(
                            pfs[j % 2][:, :],
                            k.xT[:, j * TOKC + i * 128:j * TOKC + (i + 1) * 128],
                            k.idf[:, :]))
                    t.wait_ge(k.ve.sem, ve_at(i, 11))
                    for j in range(NJ):
                        if j >= 2:
                            if j % 2 == 0:
                                t.wait_ge(k.ae.sem, ae_at(i, 6 + (j - 2) // 2 + 1))
                            else:
                                t.wait_ge(k.ve.sem, ve_at(i, 11 + (j - 3) // 2 + 1))
                        elif i >= 1:
                            if j % 2 == 0:
                                t.wait_ge(k.ae.sem, ae_at(i - 1, 10))
                            else:
                                t.wait_ge(k.ve.sem, ve_at(i - 1, 15))
                        k.pe.inc(t.transpose(
                            pbs[j % 2][:, :],
                            xn1t[:, (i % 2) * E + j * 128:(i % 2) * E + (j + 1) * 128],
                            k.idb[:, :]))

            @blk.scalar
            def _(s):
                for i in range(NT):
                    c = 8 * i
                    for j in range(0, NJ, 2):
                        s.wait_ge(k.pe.sem, pe_at(i, j + 1))
                        if i >= 2 and j == 0:
                            s.wait_ge(k.ve.sem, ve_at(i - 2, 11))
                        A(k, s, lambda i=i, j=j: s.copy(
                            x1tt[:, (i % 2) * E + j * 128:(i % 2) * E + (j + 1) * 128],
                            pfs[j % 2][:, :]))
                    s.wait_ge(k.ve.sem, ve_at(i, 10))
                    A(k, s, lambda c=c: s.activation(st7[:, c + 5:c + 6],
                                                     st7[:, c + 4:c + 5], AF.Ln))
                    A(k, s, lambda c=c: s.activation(st7[:, c + 5:c + 6],
                                                     st7[:, c + 5:c + 6], AF.Exp,
                                                     scale=-0.5))
                    for j in range(0, NJ, 2):
                        s.wait_ge(k.pe.sem, pe_at(i, 8 + j + 1))
                        A(k, s, lambda i=i, j=j: s.copy(
                            k.xnT[:, j * TOKC + i * 128:j * TOKC + (i + 1) * 128],
                            pbs[j % 2][:, :]))

            @blk.vector
            def _(v):
                for i in range(NT):
                    c = 8 * i
                    xt = x1tt[:, (i % 2) * E:(i % 2 + 1) * E]
                    xn = xn1t[:, (i % 2) * E:(i % 2 + 1) * E]
                    for j in range(1, NJ, 2):
                        v.wait_ge(k.pe.sem, pe_at(i, j + 1))
                        V(k, v, lambda i=i, j=j: v.tensor_copy(
                            x1tt[:, (i % 2) * E + j * 128:(i % 2) * E + (j + 1) * 128],
                            pfs[j % 2][:, :]))
                    v.wait_ge(k.ae.sem, ae_at(i, 4))
                    V(k, v, lambda xt=xt, c=c: v.reduce_sum(st7[:, c:c + 1], xt, axis=AX.X))
                    V(k, v, lambda xt=xt, c=c: v.scalar_tensor_tensor(
                        junk7, xt, 1.0, xt, op0=ALU.bypass, op1=ALU.mult,
                        accum_out=st7[:, c + 1:c + 2]))
                    V(k, v, lambda c=c: v.tensor_scalar_mul(
                        st7[:, c + 2:c + 3], st7[:, c:c + 1], 1.0 / E))
                    V(k, v, lambda c=c: v.tensor_scalar_mul(
                        st7[:, c + 3:c + 4], st7[:, c + 1:c + 2], 1.0 / E))
                    V(k, v, lambda c=c: v.scalar_tensor_tensor(
                        st7[:, c + 3:c + 4], st7[:, c + 2:c + 3], st7[:, c + 2:c + 3],
                        st7[:, c + 3:c + 4], op0=ALU.mult, op1=ALU.subtract))
                    V(k, v, lambda c=c: v.tensor_scalar(
                        st7[:, c + 4:c + 5], st7[:, c + 3:c + 4], -1.0, EPS,
                        op0=ALU.mult, op1=ALU.add))
                    v.wait_ge(k.ae.sem, ae_at(i, 6))
                    if i >= 2:
                        v.wait_ge(k.pe.sem, pe_at(i - 2, 16))
                        v.wait_ge(xst[i % 2].sem, xst0[i % 2] + 16 * (i // 2))
                    V(k, v, lambda xt=xt, xn=xn, c=c: v.tensor_scalar(
                        xn, xt, st7[:, c + 2:c + 3], st7[:, c + 5:c + 6],
                        op0=ALU.subtract, op1=ALU.mult))
                    for j in range(1, NJ, 2):
                        v.wait_ge(k.pe.sem, pe_at(i, 8 + j + 1))
                        V(k, v, lambda i=i, j=j: v.tensor_copy(
                            k.xnT[:, j * TOKC + i * 128:j * TOKC + (i + 1) * 128],
                            pbs[j % 2][:, :]))

            @blk.sync
            def _(s):
                for i in range(NT):
                    s.wait_ge(k.ve.sem, ve_at(i, 11))
                    xst[i % 2].inc(s.dma_start(
                        d["xn1_in"].ap()[i * 128:(i + 1) * 128, :],
                        xn1t[:, (i % 2) * E:(i % 2 + 1) * E]))

            @blk.gpsimd
            def _(g):
                k.gd.inc(g.dma_start(
                    mask1[:, :],
                    bass.AP(d["maskL1"], 0, [[4096, 128], [64, 16], [1, 16]])))
                k.gd.inc(g.dma_start(iota8[:, :], d["iota8"].ap()))
                src = bass.AP(d["wgate"], 0, [[NE, 128], [128 * NE, NJ], [1, NE]])
                k.gd.inc(g.dma_start(wg_sb[:, :], src))
        ln1m = (k.gd.mark(), xst[0].mark(), xst[1].mark())

    lgf = x1tt

    with nc.Block() as blk:
        @blk.gpsimd
        def _(g):
            wt(g, ln1m[1])
            wt(g, ln1m[2])
            k.agxn1 = coll(k, g, "AllGather", ALU.bypass, d["xn1_in"], d["xn1_out"])

    with (
        nc.psum_tensor("p7r0", [NE, 512], F32) as pr0,
        nc.psum_tensor("p7r1", [NE, 512], F32) as pr1,
        nc.psum_tensor("p7t0", [128, NE], F32) as pt0,
        nc.psum_tensor("p7t1", [128, NE], F32) as pt1,
        nc.psum_tensor("p7c", [1, NE], F32) as pcnt,
    ):
        prs = [pr0, pr1]
        pts = [pt0, pt1]
        with nc.Block() as blk:
            ve0 = k.ve.n
            pe_rec = {}

            @blk.tensor
            def _(t):
                wt(t, ln1m[0])
                for ts in range(4):
                    if ts >= 2:
                        t.wait_ge(k.ve.sem, ve0 + ts - 1)
                    for j in range(NJ):
                        k.pe.inc(t.matmul(
                            prs[ts % 2][:, :], wg_sb[:, j * NE:(j + 1) * NE],
                            k.xnT[:, j * TOKC + ts * 512:j * TOKC + (ts + 1) * 512],
                            start=(j == 0), stop=(j == NJ - 1)))
                    pe_rec[ts] = k.pe.n

            @blk.vector
            def _(v):
                for ts in range(4):
                    v.wait_ge(k.pe.sem, pe_rec[ts])
                    V(k, v, lambda ts=ts: v.tensor_copy(
                        lgf[0:NE, ts * 512:(ts + 1) * 512], prs[ts % 2][:, :]))

        with nc.Block() as blk:
            ve0 = k.ve.n
            pe_rec = {}

            @blk.tensor
            def _(t):
                for i in range(NT):
                    if i >= 2:
                        t.wait_ge(k.ve.sem, ve0 + i - 1)
                    k.pe.inc(t.transpose(pts[i % 2][:, :],
                                         lgf[0:NE, i * 128:(i + 1) * 128],
                                         k.idf[:NE, :NE]))
                    pe_rec[i] = k.pe.n

            @blk.vector
            def _(v):
                for i in range(NT):
                    v.wait_ge(k.pe.sem, pe_rec[i])
                    V(k, v, lambda i=i: v.tensor_copy(
                        lg_tm[:, i * NE:(i + 1) * NE], pts[i % 2][:, :]))

        with nc.Block() as blk:
            ve_rec = {}
            ae_rec = {}
            ve0 = k.ve.n
            ae0 = k.ae.n
            pe0c = k.pe.n

            @blk.vector
            def _(v):
                for i in range(NT):
                    lt = lg_tm[:, i * NE:(i + 1) * NE]
                    V(k, v, lambda lt=lt: v.max(r8[:, 0:8], lt))
                    V(k, v, lambda lt=lt: v.max_index(idxu[:, :], r8[:, 0:8], lt))
                    V(k, v, lambda i=i: v.tensor_copy(k.rtri[:, 2 * i:2 * i + 1],
                                                      idxu[:, 0:1]))
                    V(k, v, lambda i=i: v.tensor_copy(k.rtri[:, 2 * i + 1:2 * i + 2],
                                                      idxu[:, 1:2]))
                    V(k, v, lambda: v.tensor_scalar_mul(rsc[:, 0:1], r8[:, 0:1], -1.0))
                    ve_rec[("pre", i)] = k.ve.n
                    v.wait_ge(k.ae.sem, ae0 + 2 * i + 1)
                    V(k, v, lambda: v.reciprocal(rsc[:, 2:3], rsc[:, 1:2]))
                    V(k, v, lambda i=i: v.tensor_copy(k.rtrw[:, 2 * i:2 * i + 1],
                                                      rsc[:, 2:3]))
                    v.wait_ge(k.ae.sem, ae0 + 2 * i + 2)
                    V(k, v, lambda i=i: v.tensor_mul(k.rtrw[:, 2 * i + 1:2 * i + 2],
                                                     rsc[:, 3:4], rsc[:, 2:3]))
                    V(k, v, lambda i=i: v.tensor_scalar(
                        k.oh1[:, i * NE:(i + 1) * NE], iota8[:, :],
                        k.rtri[:, 2 * i:2 * i + 1], None, op0=ALU.is_equal))
                    V(k, v, lambda i=i: v.tensor_scalar(
                        k.oh2[:, i * NE:(i + 1) * NE], iota8[:, :],
                        k.rtri[:, 2 * i + 1:2 * i + 2], None, op0=ALU.is_equal))
                    if i >= 1:
                        v.wait_ge(k.pe.sem, pe0c + i)
                    V(k, v, lambda i=i: v.tensor_add(
                        ohs[:, :], k.oh1[:, i * NE:(i + 1) * NE],
                        k.oh2[:, i * NE:(i + 1) * NE]))
                    ve_rec[("ohs", i)] = k.ve.n

            @blk.scalar
            def _(s):
                for i in range(NT):
                    lt = lg_tm[:, i * NE:(i + 1) * NE]
                    s.wait_ge(k.ve.sem, ve_rec[("pre", i)])
                    A(k, s, lambda lt=lt: s.activation(
                        r8[:, 8:16], lt, AF.Exp, bias=rsc[:, 0:1],
                        accum_out=rsc[:, 1:2]))
                    A(k, s, lambda: s.activation(rsc[:, 3:4], r8[:, 1:2], AF.Exp,
                                                 bias=rsc[:, 0:1]))

            @blk.tensor
            def _(t):
                for i in range(NT):
                    t.wait_ge(k.ve.sem, ve_rec[("ohs", i)])
                    k.pe.inc(t.matmul(pcnt[:, :], k.onesf[:, 0:1], ohs[:, :],
                                      start=(i == 0), stop=(i == NT - 1)))
            pcm = k.pe.mark()

            @blk.vector
            def _(v):
                wt(v, pcm)
                V(k, v, lambda: v.tensor_copy(cntr[:1, :], pcnt[:1, :]))
            cvm = k.ve.mark()

            @blk.sync
            def _(s):
                wt(s, cvm)
                k.hd.inc(s.dma_start(d["cnt_in"].ap(), cntr[:1, :]))

        with nc.Block() as blk:
            @blk.gpsimd
            def _(g):
                g.wait_ge(k.hd.sem, k.hd.n)
                k.agcnt = coll(k, g, "AllGather", ALU.bypass, d["cnt_in"], d["cnt_out"])

    with (
        nc.psum_tensor("p7q0", [128, 512], F32) as pq0,
        nc.psum_tensor("p7q1", [128, 512], F32) as pq1,
        nc.psum_tensor("p7x0", [128, 128], BF16) as px0,
        nc.psum_tensor("p7x1", [128, 128], BF16) as px1,
        nc.psum_tensor("p7w0", [128, 512], F32) as pw0,
        nc.psum_tensor("p7w1", [128, 512], F32) as pw1,
    ):
        pqs = [pq0, pq1]
        pxs = [px0, px1]
        pws = [pw0, pw1]
        mats = [("Wq1", q_tm, "bq", 0.125), ("Wk1", k_tm, "bk", 1.0),
                ("Wv1_", v_tm, "bv", 1.0)]

        with nc.Block() as blk:
            lds = [k.gsp[2], k.gsp[3]]
            lds0 = [lds[0].n, lds[1].n]
            ae0 = k.ae.n
            ve0 = k.ve.n
            pe_rec = {}
            petp_rec = {}
            pe_rec_pre = k.pe.n

            @blk.gpsimd
            def _(g):
                for m in range(24):
                    mi, jo = m // 8, m % 8
                    if m >= 2:
                        g.wait_ge(k.pe.sem, pe_rec_pre + 12 * (m - 1))
                    src = bass.AP(d[mats[mi][0]], jo * 128,
                                  [[E, 128], [128 * E, NJ], [1, 128]])
                    lds[m % 2].inc(g.dma_start(
                        wst[:, (m % 2) * 1024:(m % 2 + 1) * 1024], src))

            @blk.tensor
            def _(t):
                for m in range(24):
                    mi, jo = m // 8, m % 8
                    t.wait_ge(lds[m % 2].sem, lds0[m % 2] + 16 * (m // 2 + 1))
                    if m >= 2:
                        t.wait_ge(k.ae.sem, ae0 + 3 * (m - 2) + 1)
                    for j in range(NJ):
                        k.pe.inc(t.matmul(
                            pqs[m % 2][:, :],
                            wst[:, (m % 2) * 1024 + j * 128:
                                (m % 2) * 1024 + (j + 1) * 128],
                            k.xnT[:, j * TOKC:j * TOKC + 512],
                            start=(j == 0), stop=(j == NJ - 1)))
                    pe_rec[m] = k.pe.n
                    t.wait_ge(k.ae.sem, ae0 + 3 * m + 1)
                    for tt in range(4):
                        if m * 4 + tt >= 2:
                            pn = m * 4 + tt - 2
                            pm_, ptt = pn // 4, pn % 4
                            if ptt % 2 == 0:
                                t.wait_ge(k.ae.sem, ae0 + 3 * pm_ + 1 + ptt // 2 + 1)
                            else:
                                t.wait_ge(k.ve.sem, ve0 + 2 * pm_ + (ptt - 1) // 2 + 1)
                        k.pe.inc(t.transpose(
                            pxs[tt % 2][:, :],
                            qstg[:, (m % 2) * 512 + tt * 128:
                                 (m % 2) * 512 + (tt + 1) * 128],
                            k.idb[:, :]))
                        petp_rec[(m, tt)] = k.pe.n

            @blk.scalar
            def _(s):
                for m in range(24):
                    mi, jo = m // 8, m % 8
                    nm, dstt, bn, sc = mats[mi]
                    s.wait_ge(k.pe.sem, pe_rec[m])
                    A(k, s, lambda jo=jo, bn=bn, sc=sc, m=m:
                      s.activation(qstg[:, (m % 2) * 512:(m % 2 + 1) * 512],
                                   pqs[m % 2][:, :], AF.Identity, scale=sc,
                                   bias=k.pmv[:, k.PM[bn] + jo:k.PM[bn] + jo + 1]))
                    for tt in (0, 2):
                        s.wait_ge(k.pe.sem, petp_rec[(m, tt)])
                        A(k, s, lambda dstt=dstt, jo=jo, tt=tt: s.copy(
                            dstt[:, tt * E + jo * 128:tt * E + (jo + 1) * 128],
                            pxs[tt % 2][:, :]))

            @blk.vector
            def _(v):
                for m in range(24):
                    mi, jo = m // 8, m % 8
                    dstt = mats[mi][1]
                    for tt in (1, 3):
                        v.wait_ge(k.pe.sem, petp_rec[(m, tt)])
                        V(k, v, lambda dstt=dstt, jo=jo, tt=tt: v.tensor_copy(
                            dstt[:, tt * E + jo * 128:tt * E + (jo + 1) * 128],
                            pxs[tt % 2][:, :]))

        for ts in range(4):
            with_qkv = ts < 3
            tsn = ts + 1
            with nc.Block() as blk:
                lds = [k.gsp[2], k.gsp[3]]
                lds0 = [lds[0].n, lds[1].n]
                ae0 = k.ae.n
                pe0 = k.pe.n
                ve_mask = {}
                ve_fin = {}
                E4 = Ebuf[:, :].rearrange("p (i a b) -> p i a b", i=2, b=C1)
                LV = [(32, 32), (16, 16), (8, 0)]
                M3 = mask1[:, :].rearrange("p (a b) -> p a b", b=16)

                def sl4_of(p):
                    return S2[:, (p % 2) * 8192:(p % 2 + 1) * 8192]\
                        .rearrange("p (i a b) -> p i a b", i=2, b=C1)

                def prevu(tt, m):
                    if m >= 2:
                        return (tt, m - 2)
                    if tt >= 1:
                        return (tt - 1, 22 + m)
                    return None

                mm_done = {}
                tp_done = {}
                cnt = pe0
                if with_qkv:
                    for tt in range(4):
                        for m in range(24):
                            cnt += 8
                            mm_done[(tt, m)] = cnt
                            cnt += 1
                            tp_done[(tt, m)] = cnt
                exp_done = {}
                evac1_done = {}
                tpev_done = {}
                acnt = ae0
                pending = []
                ins_at = {8: 0, 16: 1, 24: 2, 31: 3} if with_qkv else {}
                sched = []
                for p in range(32):
                    sched.append(("exp", p, 0))
                    acnt += 4
                    exp_done[p] = acnt
                    if p in ins_at:
                        tt = ins_at[p]
                        pending.append(("gate", tt, 0))
                        for m in range(24):
                            pending.append(("e1", tt, m))
                            pending.append(("tp", tt, m))
                    take = 12 if p != 31 else 10 ** 9
                    taken = 0
                    while pending and taken < take:
                        op = pending.pop(0)
                        sched.append(op)
                        if op[0] in ("e1", "tp"):
                            acnt += 1
                            taken += 1
                            if op[0] == "e1":
                                evac1_done[(op[1], op[2])] = acnt
                            else:
                                tpev_done[(op[1], op[2])] = acnt

                @blk.vector
                def _(v):
                    Vf(k, lambda: v.memset(Ebuf[:, :], 0.0))

                    def build(p):
                        tt, hp = p // 8, p % 8
                        base = tt * E + 2 * hp * C1
                        sl4 = sl4_of(p)
                        q2 = q_tm[:, base:base + 128]\
                            .rearrange("p (i c) -> p i c", i=2)
                        k2 = k_tm[:, base:base + 128]\
                            .rearrange("p (i c) -> p i c", i=2)
                        if p >= 2:
                            v.wait_ge(k.ae.sem, exp_done[p - 2])
                        for qq in range(4):
                            Kq = 16 * (qq + 1)
                            qap = q2[:, :, 16 * qq:16 * (qq + 1)]\
                                .unsqueeze(3).broadcast_to([128, 2, 16, Kq])
                            kap = k2[:, :, 0:Kq]\
                                .unsqueeze(2).broadcast_to([128, 2, 16, Kq])
                            Vf(k, lambda sl4=sl4, qq=qq, Kq=Kq, qap=qap,
                               kap=kap: v.tensor_tensor(
                                sl4[:, :, 16 * qq:16 * (qq + 1), 0:Kq],
                                qap, kap, op=ALU.mult))
                        Vbar(k, v)
                        mbc = M3.unsqueeze(1).broadcast_to([128, 2, 16, 16])
                        for qq in range(4):
                            Vf(k, lambda sl4=sl4, qq=qq, mbc=mbc: v.tensor_add(
                                sl4[:, :, 16 * qq:16 * (qq + 1),
                                    16 * qq:16 * (qq + 1)],
                                sl4[:, :, 16 * qq:16 * (qq + 1),
                                    16 * qq:16 * (qq + 1)],
                                mbc))
                        ve_mask[p] = k.ve.n

                    build(0)
                    for p in range(32):
                        tt, hp = p // 8, p % 8
                        base = tt * E + 2 * hp * C1
                        if p + 1 < 32:
                            build(p + 1)
                        v.wait_ge(k.ae.sem, exp_done[p])
                        EV4 = sl4_of(p)
                        vap = v_tm[:, base:base + 128]\
                            .rearrange("p (i c) -> p i c", i=2)\
                            .unsqueeze(2).broadcast_to([128, 2, C1, C1])
                        Vbar(k, v)
                        Vf(k, lambda EV4=EV4, vap=vap: v.tensor_tensor(
                            EV4, E4, vap, op=ALU.mult))
                        for hw, rlo in LV:
                            Vbar(k, v)
                            for b4 in (E4, EV4):
                                Vf(k, lambda b4=b4, hw=hw, rlo=rlo:
                                   v.tensor_add(
                                    b4[:, :, rlo:C1, 0:hw],
                                    b4[:, :, rlo:C1, 0:hw],
                                    b4[:, :, rlo:C1, hw:2 * hw]))
                        Vbar(k, v)
                        Vf(k, lambda EV4=EV4: v.reduce_sum(
                            Zf, E4[:, :, :, 0:8], axis=AX.X))
                        Vf(k, lambda EV4=EV4: v.reduce_sum(
                            Nf, EV4[:, :, :, 0:8], axis=AX.X))
                        Vbar(k, v)
                        Vf(k, lambda: v.reciprocal(rZf, Zf))
                        Vbar(k, v)
                        Vf(k, lambda base=base: v.tensor_mul(
                            attn_tm[:, base:base + 128],
                            Nf, rZf))
                        ve_fin[p] = k.ve.n

                @blk.tensor
                def _(t):
                    if not with_qkv:
                        return
                    for tt in range(4):
                        for m in range(24):
                            mi, jo = m // 8, m % 8
                            L = tt * 24 + m
                            t.wait_ge(lds[m % 2].sem,
                                      lds0[m % 2] + 16 * (L // 2 + 1))
                            pu = prevu(tt, m)
                            if pu is not None:
                                t.wait_ge(k.ae.sem, evac1_done[pu])
                            for j in range(NJ):
                                k.pe.inc(t.matmul(
                                    pqs[m % 2][:, 0:128],
                                    wst[:, (m % 2) * 1024 + j * 128:
                                        (m % 2) * 1024 + (j + 1) * 128],
                                    k.xnT[:, j * TOKC + tsn * 512 + tt * 128:
                                          j * TOKC + tsn * 512 + (tt + 1) * 128],
                                    start=(j == 0), stop=(j == NJ - 1)))
                            assert k.pe.n == mm_done[(tt, m)]
                            t.wait_ge(k.ae.sem, evac1_done[(tt, m)])
                            pu = prevu(tt, m)
                            if pu is not None:
                                t.wait_ge(k.ae.sem, tpev_done[pu])
                            k.pe.inc(t.transpose(
                                pxs[m % 2][:, :],
                                qstg[:, (m % 2) * 512:(m % 2) * 512 + 128],
                                k.idb[:, :]))
                            assert k.pe.n == tp_done[(tt, m)]

                @blk.gpsimd
                def _(g):
                    if not with_qkv:
                        return
                    for tt in range(4):
                        for m in range(24):
                            mi, jo = m // 8, m % 8
                            pu = prevu(tt, m)
                            if pu is not None:
                                g.wait_ge(k.pe.sem, mm_done[pu])
                            src = bass.AP(d[mats[mi][0]], jo * 128,
                                          [[E, 128], [128 * E, NJ], [1, 128]])
                            lds[m % 2].inc(g.dma_start(
                                wst[:, (m % 2) * 1024:(m % 2 + 1) * 1024], src))

                @blk.scalar
                def _(s):
                    for op, a1, a2 in sched:
                        if op == "exp":
                            p = a1
                            s.wait_ge(k.ve.sem, ve_mask[p])
                            if p >= 1:
                                s.wait_ge(k.ve.sem, ve_fin[p - 1])
                            sl4 = sl4_of(p)
                            for qq in range(4):
                                Kq = 16 * (qq + 1)
                                A(k, s, lambda sl4=sl4, qq=qq, Kq=Kq:
                                  s.activation(
                                    E4[:, :, 16 * qq:16 * (qq + 1), 0:Kq],
                                    sl4[:, :, 16 * qq:16 * (qq + 1), 0:Kq],
                                    AF.Exp))
                        elif op == "gate":
                            s.wait_ge(k.ve.sem, ve_fin[8 * a1 + 7])
                        elif op == "e1":
                            tt, m = a1, a2
                            mi, jo = m // 8, m % 8
                            nm, dstt, bn, sc = mats[mi]
                            s.wait_ge(k.pe.sem, mm_done[(tt, m)])
                            pu = prevu(tt, m)
                            if pu is not None:
                                s.wait_ge(k.pe.sem, tp_done[pu])
                            A(k, s, lambda jo=jo, bn=bn, sc=sc, m=m:
                              s.activation(
                                qstg[:, (m % 2) * 512:(m % 2) * 512 + 128],
                                pqs[m % 2][:, 0:128], AF.Identity, scale=sc,
                                bias=k.pmv[:, k.PM[bn] + jo:k.PM[bn] + jo + 1]))
                        else:
                            tt, m = a1, a2
                            mi, jo = m // 8, m % 8
                            dstt = mats[mi][1]
                            s.wait_ge(k.pe.sem, tp_done[(tt, m)])
                            A(k, s, lambda dstt=dstt, jo=jo, tt=tt, m=m:
                              s.copy(
                                dstt[:, tt * E + jo * 128:
                                     tt * E + (jo + 1) * 128],
                                pxs[m % 2][:, :]))
                    assert k.ae.n == acnt

            with nc.Block() as blk:
                ve0 = k.ve.n
                ae0 = k.ae.n
                pe_rec = {}

                @blk.tensor
                def _(t):
                    for n in range(32):
                        jo, tt = n // 4, n % 4
                        if n >= 2:
                            pn = n - 2
                            if pn % 2 == 0:
                                t.wait_ge(k.ae.sem, ae0 + pn // 2 + 1)
                            else:
                                t.wait_ge(k.ve.sem, ve0 + pn // 2 + 1)
                        k.pe.inc(t.transpose(
                            pxs[n % 2][:, :],
                            attn_tm[:, tt * E + jo * 128:tt * E + (jo + 1) * 128],
                            k.idb[:, :]))
                        pe_rec[n] = k.pe.n

                @blk.scalar
                def _(s):
                    for n in range(32):
                        jo, tt = n // 4, n % 4
                        if n % 2 == 0:
                            s.wait_ge(k.pe.sem, pe_rec[n])
                            A(k, s, lambda jo=jo, tt=tt, n=n: s.copy(
                                attnT[:, jo * 512 + tt * 128:jo * 512 + (tt + 1) * 128],
                                pxs[n % 2][:, :]))

                @blk.vector
                def _(v):
                    for n in range(32):
                        jo, tt = n // 4, n % 4
                        if n % 2 == 1:
                            v.wait_ge(k.pe.sem, pe_rec[n])
                            V(k, v, lambda jo=jo, tt=tt, n=n: v.tensor_copy(
                                attnT[:, jo * 512 + tt * 128:jo * 512 + (tt + 1) * 128],
                                pxs[n % 2][:, :]))

            with nc.Block() as blk:
                lds = [k.gsp[2], k.gsp[3]]
                lds0 = [lds[0].n, lds[1].n]
                ve0 = k.ve.n
                pe_rec = {}
                pe_rec_pre2 = k.pe.n

                @blk.gpsimd
                def _(g):
                    for jE in range(NJ):
                        if jE >= 2:
                            g.wait_ge(k.pe.sem, pe_rec_pre2 + 8 * (jE - 1))
                        src = bass.AP(d["Wo1"], jE * 128,
                                      [[E, 128], [128 * E, NJ], [1, 128]])
                        lds[jE % 2].inc(g.dma_start(
                            wst[:, (jE % 2) * 1024:(jE % 2 + 1) * 1024], src))

                @blk.tensor
                def _(t):
                    for jE in range(NJ):
                        t.wait_ge(lds[jE % 2].sem, lds0[jE % 2] + 16 * (jE // 2 + 1))
                        if jE >= 2:
                            t.wait_ge(k.ve.sem, ve0 + 2 * (jE - 1))
                        for j in range(NJ):
                            k.pe.inc(t.matmul(
                                pws[jE % 2][:, :],
                                wst[:, (jE % 2) * 1024 + j * 128:
                                    (jE % 2) * 1024 + (j + 1) * 128],
                                attnT[:, j * 512:(j + 1) * 512],
                                start=(j == 0), stop=(j == NJ - 1)))
                        pe_rec[jE] = k.pe.n

                @blk.vector
                def _(v):
                    for jE in range(NJ):
                        v.wait_ge(k.pe.sem, pe_rec[jE])
                        xsl = k.xT[:, jE * TOKC + ts * 512:jE * TOKC + (ts + 1) * 512]
                        V(k, v, lambda jE=jE, xsl=xsl: v.scalar_tensor_tensor(
                            tmpo7, pws[jE % 2][:, :],
                            k.pmv[:, k.PM["bo"] + jE:k.PM["bo"] + jE + 1],
                            xsl, op0=ALU.add, op1=ALU.add))
                        V(k, v, lambda jE=jE, xsl=xsl: v.tensor_copy(xsl, tmpo7))

    if k.dump is not None:
        dmp(k, lambda: k.xT[:, 0:256])
        dmp(k, lambda: lg_tm[:, 0:16])
        dmp(k, lambda: k.rtrw[:, 0:8])
        dmp(k, lambda: k.rtri[:, 0:8])
        dmp(k, lambda: cntr[:1, :], rows=1)
    k.p7loc = loc


def p8_moe(k):
    nc = k.nc
    d = k.d
    loc = contextlib.ExitStack()
    smu = k.salloc(loc, "smu", (128, 32), U32)
    k.tokpm = k.salloc(loc, "tokpm", (128, 16))
    sepm = k.salloc(loc, "sepm", (128, 16), U32)
    sopm = k.salloc(loc, "sopm", (128, 16), U32)
    w1c = k.rtrw

    dsc = contextlib.ExitStack()
    cnt8 = k.salloc(dsc, "cnt8", (NE, NE))
    cntT = k.salloc(dsc, "cntT", (NE, NE))
    OHp = k.salloc(dsc, "OHp", (NE, NCH))
    OHc = k.salloc(dsc, "OHc", (NE, NCH))
    base8 = k.salloc(dsc, "base8", (NE, 4))
    posr = k.salloc(dsc, "posr", (1, NCH))
    fer = k.salloc(dsc, "fer", (1, NCH))
    so2 = k.salloc(dsc, "so2", (1, NCH))
    vmr = k.salloc(dsc, "vmr", (1, NCH))
    sentb = k.salloc(dsc, "sentb", (128, 64))
    iotac = k.salloc(dsc, "iotac", (NE, 1))
    smf = k.salloc(dsc, "smf", (128, 32))
    cmhl = k.salloc(dsc, "cmhl", (1, 2048))

    with (
        nc.psum_tensor("p8a", [NE, NE], F32) as p8a,
        nc.psum_tensor("p8b", [NE, 128], F32) as p8t,
        nc.psum_tensor("p8pos", [1, 512], F32) as p8pos,
        nc.psum_tensor("p8fe", [1, 512], F32) as p8fe,
        nc.psum_tensor("p8i", [NE, 1], F32) as p8i,
        nc.psum_tensor("p8c", [128, 1], F32) as p8c,
    ):
        with nc.Block() as blk:
            @blk.gpsimd
            def _(g):
                wt(g, k.agcnt)
                k.gd.inc(g.dma_start(cnt8[:, :], d["cnt_out"].ap()))
                k.gd.inc(g.dma_start(k.tokpm[:, :], d["tokid_pm"].ap()))
                k.gd.inc(g.dma_start(k.iota8_row[:1, :], d["iota8"].ap()[0:1, :]))
            gm = k.gd.mark()

            @blk.tensor
            def _(t):
                wt(t, gm)
                k.pe.inc(t.transpose(p8a[:, :], cnt8[:, :], k.idf[:NE, :NE]))
                k.pe.inc(t.transpose(p8i[:, :], k.iota8_row[:1, :], k.idf[:1, :1]))
            pm = k.pe.mark()

            @blk.vector
            def _(v):
                wt(v, pm)
                V(k, v, lambda: v.tensor_copy(cntT[:, :], p8a[:, :]))
                V(k, v, lambda: v.tensor_tensor_scan(
                    OHc[:, 0:NE], cntT[:, :], cntT[:, :], 0.0,
                    op0=ALU.add, op1=ALU.bypass))
                V(k, v, lambda: v.tensor_sub(cntT[:, :], OHc[:, 0:NE], cntT[:, :]))
                V(k, v, lambda: v.tensor_copy(iotac[:, 0:1], p8i[:, 0:1]))

        if getattr(k, "p8stop", 99) <= 1:
            dsc.close(); loc.close(); return

        def basesel(cid):
            with nc.Block() as blk:
                @blk.vector
                def _(v, cid=cid):
                    V(k, v, lambda: v.tensor_scalar(
                        base8[:, 0:1], cntT[:, cid:cid + 1], -1.0, None,
                        op0=ALU.add))
        core_switch(k, basesel)

        if getattr(k, "p8stop", 99) <= 2:
            dsc.close(); loc.close(); return

        with nc.Block() as blk:
            pe_rec = {}
            ve0 = k.ve.n

            @blk.tensor
            def _(t):
                for n in range(2 * NT):
                    i, which = n // 2, n % 2
                    src = (k.oh1 if which == 0 else k.oh2)[:, i * NE:(i + 1) * NE]
                    if n >= 1:
                        t.wait_ge(k.ve.sem, ve0 + n)
                    k.pe.inc(t.transpose(p8t[:, :], src, k.idf[:, :]))
                    pe_rec[n] = k.pe.n

            @blk.vector
            def _(v):
                for n in range(2 * NT):
                    i, which = n // 2, n % 2
                    v.wait_ge(k.pe.sem, pe_rec[n])
                    V(k, v, lambda i=i, which=which: v.tensor_copy(
                        OHp[0:NE, :].rearrange("p (a b) -> p a b", b=2)
                        [:, i * 128:(i + 1) * 128, which:which + 1],
                        p8t[0:NE, :].unsqueeze(2)))

        if getattr(k, "p8stop", 99) <= 3:
            dsc.close(); loc.close(); return

        with nc.Block() as blk:
            ve_rec = {}
            pe0 = k.pe.n

            @blk.vector
            def _(v):
                V(k, v, lambda: v.tensor_tensor_scan(
                    OHc[:, :], OHp[:, :], OHp[:, :], 0.0,
                    op0=ALU.add, op1=ALU.bypass))
                V(k, v, lambda: v.tensor_scalar(
                    OHc[:, :], OHc[:, :], base8[:, 0:1], None, op0=ALU.add))
                V(k, v, lambda: v.tensor_mul(OHc[:, :], OHc[:, :], OHp[:, :]))
                ve_rec[0] = k.ve.n
                for ch in range(NCH // 512):
                    v.wait_ge(k.pe.sem, pe0 + 2 * ch + 2)
                    V(k, v, lambda ch=ch: v.tensor_copy(
                        posr[:1, ch * 512:(ch + 1) * 512], p8pos[:1, :]))
                    V(k, v, lambda ch=ch: v.tensor_copy(
                        fer[:1, ch * 512:(ch + 1) * 512], p8fe[:1, :]))
                V(k, v, lambda: v.tensor_scalar(
                    vmr[:1, :], posr[:1, :], float(CAP - 1), None, op0=ALU.is_gt))
                V(k, v, lambda: v.tensor_scalar(
                    vmr[:1, :], vmr[:1, :], SENT, None, op0=ALU.mult))
                V(k, v, lambda: v.scalar_tensor_tensor(
                    so2[:1, :], fer[:1, :], float(CAP), posr[:1, :],
                    op0=ALU.mult, op1=ALU.add))
                V(k, v, lambda: v.tensor_add(so2[:1, :], so2[:1, :], vmr[:1, :]))
                V(k, v, lambda: v.memset(sentb[:, :], SENT))

            @blk.tensor
            def _(t):
                wt(t, (k.ve.sem, ve_rec[0]))
                for ch in range(NCH // 512):
                    if ch >= 1:
                        t.wait_ge(k.ve.sem, ve_rec[0] + 2 * ch)
                    k.pe.inc(t.matmul(p8pos[:, :], k.onesf[0:NE, 0:1],
                                      OHc[:, ch * 512:(ch + 1) * 512],
                                      start=True, stop=True))
                    k.pe.inc(t.matmul(p8fe[:, :], iotac[:, 0:1],
                                      OHp[:, ch * 512:(ch + 1) * 512],
                                      start=True, stop=True))

        if getattr(k, "p8stop", 99) <= 4:
            dsc.close(); loc.close(); return

        with nc.Block() as blk:
            pe_rec = {}
            ve0 = k.ve.n

            @blk.tensor
            def _(t):
                n = 0
                for which in range(2):
                    for gch in range(16):
                        if n >= 1:
                            t.wait_ge(k.ve.sem, ve0 + n + (1 if n >= 17 else 0))
                        ap = so2[:1, :].rearrange("p (g c w) -> p g c w", c=128, w=2)\
                            [:, gch, :, which]
                        k.pe.inc(t.transpose(p8c[:, :], ap, k.idf[:1, :1]))
                        pe_rec[n] = k.pe.n
                        n += 1

            @blk.vector
            def _(v):
                n = 0
                for which in range(2):
                    dst = sepm if which == 0 else sopm
                    for gch in range(16):
                        v.wait_ge(k.pe.sem, pe_rec[n])
                        V(k, v, lambda dst=dst, gch=gch: v.tensor_copy(
                            smf[:, gch:gch + 1], p8c[:, :]))
                        n += 1
                    V(k, v, lambda dst=dst: v.tensor_copy(dst[:, :], smf[:, 0:16]))

        if getattr(k, "p8stop", 99) <= 5:
            if k.dump is not None:
                k.dbg_col = 0
                dmp(k, lambda: posr[:1, 0:512], rows=1)
                dmp(k, lambda: fer[:1, 0:512], rows=1)
                dmp(k, lambda: OHp[0:NE, 0:512], rows=NE)
                dmp(k, lambda: OHc[0:NE, 0:256], rows=NE)
                dmp(k, lambda: cntT[0:NE, 0:NE], rows=NE)
                dmp(k, lambda: base8[0:NE, 0:1], rows=NE)

            dsc.close(); loc.close(); return

        with nc.Block() as blk:
            vem = k.ve.mark()

            @blk.gpsimd
            def _(g):
                wt(g, vem)
                breg = g.alloc_register("bnd8")
                g.reg_mov(breg, NC * CAP - 1)
                for rep in range(4):
                    k.gd.inc(g.dma_start(
                        bass.AP(d["cm_in"], rep * 8192, [[64, 128], [1, 64]]),
                        sentb[:, :]))
                g.wait_ge(k.gd.sem, k.gd.n)
                for which, idx in ((0, sepm), (1, sopm)):
                    for gch in range(16):
                        k.gd.inc(g.indirect_dma_start(
                            d["cm_in"].ap(),
                            bass.IndirectOffsetOnAxis(idx[:, gch:gch + 1], 1),
                            k.tokpm[:, gch:gch + 1], None,
                            bounds_check=breg, oob_is_err=False))
                g.wait_ge(k.gd.sem, k.gd.n)
                k.armn = coll(k, g, "AllReduce", ALU.min, d["cm_in"], d["cm_out1"])

        def ownersel(cid):
            with nc.Block() as blk:
                @blk.sync
                def _(s, cid=cid):
                    wt(s, k.armn)
                    k.hd.inc(s.dma_start(
                        cmhl[:1, :],
                        bass.AP(d["cm_out1"], cid * CAP, [[1, 2048]])))
                    s.wait_ge(k.hd.sem, k.hd.n)
                hm = k.hd.mark()
                pe_rec = {}
                ve0 = k.ve.n

                @blk.tensor
                def _(t, cid=cid):
                    wt(t, hm)
                    for gch in range(16):
                        if gch >= 1:
                            t.wait_ge(k.ve.sem, ve0 + gch)
                        k.pe.inc(t.transpose(
                            p8c[:, :], cmhl[:1, gch * 128:(gch + 1) * 128],
                            k.idf[:1, :1]))
                        pe_rec[gch] = k.pe.n

                @blk.vector
                def _(v, cid=cid):
                    for gch in range(16):
                        v.wait_ge(k.pe.sem, pe_rec[gch])
                        V(k, v, lambda gch=gch: v.tensor_copy(
                            smf[:, gch:gch + 1], p8c[:, :]))
        core_switch(k, ownersel)

        def ownersel2(cid):
            with nc.Block() as blk:
                @blk.sync
                def _(s, cid=cid):
                    k.hd.inc(s.dma_start(
                        cmhl[:1, :],
                        bass.AP(d["cm_out1"], cid * CAP + 2048, [[1, 2048]])))
                    s.wait_ge(k.hd.sem, k.hd.n)
                hm = k.hd.mark()
                pe_rec = {}
                ve0 = k.ve.n

                @blk.tensor
                def _(t, cid=cid):
                    wt(t, hm)
                    for gch in range(16):
                        if gch >= 1:
                            t.wait_ge(k.ve.sem, ve0 + gch)
                        k.pe.inc(t.transpose(
                            p8c[:, :], cmhl[:1, gch * 128:(gch + 1) * 128],
                            k.idf[:1, :1]))
                        pe_rec[gch] = k.pe.n

                @blk.vector
                def _(v, cid=cid):
                    for gch in range(16):
                        v.wait_ge(k.pe.sem, pe_rec[gch])
                        V(k, v, lambda gch=gch: v.tensor_copy(
                            smf[:, 16 + gch:16 + gch + 1], p8c[:, :]))
        core_switch(k, ownersel2)

        with nc.Block() as blk:
            @blk.vector
            def _(v):
                V(k, v, lambda: v.tensor_copy(smu[:, :], smf[:, :]))
    dsc.close()

    xsc = contextlib.ExitStack()
    exp_xT = k.salloc(xsc, "exp_xT", (128, NJ * CAP), BF16)
    esc = contextlib.ExitStack()
    exprow = k.salloc(esc, "exprow", (128, E), BF16)
    with nc.psum_tensor("p8x", [128, E], BF16) as p8x:
        with nc.Block() as blk:
            gsw = [k.gsp[0], k.gsp[1]]
            gsw0 = [gsw[0].n, gsw[1].n]
            ve0 = k.ve.n
            pe_rec = {}
            pe_rec_pre = k.pe.n

            @blk.vector
            def _(v):
                for g_ in range(32):
                    if g_ >= 1:
                        v.wait_ge(k.pe.sem, pe_rec_pre + 8 * g_)
                    V(k, v, lambda: v.memset(exprow[:, :], 0.0))
                    v.wait_ge(k.pe.sem, pe_rec_pre + 8 * (g_ + 1))
                    V(k, v, lambda g_=g_: v.tensor_copy(
                        exp_xT[:, :].rearrange("p (j t) -> p j t", t=CAP)
                        [:, :, g_ * 128:(g_ + 1) * 128],
                        p8x[:, :].rearrange("p (j t) -> p j t", t=128)))

            @blk.gpsimd
            def _(g):
                wt(g, k.agxn1)
                breg = g.alloc_register("bnd9")
                g.reg_mov(breg, NC * TOKC - 1)
                for g_ in range(32):
                    g.wait_ge(k.ve.sem, ve0 + 2 * g_ + 1)
                    gsw[g_ % 2].inc(g.indirect_dma_start(
                        exprow[:, :], None,
                        bass.AP(d["xn1_out"], 0, [[E, NC * TOKC], [1, E]]),
                        bass.IndirectOffsetOnAxis(smu[:, g_:g_ + 1], 0),
                        bounds_check=breg, oob_is_err=False))

            @blk.tensor
            def _(t):
                for g_ in range(32):
                    t.wait_ge(gsw[g_ % 2].sem, gsw0[g_ % 2] + 16 * (g_ // 2 + 1))
                    if g_ >= 1:
                        t.wait_ge(k.ve.sem, ve0 + 2 * g_)
                    for j in range(NJ):
                        k.pe.inc(t.transpose(
                            p8x[:, j * 128:(j + 1) * 128],
                            exprow[:, j * 128:(j + 1) * 128], k.idb[:, :]))
                    pe_rec[g_] = k.pe.n
    esc.close()

    fsc = contextlib.ExitStack()
    wste = k.salloc(fsc, "wste", (128, 2 * 12 * 128), BF16)
    h1sb = k.salloc(fsc, "h1sb", (128, 24 * 512), BF16)
    asil = k.salloc(fsc, "asil", (128, 512), BF16)
    hp = k.salloc(fsc, "hp", (128, 12 * 512), BF16)
    outsb = k.salloc(fsc, "outsb", (128, 2 * 512), BF16)
    eostg = k.salloc(fsc, "eostg", (128, 4 * E), BF16)
    with (
        nc.psum_tensor("p8h0", [128, 512], F32) as ph0,
        nc.psum_tensor("p8h1", [128, 512], F32) as ph1,
        nc.psum_tensor("p8e", [128, 128], BF16) as p8e,
    ):
        phs = [ph0, ph1]
        for st_ in range(8):
            ssl = slice(st_ * 512, (st_ + 1) * 512)
            with nc.Block() as blk:
                gsw = [k.gsp[2], k.gsp[3]]
                gsw0 = [gsw[0].n, gsw[1].n]
                ae0 = k.ae.n
                pe_rec = {}
                pe_pre = k.pe.n

                @blk.gpsimd
                def _(g):
                    for fc in range(24):
                        if fc >= 2:
                            g.wait_ge(k.pe.sem, pe_pre + 8 * (fc - 1))
                        src = bass.AP(d["cfc"], fc * 128,
                                      [[DFF, 128], [128 * DFF, NJ], [1, 128]])
                        gsw[fc % 2].inc(g.dma_start(
                            wste[:, (fc % 2) * 1024:(fc % 2 + 1) * 1024], src))

                @blk.tensor
                def _(t):
                    for fc in range(24):
                        t.wait_ge(gsw[fc % 2].sem, gsw0[fc % 2] + 16 * (fc // 2 + 1))
                        if fc >= 2:
                            t.wait_ge(k.ae.sem, ae0 + fc - 1)
                        for j in range(NJ):
                            k.pe.inc(t.matmul(
                                phs[fc % 2][:, :],
                                wste[:, (fc % 2) * 1024 + j * 128:
                                     (fc % 2) * 1024 + (j + 1) * 128],
                                exp_xT[:, j * CAP + st_ * 512:j * CAP + (st_ + 1) * 512],
                                start=(j == 0), stop=(j == NJ - 1)))
                        pe_rec[fc] = k.pe.n

                @blk.scalar
                def _(s):
                    for fc in range(24):
                        s.wait_ge(k.pe.sem, pe_rec[fc])
                        A(k, s, lambda fc=fc: s.copy(
                            h1sb[:, fc * 512:(fc + 1) * 512], phs[fc % 2][:, :]))

            with nc.Block() as blk:
                ae0 = k.ae.n
                ve_pre = k.ve.n

                @blk.scalar
                def _(s):
                    for fc in range(12):
                        if fc >= 1:
                            s.wait_ge(k.ve.sem, ve_pre + fc)
                        A(k, s, lambda fc=fc: s.activation(
                            asil[:, :], h1sb[:, fc * 512:(fc + 1) * 512], AF.Silu))

                @blk.vector
                def _(v):
                    for fc in range(12):
                        v.wait_ge(k.ae.sem, ae0 + fc + 1)
                        V(k, v, lambda fc=fc: v.tensor_mul(
                            hp[:, fc * 512:(fc + 1) * 512], asil[:, :],
                            h1sb[:, (12 + fc) * 512:(13 + fc) * 512]))

            with nc.Block() as blk:
                gsw = [k.gsp[2], k.gsp[3]]
                gsw0 = [gsw[0].n, gsw[1].n]
                ae0 = k.ae.n
                ve0 = k.ve.n
                pe_rec = {}
                petp = {}
                pe_pre = k.pe.n
                hdm0 = k.hd.mark()

                @blk.gpsimd
                def _(g):
                    for jE in range(NJ):
                        if jE >= 2:
                            g.wait_ge(k.pe.sem, pe_pre + 16 * (jE - 1))
                        src = bass.AP(d["cpj"], jE * 128,
                                      [[E, 128], [128 * E, 12], [1, 128]])
                        gsw[jE % 2].inc(g.dma_start(
                            wste[:, (jE % 2) * 1536:(jE % 2) * 1536 + 1536], src))

                @blk.tensor
                def _(t):
                    for jE in range(NJ):
                        t.wait_ge(gsw[jE % 2].sem, gsw0[jE % 2] + 16 * (jE // 2 + 1))
                        if jE >= 2:
                            t.wait_ge(k.ae.sem, ae0 + 3 * (jE - 2) + 1)
                        for fc in range(12):
                            k.pe.inc(t.matmul(
                                phs[jE % 2][:, :],
                                wste[:, (jE % 2) * 1536 + fc * 128:
                                     (jE % 2) * 1536 + (fc + 1) * 128],
                                hp[:, fc * 512:(fc + 1) * 512],
                                start=(fc == 0), stop=(fc == 11)))
                        pe_rec[jE] = k.pe.n
                        t.wait_ge(k.ae.sem, ae0 + 3 * jE + 1)
                        for tt in range(4):
                            gidx = jE * 4 + tt
                            if gidx >= 1:
                                pg = gidx - 1
                                pj, ptt = pg // 4, pg % 4
                                if ptt % 2 == 0:
                                    t.wait_ge(k.ae.sem, ae0 + 3 * pj + 1 + ptt // 2 + 1)
                                else:
                                    t.wait_ge(k.ve.sem, ve0 + 2 * pj + (ptt - 1) // 2 + 1)
                            k.pe.inc(t.transpose(
                                p8e[:, :],
                                outsb[:, (jE % 2) * 512 + tt * 128:
                                      (jE % 2) * 512 + (tt + 1) * 128],
                                k.idb[:, :]))
                            petp[(jE, tt)] = k.pe.n

                @blk.scalar
                def _(s):
                    wt(s, hdm0)
                    for jE in range(NJ):
                        s.wait_ge(k.pe.sem, pe_rec[jE])
                        A(k, s, lambda jE=jE: s.copy(
                            outsb[:, (jE % 2) * 512:(jE % 2 + 1) * 512],
                            phs[jE % 2][:, :]))
                        for tt in (0, 2):
                            s.wait_ge(k.pe.sem, petp[(jE, tt)])
                            A(k, s, lambda jE=jE, tt=tt: s.copy(
                                eostg[:, tt * E + jE * 128:tt * E + (jE + 1) * 128],
                                p8e[:, :]))

                @blk.vector
                def _(v):
                    wt(v, hdm0)
                    for jE in range(NJ):
                        for tt in (1, 3):
                            v.wait_ge(k.pe.sem, petp[(jE, tt)])
                            V(k, v, lambda jE=jE, tt=tt: v.tensor_copy(
                                eostg[:, tt * E + jE * 128:tt * E + (jE + 1) * 128],
                                p8e[:, :]))
                vem = k.ve.mark()
                aem = k.ae.mark()

                @blk.sync
                def _(s):
                    wt(s, vem)
                    wt(s, aem)
                    for tt in range(4):
                        k.hd.inc(s.dma_start(
                            d["eo_in"].ap()[st_ * 512 + tt * 128:
                                            st_ * 512 + (tt + 1) * 128, :],
                            eostg[:, tt * E:(tt + 1) * E]))
    fsc.close()
    xsc.close()

    with nc.Block() as blk:
        @blk.gpsimd
        def _(g):
            g.wait_ge(k.hd.sem, k.hd.n)
            k.ageo = coll(k, g, "AllGather", ALU.bypass, d["eo_in"],
                          d["eo_out"])

    csc = contextlib.ExitStack()
    geb = k.salloc(csc, "geb", (128, E), BF16)
    gob = k.salloc(csc, "gob", (128, E), BF16)
    tmpf = k.salloc(csc, "tmpf", (128, E))
    moeb = k.salloc(csc, "moeb", (128, E))
    ysb = k.salloc(csc, "ysb", (128, 2 * E))
    with nc.psum_tensor("p8y", [128, E], F32) as p8y:
        with nc.Block() as blk:
            gsw = [k.gsp[0], k.gsp[1]]
            gsw0 = [gsw[0].n, gsw[1].n]
            yst = [k.sp[0], k.sp[1]]
            yst0 = [yst[0].n, yst[1].n]
            ve0 = k.ve.n
            pe_rec = {}
            pe_pre = k.pe.n

            @blk.vector
            def _(v):
                for g_ in range(NT):
                    if g_ >= 1:
                        v.wait_ge(gsw[0].sem, gsw0[0] + 16 * g_)
                        v.wait_ge(gsw[1].sem, gsw0[1] + 16 * g_)
                    V(k, v, lambda: v.memset(geb[:, :], 0.0))
                    V(k, v, lambda: v.memset(gob[:, :], 0.0))
                    v.wait_ge(gsw[0].sem, gsw0[0] + 16 * (g_ + 1))
                    v.wait_ge(gsw[1].sem, gsw0[1] + 16 * (g_ + 1))
                    V(k, v, lambda g_=g_: v.tensor_scalar(
                        tmpf[:, :], geb[:, :], k.rtrw[:, 2 * g_:2 * g_ + 1], None,
                        op0=ALU.mult))
                    V(k, v, lambda g_=g_: v.scalar_tensor_tensor(
                        moeb[:, :], gob[:, :], k.rtrw[:, 2 * g_ + 1:2 * g_ + 2],
                        tmpf[:, :], op0=ALU.mult, op1=ALU.add))
                    v.wait_ge(k.pe.sem, pe_pre + 8 * (g_ + 1))
                    if g_ >= 2:
                        v.wait_ge(yst[g_ % 2].sem, yst0[g_ % 2] + 16 * (g_ // 2))
                    V(k, v, lambda g_=g_: v.tensor_add(
                        ysb[:, (g_ % 2) * E:(g_ % 2 + 1) * E], p8y[:, :], moeb[:, :]))

            @blk.gpsimd
            def _(g):
                wt(g, k.ageo)
                breg = g.alloc_register("bnd10")
                g.reg_mov(breg, NC * CAP - 1)
                for g_ in range(NT):
                    g.wait_ge(k.ve.sem, ve0 + 5 * g_ + 2)
                    gsw[0].inc(g.indirect_dma_start(
                        geb[:, :], None,
                        bass.AP(d["eo_out"], 0, [[E, NC * CAP], [1, E]]),
                        bass.IndirectOffsetOnAxis(sepm[:, g_:g_ + 1], 0),
                        bounds_check=breg, oob_is_err=False))
                    gsw[1].inc(g.indirect_dma_start(
                        gob[:, :], None,
                        bass.AP(d["eo_out"], 0, [[E, NC * CAP], [1, E]]),
                        bass.IndirectOffsetOnAxis(sopm[:, g_:g_ + 1], 0),
                        bounds_check=breg, oob_is_err=False))

            @blk.tensor
            def _(t):
                for g_ in range(NT):
                    if g_ >= 1:
                        t.wait_ge(k.ve.sem, ve0 + 5 * g_)
                    for j in range(NJ):
                        k.pe.inc(t.transpose(
                            p8y[:, j * 128:(j + 1) * 128],
                            k.xT[:, j * TOKC + g_ * 128:j * TOKC + (g_ + 1) * 128],
                            k.idf[:, :]))

            @blk.sync
            def _(s):
                for g_ in range(NT):
                    s.wait_ge(k.ve.sem, ve0 + 5 * (g_ + 1))
                    yst[g_ % 2].inc(s.dma_start(
                        d["y"].ap()[g_ * 128:(g_ + 1) * 128, :],
                        ysb[:, (g_ % 2) * E:(g_ % 2 + 1) * E]))
    csc.close()
    loc.close()


def _consts(c):
    ident = np.eye(128, dtype=np.float32)
    m0 = np.zeros((128, 4 * 512), np.float32)
    kp = np.arange(128)[:, None]
    qq = np.arange(512)[None, :]
    for vvar in range(4):
        m0[:, vvar * 512:(vvar + 1) * 512] = (kp <= qq - vvar * 128)
    qq1 = np.arange(C1)[:, None]
    kk1 = np.arange(C1)[None, :]
    m1 = np.broadcast_to(
        np.where(kk1 > qq1, np.float32(-1e30), np.float32(0.0)).reshape(1, -1),
        (128, C1 * C1)).copy()
    iota8 = np.broadcast_to(np.arange(NE, dtype=np.float32), (128, NE)).copy()
    ch = np.arange(128)[:, None] + 128 * np.arange(16)[None, :]
    tokid = (2048 * c + ch).astype(np.float32)
    return {
        "identf": ident,
        "identb": ident.astype(BFD),
        "maskL0": m0.astype(BFD),
        "maskL1": m1.astype(BFD),
        "iota8": iota8,
        "tokid_pm": tokid,
        "onesf": np.ones((128, 128), np.float32),
    }


def make_inmaps(inputs):
    f = {n: np.ascontiguousarray(np.asarray(v, dtype=np.float32))
         for n, v in inputs.items()}
    x = f["L0_attn_x"].reshape(B * S, E)
    gw1T = np.ascontiguousarray(f["L0_ffn_gate_w1"].T)
    gw2T = np.ascontiguousarray(f["L0_ffn_gate_w2"].T)
    in_maps = []
    for c in range(NC):
        fsl = slice(c * FSL, (c + 1) * FSL)
        hsl = slice(c * HSL, (c + 1) * HSL)
        m = {
            "x": np.ascontiguousarray(x[c * TOKC:(c + 1) * TOKC]),
            "wq0": f["L0_attn_w_q"], "wk0": f["L0_attn_w_k"],
            "wv1": f["L0_attn_w_v1"], "wv2": f["L0_attn_w_v2"],
            "gw1T": gw1T, "gb1": f["L0_ffn_gate_b1"].reshape(1, E),
            "gw2T": gw2T, "gb2": f["L0_ffn_gate_b2"].reshape(1, NE),
            "W1T_s": np.ascontiguousarray(f["L0_ffn_W1"][fsl].T),
            "TW1T_s": np.ascontiguousarray(
                f["L0_ffn_TW1"][:, fsl, :].transpose(0, 2, 1)),
            "W2T_s": np.ascontiguousarray(f["L0_ffn_W2"][hsl].T),
            "TW2T_s": np.ascontiguousarray(
                f["L0_ffn_TW2"][:, hsl, :].transpose(0, 2, 1)),
            "b1": f["L0_ffn_b1"].reshape(1, F0), "Tb1": f["L0_ffn_Tb1"],
            "b2": f["L0_ffn_b2"].reshape(1, E), "Tb2": f["L0_ffn_Tb2"],
            "Wq1": f["L1_attn_Wq"], "Wk1": f["L1_attn_Wk"],
            "Wv1_": f["L1_attn_Wv"], "Wo1": f["L1_attn_Wo"],
            "bq": f["L1_attn_bq"].reshape(1, E), "bk": f["L1_attn_bk"].reshape(1, E),
            "bv": f["L1_attn_bv"].reshape(1, E), "bo": f["L1_attn_bo"].reshape(1, E),
            "wgate": f["L1_ffn_w_gate"],
            "cfc": np.ascontiguousarray(f["L1_ffn_c_fc"][c]),
            "cpj": np.ascontiguousarray(f["L1_ffn_c_proj"][c]),
        }
        m.update(_consts(c))
        in_maps.append(m)
    return in_maps


def kernel(**inputs):
    nc, _ = build_nc()
    in_maps = make_inmaps(inputs)
    res = run_bass_kernel_spmd(nc, in_maps, core_ids=list(range(NC)))
    outs = [res.results[c]["y"] for c in range(NC)]
    return np.concatenate(outs, axis=0).reshape(B, S, E).astype(np.float32)


# version 14
# speedup vs baseline: 1.0497x; 1.0016x over previous
import contextlib
import numpy as np
import ml_dtypes
import concourse.bass as bass
import concourse.mybir as mybir
from concourse.bass_utils import run_bass_kernel_spmd

F32 = mybir.dt.float32
BF16 = mybir.dt.bfloat16
U32 = mybir.dt.uint32
AF = mybir.ActivationFunctionType
ALU = mybir.AluOpType
AX = mybir.AxisListType

NC = 8
B = 2
S = 8192
E = 1024
TOKC = 2048
NT = TOKC // 128
NJ = E // 128
KD = 64
F0 = 4096
FSL = F0 // NC
HSL = E // NC
H1 = 16
C1 = 64
NE = 8
CAP = 4096
DFF = 3072
NCH = 2 * TOKC
KVN = KD * TOKC + TOKC * 65
MW1N = E * FSL
MW2N = F0 * HSL
MWN = 2 * MW1N + 2 * MW2N
EPS = 1e-5
SENT = 1.0e9
BFD = ml_dtypes.bfloat16


class Ctr:
    def __init__(self, nc, name, step=1):
        self.sem = nc.alloc_semaphore(name)
        self.n = 0
        self.step = step

    def inc(self, inst, times=1):
        inst.then_inc(self.sem, self.step * times)
        self.n += self.step * times
        return self.n

    def mark(self):
        return (self.sem, self.n)


def wt(eng, mark):
    sem, val = mark
    if val > 0:
        eng.wait_ge(sem, val)


class K:
    pass


def V(k, v, fn):
    if k.ve.n:
        v.wait_ge(k.ve.sem, k.ve.n)
    return k.ve.inc(fn())


def Vf(k, fn):
    return k.ve.inc(fn())


def Vbar(k, v):
    if k.ve.n:
        v.wait_ge(k.ve.sem, k.ve.n)


def A(k, s, fn):
    if k.ae.n:
        s.wait_ge(k.ae.sem, k.ae.n)
    return k.ae.inc(fn())


def G(k, g, fn):
    if k.ge.n:
        g.wait_ge(k.ge.sem, k.ge.n)
    return k.ge.inc(fn())


def build_nc(dump=None, upto=99):
    nc = bass.Bass("TRN2", target_bir_lowering=False)
    k = K()
    k.nc = nc
    k.dump = dump
    k.upto = upto
    d = {}
    k.d = d

    def din(name, shape, dt=F32):
        d[name] = nc.dram_tensor(name, list(shape), dt, kind="ExternalInput")

    def dint(name, shape, dt=F32, shared=False):
        if shared:
            d[name] = nc.dram_tensor(name, list(shape), dt, kind="Internal",
                                     addr_space="Shared")
        else:
            d[name] = nc.dram_tensor(name, list(shape), dt, kind="Internal")

    din("x", (TOKC, E))
    din("wq0", (E, KD)); din("wk0", (E, KD)); din("wv1", (E, KD)); din("wv2", (KD, E))
    din("gw1T", (E, E)); din("gb1", (1, E)); din("gw2T", (E, NE)); din("gb2", (1, NE))
    din("W1T_s", (E, FSL)); din("TW1T_s", (NE, E, FSL))
    din("W2T_s", (F0, HSL)); din("TW2T_s", (NE, F0, HSL))
    din("b1", (1, F0)); din("Tb1", (NE, F0)); din("b2", (1, E)); din("Tb2", (NE, E))
    din("Wq1", (E, E)); din("Wk1", (E, E)); din("Wv1_", (E, E)); din("Wo1", (E, E))
    din("bq", (1, E)); din("bk", (1, E)); din("bv", (1, E)); din("bo", (1, E))
    din("wgate", (E, NE))
    din("cfc", (E, DFF)); din("cpj", (DFF // 2, E))
    din("identf", (128, 128)); din("identb", (128, 128), BF16)
    din("maskL0", (128, 4 * 512), BF16)
    din("maskL1", (128, C1 * C1), BF16)
    din("iota8", (128, NE))
    din("tokid_pm", (128, 16))
    din("onesf", (128, 128))
    d["y"] = nc.dram_tensor("y", [TOKC, E], F32, kind="ExternalOutput")
    if dump is not None:
        d["dbg"] = nc.dram_tensor("dbg", [128, 2048], F32, kind="ExternalOutput")

    dint("ar_ph_in", (B, 128, NJ)); dint("ar_ph_out", (B, 128, NJ))
    dint("kv_in", (1, KVN), BF16); dint("kv_out", (NC, KVN), BF16, shared=True)
    dint("mw_in", (1, MWN), BF16); dint("mw_out", (NC, MWN), BF16, shared=True)
    dint("mw1t_mine", (E, F0), BF16)
    dint("mw2t_mine", (F0, E), BF16)
    dint("u_dram", (F0 // 128, 128, TOKC), BF16)
    dint("xn1_in", (TOKC, E), BF16); dint("xn1_out", (NC * TOKC, E), BF16, shared=True)
    dint("cnt_in", (1, NE)); dint("cnt_out", (NC, NE))
    dint("cm_in", (1, NC * CAP)); dint("cm_out1", (1, NC * CAP))
    dint("eo_in", (CAP, E), BF16); dint("eo_out", (NC * CAP, E), BF16, shared=True)

    k.gd = Ctr(nc, "gdma", 16)
    k.hd = Ctr(nc, "hdma", 16)
    k.ve = Ctr(nc, "vsem")
    k.ae = Ctr(nc, "asem")
    k.ge = Ctr(nc, "gsem")
    k.pe = Ctr(nc, "pesem")
    k.cc = [Ctr(nc, "cc%d" % i) for i in range(10)]
    k.ncc = 0
    k.sp = [Ctr(nc, "sp%d" % i, 16) for i in range(10)]
    k.gsp = [Ctr(nc, "gsp%d" % i, 16) for i in range(4)]
    k.all_ctrs = [k.gd, k.hd, k.ve, k.ae, k.ge, k.pe] + k.cc + k.sp + k.gsp

    k.sbn = {}

    def salloc(stack, name, shape, dt=F32):
        t = stack.enter_context(nc.sbuf_tensor(name, list(shape), dt))
        k.sbn[name] = t
        return t

    k.salloc = salloc
    glob = contextlib.ExitStack()
    k.glob = glob
    k.xT = salloc(glob, "xT", (128, NJ * TOKC))
    k.idf = salloc(glob, "idf", (128, 128))
    k.idb = salloc(glob, "idb", (128, 128), BF16)
    k.onesf = salloc(glob, "ones_sb", (128, 128))
    k.pmv = salloc(glob, "pmv", (128, 128))
    k.PM = {"gb1": 0, "b1": 8, "b2": 40, "bq": 48, "bk": 56, "bv": 64,
            "bo": 72, "mb1": 80, "mb2": 112}
    k.rtrw = salloc(glob, "rtrw", (128, NT * 2))
    k.iota8_row = salloc(glob, "iota8r", (1, NE))
    k.rtri = salloc(glob, "rtri", (128, NT * 2))
    k.oh1 = salloc(glob, "oh1", (128, NT * NE))
    k.oh2 = salloc(glob, "oh2", (128, NT * NE))
    if dump is not None:
        k.dbg_sb = salloc(glob, "dbg_sb", (128, 128))
        k.dbg_col = 0

    k.engines = [nc.gpsimd, nc.scalar, nc.tensor, nc.vector, nc.sync]
    k.pid = [e.partition_id() for e in k.engines]

    span17 = contextlib.ExitStack()
    k.xnT = salloc(span17, "xnT", (128, NJ * TOKC), BF16)
    span35 = contextlib.ExitStack()
    k.qp = salloc(span35, "qp", (KD, TOKC), BF16)
    k.wv2f = salloc(span35, "wv2f", (KD, E))
    k.mask0 = salloc(span35, "mask0", (128, 4 * 512), BF16)

    p1_ln(k)
    if upto >= 3:
        p3_proj(k)
    if upto >= 4:
        p4_gmw(k)
    if upto >= 5:
        p5_attn0(k)
    span35.close()
    if upto >= 6:
        p6_ffn(k)
    if upto >= 7:
        p7_l1(k)
    if hasattr(k, 'p7loc'):
        k.p7loc.close()
    span17.close()
    if upto >= 8:
        p8_moe(k)
    finish(k)
    return nc, k


def core_switch(k, body_fn):
    snap = [c.n for c in k.all_ctrs]
    final = None
    for cid in k.nc.Switch(engines=k.engines, index=k.pid, n=NC):
        for c, v in zip(k.all_ctrs, snap):
            c.n = v
        body_fn(cid)
        now = [c.n for c in k.all_ctrs]
        if final is None:
            final = now
        else:
            assert now == final, ("switch bodies diverge in sem counts",
                                  now, final)
    for c, v in zip(k.all_ctrs, final):
        c.n = v


def coll(k, gp, kind, op, src, dst):
    cc = k.cc[k.ncc]
    k.ncc += 1
    sap = src if isinstance(src, bass.AP) else src.ap()
    dap = dst if isinstance(dst, bass.AP) else dst.ap()
    inst = gp.collective_compute(
        kind, op, replica_groups=[list(range(NC))],
        ins=[sap.opt()], outs=[dap.opt()])
    inst.then_inc(cc.sem, 1)
    cc.n += 1
    return cc.mark()


def dmp(k, get_ap, rows=128):
    if k.dump is None:
        return
    nc = k.nc
    src_ap = get_ap()
    ncol = 1
    for dsz in src_ap.shape[1:]:
        ncol *= dsz
    for c0 in range(0, ncol, 128):
        w = min(128, ncol - c0)
        col = k.dbg_col + c0
        sub = src_ap[:, c0:c0 + w] if len(src_ap.shape) == 2 else \
            src_ap.rearrange("p a b -> p (a b)")[:, c0:c0 + w]
        with nc.Block() as blk:
            @blk.vector
            def _(v, sub=sub, w=w):
                V(k, v, lambda: v.tensor_copy(k.dbg_sb[:rows, 0:w], sub))
            mk = k.ve.n

            @blk.sync
            def _(s, w=w, col=col, mk=mk):
                s.wait_ge(k.ve.sem, mk)
                k.hd.inc(s.dma_start(
                    bass.AP(k.d["dbg"], col, [[2048, rows], [1, w]]),
                    k.dbg_sb[:rows, 0:w]))
                s.wait_ge(k.hd.sem, k.hd.n)
    k.dbg_col += ncol


def finish(k):
    nc = k.nc
    d = k.d
    if k.upto < 8:
        ytile = nc.alloc_sbuf_tensor("fin_yt", [128, E], F32)
        with nc.psum_tensor("finp", [128, 128], F32) as ps:
            for i in range(NT):
                with nc.Block() as blk:
                    pe_af = {}

                    @blk.tensor
                    def _(t, i=i):
                        for j in range(NJ):
                            if j >= 1:
                                t.wait_ge(k.ve.sem, k.ve.n + j)
                            k.pe.inc(t.transpose(
                                ps[:, :],
                                k.xT[:, j * TOKC + i * 128:j * TOKC + (i + 1) * 128],
                                k.idf[:, :]))
                            pe_af[j] = k.pe.n

                    ve0 = k.ve.n

                    @blk.vector
                    def _(v, i=i):
                        for j in range(NJ):
                            v.wait_ge(k.pe.sem, pe_af[j])
                            V(k, v, lambda j=j: v.tensor_copy(
                                ytile[:, j * 128:(j + 1) * 128], ps[:, :]))
                    vm = k.ve.mark()

                    @blk.sync
                    def _(s, i=i):
                        wt(s, vm)
                        k.hd.inc(s.dma_start(d["y"].ap()[i * 128:(i + 1) * 128, :],
                                             ytile[:, :]))
                        s.wait_ge(k.hd.sem, k.hd.n)
    with nc.Block() as blk:
        @blk.sync
        def _(s):
            s.wait_ge(k.hd.sem, k.hd.n)
            for c_ in k.sp + k.gsp:
                if c_.n:
                    s.wait_ge(c_.sem, c_.n)

        @blk.gpsimd
        def _(g):
            g.wait_ge(k.gd.sem, k.gd.n)


def p1_ln(k):
    nc = k.nc
    d = k.d
    loc = contextlib.ExitStack()
    xtt = k.salloc(loc, "xtt", (128, 2 * E))
    xnt = k.salloc(loc, "xnt", (128, 2 * E), BF16)
    junk = k.salloc(loc, "p1junk", (128, E), BF16)
    st = k.salloc(loc, "p1st", (128, 8 * NT))

    with nc.Block() as blk:
        @blk.gpsimd
        def _(g):
            k.gd.inc(g.dma_start(k.idf[:, :], d["identf"].ap()))
            k.gd.inc(g.dma_start(k.idb[:, :], d["identb"].ap()))
            k.gd.inc(g.dma_start(k.onesf[:, :], d["onesf"].ap()))
            k.gd.inc(g.dma_start(k.mask0[:, :], d["maskL0"].ap()))
            k.gd.inc(g.dma_start(k.wv2f[:, :], d["wv2"].ap()))

        if k.dump is not None:
            @blk.vector
            def _(v):
                V(k, v, lambda: v.memset(k.dbg_sb[:, :], 0.0))

    with (
        nc.psum_tensor("p1f0", [128, 128], F32) as pf0,
        nc.psum_tensor("p1f1", [128, 128], F32) as pf1,
        nc.psum_tensor("p1b0", [128, 128], BF16) as pb0,
        nc.psum_tensor("p1b1", [128, 128], BF16) as pb1,
    ):
        pfs = [pf0, pf1]
        pbs = [pb0, pb1]
        with nc.Block() as blk:
            xs = [k.sp[0], k.sp[1]]
            xs0 = [xs[0].n, xs[1].n]
            gdm = k.gd.mark()
            ve0 = k.ve.n
            ae0 = k.ae.n
            pe0 = k.pe.n
            DVE_PER = 15
            ACT_PER = 10
            PE_PER = 16

            def ve_at(i, op):
                return ve0 + DVE_PER * i + op

            def ae_at(i, op):
                return ae0 + ACT_PER * i + op

            def pe_at(i, op):
                return pe0 + PE_PER * i + op

            @blk.sync
            def _(s):
                for i in range(NT):
                    if i >= 2:
                        s.wait_ge(k.pe.sem, pe_at(i - 2, 8))
                        s.wait_ge(k.ve.sem, ve_at(i - 2, 7))
                    xs[i % 2].inc(s.dma_start(xtt[:, (i % 2) * E:(i % 2 + 1) * E],
                                              d["x"].ap()[i * 128:(i + 1) * 128, :]))

            @blk.tensor
            def _(t):
                wt(t, gdm)
                for i in range(NT):
                    xt = xtt[:, (i % 2) * E:(i % 2 + 1) * E]
                    xn = xnt[:, (i % 2) * E:(i % 2 + 1) * E]
                    t.wait_ge(xs[i % 2].sem, xs0[i % 2] + 16 * (i // 2 + 1))
                    for j in range(NJ):
                        if i * NJ + j >= 2:
                            pj = (i * NJ + j) - 2
                            pi, pjj = pj // NJ, pj % NJ
                            if pjj % 2 == 0:
                                t.wait_ge(k.ae.sem, ae_at(pi, 2 + pjj // 2 + 1))
                            else:
                                t.wait_ge(k.ve.sem, ve_at(pi, 6 + 1 + (pjj - 1) // 2 + 1))
                        k.pe.inc(t.transpose(pfs[j % 2][:, :],
                                             xt[:, j * 128:(j + 1) * 128],
                                             k.idf[:, :]))
                    t.wait_ge(k.ve.sem, ve_at(i, 7))
                    for j in range(NJ):
                        if i * NJ + j >= 2:
                            pj = (i * NJ + j) - 2
                            pi, pjj = pj // NJ, pj % NJ
                            if pjj % 2 == 0:
                                t.wait_ge(k.ae.sem, ae_at(pi, 6 + pjj // 2 + 1))
                            else:
                                t.wait_ge(k.ve.sem, ve_at(pi, 10 + 1 + (pjj - 1) // 2 + 1))
                        k.pe.inc(t.transpose(pbs[j % 2][:, :],
                                             xn[:, j * 128:(j + 1) * 128],
                                             k.idb[:, :]))

            @blk.vector
            def _(v):
                for i in range(NT):
                    c = 8 * i
                    xt = xtt[:, (i % 2) * E:(i % 2 + 1) * E]
                    xn = xnt[:, (i % 2) * E:(i % 2 + 1) * E]
                    v.wait_ge(xs[i % 2].sem, xs0[i % 2] + 16 * (i // 2 + 1))
                    V(k, v, lambda xt=xt, c=c: v.reduce_sum(st[:, c:c + 1], xt, axis=AX.X))
                    V(k, v, lambda xt=xt, c=c: v.scalar_tensor_tensor(
                        junk[:, :], xt, 1.0, xt, op0=ALU.bypass, op1=ALU.mult,
                        accum_out=st[:, c + 1:c + 2]))
                    V(k, v, lambda c=c: v.tensor_scalar_mul(
                        st[:, c + 2:c + 3], st[:, c:c + 1], 1.0 / E))
                    V(k, v, lambda c=c: v.tensor_scalar_mul(
                        st[:, c + 3:c + 4], st[:, c + 1:c + 2], 1.0 / E))
                    V(k, v, lambda c=c: v.scalar_tensor_tensor(
                        st[:, c + 3:c + 4], st[:, c + 2:c + 3], st[:, c + 2:c + 3],
                        st[:, c + 3:c + 4], op0=ALU.mult, op1=ALU.subtract))
                    V(k, v, lambda c=c: v.tensor_scalar(
                        st[:, c + 4:c + 5], st[:, c + 3:c + 4], -1.0, EPS,
                        op0=ALU.mult, op1=ALU.add))
                    v.wait_ge(k.ae.sem, ae_at(i, 2))
                    V(k, v, lambda xt=xt, xn=xn, c=c: v.tensor_scalar(
                        xn, xt, st[:, c + 2:c + 3], st[:, c + 5:c + 6],
                        op0=ALU.subtract, op1=ALU.mult))
                    for j in range(1, NJ, 2):
                        v.wait_ge(k.pe.sem, pe_at(i, j + 1))
                        V(k, v, lambda i=i, j=j: v.tensor_copy(
                            k.xT[:, j * TOKC + i * 128:j * TOKC + (i + 1) * 128],
                            pfs[j % 2][:, :]))
                    for j in range(1, NJ, 2):
                        v.wait_ge(k.pe.sem, pe_at(i, 8 + j + 1))
                        V(k, v, lambda i=i, j=j: v.tensor_copy(
                            k.xnT[:, j * TOKC + i * 128:j * TOKC + (i + 1) * 128],
                            pbs[j % 2][:, :]))

            @blk.scalar
            def _(s):
                for i in range(NT):
                    c = 8 * i
                    s.wait_ge(k.ve.sem, ve_at(i, 6))
                    A(k, s, lambda c=c: s.activation(st[:, c + 5:c + 6],
                                                     st[:, c + 4:c + 5], AF.Ln))
                    A(k, s, lambda c=c: s.activation(st[:, c + 5:c + 6],
                                                     st[:, c + 5:c + 6], AF.Exp,
                                                     scale=-0.5))
                    for j in range(0, NJ, 2):
                        s.wait_ge(k.pe.sem, pe_at(i, j + 1))
                        A(k, s, lambda i=i, j=j: s.copy(
                            k.xT[:, j * TOKC + i * 128:j * TOKC + (i + 1) * 128],
                            pfs[j % 2][:, :]))
                    for j in range(0, NJ, 2):
                        s.wait_ge(k.pe.sem, pe_at(i, 8 + j + 1))
                        A(k, s, lambda i=i, j=j: s.copy(
                            k.xnT[:, j * TOKC + i * 128:j * TOKC + (i + 1) * 128],
                            pbs[j % 2][:, :]))

    loc.close()
    if k.dump is not None:
        dmp(k, lambda: k.xT[:, 0:256])
        dmp(k, lambda: k.xnT[:, 0:256])


def p3_proj(k):
    nc = k.nc
    d = k.d
    loc = contextlib.ExitStack()
    kown = k.salloc(loc, "kown", (KD, TOKC), BF16)
    vstage = k.salloc(loc, "vstage", (KD, TOKC), BF16)
    v65own = k.salloc(loc, "v65own", (128, NT * 65), BF16)
    wq_sb = k.salloc(loc, "wq_sb", (128, NJ * KD), BF16)
    wk_sb = k.salloc(loc, "wk_sb", (128, NJ * KD), BF16)
    wv_sb = k.salloc(loc, "wv_sb", (128, NJ * KD), BF16)
    gw1T_sb = k.salloc(loc, "gw1T_sb", (128, NJ * E), BF16)
    relu_junk = k.salloc(loc, "relu_junk", (128, 512), BF16)
    phacc = k.salloc(loc, "phacc", (128, NJ * 4))
    zrow = k.salloc(loc, "zrow8", (128, NJ))
    vrow = k.salloc(loc, "vrow", (1, F0))
    k.ph = k.salloc(loc, "ph", (128, NJ))

    with nc.psum_tensor("pmps", [128, 32], F32) as pmps:
        for name, n, col in [("gb1", E, k.PM["gb1"]), ("b1", F0, k.PM["b1"]),
                             ("b2", E, k.PM["b2"]), ("bq", E, k.PM["bq"]),
                             ("bk", E, k.PM["bk"]), ("bv", E, k.PM["bv"]),
                             ("bo", E, k.PM["bo"])]:
            with nc.Block() as blk:
                @blk.sync
                def _(s, name=name, n=n):
                    k.hd.inc(s.dma_start(vrow[:1, 0:n], d[name].ap()))
                hm = k.hd.mark()

                @blk.tensor
                def _(t, n=n):
                    wt(t, hm)
                    for ch in range(n // 128):
                        k.pe.inc(t.transpose(pmps[:, ch:ch + 1],
                                             vrow[:1, ch * 128:(ch + 1) * 128],
                                             k.idf[:1, :1]))
                pm = k.pe.mark()

                @blk.vector
                def _(v, n=n, col=col):
                    wt(v, pm)
                    V(k, v, lambda: v.tensor_copy(k.pmv[:, col:col + n // 128],
                                                  pmps[:, 0:n // 128]))

    with nc.Block() as blk:
        @blk.gpsimd
        def _(g):
            for nm, dst in [("wq0", wq_sb), ("wk0", wk_sb), ("wv1", wv_sb)]:
                src = bass.AP(d[nm], 0, [[KD, 128], [128 * KD, NJ], [1, KD]])
                k.gd.inc(g.dma_start(dst[:, :], src))
            src = bass.AP(d["gw1T"], 0, [[E, 128], [128 * E, NJ], [1, E]])
            k.gd.inc(g.dma_start(gw1T_sb[:, :], src))

    with (
        nc.psum_tensor("p3q", [KD, 512], F32) as psq,
        nc.psum_tensor("p3k", [KD, 512], F32) as psk,
        nc.psum_tensor("p3v", [KD, 512], F32) as psv,
    ):
        with nc.Block() as blk:
            ae0 = k.ae.n
            pe_mark = {}
            gdm3 = k.gd.mark()

            @blk.tensor
            def _(t):
                wt(t, gdm3)
                for ts in range(4):
                    if ts > 0:
                        t.wait_ge(k.ae.sem, ae0 + 3 * ts)
                    for pi, (ps, wsb) in enumerate([(psq, wq_sb), (psk, wk_sb),
                                                    (psv, wv_sb)]):
                        for j in range(NJ):
                            k.pe.inc(t.matmul(
                                ps[:, :], wsb[:, j * KD:(j + 1) * KD],
                                k.xnT[:, j * TOKC + ts * 512:j * TOKC + (ts + 1) * 512],
                                start=(j == 0), stop=(j == NJ - 1)))
                        pe_mark[(pi, ts)] = k.pe.n

            @blk.scalar
            def _(s):
                for ts in range(4):
                    for pi, (ps, dst, sc) in enumerate([
                            (psq, k.qp, 0.125), (psk, kown, 1.0),
                            (psv, vstage, 1.0)]):
                        s.wait_ge(k.pe.sem, pe_mark[(pi, ts)])
                        A(k, s, lambda ps=ps, dst=dst, sc=sc, ts=ts: s.activation(
                            dst[:, ts * 512:(ts + 1) * 512], ps[:, :], AF.Copy,
                            scale=sc))

    with nc.psum_tensor("p3vt", [128, KD], BF16) as psvt:
        with nc.Block() as blk:
            @blk.vector
            def _(v):
                V(k, v, lambda: v.memset(v65own[:, :], 1.0))
            vm0 = k.ve.mark()
            pe2 = {}

            @blk.tensor
            def _(t):
                wt(t, vm0)
                for i in range(NT):
                    if i >= 1:
                        t.wait_ge(k.ve.sem, vm0[1] + i)
                    k.pe.inc(t.transpose(psvt[:, :],
                                         vstage[:, i * 128:(i + 1) * 128],
                                         k.idb[:KD, :KD]))
                    pe2[i] = k.pe.n

            @blk.vector
            def _(v):
                for i in range(NT):
                    v.wait_ge(k.pe.sem, pe2[i])
                    V(k, v, lambda i=i: v.tensor_copy(
                        v65own[:, i * 65:i * 65 + KD], psvt[:, :]))

    with nc.psum_tensor("p3h", [128, 512], F32) as psh:
        with nc.Block() as blk:
            ae0 = k.ae.n
            pe3 = {}

            @blk.tensor
            def _(t):
                wt(t, k.gd.mark())
                n = 0
                for oc in range(NJ):
                    for ts in range(4):
                        if n >= 1:
                            t.wait_ge(k.ae.sem, ae0 + n)
                        for j in range(NJ):
                            k.pe.inc(t.matmul(
                                psh[:, :],
                                gw1T_sb[:, j * E + oc * 128:j * E + (oc + 1) * 128],
                                k.xnT[:, j * TOKC + ts * 512:j * TOKC + (ts + 1) * 512],
                                start=(j == 0), stop=(j == NJ - 1)))
                        pe3[n] = k.pe.n
                        n += 1

            @blk.scalar
            def _(s):
                n = 0
                for oc in range(NJ):
                    for ts in range(4):
                        s.wait_ge(k.pe.sem, pe3[n])
                        A(k, s, lambda oc=oc, ts=ts: s.activation(
                            relu_junk[:, :], psh[:, :], AF.Relu,
                            bias=k.pmv[:, k.PM["gb1"] + oc:k.PM["gb1"] + oc + 1],
                            accum_out=phacc[:, oc * 4 + ts:oc * 4 + ts + 1]))
                        n += 1

    with nc.Block() as blk:
        @blk.vector
        def _(v):
            pa = phacc[:, :].rearrange("p (o t) -> p o t", t=4)
            V(k, v, lambda: v.reduce_sum(k.ph[:, :], pa, axis=AX.X))
            V(k, v, lambda: v.memset(zrow[:, :], 0.0))

        @blk.sync
        def _(s):
            k.hd.inc(s.dma_start(
                bass.AP(d["kv_in"], 0, [[TOKC, KD], [1, TOKC]]), kown[:, :]))
            k.hd.inc(s.dma_start(
                bass.AP(d["kv_in"], KD * TOKC, [[65, 128], [65 * 128, NT], [1, 65]]),
                v65own[:, :].rearrange("p (i c) -> p i c", c=65)))

    def phstore(cid):
        b = cid // 4
        with nc.Block() as blk:
            @blk.sync
            def _(s, b=b):
                k.hd.inc(s.dma_start(d["ar_ph_in"].ap()[b, :, :], k.ph[:, :]))
                k.hd.inc(s.dma_start(d["ar_ph_in"].ap()[1 - b, :, :], zrow[:, :]))
    core_switch(k, phstore)

    with nc.Block() as blk:
        @blk.gpsimd
        def _(g):
            g.wait_ge(k.hd.sem, k.hd.n)
            k.arph = coll(k, g, "AllReduce", ALU.add, d["ar_ph_in"], d["ar_ph_out"])
            k.agkv = coll(k, g, "AllGather", ALU.bypass, d["kv_in"], d["kv_out"])

    if k.dump is not None:
        dmp(k, lambda: k.qp[:KD, 0:128], rows=KD)
        dmp(k, lambda: k.ph[:, :])
    loc.close()


def p4_gmw(k):
    nc = k.nc
    d = k.d
    loc = contextlib.ExitStack()
    ph2 = k.salloc(loc, "ph2", (128, 2 * NJ))
    ph2b = k.salloc(loc, "ph2b", (128, 2 * NJ), BF16)
    gw2T_sb = k.salloc(loc, "gw2T_sb", (128, NJ * NE), BF16)
    gb2_sb = k.salloc(loc, "gb2_sb", (1, NE))
    tb1_sb = k.salloc(loc, "tb1_sb", (NE, F0), BF16)
    tb2_sb = k.salloc(loc, "tb2_sb", (NE, E), BF16)
    stg = k.salloc(loc, "mwstg", (128, 2 * 9 * 512), BF16)
    mwo = k.salloc(loc, "mwo", (128, 2 * 512), BF16)
    k.g2 = k.salloc(loc, "g2", (1, 2 * NE))
    k.gcol = k.salloc(loc, "gcol", (NE, 2), BF16)
    k.gbc = k.salloc(loc, "gbc", (128, 2 * NE))
    k.ige = k.salloc(loc, "ige", (128, 17 * 128), BF16)

    with nc.Block() as blk:
        @blk.gpsimd
        def _(g):
            wt(g, k.arph)
            for b in range(B):
                k.gd.inc(g.dma_start(ph2[:, b * NJ:(b + 1) * NJ],
                                     d["ar_ph_out"].ap()[b, :, :]))
            src = bass.AP(d["gw2T"], 0, [[NE, 128], [128 * NE, NJ], [1, NE]])
            k.gd.inc(g.dma_start(gw2T_sb[:, :], src))
            k.gd.inc(g.dma_start(gb2_sb[:1, :], d["gb2"].ap()))
            k.gd.inc(g.dma_start(tb1_sb[:, :], d["Tb1"].ap()))
            k.gd.inc(g.dma_start(tb2_sb[:, :], d["Tb2"].ap()))
        gm = k.gd.mark()

        @blk.vector
        def _(v):
            wt(v, gm)
            V(k, v, lambda: v.tensor_scalar_mul(ph2b[:, :], ph2[:, :], 1.0 / S))

    with (
        nc.psum_tensor("p4g", [1, 2 * NE], F32) as psg,
        nc.psum_tensor("p4b", [128, 2 * NE], F32) as psgb,
        nc.psum_tensor("p4gc", [NE, 2], F32) as psgc,
        nc.psum_tensor("p4m1", [128, 64], F32) as psmb1,
        nc.psum_tensor("p4m2", [128, 16], F32) as psmb2,
    ):
        with nc.Block() as blk:
            @blk.tensor
            def _(t):
                for b in range(B):
                    for j in range(NJ):
                        k.pe.inc(t.matmul(
                            psg[:, b * NE:(b + 1) * NE],
                            ph2b[:, b * NJ + j:b * NJ + j + 1],
                            gw2T_sb[:, j * NE:(j + 1) * NE],
                            start=(j == 0), stop=(j == NJ - 1)))
            pm = k.pe.mark()

            @blk.vector
            def _(v):
                wt(v, pm)
                for b in range(B):
                    V(k, v, lambda b=b: v.tensor_tensor(
                        k.g2[:1, b * NE:(b + 1) * NE], psg[:1, b * NE:(b + 1) * NE],
                        gb2_sb[:1, :], op=ALU.add))
            vm = k.ve.mark()

            @blk.tensor
            def _(t):
                wt(t, vm)
                k.pe.inc(t.matmul(psgb[:, :], k.onesf[:1, :], k.g2[:1, :],
                                  start=True, stop=True))
                for b in range(B):
                    k.pe.inc(t.transpose(psgc[:, b:b + 1],
                                         k.g2[:1, b * NE:(b + 1) * NE],
                                         k.idf[:1, :1]))
            pm2 = k.pe.mark()

            @blk.vector
            def _(v):
                wt(v, pm2)
                V(k, v, lambda: v.tensor_copy(k.gbc[:, :], psgb[:, :]))
                V(k, v, lambda: v.tensor_copy(k.gcol[:, :], psgc[:, :]))

        with nc.Block() as blk:
            @blk.vector
            def _(v):
                V(k, v, lambda: v.tensor_copy(k.ige[:, 0:128], k.idb[:, :]))
                for b in range(B):
                    for e in range(NE):
                        i = 1 + b * NE + e
                        V(k, v, lambda i=i, b=b, e=e: v.tensor_scalar_mul(
                            k.ige[:, i * 128:(i + 1) * 128], k.idb[:, :],
                            k.gbc[:, b * NE + e:b * NE + e + 1]))

        with nc.Block() as blk:
            @blk.tensor
            def _(t):
                for b in range(B):
                    for ch in range(F0 // 128):
                        k.pe.inc(t.matmul(psmb1[:, b * 32 + ch:b * 32 + ch + 1],
                                          tb1_sb[:, ch * 128:(ch + 1) * 128],
                                          k.gcol[:, b:b + 1],
                                          start=True, stop=True))
                    for ch in range(NJ):
                        k.pe.inc(t.matmul(psmb2[:, b * 8 + ch:b * 8 + ch + 1],
                                          tb2_sb[:, ch * 128:(ch + 1) * 128],
                                          k.gcol[:, b:b + 1],
                                          start=True, stop=True))
            k.pm3 = k.pe.mark()

        def mbsel(cid):
            b = cid // 4
            with nc.Block() as blk:
                @blk.vector
                def _(v, b=b):
                    wt(v, k.pm3)
                    V(k, v, lambda b=b: v.tensor_tensor(
                        k.pmv[:, k.PM["mb1"]:k.PM["mb1"] + 32],
                        psmb1[:, b * 32:(b + 1) * 32],
                        k.pmv[:, k.PM["b1"]:k.PM["b1"] + 32], op=ALU.add))
                    V(k, v, lambda b=b: v.tensor_tensor(
                        k.pmv[:, k.PM["mb2"]:k.PM["mb2"] + 8],
                        psmb2[:, b * 8:(b + 1) * 8],
                        k.pmv[:, k.PM["b2"]:k.PM["b2"] + 8], op=ALU.add))
        core_switch(k, mbsel)

    jobs = [("w1", jh) for jh in range(NJ)] + [("w2", jf) for jf in range(F0 // 128)]
    npair = 2 * len(jobs)

    def pair_region(m):
        n, b = m // 2, m % 2
        kind, idx = jobs[n]
        if kind == "w1":
            off = b * MW1N + idx * 128 * 512
            return bass.AP(d["mw_in"], off, [[512, 128], [1, 512]]), 512
        off = 2 * MW1N + b * MW2N + idx * 128 * 128
        return bass.AP(d["mw_in"], off, [[128, 128], [1, 128]]), 128

    with (
        nc.psum_tensor("p4c0", [128, 512], F32) as pc0,
        nc.psum_tensor("p4c1", [128, 512], F32) as pc1,
    ):
        pcs = [pc0, pc1]
        with nc.Block() as blk:
            ldm = [k.gsp[0], k.gsp[1]]
            ldm0 = [ldm[0].n, ldm[1].n]
            stm = [k.sp[2], k.sp[3]]
            stm0 = [stm[0].n, stm[1].n]
            pe0 = k.pe.n
            ae0 = k.ae.n

            @blk.gpsimd
            def _(g):
                for n, (kind, idx) in enumerate(jobs):
                    if n >= 2:
                        g.wait_ge(k.pe.sem, pe0 + 18 * (n - 1))
                    buf = stg[:, (n % 2) * 9 * 512:(n % 2 + 1) * 9 * 512]
                    sl = ldm[n % 2]
                    if kind == "w1":
                        sl.inc(g.dma_start(
                            buf[:, 0:512],
                            d["W1T_s"].ap()[idx * 128:(idx + 1) * 128, :]))
                        for e in range(NE):
                            sl.inc(g.dma_start(
                                buf[:, (1 + e) * 512:(2 + e) * 512],
                                d["TW1T_s"].ap()[e, idx * 128:(idx + 1) * 128, :]))
                    else:
                        sl.inc(g.dma_start(
                            buf[:, 0:128],
                            d["W2T_s"].ap()[idx * 128:(idx + 1) * 128, :]))
                        for e in range(NE):
                            sl.inc(g.dma_start(
                                buf[:, 512 + e * 128:512 + (e + 1) * 128],
                                d["TW2T_s"].ap()[e, idx * 128:(idx + 1) * 128, :]))

            @blk.tensor
            def _(t):
                for m in range(npair):
                    n, b = m // 2, m % 2
                    kind, idx = jobs[n]
                    buf = stg[:, (n % 2) * 9 * 512:(n % 2 + 1) * 9 * 512]
                    t.wait_ge(ldm[n % 2].sem, ldm0[n % 2] + 16 * 9 * (n // 2 + 1))
                    if m >= 2:
                        t.wait_ge(k.ae.sem, ae0 + m - 1)
                    ps = pcs[m % 2]
                    if kind == "w1":
                        w0, wd = buf[:, 0:512], 512
                        srcs = [buf[:, (1 + e) * 512:(2 + e) * 512] for e in range(NE)]
                    else:
                        w0, wd = buf[:, 0:128], 128
                        srcs = [buf[:, 512 + e * 128:512 + (e + 1) * 128]
                                for e in range(NE)]
                    k.pe.inc(t.matmul(ps[:, 0:wd], k.ige[:, 0:128], w0,
                                      start=True, stop=False))
                    for e in range(NE):
                        ig = k.ige[:, (1 + b * NE + e) * 128:(2 + b * NE + e) * 128]
                        k.pe.inc(t.matmul(ps[:, 0:wd], ig, srcs[e],
                                          start=False, stop=(e == NE - 1)))

            @blk.scalar
            def _(s):
                for m in range(npair):
                    s.wait_ge(k.pe.sem, pe0 + 9 * (m + 1))
                    if m >= 2:
                        pm_ = m - 2
                        s.wait_ge(stm[pm_ % 2].sem,
                                  stm0[pm_ % 2] + 16 * (pm_ // 2 + 1))
                    _, w = pair_region(m)
                    A(k, s, lambda m=m, w=w: s.copy(
                        mwo[:, (m % 2) * 512:(m % 2) * 512 + w],
                        pcs[m % 2][:, 0:w]))

            @blk.sync
            def _(s):
                for m in range(npair):
                    s.wait_ge(k.ae.sem, ae0 + m + 1)
                    ap, w = pair_region(m)
                    stm[m % 2].inc(s.dma_start(
                        ap, mwo[:, (m % 2) * 512:(m % 2) * 512 + w]))

    with nc.Block() as blk:
        @blk.gpsimd
        def _(g):
            g.wait_ge(k.sp[2].sem, k.sp[2].n)
            g.wait_ge(k.sp[3].sem, k.sp[3].n)
            k.agmw = coll(k, g, "AllGather", ALU.bypass, d["mw_in"], d["mw_out"])

    if k.dump is not None:
        dmp(k, lambda: k.gbc[:, 0:16])
        dmp(k, lambda: k.pmv[:, 80:120])
        mwchk = k.salloc(loc, "mwchk", (128, 512), BF16)
        with nc.Block() as blk:
            @blk.gpsimd
            def _(g):
                wt(g, k.agmw)
                k.gd.inc(g.dma_start(
                    mwchk[:, :], bass.AP(d["mw_out"], 0, [[512, 128], [1, 512]])))
        gm2 = k.gd.mark()
        with nc.Block() as blk:
            @blk.vector
            def _(v):
                wt(v, gm2)
        dmp(k, lambda: mwchk[:, :])
    loc.close()


def p5_attn0(k):
    nc = k.nc
    d = k.d
    loc = contextlib.ExitStack()
    ksb = k.salloc(loc, "ksb", (KD, S), BF16)
    v65 = k.salloc(loc, "v65", (128, (S // 128) * 65), BF16)
    peb = k.salloc(loc, "peb", (128, 2 * 512), BF16)
    pvs = k.salloc(loc, "pvs", (KD, 512))
    zrow = k.salloc(loc, "zrow", (1, 512))
    rz = k.salloc(loc, "rz", (1, 512))
    rzs = k.salloc(loc, "rzs", (128, 512))
    tmpb = k.salloc(loc, "p5tmp", (128, 512))

    with (
        nc.psum_tensor("p5s0", [128, 512], F32) as ps0,
        nc.psum_tensor("p5s1", [128, 512], F32) as ps1,
        nc.psum_tensor("p5pv", [65, 512], F32) as ps65,
        nc.psum_tensor("p5bc", [128, 512], F32) as psB,
        nc.psum_tensor("p5o0", [128, 512], F32) as po0,
        nc.psum_tensor("p5o1", [128, 512], F32) as po1,
    ):
        pss = [ps0, ps1]
        pos = [po0, po1]

        def body(cid):
            b = cid // 4
            q0 = (cid % 4) * TOKC
            nreg = (cid % 4) + 1

            with nc.Block() as blk:
                @blk.sync
                def _(s):
                    wt(s, k.agkv)
                    for r in range(nreg):
                        src = bass.AP(d["kv_out"], (4 * b + r) * KVN,
                                      [[TOKC, KD], [1, TOKC]])
                        k.hd.inc(s.dma_start(ksb[:, r * TOKC:(r + 1) * TOKC], src))
                        srcv = bass.AP(d["kv_out"],
                                       (4 * b + r) * KVN + KD * TOKC,
                                       [[65, 128], [65 * 128, NT], [1, 65]])
                        k.hd.inc(s.dma_start(
                            v65[:, r * NT * 65:(r + 1) * NT * 65]
                            .rearrange("p (i c) -> p i c", c=65), srcv))
                    for _ in range(2 * (4 - nreg)):
                        s.wait_ge(k.hd.sem, k.hd.n)
                        k.hd.inc(s.nop(nofuse=True))

            for i in range(4):
                nkt = (q0 + 512 * (i + 1)) // 128
                nkt_max = (3 * TOKC + 512 * (i + 1)) // 128
                qs = slice(i * 512, (i + 1) * 512)
                with nc.Block() as blk:
                    pe0 = k.pe.n
                    ae0 = k.ae.n
                    ve0 = k.ve.n
                    hdm = k.hd.mark()
                    s_af = {}
                    pv_af = {}

                    def diag_vi(kt):
                        return kt - (nkt - 4) if kt >= nkt - 4 else None

                    @blk.tensor
                    def _(t):
                        wt(t, hdm)

                        def S(kt):
                            if kt >= 2:
                                t.wait_ge(k.ae.sem, ae0 + kt - 1)
                            k.pe.inc(t.matmul(
                                pss[kt % 2][:, :],
                                ksb[:, kt * 128:(kt + 1) * 128],
                                k.qp[:, qs], start=True, stop=True))
                            s_af[kt] = k.pe.n

                        S(0)
                        for kt in range(nkt):
                            if kt + 1 < nkt:
                                S(kt + 1)
                            t.wait_ge(k.ae.sem, ae0 + kt + 1)
                            vi = diag_vi(kt)
                            if vi is not None:
                                t.wait_ge(k.ve.sem, ve0 + vi + 1)
                            k.pe.inc(t.matmul(
                                ps65[:, :], v65[:, kt * 65:(kt + 1) * 65],
                                peb[:, (kt % 2) * 512:(kt % 2 + 1) * 512],
                                start=(kt == 0), stop=(kt == nkt - 1)))
                            pv_af[kt] = k.pe.n
                        t.wait_ge(k.ve.sem, ve0 + 6)
                        k.pe.inc(t.matmul(psB[:, :], k.onesf[:1, :], rz[:1, :],
                                          start=True, stop=True))
                        bc_af = k.pe.n
                        t.wait_ge(k.ae.sem, ae0 + nkt + 1)
                        for j in range(NJ):
                            if j >= 2:
                                t.wait_ge(k.ve.sem, ve0 + 8 + 2 * (j - 2))
                            k.pe.inc(t.matmul(pos[j % 2][:, :],
                                              k.wv2f[:, j * 128:(j + 1) * 128],
                                              pvs[:, :], start=True, stop=True))
                            pv_af[("o", j)] = k.pe.n
                        for _ in range(2 * (nkt_max - nkt)):
                            t.wait_ge(k.pe.sem, k.pe.n)
                            k.pe.inc(t.nop(nofuse=True))

                    @blk.scalar
                    def _(s):
                        for kt in range(nkt):
                            s.wait_ge(k.pe.sem, s_af[kt])
                            if kt >= 2:
                                s.wait_ge(k.pe.sem, pv_af[kt - 2])
                            A(k, s, lambda kt=kt: s.activation(
                                peb[:, (kt % 2) * 512:(kt % 2 + 1) * 512],
                                pss[kt % 2][:, :], AF.Exp))
                        s.wait_ge(k.pe.sem, pv_af[nkt - 1])
                        A(k, s, lambda: s.copy(pvs[:, :], ps65[0:KD, :]))
                        for _ in range(nkt_max - nkt):
                            s.wait_ge(k.ae.sem, k.ae.n)
                            k.ae.inc(s.nop(nofuse=True))

                    @blk.vector
                    def _(v):
                        for kt in range(nkt - 4, nkt):
                            vi = kt - (nkt - 4)
                            dv = (kt * 128 - (q0 + 512 * i)) // 128
                            v.wait_ge(k.ae.sem, ae0 + kt + 1)
                            V(k, v, lambda kt=kt, dv=dv: v.tensor_tensor(
                                peb[:, (kt % 2) * 512:(kt % 2 + 1) * 512],
                                peb[:, (kt % 2) * 512:(kt % 2 + 1) * 512],
                                k.mask0[:, dv * 512:(dv + 1) * 512], op=ALU.mult))
                        v.wait_ge(k.pe.sem, pv_af[nkt - 1])
                        V(k, v, lambda: v.tensor_copy(zrow[:1, :], ps65[KD:KD + 1, :]))
                        V(k, v, lambda: v.reciprocal(rz[:1, :], zrow[:1, :]))
                        v.wait_ge(k.pe.sem, pv_af[nkt - 1] + 1)
                        V(k, v, lambda: v.tensor_copy(rzs[:, :], psB[:, :]))
                        for j in range(NJ):
                            v.wait_ge(k.pe.sem, pv_af[("o", j)])
                            V(k, v, lambda j=j: v.tensor_mul(
                                tmpb[:, :], pos[j % 2][:, :], rzs[:, :]))
                            xsl = k.xT[:, j * TOKC + i * 512:j * TOKC + (i + 1) * 512]
                            V(k, v, lambda xsl=xsl: v.tensor_add(xsl, xsl, tmpb[:, :]))

        core_switch(k, body)
    if k.dump is not None:
        dmp(k, lambda: k.xT[:, 0:256])
    loc.close()


def p6_ffn(k):
    nc = k.nc
    d = k.d
    loc = contextlib.ExitStack()
    stat1 = k.salloc(loc, "stat1", (128, 2 * 8 * 128), BF16)
    ugel = k.salloc(loc, "ugel", (128, 2 * 512), BF16)
    u_all = k.salloc(loc, "u_all", (128, 32 * 512), BF16)
    stat2 = k.salloc(loc, "stat2", (128, 2 * 8 * 128), BF16)
    tmpo = k.salloc(loc, "p6tmp", (128, 512))

    def mwcopy(cid):
        b = cid // 4
        with nc.Block() as blk:
            @blk.sync
            def _(s, b=b):
                wt(s, k.agmw)
                for r in range(NC):
                    src = bass.AP(d["mw_out"], r * MWN + b * MW1N,
                                  [[512, 1024], [1, 512]])
                    dst = bass.AP(d["mw1t_mine"], r * 512, [[4096, 1024], [1, 512]])
                    k.hd.inc(s.dma_start(dst, src))
                    src2 = bass.AP(d["mw_out"], r * MWN + 2 * MW1N + b * MW2N,
                                   [[128, 4096], [1, 128]])
                    dst2 = bass.AP(d["mw2t_mine"], r * 128, [[1024, 4096], [1, 128]])
                    k.hd.inc(s.dma_start(dst2, src2))
    core_switch(k, mwcopy)
    mwcm = k.hd.mark()

    with (
        nc.psum_tensor("p6u0", [128, 512], F32) as pu0,
        nc.psum_tensor("p6u1", [128, 512], F32) as pu1,
    ):
        pus = [pu0, pu1]
        with nc.Block() as blk:
            lds = [k.sp[4], k.sp[5]]
            lds0 = [lds[0].n, lds[1].n]
            sts = [k.sp[6], k.sp[7]]
            sts0 = [sts[0].n, sts[1].n]
            pe0 = k.pe.n
            ae0 = k.ae.n
            pe_rec = {}

            @blk.sync
            def _(s):
                wt(s, mwcm)
                for ft in range(32):
                    if ft >= 2:
                        s.wait_ge(k.pe.sem, pe0 + 8 * (4 * (ft - 2) + 4))
                    src = bass.AP(d["mw1t_mine"], ft * 128,
                                  [[4096, 128], [128 * 4096, 8], [1, 128]])
                    lds[ft % 2].inc(s.dma_start(
                        stat1[:, (ft % 2) * 1024:(ft % 2 + 1) * 1024], src))

            @blk.tensor
            def _(t):
                for m in range(128):
                    ft, ts = m // 4, m % 4
                    t.wait_ge(lds[ft % 2].sem, lds0[ft % 2] + 16 * (ft // 2 + 1))
                    if m >= 2:
                        t.wait_ge(k.ae.sem, ae0 + m - 1)
                    for jh in range(NJ):
                        k.pe.inc(t.matmul(
                            pus[m % 2][:, :],
                            stat1[:, (ft % 2) * 1024 + jh * 128:
                                  (ft % 2) * 1024 + (jh + 1) * 128],
                            k.xnT[:, jh * TOKC + ts * 512:jh * TOKC + (ts + 1) * 512],
                            start=(jh == 0), stop=(jh == NJ - 1)))
                    pe_rec[m] = k.pe.n

            @blk.scalar
            def _(s):
                for m in range(128):
                    ft = m // 4
                    s.wait_ge(k.pe.sem, pe_rec[m])
                    if m >= 2:
                        pm_ = m - 2
                        s.wait_ge(sts[pm_ % 2].sem, sts0[pm_ % 2] + 16 * (pm_ // 2 + 1))
                    A(k, s, lambda m=m, ft=ft: s.activation(
                        ugel[:, (m % 2) * 512:(m % 2 + 1) * 512], pus[m % 2][:, :],
                        AF.Gelu_apprx_tanh,
                        bias=k.pmv[:, k.PM["mb1"] + ft:k.PM["mb1"] + ft + 1]))

            @blk.gpsimd
            def _(g):
                for m in range(128):
                    ft, ts = m // 4, m % 4
                    g.wait_ge(k.ae.sem, ae0 + m + 1)
                    sts[m % 2].inc(g.dma_start(
                        d["u_dram"].ap()[ft, :, ts * 512:(ts + 1) * 512],
                        ugel[:, (m % 2) * 512:(m % 2 + 1) * 512]))

    ustm = [k.sp[6].mark(), k.sp[7].mark()]
    for ts in range(4):
        with contextlib.ExitStack() as pstk:
            psE = [pstk.enter_context(
                nc.psum_tensor("p6o%d_%d" % (ts, j), [128, 512], F32))
                for j in range(NJ)]
            with nc.Block() as blk:
                ldu = k.sp[8]
                ldu0 = ldu.n
                ld2 = [k.sp[4], k.sp[5]]
                ld20 = [ld2[0].n, ld2[1].n]
                pe0 = k.pe.n
                pe_rec = {}

                @blk.sync
                def _(s):
                    wt(s, ustm[0])
                    wt(s, ustm[1])
                    src = bass.AP(d["u_dram"], ts * 512,
                                  [[TOKC, 128], [128 * TOKC, 32], [1, 512]])
                    ldu.inc(s.dma_start(u_all[:, :], src))
                    for ft in range(32):
                        if ft >= 2:
                            s.wait_ge(k.pe.sem, pe0 + 8 * (ft - 1))
                        src2 = bass.AP(d["mw2t_mine"], ft * 128 * 1024,
                                       [[1024, 128], [128, 8], [1, 128]])
                        ld2[ft % 2].inc(s.dma_start(
                            stat2[:, (ft % 2) * 1024:(ft % 2 + 1) * 1024], src2))

                @blk.tensor
                def _(t):
                    t.wait_ge(ldu.sem, ldu0 + 16)
                    for ft in range(32):
                        t.wait_ge(ld2[ft % 2].sem, ld20[ft % 2] + 16 * (ft // 2 + 1))
                        for jE in range(NJ):
                            k.pe.inc(t.matmul(
                                psE[jE][:, :],
                                stat2[:, (ft % 2) * 1024 + jE * 128:
                                      (ft % 2) * 1024 + (jE + 1) * 128],
                                u_all[:, ft * 512:(ft + 1) * 512],
                                start=(ft == 0), stop=(ft == 31)))
                        pe_rec[ft] = k.pe.n
                pem = k.pe.mark()

                @blk.vector
                def _(v):
                    wt(v, pem)
                    for jE in range(NJ):
                        xsl = k.xT[:, jE * TOKC + ts * 512:jE * TOKC + (ts + 1) * 512]
                        V(k, v, lambda jE=jE, xsl=xsl: v.scalar_tensor_tensor(
                            tmpo[:, :], psE[jE][:, :],
                            k.pmv[:, k.PM["mb2"] + jE:k.PM["mb2"] + jE + 1],
                            xsl, op0=ALU.add, op1=ALU.add))
                        V(k, v, lambda jE=jE, xsl=xsl: v.tensor_copy(xsl, tmpo[:, :]))

    if k.dump is not None:
        dmp(k, lambda: k.xT[:, 0:256])
    loc.close()


def p7_l1(k):
    nc = k.nc
    d = k.d
    loc = contextlib.ExitStack()
    x1tt = k.salloc(loc, "x1tt", (128, 2 * E))
    xn1t = k.salloc(loc, "xn1t", (128, 2 * E), BF16)
    st7 = k.salloc(loc, "p7st", (128, 8 * NT))
    mask1 = k.salloc(loc, "mask1", (128, 256), BF16)
    iota8 = k.salloc(loc, "iota8_sb", (128, NE))
    wg_sb = k.salloc(loc, "wg_sb", (128, NJ * NE), BF16)
    lg_tm = k.salloc(loc, "lg_tm", (128, NT * NE))
    r8 = k.salloc(loc, "r8", (128, 16))
    rsc = k.salloc(loc, "rsc", (128, 8))
    idxu = k.salloc(loc, "idxu", (128, 8), U32)
    ohs = k.salloc(loc, "ohs", (128, 8))
    cntr = k.salloc(loc, "cntr", (1, NE))
    wst = k.salloc(loc, "wst", (128, 2 * 8 * 128), BF16)
    qstg = k.salloc(loc, "qstg", (128, 2 * 512), BF16)
    q_tm = k.salloc(loc, "q_tm", (128, 4 * E), BF16)
    k_tm = k.salloc(loc, "k_tm", (128, 4 * E), BF16)
    v_tm = k.salloc(loc, "v_tm", (128, 4 * E), BF16)
    S2 = k.salloc(loc, "S2", (128, 4 * C1 * C1), BF16)
    Ebuf = k.salloc(loc, "Ebuf", (128, 2 * C1 * C1), BF16)
    junk7 = Ebuf[:, 0:E]
    attn_tm = k.salloc(loc, "attn_tm", (128, 4 * E), BF16)
    attnT = k.salloc(loc, "attnT", (128, NJ * 512), BF16)
    tmpo7 = x1tt[:, 0:512]
    Nf = x1tt[:, 512:512 + 2 * C1]
    Zf = x1tt[:, 640:640 + 2 * C1]
    rZf = x1tt[:, 768:768 + 2 * C1]
    lgf = None

    with (
        nc.psum_tensor("p7f0", [128, 128], F32) as pf0,
        nc.psum_tensor("p7f1", [128, 128], F32) as pf1,
        nc.psum_tensor("p7b0", [128, 128], BF16) as pb0,
        nc.psum_tensor("p7b1", [128, 128], BF16) as pb1,
    ):
        pfs = [pf0, pf1]
        pbs = [pb0, pb1]
        with nc.Block() as blk:
            xst = [k.sp[0], k.sp[1]]
            xst0 = [xst[0].n, xst[1].n]
            ve0 = k.ve.n
            ae0 = k.ae.n
            pe0 = k.pe.n

            def pe_at(i, op):
                return pe0 + 16 * i + op

            def ae_at(i, op):
                return ae0 + 10 * i + op

            def ve_at(i, op):
                return ve0 + 15 * i + op

            @blk.tensor
            def _(t):
                for i in range(NT):
                    for j in range(NJ):
                        if j >= 2:
                            if j % 2 == 0:
                                t.wait_ge(k.ae.sem, ae_at(i, (j - 2) // 2 + 1))
                            else:
                                t.wait_ge(k.ve.sem, ve_at(i, (j - 3) // 2 + 1))
                        elif i >= 1:
                            if j % 2 == 0:
                                t.wait_ge(k.ae.sem, ae_at(i - 1, 4))
                            else:
                                t.wait_ge(k.ve.sem, ve_at(i - 1, 4))
                        k.pe.inc(t.transpose(
                            pfs[j % 2][:, :],
                            k.xT[:, j * TOKC + i * 128:j * TOKC + (i + 1) * 128],
                            k.idf[:, :]))
                    t.wait_ge(k.ve.sem, ve_at(i, 11))
                    for j in range(NJ):
                        if j >= 2:
                            if j % 2 == 0:
                                t.wait_ge(k.ae.sem, ae_at(i, 6 + (j - 2) // 2 + 1))
                            else:
                                t.wait_ge(k.ve.sem, ve_at(i, 11 + (j - 3) // 2 + 1))
                        elif i >= 1:
                            if j % 2 == 0:
                                t.wait_ge(k.ae.sem, ae_at(i - 1, 10))
                            else:
                                t.wait_ge(k.ve.sem, ve_at(i - 1, 15))
                        k.pe.inc(t.transpose(
                            pbs[j % 2][:, :],
                            xn1t[:, (i % 2) * E + j * 128:(i % 2) * E + (j + 1) * 128],
                            k.idb[:, :]))

            @blk.scalar
            def _(s):
                for i in range(NT):
                    c = 8 * i
                    for j in range(0, NJ, 2):
                        s.wait_ge(k.pe.sem, pe_at(i, j + 1))
                        if i >= 2 and j == 0:
                            s.wait_ge(k.ve.sem, ve_at(i - 2, 11))
                        A(k, s, lambda i=i, j=j: s.copy(
                            x1tt[:, (i % 2) * E + j * 128:(i % 2) * E + (j + 1) * 128],
                            pfs[j % 2][:, :]))
                    s.wait_ge(k.ve.sem, ve_at(i, 10))
                    A(k, s, lambda c=c: s.activation(st7[:, c + 5:c + 6],
                                                     st7[:, c + 4:c + 5], AF.Ln))
                    A(k, s, lambda c=c: s.activation(st7[:, c + 5:c + 6],
                                                     st7[:, c + 5:c + 6], AF.Exp,
                                                     scale=-0.5))
                    for j in range(0, NJ, 2):
                        s.wait_ge(k.pe.sem, pe_at(i, 8 + j + 1))
                        A(k, s, lambda i=i, j=j: s.copy(
                            k.xnT[:, j * TOKC + i * 128:j * TOKC + (i + 1) * 128],
                            pbs[j % 2][:, :]))

            @blk.vector
            def _(v):
                for i in range(NT):
                    c = 8 * i
                    xt = x1tt[:, (i % 2) * E:(i % 2 + 1) * E]
                    xn = xn1t[:, (i % 2) * E:(i % 2 + 1) * E]
                    for j in range(1, NJ, 2):
                        v.wait_ge(k.pe.sem, pe_at(i, j + 1))
                        V(k, v, lambda i=i, j=j: v.tensor_copy(
                            x1tt[:, (i % 2) * E + j * 128:(i % 2) * E + (j + 1) * 128],
                            pfs[j % 2][:, :]))
                    v.wait_ge(k.ae.sem, ae_at(i, 4))
                    V(k, v, lambda xt=xt, c=c: v.reduce_sum(st7[:, c:c + 1], xt, axis=AX.X))
                    V(k, v, lambda xt=xt, c=c: v.scalar_tensor_tensor(
                        junk7, xt, 1.0, xt, op0=ALU.bypass, op1=ALU.mult,
                        accum_out=st7[:, c + 1:c + 2]))
                    V(k, v, lambda c=c: v.tensor_scalar_mul(
                        st7[:, c + 2:c + 3], st7[:, c:c + 1], 1.0 / E))
                    V(k, v, lambda c=c: v.tensor_scalar_mul(
                        st7[:, c + 3:c + 4], st7[:, c + 1:c + 2], 1.0 / E))
                    V(k, v, lambda c=c: v.scalar_tensor_tensor(
                        st7[:, c + 3:c + 4], st7[:, c + 2:c + 3], st7[:, c + 2:c + 3],
                        st7[:, c + 3:c + 4], op0=ALU.mult, op1=ALU.subtract))
                    V(k, v, lambda c=c: v.tensor_scalar(
                        st7[:, c + 4:c + 5], st7[:, c + 3:c + 4], -1.0, EPS,
                        op0=ALU.mult, op1=ALU.add))
                    v.wait_ge(k.ae.sem, ae_at(i, 6))
                    if i >= 2:
                        v.wait_ge(k.pe.sem, pe_at(i - 2, 16))
                        v.wait_ge(xst[i % 2].sem, xst0[i % 2] + 16 * (i // 2))
                    V(k, v, lambda xt=xt, xn=xn, c=c: v.tensor_scalar(
                        xn, xt, st7[:, c + 2:c + 3], st7[:, c + 5:c + 6],
                        op0=ALU.subtract, op1=ALU.mult))
                    for j in range(1, NJ, 2):
                        v.wait_ge(k.pe.sem, pe_at(i, 8 + j + 1))
                        V(k, v, lambda i=i, j=j: v.tensor_copy(
                            k.xnT[:, j * TOKC + i * 128:j * TOKC + (i + 1) * 128],
                            pbs[j % 2][:, :]))

            @blk.sync
            def _(s):
                for i in range(NT):
                    s.wait_ge(k.ve.sem, ve_at(i, 11))
                    xst[i % 2].inc(s.dma_start(
                        d["xn1_in"].ap()[i * 128:(i + 1) * 128, :],
                        xn1t[:, (i % 2) * E:(i % 2 + 1) * E]))

            @blk.gpsimd
            def _(g):
                k.gd.inc(g.dma_start(
                    mask1[:, :],
                    bass.AP(d["maskL1"], 0, [[4096, 128], [64, 16], [1, 16]])))
                k.gd.inc(g.dma_start(iota8[:, :], d["iota8"].ap()))
                src = bass.AP(d["wgate"], 0, [[NE, 128], [128 * NE, NJ], [1, NE]])
                k.gd.inc(g.dma_start(wg_sb[:, :], src))
        ln1m = (k.gd.mark(), xst[0].mark(), xst[1].mark())

    lgf = x1tt

    with nc.Block() as blk:
        @blk.gpsimd
        def _(g):
            wt(g, ln1m[1])
            wt(g, ln1m[2])
            k.agxn1 = coll(k, g, "AllGather", ALU.bypass, d["xn1_in"], d["xn1_out"])

    with (
        nc.psum_tensor("p7r0", [NE, 512], F32) as pr0,
        nc.psum_tensor("p7r1", [NE, 512], F32) as pr1,
        nc.psum_tensor("p7t0", [128, NE], F32) as pt0,
        nc.psum_tensor("p7t1", [128, NE], F32) as pt1,
        nc.psum_tensor("p7c", [1, NE], F32) as pcnt,
    ):
        prs = [pr0, pr1]
        pts = [pt0, pt1]
        with nc.Block() as blk:
            ve0 = k.ve.n
            pe_rec = {}

            @blk.tensor
            def _(t):
                wt(t, ln1m[0])
                for ts in range(4):
                    if ts >= 2:
                        t.wait_ge(k.ve.sem, ve0 + ts - 1)
                    for j in range(NJ):
                        k.pe.inc(t.matmul(
                            prs[ts % 2][:, :], wg_sb[:, j * NE:(j + 1) * NE],
                            k.xnT[:, j * TOKC + ts * 512:j * TOKC + (ts + 1) * 512],
                            start=(j == 0), stop=(j == NJ - 1)))
                    pe_rec[ts] = k.pe.n

            @blk.vector
            def _(v):
                for ts in range(4):
                    v.wait_ge(k.pe.sem, pe_rec[ts])
                    V(k, v, lambda ts=ts: v.tensor_copy(
                        lgf[0:NE, ts * 512:(ts + 1) * 512], prs[ts % 2][:, :]))

        with nc.Block() as blk:
            ve0 = k.ve.n
            pe_rec = {}

            @blk.tensor
            def _(t):
                for i in range(NT):
                    if i >= 2:
                        t.wait_ge(k.ve.sem, ve0 + i - 1)
                    k.pe.inc(t.transpose(pts[i % 2][:, :],
                                         lgf[0:NE, i * 128:(i + 1) * 128],
                                         k.idf[:NE, :NE]))
                    pe_rec[i] = k.pe.n

            @blk.vector
            def _(v):
                for i in range(NT):
                    v.wait_ge(k.pe.sem, pe_rec[i])
                    V(k, v, lambda i=i: v.tensor_copy(
                        lg_tm[:, i * NE:(i + 1) * NE], pts[i % 2][:, :]))

        with nc.Block() as blk:
            ve_rec = {}
            ae_rec = {}
            ve0 = k.ve.n
            ae0 = k.ae.n
            pe0c = k.pe.n

            @blk.vector
            def _(v):
                for i in range(NT):
                    lt = lg_tm[:, i * NE:(i + 1) * NE]
                    V(k, v, lambda lt=lt: v.max(r8[:, 0:8], lt))
                    V(k, v, lambda lt=lt: v.max_index(idxu[:, :], r8[:, 0:8], lt))
                    V(k, v, lambda i=i: v.tensor_copy(k.rtri[:, 2 * i:2 * i + 1],
                                                      idxu[:, 0:1]))
                    V(k, v, lambda i=i: v.tensor_copy(k.rtri[:, 2 * i + 1:2 * i + 2],
                                                      idxu[:, 1:2]))
                    V(k, v, lambda: v.tensor_scalar_mul(rsc[:, 0:1], r8[:, 0:1], -1.0))
                    ve_rec[("pre", i)] = k.ve.n
                    v.wait_ge(k.ae.sem, ae0 + 2 * i + 1)
                    V(k, v, lambda: v.reciprocal(rsc[:, 2:3], rsc[:, 1:2]))
                    V(k, v, lambda i=i: v.tensor_copy(k.rtrw[:, 2 * i:2 * i + 1],
                                                      rsc[:, 2:3]))
                    v.wait_ge(k.ae.sem, ae0 + 2 * i + 2)
                    V(k, v, lambda i=i: v.tensor_mul(k.rtrw[:, 2 * i + 1:2 * i + 2],
                                                     rsc[:, 3:4], rsc[:, 2:3]))
                    V(k, v, lambda i=i: v.tensor_scalar(
                        k.oh1[:, i * NE:(i + 1) * NE], iota8[:, :],
                        k.rtri[:, 2 * i:2 * i + 1], None, op0=ALU.is_equal))
                    V(k, v, lambda i=i: v.tensor_scalar(
                        k.oh2[:, i * NE:(i + 1) * NE], iota8[:, :],
                        k.rtri[:, 2 * i + 1:2 * i + 2], None, op0=ALU.is_equal))
                    if i >= 1:
                        v.wait_ge(k.pe.sem, pe0c + i)
                    V(k, v, lambda i=i: v.tensor_add(
                        ohs[:, :], k.oh1[:, i * NE:(i + 1) * NE],
                        k.oh2[:, i * NE:(i + 1) * NE]))
                    ve_rec[("ohs", i)] = k.ve.n

            @blk.scalar
            def _(s):
                for i in range(NT):
                    lt = lg_tm[:, i * NE:(i + 1) * NE]
                    s.wait_ge(k.ve.sem, ve_rec[("pre", i)])
                    A(k, s, lambda lt=lt: s.activation(
                        r8[:, 8:16], lt, AF.Exp, bias=rsc[:, 0:1],
                        accum_out=rsc[:, 1:2]))
                    A(k, s, lambda: s.activation(rsc[:, 3:4], r8[:, 1:2], AF.Exp,
                                                 bias=rsc[:, 0:1]))

            @blk.tensor
            def _(t):
                for i in range(NT):
                    t.wait_ge(k.ve.sem, ve_rec[("ohs", i)])
                    k.pe.inc(t.matmul(pcnt[:, :], k.onesf[:, 0:1], ohs[:, :],
                                      start=(i == 0), stop=(i == NT - 1)))
            pcm = k.pe.mark()

            @blk.vector
            def _(v):
                wt(v, pcm)
                V(k, v, lambda: v.tensor_copy(cntr[:1, :], pcnt[:1, :]))
            cvm = k.ve.mark()

            @blk.sync
            def _(s):
                wt(s, cvm)
                k.hd.inc(s.dma_start(d["cnt_in"].ap(), cntr[:1, :]))


    with (
        nc.psum_tensor("p7q0", [128, 512], F32) as pq0,
        nc.psum_tensor("p7q1", [128, 512], F32) as pq1,
        nc.psum_tensor("p7x0", [128, 128], BF16) as px0,
        nc.psum_tensor("p7x1", [128, 128], BF16) as px1,
        nc.psum_tensor("p7w0", [128, 512], F32) as pw0,
        nc.psum_tensor("p7w1", [128, 512], F32) as pw1,
    ):
        pqs = [pq0, pq1]
        pxs = [px0, px1]
        pws = [pw0, pw1]
        mats = [("Wq1", q_tm, "bq", 0.125), ("Wk1", k_tm, "bk", 1.0),
                ("Wv1_", v_tm, "bv", 1.0)]

        with nc.Block() as blk:
            lds = [k.gsp[2], k.gsp[3]]
            lds0 = [lds[0].n, lds[1].n]
            ae0 = k.ae.n
            ve0 = k.ve.n
            pe_rec = {}
            petp_rec = {}
            pe_rec_pre = k.pe.n

            @blk.gpsimd
            def _(g):
                g.wait_ge(k.hd.sem, k.hd.n)
                k.agcnt = coll(k, g, "AllGather", ALU.bypass,
                               d["cnt_in"], d["cnt_out"])
                for m in range(24):
                    mi, jo = m // 8, m % 8
                    if m >= 2:
                        g.wait_ge(k.pe.sem, pe_rec_pre + 12 * (m - 1))
                    src = bass.AP(d[mats[mi][0]], jo * 128,
                                  [[E, 128], [128 * E, NJ], [1, 128]])
                    lds[m % 2].inc(g.dma_start(
                        wst[:, (m % 2) * 1024:(m % 2 + 1) * 1024], src))

            @blk.tensor
            def _(t):
                for m in range(24):
                    mi, jo = m // 8, m % 8
                    t.wait_ge(lds[m % 2].sem, lds0[m % 2] + 16 * (m // 2 + 1))
                    if m >= 2:
                        t.wait_ge(k.ae.sem, ae0 + 3 * (m - 2) + 1)
                    for j in range(NJ):
                        k.pe.inc(t.matmul(
                            pqs[m % 2][:, :],
                            wst[:, (m % 2) * 1024 + j * 128:
                                (m % 2) * 1024 + (j + 1) * 128],
                            k.xnT[:, j * TOKC:j * TOKC + 512],
                            start=(j == 0), stop=(j == NJ - 1)))
                    pe_rec[m] = k.pe.n
                    t.wait_ge(k.ae.sem, ae0 + 3 * m + 1)
                    for tt in range(4):
                        if m * 4 + tt >= 2:
                            pn = m * 4 + tt - 2
                            pm_, ptt = pn // 4, pn % 4
                            if ptt % 2 == 0:
                                t.wait_ge(k.ae.sem, ae0 + 3 * pm_ + 1 + ptt // 2 + 1)
                            else:
                                t.wait_ge(k.ve.sem, ve0 + 2 * pm_ + (ptt - 1) // 2 + 1)
                        k.pe.inc(t.transpose(
                            pxs[tt % 2][:, :],
                            qstg[:, (m % 2) * 512 + tt * 128:
                                 (m % 2) * 512 + (tt + 1) * 128],
                            k.idb[:, :]))
                        petp_rec[(m, tt)] = k.pe.n

            @blk.scalar
            def _(s):
                for m in range(24):
                    mi, jo = m // 8, m % 8
                    nm, dstt, bn, sc = mats[mi]
                    s.wait_ge(k.pe.sem, pe_rec[m])
                    A(k, s, lambda jo=jo, bn=bn, sc=sc, m=m:
                      s.activation(qstg[:, (m % 2) * 512:(m % 2 + 1) * 512],
                                   pqs[m % 2][:, :], AF.Identity, scale=sc,
                                   bias=k.pmv[:, k.PM[bn] + jo:k.PM[bn] + jo + 1]))
                    for tt in (0, 2):
                        s.wait_ge(k.pe.sem, petp_rec[(m, tt)])
                        A(k, s, lambda dstt=dstt, jo=jo, tt=tt: s.copy(
                            dstt[:, tt * E + jo * 128:tt * E + (jo + 1) * 128],
                            pxs[tt % 2][:, :]))

            @blk.vector
            def _(v):
                for m in range(24):
                    mi, jo = m // 8, m % 8
                    dstt = mats[mi][1]
                    for tt in (1, 3):
                        v.wait_ge(k.pe.sem, petp_rec[(m, tt)])
                        V(k, v, lambda dstt=dstt, jo=jo, tt=tt: v.tensor_copy(
                            dstt[:, tt * E + jo * 128:tt * E + (jo + 1) * 128],
                            pxs[tt % 2][:, :]))

        for ts in range(4):
            with_qkv = ts < 3
            tsn = ts + 1
            with nc.Block() as blk:
                lds = [k.gsp[2], k.gsp[3]]
                lds0 = [lds[0].n, lds[1].n]
                ae0 = k.ae.n
                pe0 = k.pe.n
                ve_mask = {}
                ve_fin = {}
                E4 = Ebuf[:, :].rearrange("p (i a b) -> p i a b", i=2, b=C1)
                LV = [(32, 32), (16, 16), (8, 0)]
                M3 = mask1[:, :].rearrange("p (a b) -> p a b", b=16)

                def sl4_of(p):
                    return S2[:, (p % 2) * 8192:(p % 2 + 1) * 8192]\
                        .rearrange("p (i a b) -> p i a b", i=2, b=C1)

                def prevu(tt, m):
                    if m >= 2:
                        return (tt, m - 2)
                    if tt >= 1:
                        return (tt - 1, 22 + m)
                    return None

                mm_done = {}
                tp_done = {}
                cnt = pe0
                if with_qkv:
                    for tt in range(4):
                        for m in range(24):
                            cnt += 8
                            mm_done[(tt, m)] = cnt
                            cnt += 1
                            tp_done[(tt, m)] = cnt
                exp_done = {}
                evac1_done = {}
                tpev_done = {}
                acnt = ae0
                pending = []
                ins_at = {8: 0, 16: 1, 24: 2, 31: 3} if with_qkv else {}
                sched = []
                for p in range(32):
                    sched.append(("exp", p, 0))
                    acnt += 4
                    exp_done[p] = acnt
                    if p in ins_at:
                        tt = ins_at[p]
                        pending.append(("gate", tt, 0))
                        for m in range(24):
                            pending.append(("e1", tt, m))
                            pending.append(("tp", tt, m))
                    take = 12 if p != 31 else 10 ** 9
                    taken = 0
                    while pending and taken < take:
                        op = pending.pop(0)
                        sched.append(op)
                        if op[0] in ("e1", "tp"):
                            acnt += 1
                            taken += 1
                            if op[0] == "e1":
                                evac1_done[(op[1], op[2])] = acnt
                            else:
                                tpev_done[(op[1], op[2])] = acnt

                @blk.vector
                def _(v):
                    Vf(k, lambda: v.memset(Ebuf[:, :], 0.0))

                    def build(p):
                        tt, hp = p // 8, p % 8
                        base = tt * E + 2 * hp * C1
                        sl4 = sl4_of(p)
                        q2 = q_tm[:, base:base + 128]\
                            .rearrange("p (i c) -> p i c", i=2)
                        k2 = k_tm[:, base:base + 128]\
                            .rearrange("p (i c) -> p i c", i=2)
                        if p >= 2:
                            v.wait_ge(k.ae.sem, exp_done[p - 2])
                        for qq in range(4):
                            Kq = 16 * (qq + 1)
                            qap = q2[:, :, 16 * qq:16 * (qq + 1)]\
                                .unsqueeze(3).broadcast_to([128, 2, 16, Kq])
                            kap = k2[:, :, 0:Kq]\
                                .unsqueeze(2).broadcast_to([128, 2, 16, Kq])
                            Vf(k, lambda sl4=sl4, qq=qq, Kq=Kq, qap=qap,
                               kap=kap: v.tensor_tensor(
                                sl4[:, :, 16 * qq:16 * (qq + 1), 0:Kq],
                                qap, kap, op=ALU.mult))
                        Vbar(k, v)
                        mbc = M3.unsqueeze(1).broadcast_to([128, 2, 16, 16])
                        for qq in range(4):
                            Vf(k, lambda sl4=sl4, qq=qq, mbc=mbc: v.tensor_add(
                                sl4[:, :, 16 * qq:16 * (qq + 1),
                                    16 * qq:16 * (qq + 1)],
                                sl4[:, :, 16 * qq:16 * (qq + 1),
                                    16 * qq:16 * (qq + 1)],
                                mbc))
                        ve_mask[p] = k.ve.n

                    build(0)
                    for p in range(32):
                        tt, hp = p // 8, p % 8
                        base = tt * E + 2 * hp * C1
                        if p + 1 < 32:
                            build(p + 1)
                        v.wait_ge(k.ae.sem, exp_done[p])
                        EV4 = sl4_of(p)
                        vap = v_tm[:, base:base + 128]\
                            .rearrange("p (i c) -> p i c", i=2)\
                            .unsqueeze(2).broadcast_to([128, 2, C1, C1])
                        Vbar(k, v)
                        Vf(k, lambda EV4=EV4, vap=vap: v.tensor_tensor(
                            EV4, E4, vap, op=ALU.mult))
                        for hw, rlo in LV:
                            Vbar(k, v)
                            for b4 in (E4, EV4):
                                Vf(k, lambda b4=b4, hw=hw, rlo=rlo:
                                   v.tensor_add(
                                    b4[:, :, rlo:C1, 0:hw],
                                    b4[:, :, rlo:C1, 0:hw],
                                    b4[:, :, rlo:C1, hw:2 * hw]))
                        Vbar(k, v)
                        Vf(k, lambda EV4=EV4: v.reduce_sum(
                            Zf, E4[:, :, :, 0:8], axis=AX.X))
                        Vf(k, lambda EV4=EV4: v.reduce_sum(
                            Nf, EV4[:, :, :, 0:8], axis=AX.X))
                        Vbar(k, v)
                        Vf(k, lambda: v.reciprocal(rZf, Zf))
                        Vbar(k, v)
                        Vf(k, lambda base=base: v.tensor_mul(
                            attn_tm[:, base:base + 128],
                            Nf, rZf))
                        ve_fin[p] = k.ve.n

                @blk.tensor
                def _(t):
                    if not with_qkv:
                        return
                    for tt in range(4):
                        for m in range(24):
                            mi, jo = m // 8, m % 8
                            L = tt * 24 + m
                            t.wait_ge(lds[m % 2].sem,
                                      lds0[m % 2] + 16 * (L // 2 + 1))
                            pu = prevu(tt, m)
                            if pu is not None:
                                t.wait_ge(k.ae.sem, evac1_done[pu])
                            for j in range(NJ):
                                k.pe.inc(t.matmul(
                                    pqs[m % 2][:, 0:128],
                                    wst[:, (m % 2) * 1024 + j * 128:
                                        (m % 2) * 1024 + (j + 1) * 128],
                                    k.xnT[:, j * TOKC + tsn * 512 + tt * 128:
                                          j * TOKC + tsn * 512 + (tt + 1) * 128],
                                    start=(j == 0), stop=(j == NJ - 1)))
                            assert k.pe.n == mm_done[(tt, m)]
                            t.wait_ge(k.ae.sem, evac1_done[(tt, m)])
                            pu = prevu(tt, m)
                            if pu is not None:
                                t.wait_ge(k.ae.sem, tpev_done[pu])
                            k.pe.inc(t.transpose(
                                pxs[m % 2][:, :],
                                qstg[:, (m % 2) * 512:(m % 2) * 512 + 128],
                                k.idb[:, :]))
                            assert k.pe.n == tp_done[(tt, m)]

                @blk.gpsimd
                def _(g):
                    if not with_qkv:
                        return
                    for tt in range(4):
                        for m in range(24):
                            mi, jo = m // 8, m % 8
                            pu = prevu(tt, m)
                            if pu is not None:
                                g.wait_ge(k.pe.sem, mm_done[pu])
                            src = bass.AP(d[mats[mi][0]], jo * 128,
                                          [[E, 128], [128 * E, NJ], [1, 128]])
                            lds[m % 2].inc(g.dma_start(
                                wst[:, (m % 2) * 1024:(m % 2 + 1) * 1024], src))

                @blk.scalar
                def _(s):
                    for op, a1, a2 in sched:
                        if op == "exp":
                            p = a1
                            s.wait_ge(k.ve.sem, ve_mask[p])
                            if p >= 1:
                                s.wait_ge(k.ve.sem, ve_fin[p - 1])
                            sl4 = sl4_of(p)
                            for qq in range(4):
                                Kq = 16 * (qq + 1)
                                A(k, s, lambda sl4=sl4, qq=qq, Kq=Kq:
                                  s.activation(
                                    E4[:, :, 16 * qq:16 * (qq + 1), 0:Kq],
                                    sl4[:, :, 16 * qq:16 * (qq + 1), 0:Kq],
                                    AF.Exp))
                        elif op == "gate":
                            s.wait_ge(k.ve.sem, ve_fin[8 * a1 + 7])
                        elif op == "e1":
                            tt, m = a1, a2
                            mi, jo = m // 8, m % 8
                            nm, dstt, bn, sc = mats[mi]
                            s.wait_ge(k.pe.sem, mm_done[(tt, m)])
                            pu = prevu(tt, m)
                            if pu is not None:
                                s.wait_ge(k.pe.sem, tp_done[pu])
                            A(k, s, lambda jo=jo, bn=bn, sc=sc, m=m:
                              s.activation(
                                qstg[:, (m % 2) * 512:(m % 2) * 512 + 128],
                                pqs[m % 2][:, 0:128], AF.Identity, scale=sc,
                                bias=k.pmv[:, k.PM[bn] + jo:k.PM[bn] + jo + 1]))
                        else:
                            tt, m = a1, a2
                            mi, jo = m // 8, m % 8
                            dstt = mats[mi][1]
                            s.wait_ge(k.pe.sem, tp_done[(tt, m)])
                            A(k, s, lambda dstt=dstt, jo=jo, tt=tt, m=m:
                              s.copy(
                                dstt[:, tt * E + jo * 128:
                                     tt * E + (jo + 1) * 128],
                                pxs[m % 2][:, :]))
                    assert k.ae.n == acnt

            with nc.Block() as blk:
                ve0 = k.ve.n
                ae0 = k.ae.n
                pe_rec = {}

                @blk.tensor
                def _(t):
                    for n in range(32):
                        jo, tt = n // 4, n % 4
                        if n >= 2:
                            pn = n - 2
                            if pn % 2 == 0:
                                t.wait_ge(k.ae.sem, ae0 + pn // 2 + 1)
                            else:
                                t.wait_ge(k.ve.sem, ve0 + pn // 2 + 1)
                        k.pe.inc(t.transpose(
                            pxs[n % 2][:, :],
                            attn_tm[:, tt * E + jo * 128:tt * E + (jo + 1) * 128],
                            k.idb[:, :]))
                        pe_rec[n] = k.pe.n

                @blk.scalar
                def _(s):
                    for n in range(32):
                        jo, tt = n // 4, n % 4
                        if n % 2 == 0:
                            s.wait_ge(k.pe.sem, pe_rec[n])
                            A(k, s, lambda jo=jo, tt=tt, n=n: s.copy(
                                attnT[:, jo * 512 + tt * 128:jo * 512 + (tt + 1) * 128],
                                pxs[n % 2][:, :]))

                @blk.vector
                def _(v):
                    for n in range(32):
                        jo, tt = n // 4, n % 4
                        if n % 2 == 1:
                            v.wait_ge(k.pe.sem, pe_rec[n])
                            V(k, v, lambda jo=jo, tt=tt, n=n: v.tensor_copy(
                                attnT[:, jo * 512 + tt * 128:jo * 512 + (tt + 1) * 128],
                                pxs[n % 2][:, :]))

            with nc.Block() as blk:
                lds = [k.gsp[2], k.gsp[3]]
                lds0 = [lds[0].n, lds[1].n]
                ve0 = k.ve.n
                pe_rec = {}
                pe_rec_pre2 = k.pe.n

                @blk.gpsimd
                def _(g):
                    for jE in range(NJ):
                        if jE >= 2:
                            g.wait_ge(k.pe.sem, pe_rec_pre2 + 8 * (jE - 1))
                        src = bass.AP(d["Wo1"], jE * 128,
                                      [[E, 128], [128 * E, NJ], [1, 128]])
                        lds[jE % 2].inc(g.dma_start(
                            wst[:, (jE % 2) * 1024:(jE % 2 + 1) * 1024], src))

                @blk.tensor
                def _(t):
                    for jE in range(NJ):
                        t.wait_ge(lds[jE % 2].sem, lds0[jE % 2] + 16 * (jE // 2 + 1))
                        if jE >= 2:
                            t.wait_ge(k.ve.sem, ve0 + 2 * (jE - 1))
                        for j in range(NJ):
                            k.pe.inc(t.matmul(
                                pws[jE % 2][:, :],
                                wst[:, (jE % 2) * 1024 + j * 128:
                                    (jE % 2) * 1024 + (j + 1) * 128],
                                attnT[:, j * 512:(j + 1) * 512],
                                start=(j == 0), stop=(j == NJ - 1)))
                        pe_rec[jE] = k.pe.n

                @blk.vector
                def _(v):
                    for jE in range(NJ):
                        v.wait_ge(k.pe.sem, pe_rec[jE])
                        xsl = k.xT[:, jE * TOKC + ts * 512:jE * TOKC + (ts + 1) * 512]
                        V(k, v, lambda jE=jE, xsl=xsl: v.scalar_tensor_tensor(
                            tmpo7, pws[jE % 2][:, :],
                            k.pmv[:, k.PM["bo"] + jE:k.PM["bo"] + jE + 1],
                            xsl, op0=ALU.add, op1=ALU.add))
                        V(k, v, lambda jE=jE, xsl=xsl: v.tensor_copy(xsl, tmpo7))

    if k.dump is not None:
        dmp(k, lambda: k.xT[:, 0:256])
        dmp(k, lambda: lg_tm[:, 0:16])
        dmp(k, lambda: k.rtrw[:, 0:8])
        dmp(k, lambda: k.rtri[:, 0:8])
        dmp(k, lambda: cntr[:1, :], rows=1)
    k.p7loc = loc


def p8_moe(k):
    nc = k.nc
    d = k.d
    loc = contextlib.ExitStack()
    smu = k.salloc(loc, "smu", (128, 32), U32)
    k.tokpm = k.salloc(loc, "tokpm", (128, 16))
    sepm = k.salloc(loc, "sepm", (128, 16), U32)
    sopm = k.salloc(loc, "sopm", (128, 16), U32)
    w1c = k.rtrw

    dsc = contextlib.ExitStack()
    cnt8 = k.salloc(dsc, "cnt8", (NE, NE))
    cntT = k.salloc(dsc, "cntT", (NE, NE))
    OHp = k.salloc(dsc, "OHp", (NE, NCH))
    OHc = k.salloc(dsc, "OHc", (NE, NCH))
    base8 = k.salloc(dsc, "base8", (NE, 4))
    posr = k.salloc(dsc, "posr", (1, NCH))
    fer = k.salloc(dsc, "fer", (1, NCH))
    so2 = k.salloc(dsc, "so2", (1, NCH))
    vmr = k.salloc(dsc, "vmr", (1, NCH))
    sentb = k.salloc(dsc, "sentb", (128, 64))
    iotac = k.salloc(dsc, "iotac", (NE, 1))
    smf = k.salloc(dsc, "smf", (128, 32))
    cmhl = k.salloc(dsc, "cmhl", (1, 2048))

    with (
        nc.psum_tensor("p8a", [NE, NE], F32) as p8a,
        nc.psum_tensor("p8b", [NE, 128], F32) as p8t,
        nc.psum_tensor("p8pos", [1, 512], F32) as p8pos,
        nc.psum_tensor("p8fe", [1, 512], F32) as p8fe,
        nc.psum_tensor("p8i", [NE, 1], F32) as p8i,
        nc.psum_tensor("p8c", [128, 1], F32) as p8c,
    ):
        with nc.Block() as blk:
            @blk.gpsimd
            def _(g):
                wt(g, k.agcnt)
                k.gd.inc(g.dma_start(cnt8[:, :], d["cnt_out"].ap()))
                k.gd.inc(g.dma_start(k.tokpm[:, :], d["tokid_pm"].ap()))
                k.gd.inc(g.dma_start(k.iota8_row[:1, :], d["iota8"].ap()[0:1, :]))
            gm = k.gd.mark()

            @blk.tensor
            def _(t):
                wt(t, gm)
                k.pe.inc(t.transpose(p8a[:, :], cnt8[:, :], k.idf[:NE, :NE]))
                k.pe.inc(t.transpose(p8i[:, :], k.iota8_row[:1, :], k.idf[:1, :1]))
            pm = k.pe.mark()

            @blk.vector
            def _(v):
                wt(v, pm)
                V(k, v, lambda: v.tensor_copy(cntT[:, :], p8a[:, :]))
                V(k, v, lambda: v.tensor_tensor_scan(
                    OHc[:, 0:NE], cntT[:, :], cntT[:, :], 0.0,
                    op0=ALU.add, op1=ALU.bypass))
                V(k, v, lambda: v.tensor_sub(cntT[:, :], OHc[:, 0:NE], cntT[:, :]))
                V(k, v, lambda: v.tensor_copy(iotac[:, 0:1], p8i[:, 0:1]))

        if getattr(k, "p8stop", 99) <= 1:
            dsc.close(); loc.close(); return

        def basesel(cid):
            with nc.Block() as blk:
                @blk.vector
                def _(v, cid=cid):
                    V(k, v, lambda: v.tensor_scalar(
                        base8[:, 0:1], cntT[:, cid:cid + 1], -1.0, None,
                        op0=ALU.add))
        core_switch(k, basesel)

        if getattr(k, "p8stop", 99) <= 2:
            dsc.close(); loc.close(); return

        with nc.Block() as blk:
            pe_rec = {}
            ve0 = k.ve.n

            @blk.tensor
            def _(t):
                for n in range(2 * NT):
                    i, which = n // 2, n % 2
                    src = (k.oh1 if which == 0 else k.oh2)[:, i * NE:(i + 1) * NE]
                    if n >= 1:
                        t.wait_ge(k.ve.sem, ve0 + n)
                    k.pe.inc(t.transpose(p8t[:, :], src, k.idf[:, :]))
                    pe_rec[n] = k.pe.n

            @blk.vector
            def _(v):
                for n in range(2 * NT):
                    i, which = n // 2, n % 2
                    v.wait_ge(k.pe.sem, pe_rec[n])
                    V(k, v, lambda i=i, which=which: v.tensor_copy(
                        OHp[0:NE, :].rearrange("p (a b) -> p a b", b=2)
                        [:, i * 128:(i + 1) * 128, which:which + 1],
                        p8t[0:NE, :].unsqueeze(2)))

        if getattr(k, "p8stop", 99) <= 3:
            dsc.close(); loc.close(); return

        with nc.Block() as blk:
            ve_rec = {}
            pe0 = k.pe.n

            @blk.vector
            def _(v):
                V(k, v, lambda: v.tensor_tensor_scan(
                    OHc[:, :], OHp[:, :], OHp[:, :], 0.0,
                    op0=ALU.add, op1=ALU.bypass))
                V(k, v, lambda: v.tensor_scalar(
                    OHc[:, :], OHc[:, :], base8[:, 0:1], None, op0=ALU.add))
                V(k, v, lambda: v.tensor_mul(OHc[:, :], OHc[:, :], OHp[:, :]))
                ve_rec[0] = k.ve.n
                for ch in range(NCH // 512):
                    v.wait_ge(k.pe.sem, pe0 + 2 * ch + 2)
                    V(k, v, lambda ch=ch: v.tensor_copy(
                        posr[:1, ch * 512:(ch + 1) * 512], p8pos[:1, :]))
                    V(k, v, lambda ch=ch: v.tensor_copy(
                        fer[:1, ch * 512:(ch + 1) * 512], p8fe[:1, :]))
                V(k, v, lambda: v.tensor_scalar(
                    vmr[:1, :], posr[:1, :], float(CAP - 1), None, op0=ALU.is_gt))
                V(k, v, lambda: v.tensor_scalar(
                    vmr[:1, :], vmr[:1, :], SENT, None, op0=ALU.mult))
                V(k, v, lambda: v.scalar_tensor_tensor(
                    so2[:1, :], fer[:1, :], float(CAP), posr[:1, :],
                    op0=ALU.mult, op1=ALU.add))
                V(k, v, lambda: v.tensor_add(so2[:1, :], so2[:1, :], vmr[:1, :]))
                V(k, v, lambda: v.memset(sentb[:, :], SENT))

            @blk.tensor
            def _(t):
                wt(t, (k.ve.sem, ve_rec[0]))
                for ch in range(NCH // 512):
                    if ch >= 1:
                        t.wait_ge(k.ve.sem, ve_rec[0] + 2 * ch)
                    k.pe.inc(t.matmul(p8pos[:, :], k.onesf[0:NE, 0:1],
                                      OHc[:, ch * 512:(ch + 1) * 512],
                                      start=True, stop=True))
                    k.pe.inc(t.matmul(p8fe[:, :], iotac[:, 0:1],
                                      OHp[:, ch * 512:(ch + 1) * 512],
                                      start=True, stop=True))

        if getattr(k, "p8stop", 99) <= 4:
            dsc.close(); loc.close(); return

        with nc.Block() as blk:
            pe_rec = {}
            ve0 = k.ve.n

            @blk.tensor
            def _(t):
                n = 0
                for which in range(2):
                    for gch in range(16):
                        if n >= 1:
                            t.wait_ge(k.ve.sem, ve0 + n + (1 if n >= 17 else 0))
                        ap = so2[:1, :].rearrange("p (g c w) -> p g c w", c=128, w=2)\
                            [:, gch, :, which]
                        k.pe.inc(t.transpose(p8c[:, :], ap, k.idf[:1, :1]))
                        pe_rec[n] = k.pe.n
                        n += 1

            @blk.vector
            def _(v):
                n = 0
                for which in range(2):
                    dst = sepm if which == 0 else sopm
                    for gch in range(16):
                        v.wait_ge(k.pe.sem, pe_rec[n])
                        V(k, v, lambda dst=dst, gch=gch: v.tensor_copy(
                            smf[:, gch:gch + 1], p8c[:, :]))
                        n += 1
                    V(k, v, lambda dst=dst: v.tensor_copy(dst[:, :], smf[:, 0:16]))

        if getattr(k, "p8stop", 99) <= 5:
            if k.dump is not None:
                k.dbg_col = 0
                dmp(k, lambda: posr[:1, 0:512], rows=1)
                dmp(k, lambda: fer[:1, 0:512], rows=1)
                dmp(k, lambda: OHp[0:NE, 0:512], rows=NE)
                dmp(k, lambda: OHc[0:NE, 0:256], rows=NE)
                dmp(k, lambda: cntT[0:NE, 0:NE], rows=NE)
                dmp(k, lambda: base8[0:NE, 0:1], rows=NE)

            dsc.close(); loc.close(); return

        with nc.Block() as blk:
            vem = k.ve.mark()

            @blk.gpsimd
            def _(g):
                wt(g, vem)
                breg = g.alloc_register("bnd8")
                g.reg_mov(breg, NC * CAP - 1)
                for rep in range(4):
                    k.gd.inc(g.dma_start(
                        bass.AP(d["cm_in"], rep * 8192, [[64, 128], [1, 64]]),
                        sentb[:, :]))
                g.wait_ge(k.gd.sem, k.gd.n)
                for which, idx in ((0, sepm), (1, sopm)):
                    for gch in range(16):
                        k.gd.inc(g.indirect_dma_start(
                            d["cm_in"].ap(),
                            bass.IndirectOffsetOnAxis(idx[:, gch:gch + 1], 1),
                            k.tokpm[:, gch:gch + 1], None,
                            bounds_check=breg, oob_is_err=False))
                g.wait_ge(k.gd.sem, k.gd.n)
                k.armn = coll(k, g, "AllReduce", ALU.min, d["cm_in"], d["cm_out1"])

        def ownersel(cid):
            with nc.Block() as blk:
                @blk.sync
                def _(s, cid=cid):
                    wt(s, k.armn)
                    k.hd.inc(s.dma_start(
                        cmhl[:1, :],
                        bass.AP(d["cm_out1"], cid * CAP, [[1, 2048]])))
                    s.wait_ge(k.hd.sem, k.hd.n)
                hm = k.hd.mark()
                pe_rec = {}
                ve0 = k.ve.n

                @blk.tensor
                def _(t, cid=cid):
                    wt(t, hm)
                    for gch in range(16):
                        if gch >= 1:
                            t.wait_ge(k.ve.sem, ve0 + gch)
                        k.pe.inc(t.transpose(
                            p8c[:, :], cmhl[:1, gch * 128:(gch + 1) * 128],
                            k.idf[:1, :1]))
                        pe_rec[gch] = k.pe.n

                @blk.vector
                def _(v, cid=cid):
                    for gch in range(16):
                        v.wait_ge(k.pe.sem, pe_rec[gch])
                        V(k, v, lambda gch=gch: v.tensor_copy(
                            smf[:, gch:gch + 1], p8c[:, :]))
        core_switch(k, ownersel)

        def ownersel2(cid):
            with nc.Block() as blk:
                @blk.sync
                def _(s, cid=cid):
                    k.hd.inc(s.dma_start(
                        cmhl[:1, :],
                        bass.AP(d["cm_out1"], cid * CAP + 2048, [[1, 2048]])))
                    s.wait_ge(k.hd.sem, k.hd.n)
                hm = k.hd.mark()
                pe_rec = {}
                ve0 = k.ve.n

                @blk.tensor
                def _(t, cid=cid):
                    wt(t, hm)
                    for gch in range(16):
                        if gch >= 1:
                            t.wait_ge(k.ve.sem, ve0 + gch)
                        k.pe.inc(t.transpose(
                            p8c[:, :], cmhl[:1, gch * 128:(gch + 1) * 128],
                            k.idf[:1, :1]))
                        pe_rec[gch] = k.pe.n

                @blk.vector
                def _(v, cid=cid):
                    for gch in range(16):
                        v.wait_ge(k.pe.sem, pe_rec[gch])
                        V(k, v, lambda gch=gch: v.tensor_copy(
                            smf[:, 16 + gch:16 + gch + 1], p8c[:, :]))
        core_switch(k, ownersel2)

        with nc.Block() as blk:
            @blk.vector
            def _(v):
                V(k, v, lambda: v.tensor_copy(smu[:, :], smf[:, :]))
    dsc.close()

    xsc = contextlib.ExitStack()
    exp_xT = k.salloc(xsc, "exp_xT", (128, NJ * CAP), BF16)
    esc = contextlib.ExitStack()
    exprow = k.salloc(esc, "exprow", (128, E), BF16)
    with nc.psum_tensor("p8x", [128, E], BF16) as p8x:
        with nc.Block() as blk:
            gsw = [k.gsp[0], k.gsp[1]]
            gsw0 = [gsw[0].n, gsw[1].n]
            ve0 = k.ve.n
            pe_rec = {}
            pe_rec_pre = k.pe.n

            @blk.vector
            def _(v):
                for g_ in range(32):
                    if g_ >= 1:
                        v.wait_ge(k.pe.sem, pe_rec_pre + 8 * g_)
                    V(k, v, lambda: v.memset(exprow[:, :], 0.0))
                    v.wait_ge(k.pe.sem, pe_rec_pre + 8 * (g_ + 1))
                    V(k, v, lambda g_=g_: v.tensor_copy(
                        exp_xT[:, :].rearrange("p (j t) -> p j t", t=CAP)
                        [:, :, g_ * 128:(g_ + 1) * 128],
                        p8x[:, :].rearrange("p (j t) -> p j t", t=128)))

            @blk.gpsimd
            def _(g):
                wt(g, k.agxn1)
                breg = g.alloc_register("bnd9")
                g.reg_mov(breg, NC * TOKC - 1)
                for g_ in range(32):
                    g.wait_ge(k.ve.sem, ve0 + 2 * g_ + 1)
                    gsw[g_ % 2].inc(g.indirect_dma_start(
                        exprow[:, :], None,
                        bass.AP(d["xn1_out"], 0, [[E, NC * TOKC], [1, E]]),
                        bass.IndirectOffsetOnAxis(smu[:, g_:g_ + 1], 0),
                        bounds_check=breg, oob_is_err=False))

            @blk.tensor
            def _(t):
                for g_ in range(32):
                    t.wait_ge(gsw[g_ % 2].sem, gsw0[g_ % 2] + 16 * (g_ // 2 + 1))
                    if g_ >= 1:
                        t.wait_ge(k.ve.sem, ve0 + 2 * g_)
                    for j in range(NJ):
                        k.pe.inc(t.transpose(
                            p8x[:, j * 128:(j + 1) * 128],
                            exprow[:, j * 128:(j + 1) * 128], k.idb[:, :]))
                    pe_rec[g_] = k.pe.n
    esc.close()

    fsc = contextlib.ExitStack()
    wste = k.salloc(fsc, "wste", (128, 2 * 12 * 128), BF16)
    h1sb = k.salloc(fsc, "h1sb", (128, 24 * 512), BF16)
    asil = k.salloc(fsc, "asil", (128, 512), BF16)
    hp = k.salloc(fsc, "hp", (128, 12 * 512), BF16)
    outsb = k.salloc(fsc, "outsb", (128, 2 * 512), BF16)
    eostg = k.salloc(fsc, "eostg", (128, 4 * E), BF16)
    with (
        nc.psum_tensor("p8h0", [128, 512], F32) as ph0,
        nc.psum_tensor("p8h1", [128, 512], F32) as ph1,
        nc.psum_tensor("p8e", [128, 128], BF16) as p8e,
    ):
        phs = [ph0, ph1]
        for st_ in range(8):
            ssl = slice(st_ * 512, (st_ + 1) * 512)
            with nc.Block() as blk:
                gsw = [k.gsp[2], k.gsp[3]]
                gsw0 = [gsw[0].n, gsw[1].n]
                ae0 = k.ae.n
                pe_rec = {}
                pe_pre = k.pe.n

                @blk.gpsimd
                def _(g):
                    for fc in range(24):
                        if fc >= 2:
                            g.wait_ge(k.pe.sem, pe_pre + 8 * (fc - 1))
                        src = bass.AP(d["cfc"], fc * 128,
                                      [[DFF, 128], [128 * DFF, NJ], [1, 128]])
                        gsw[fc % 2].inc(g.dma_start(
                            wste[:, (fc % 2) * 1024:(fc % 2 + 1) * 1024], src))

                @blk.tensor
                def _(t):
                    for fc in range(24):
                        t.wait_ge(gsw[fc % 2].sem, gsw0[fc % 2] + 16 * (fc // 2 + 1))
                        if fc >= 2:
                            t.wait_ge(k.ae.sem, ae0 + fc - 1)
                        for j in range(NJ):
                            k.pe.inc(t.matmul(
                                phs[fc % 2][:, :],
                                wste[:, (fc % 2) * 1024 + j * 128:
                                     (fc % 2) * 1024 + (j + 1) * 128],
                                exp_xT[:, j * CAP + st_ * 512:j * CAP + (st_ + 1) * 512],
                                start=(j == 0), stop=(j == NJ - 1)))
                        pe_rec[fc] = k.pe.n

                @blk.scalar
                def _(s):
                    for fc in range(24):
                        s.wait_ge(k.pe.sem, pe_rec[fc])
                        A(k, s, lambda fc=fc: s.copy(
                            h1sb[:, fc * 512:(fc + 1) * 512], phs[fc % 2][:, :]))

            with nc.Block() as blk:
                ae0 = k.ae.n
                ve_pre = k.ve.n

                @blk.scalar
                def _(s):
                    for fc in range(12):
                        if fc >= 1:
                            s.wait_ge(k.ve.sem, ve_pre + fc)
                        A(k, s, lambda fc=fc: s.activation(
                            asil[:, :], h1sb[:, fc * 512:(fc + 1) * 512], AF.Silu))

                @blk.vector
                def _(v):
                    for fc in range(12):
                        v.wait_ge(k.ae.sem, ae0 + fc + 1)
                        V(k, v, lambda fc=fc: v.tensor_mul(
                            hp[:, fc * 512:(fc + 1) * 512], asil[:, :],
                            h1sb[:, (12 + fc) * 512:(13 + fc) * 512]))

            with nc.Block() as blk:
                gsw = [k.gsp[2], k.gsp[3]]
                gsw0 = [gsw[0].n, gsw[1].n]
                ae0 = k.ae.n
                ve0 = k.ve.n
                pe_rec = {}
                petp = {}
                pe_pre = k.pe.n
                hdm0 = k.hd.mark()

                @blk.gpsimd
                def _(g):
                    for jE in range(NJ):
                        if jE >= 2:
                            g.wait_ge(k.pe.sem, pe_pre + 16 * (jE - 1))
                        src = bass.AP(d["cpj"], jE * 128,
                                      [[E, 128], [128 * E, 12], [1, 128]])
                        gsw[jE % 2].inc(g.dma_start(
                            wste[:, (jE % 2) * 1536:(jE % 2) * 1536 + 1536], src))

                @blk.tensor
                def _(t):
                    for jE in range(NJ):
                        t.wait_ge(gsw[jE % 2].sem, gsw0[jE % 2] + 16 * (jE // 2 + 1))
                        if jE >= 2:
                            t.wait_ge(k.ae.sem, ae0 + 3 * (jE - 2) + 1)
                        for fc in range(12):
                            k.pe.inc(t.matmul(
                                phs[jE % 2][:, :],
                                wste[:, (jE % 2) * 1536 + fc * 128:
                                     (jE % 2) * 1536 + (fc + 1) * 128],
                                hp[:, fc * 512:(fc + 1) * 512],
                                start=(fc == 0), stop=(fc == 11)))
                        pe_rec[jE] = k.pe.n
                        t.wait_ge(k.ae.sem, ae0 + 3 * jE + 1)
                        for tt in range(4):
                            gidx = jE * 4 + tt
                            if gidx >= 1:
                                pg = gidx - 1
                                pj, ptt = pg // 4, pg % 4
                                if ptt % 2 == 0:
                                    t.wait_ge(k.ae.sem, ae0 + 3 * pj + 1 + ptt // 2 + 1)
                                else:
                                    t.wait_ge(k.ve.sem, ve0 + 2 * pj + (ptt - 1) // 2 + 1)
                            k.pe.inc(t.transpose(
                                p8e[:, :],
                                outsb[:, (jE % 2) * 512 + tt * 128:
                                      (jE % 2) * 512 + (tt + 1) * 128],
                                k.idb[:, :]))
                            petp[(jE, tt)] = k.pe.n

                @blk.scalar
                def _(s):
                    wt(s, hdm0)
                    for jE in range(NJ):
                        s.wait_ge(k.pe.sem, pe_rec[jE])
                        A(k, s, lambda jE=jE: s.copy(
                            outsb[:, (jE % 2) * 512:(jE % 2 + 1) * 512],
                            phs[jE % 2][:, :]))
                        for tt in (0, 2):
                            s.wait_ge(k.pe.sem, petp[(jE, tt)])
                            A(k, s, lambda jE=jE, tt=tt: s.copy(
                                eostg[:, tt * E + jE * 128:tt * E + (jE + 1) * 128],
                                p8e[:, :]))

                @blk.vector
                def _(v):
                    wt(v, hdm0)
                    for jE in range(NJ):
                        for tt in (1, 3):
                            v.wait_ge(k.pe.sem, petp[(jE, tt)])
                            V(k, v, lambda jE=jE, tt=tt: v.tensor_copy(
                                eostg[:, tt * E + jE * 128:tt * E + (jE + 1) * 128],
                                p8e[:, :]))
                vem = k.ve.mark()
                aem = k.ae.mark()

                @blk.sync
                def _(s):
                    wt(s, vem)
                    wt(s, aem)
                    for tt in range(4):
                        k.hd.inc(s.dma_start(
                            d["eo_in"].ap()[st_ * 512 + tt * 128:
                                            st_ * 512 + (tt + 1) * 128, :],
                            eostg[:, tt * E:(tt + 1) * E]))
    fsc.close()
    xsc.close()


    csc = contextlib.ExitStack()
    geb = k.salloc(csc, "geb", (128, E), BF16)
    gob = k.salloc(csc, "gob", (128, E), BF16)
    tmpf = k.salloc(csc, "tmpf", (128, E))
    moeb = k.salloc(csc, "moeb", (128, E))
    ysb = k.salloc(csc, "ysb", (128, 2 * E))
    with nc.psum_tensor("p8y", [128, E], F32) as p8y:
        with nc.Block() as blk:
            gsw = [k.gsp[0], k.gsp[1]]
            gsw0 = [gsw[0].n, gsw[1].n]
            yst = [k.sp[0], k.sp[1]]
            yst0 = [yst[0].n, yst[1].n]
            ve0 = k.ve.n
            pe_rec = {}
            pe_pre = k.pe.n

            @blk.vector
            def _(v):
                for g_ in range(NT):
                    if g_ >= 1:
                        v.wait_ge(gsw[0].sem, gsw0[0] + 16 * g_)
                        v.wait_ge(gsw[1].sem, gsw0[1] + 16 * g_)
                    V(k, v, lambda: v.memset(geb[:, :], 0.0))
                    V(k, v, lambda: v.memset(gob[:, :], 0.0))
                    v.wait_ge(gsw[0].sem, gsw0[0] + 16 * (g_ + 1))
                    v.wait_ge(gsw[1].sem, gsw0[1] + 16 * (g_ + 1))
                    V(k, v, lambda g_=g_: v.tensor_scalar(
                        tmpf[:, :], geb[:, :], k.rtrw[:, 2 * g_:2 * g_ + 1], None,
                        op0=ALU.mult))
                    V(k, v, lambda g_=g_: v.scalar_tensor_tensor(
                        moeb[:, :], gob[:, :], k.rtrw[:, 2 * g_ + 1:2 * g_ + 2],
                        tmpf[:, :], op0=ALU.mult, op1=ALU.add))
                    v.wait_ge(k.pe.sem, pe_pre + 8 * (g_ + 1))
                    if g_ >= 2:
                        v.wait_ge(yst[g_ % 2].sem, yst0[g_ % 2] + 16 * (g_ // 2))
                    V(k, v, lambda g_=g_: v.tensor_add(
                        ysb[:, (g_ % 2) * E:(g_ % 2 + 1) * E], p8y[:, :], moeb[:, :]))

            @blk.gpsimd
            def _(g):
                g.wait_ge(k.hd.sem, k.hd.n)
                k.ageo = coll(k, g, "AllGather", ALU.bypass, d["eo_in"],
                              d["eo_out"])
                wt(g, k.ageo)
                breg = g.alloc_register("bnd10")
                g.reg_mov(breg, NC * CAP - 1)
                for g_ in range(NT):
                    g.wait_ge(k.ve.sem, ve0 + 5 * g_ + 2)
                    gsw[0].inc(g.indirect_dma_start(
                        geb[:, :], None,
                        bass.AP(d["eo_out"], 0, [[E, NC * CAP], [1, E]]),
                        bass.IndirectOffsetOnAxis(sepm[:, g_:g_ + 1], 0),
                        bounds_check=breg, oob_is_err=False))
                    gsw[1].inc(g.indirect_dma_start(
                        gob[:, :], None,
                        bass.AP(d["eo_out"], 0, [[E, NC * CAP], [1, E]]),
                        bass.IndirectOffsetOnAxis(sopm[:, g_:g_ + 1], 0),
                        bounds_check=breg, oob_is_err=False))

            @blk.tensor
            def _(t):
                for g_ in range(NT):
                    if g_ >= 1:
                        t.wait_ge(k.ve.sem, ve0 + 5 * g_)
                    for j in range(NJ):
                        k.pe.inc(t.transpose(
                            p8y[:, j * 128:(j + 1) * 128],
                            k.xT[:, j * TOKC + g_ * 128:j * TOKC + (g_ + 1) * 128],
                            k.idf[:, :]))

            @blk.sync
            def _(s):
                for g_ in range(NT):
                    s.wait_ge(k.ve.sem, ve0 + 5 * (g_ + 1))
                    yst[g_ % 2].inc(s.dma_start(
                        d["y"].ap()[g_ * 128:(g_ + 1) * 128, :],
                        ysb[:, (g_ % 2) * E:(g_ % 2 + 1) * E]))
    csc.close()
    loc.close()


def _consts(c):
    ident = np.eye(128, dtype=np.float32)
    m0 = np.zeros((128, 4 * 512), np.float32)
    kp = np.arange(128)[:, None]
    qq = np.arange(512)[None, :]
    for vvar in range(4):
        m0[:, vvar * 512:(vvar + 1) * 512] = (kp <= qq - vvar * 128)
    qq1 = np.arange(C1)[:, None]
    kk1 = np.arange(C1)[None, :]
    m1 = np.broadcast_to(
        np.where(kk1 > qq1, np.float32(-1e30), np.float32(0.0)).reshape(1, -1),
        (128, C1 * C1)).copy()
    iota8 = np.broadcast_to(np.arange(NE, dtype=np.float32), (128, NE)).copy()
    ch = np.arange(128)[:, None] + 128 * np.arange(16)[None, :]
    tokid = (2048 * c + ch).astype(np.float32)
    return {
        "identf": ident,
        "identb": ident.astype(BFD),
        "maskL0": m0.astype(BFD),
        "maskL1": m1.astype(BFD),
        "iota8": iota8,
        "tokid_pm": tokid,
        "onesf": np.ones((128, 128), np.float32),
    }


def make_inmaps(inputs):
    f = {n: np.ascontiguousarray(np.asarray(v, dtype=np.float32))
         for n, v in inputs.items()}
    x = f["L0_attn_x"].reshape(B * S, E)
    gw1T = np.ascontiguousarray(f["L0_ffn_gate_w1"].T)
    gw2T = np.ascontiguousarray(f["L0_ffn_gate_w2"].T)
    in_maps = []
    for c in range(NC):
        fsl = slice(c * FSL, (c + 1) * FSL)
        hsl = slice(c * HSL, (c + 1) * HSL)
        m = {
            "x": np.ascontiguousarray(x[c * TOKC:(c + 1) * TOKC]),
            "wq0": f["L0_attn_w_q"], "wk0": f["L0_attn_w_k"],
            "wv1": f["L0_attn_w_v1"], "wv2": f["L0_attn_w_v2"],
            "gw1T": gw1T, "gb1": f["L0_ffn_gate_b1"].reshape(1, E),
            "gw2T": gw2T, "gb2": f["L0_ffn_gate_b2"].reshape(1, NE),
            "W1T_s": np.ascontiguousarray(f["L0_ffn_W1"][fsl].T),
            "TW1T_s": np.ascontiguousarray(
                f["L0_ffn_TW1"][:, fsl, :].transpose(0, 2, 1)),
            "W2T_s": np.ascontiguousarray(f["L0_ffn_W2"][hsl].T),
            "TW2T_s": np.ascontiguousarray(
                f["L0_ffn_TW2"][:, hsl, :].transpose(0, 2, 1)),
            "b1": f["L0_ffn_b1"].reshape(1, F0), "Tb1": f["L0_ffn_Tb1"],
            "b2": f["L0_ffn_b2"].reshape(1, E), "Tb2": f["L0_ffn_Tb2"],
            "Wq1": f["L1_attn_Wq"], "Wk1": f["L1_attn_Wk"],
            "Wv1_": f["L1_attn_Wv"], "Wo1": f["L1_attn_Wo"],
            "bq": f["L1_attn_bq"].reshape(1, E), "bk": f["L1_attn_bk"].reshape(1, E),
            "bv": f["L1_attn_bv"].reshape(1, E), "bo": f["L1_attn_bo"].reshape(1, E),
            "wgate": f["L1_ffn_w_gate"],
            "cfc": np.ascontiguousarray(f["L1_ffn_c_fc"][c]),
            "cpj": np.ascontiguousarray(f["L1_ffn_c_proj"][c]),
        }
        m.update(_consts(c))
        in_maps.append(m)
    return in_maps


def kernel(**inputs):
    nc, _ = build_nc()
    in_maps = make_inmaps(inputs)
    res = run_bass_kernel_spmd(nc, in_maps, core_ids=list(range(NC)))
    outs = [res.results[c]["y"] for c in range(NC)]
    return np.concatenate(outs, axis=0).reshape(B, S, E).astype(np.float32)
